# version 02
# baseline (speedup 1.0000x reference)
import math
import numpy as np
import concourse.bass as bass
import concourse.mybir as mybir
from concourse.bass_utils import run_bass_kernel_spmd

F32 = mybir.dt.float32
BF16 = mybir.dt.bfloat16
I32 = mybir.dt.int32
AF = mybir.ActivationFunctionType
ALU = mybir.AluOpType
AX = mybir.AxisListType

D = 1024
S = 4096
NCORES = 8
DCH = 8
TT = 512
NTT = S // TT

ENGINES = ("sync", "vector", "scalar", "tensor", "gpsimd")


class Sem:
    __slots__ = ("h", "n", "name")

    def __init__(self, handle, name):
        self.h = handle
        self.n = 0
        self.name = name


class Prog:
    def __init__(self, nc):
        self.nc = nc
        self.q = {e: [] for e in ENGINES}
        self.sems = {}

    def sem(self, name):
        if name not in self.sems:
            h = self.nc.semaphore(name).__enter__()
            self.sems[name] = Sem(h, name)
        return self.sems[name]

    def op(self, eng, fn, inc=None, inc_by=1):
        if inc is not None:
            inc.n += inc_by
            def g(fn=fn, h=inc.h, d=inc_by):
                fn(self.nc).then_inc(h, d)
            self.q[eng].append(g)
            return inc.n
        self.q[eng].append(lambda fn=fn: fn(self.nc))
        return None

    def wait(self, eng, sem, val):
        self.q[eng].append(
            lambda h=sem.h, v=val, e=eng: getattr(self.nc, e).wait_ge(h, v))

    def dma(self, eng, out, in_, inc, slow=False):
        return self.op(
            eng, lambda nc, o=out, i=in_, e=eng, s=slow: getattr(nc, e).dma_start(
                out=o, in_=i, allow_slow_non_contiguous=s),
            inc=inc, inc_by=16)

    def run_block(self):
        with self.nc.Block() as block:
            for e in ENGINES:
                ops = self.q[e]
                if not ops:
                    continue
                def make(eng_ops):
                    def _(engine):
                        for f in eng_ops:
                            f()
                    return _
                getattr(block, e)(make(ops))


class Ctx:
    def __init__(self, nc, stack, dr):
        self.nc = nc
        self.p = Prog(nc)
        self.dr = dr
        e = stack.enter_context
        self.RX = e(nc.sbuf_tensor("RX", [128, DCH, S], F32))
        self.FA = e(nc.sbuf_tensor("FA", [128, 8192], F32))
        self.FAi = self.FA.bitcast(I32)
        self.KB = e(nc.sbuf_tensor("KB", [128, 10240], BF16))
        self.WB = e(nc.sbuf_tensor("WB", [128, 4096], BF16))
        self.WS = e(nc.sbuf_tensor("WS", [128, 1024], F32))
        self.SMF = e(nc.sbuf_tensor("SMF", [128, 2048], F32))
        self.SMB = e(nc.sbuf_tensor("SMB", [128, 2048], BF16))
        self.IDT = e(nc.sbuf_tensor("IDT", [128, 128], BF16))
        self.ONEROW = e(nc.sbuf_tensor("ONEROW", [1, 128], F32))
        self.ONEROW16 = e(nc.sbuf_tensor("ONEROW16", [1, 128], BF16))
        self.ONECOL16 = e(nc.sbuf_tensor("ONECOL16", [128, 1], BF16))
        self.CAUS = e(nc.sbuf_tensor("CAUS", [128, 896], BF16))
        self.OMG = e(nc.sbuf_tensor("OMG", [128, 256], BF16))
        self.TRI8 = e(nc.sbuf_tensor("TRI8", [128, 64], F32))
        self.PSB = [e(nc.psum_tensor(f"PS{i}", [128, 512], F32)) for i in range(6)]
        self.PT = e(nc.psum_tensor("PT", [128, 1024], BF16))
        self.PR = e(nc.psum_tensor("PR", [128, 512], F32))
        self.ONES128 = e(nc.sbuf_tensor("ONES128", [128, 128], BF16))
        p = self.p
        self.sv = p.sem("sv")
        self.ss = p.sem("ss")
        self.st = p.sem("st_")
        self.sg = p.sem("sg")
        self.dl = p.sem("dl")
        self.ds = p.sem("ds")
        self.dg = p.sem("dg")
        self.so = p.sem("so")
        self.scr = nc.dram_tensor("scr", [63 * 1024 * 1024], F32, kind="Internal")
        self.scrB = nc.dram_tensor("scrB", [12 * 1024 * 1024], F32, kind="Internal")
        self.bank_free = {}

    def pe_guard(self, bank):
        if bank in self.bank_free:
            sem, val = self.bank_free.pop(bank)
            self.p.wait("tensor", sem, val)

    def bank_consumed(self, bank, sem, val):
        self.bank_free[bank] = (sem, val)

    def scr_ap(self, off_bytes, shape, dtype=F32):
        t = self.scr
        if dtype != F32:
            t = t.bitcast(dtype)
        elems = 1
        for s in shape:
            elems *= s
        esz = 4 if dtype == F32 else 2
        assert off_bytes % 4 == 0
        flat = t.ap()
        start = off_bytes // esz
        ap = bass.AP(flat.tensor, start, [[shape[1], shape[0]], [1, shape[1]]])
        return ap


def vsmall(c, fn):
    v = c.p.op("vector", fn, inc=c.sv)
    c.p.wait("vector", c.sv, v)
    return v


def ssmall(c, fn):
    v = c.p.op("scalar", fn, inc=c.ss)
    c.p.wait("scalar", c.ss, v)
    return v


def kb(c, start, shape):
    n = 1
    for s in shape[1:]:
        n *= s
    ap = c.KB[0:shape[0], start:start + n]
    if len(shape) == 3:
        ap = ap.rearrange("p (a b) -> p a b", b=shape[2])
    return ap


def fa(c, start, shape, dtype=F32):
    n = 1
    for s in shape[1:]:
        n *= s
    base = c.FA if dtype == F32 else c.FAi
    ap = base[0:shape[0], start:start + n]
    if len(shape) == 3:
        ap = ap.rearrange("p (a b) -> p a b", b=shape[2])
    return ap



def _t(a):
    return np.ascontiguousarray(np.asarray(a, np.float32).T)


def _c(a):
    return np.ascontiguousarray(np.asarray(a, np.float32))


def host_constants():
    c = {}
    c["c_ident"] = np.eye(128, dtype=np.float32)
    c["c_onerow"] = np.ones((1, 128), np.float32)
    c["c_onecol"] = np.ones((128, 1), np.float32)
    ki = np.arange(128)[:, None]
    u = np.arange(896)[None, :]
    c["c_causal"] = ((u - 384) >= ki).astype(np.float32)
    tri = (np.arange(8)[:, None] > np.arange(8)[None, :]).astype(np.float32)
    c["c_tri8"] = np.tile(tri.reshape(1, 64), (128, 1)).astype(np.float32)
    sels = np.zeros((16, 2048), np.float32)
    for k in range(16):
        sels[k, k * 128:(k + 1) * 128] = 1.0
    c["c_sels"] = sels
    exp8 = np.zeros((8, 512), np.float32)
    for hc in range(4):
        for pp in range(128):
            exp8[2 * hc + pp // 64, hc * 128 + pp] = 1.0
    c["c_exp8"] = exp8
    return c


def prep_core_inputs(core, inp, consts):
    m = dict(consts)
    m["x_T"] = _t(inp["L0_attn_a_X"][core])
    m["l0a_qT"] = _t(inp["L0_attn_a_q_proj"])
    m["l0a_kT"] = _t(inp["L0_attn_a_k_proj"])
    m["l0a_vT"] = _t(inp["L0_attn_a_v_proj"])
    m["l0a_oT"] = _t(inp["L0_attn_a_o_proj"])
    for w in ("Wq", "Wk", "Wv", "Wo"):
        m[f"l0b_{w}"] = _c(inp[f"L0_attn_b_{w}"])
    for b in ("bq", "bk", "bv", "bo"):
        m[f"l0b_{b}"] = _c(inp[f"L0_attn_b_{b}"]).reshape(8, 128).T.copy()
    m["ff_rw1"] = _c(inp["L0_ffn_router_w1"])
    m["ff_rb1"] = _c(inp["L0_ffn_router_b1"]).reshape(2, 128).T.copy()
    m["ff_rw2"] = _c(inp["L0_ffn_router_w2"])
    m["ff_rb2"] = np.tile(_c(inp["L0_ffn_router_b2"]).reshape(1, 8), (128, 1))
    m["ff_hw1"] = _c(inp["L0_ffn_halt_w1"])
    m["ff_hb1"] = _c(inp["L0_ffn_halt_b1"]).reshape(2, 128).T.copy()
    m["ff_hw2"] = _c(inp["L0_ffn_halt_w2"])
    m["ff_hb2"] = np.full((1, 1), float(np.asarray(inp["L0_ffn_halt_b2"]).reshape(-1)[0]), np.float32)
    ew1 = np.asarray(inp["L0_ffn_expert_w1"], np.float32)
    m["ff_ew1"] = np.ascontiguousarray(ew1.transpose(1, 0, 2).reshape(2048, 512))
    m["ff_eb1"] = _c(inp["L0_ffn_expert_b1"]).reshape(512).reshape(4, 128).T.copy()
    m["ff_ew2"] = _c(inp["L0_ffn_expert_w2"]).reshape(512, 3072)
    m["ff_eb2"] = _c(inp["L0_ffn_expert_b2"])
    m["ff_sw1"] = _c(inp["L0_ffn_shared_w1"])
    m["ff_sb1"] = _c(inp["L0_ffn_shared_b1"]).reshape(64, 1)
    m["ff_sw2"] = _c(inp["L0_ffn_shared_w2"])
    m["ff_sb2"] = _c(inp["L0_ffn_shared_b2"]).reshape(24, 128).T.copy()
    m["ff_swt"] = np.full((1, 1), float(np.asarray(inp["L0_ffn_shared_weight"])), np.float32)
    m["ff_qw1"] = _c(inp["L0_ffn_refine_w1"])
    m["ff_qb1"] = _c(inp["L0_ffn_refine_b1"]).reshape(16, 128).T.copy()
    m["ff_qw2"] = _c(inp["L0_ffn_refine_w2"])
    m["ff_qb2"] = _c(inp["L0_ffn_refine_b2"]).reshape(8, 128).T.copy()
    m["ff_iw"] = _c(inp["L0_ffn_integration_weight"]).reshape(8, 128).T.copy()
    m["ff_mu"] = _c(inp["L0_ffn_mu"]).reshape(8, 128).T.copy()
    m["l1a_Wq"] = _c(inp["L1_attn_a_Wq"])
    m["l1a_Wk"] = _c(inp["L1_attn_a_Wk"])
    m["l1a_Wv"] = _c(inp["L1_attn_a_Wv"])
    omega = np.asarray(inp["L1_attn_a_omega"], np.float32)
    m["l1a_omg"] = np.concatenate(
        [omega.T, np.full((64, 256), -0.5, np.float32)], axis=0)
    m["l1b_Win"] = _c(inp["L1_attn_b_W_in"])
    cw = _c(inp["L1_attn_b_conv_w"])
    m["l1b_cw"] = cw.reshape(16, 128, 3).transpose(1, 0, 2).reshape(128, 48).copy()
    m["l1b_cb"] = _c(inp["L1_attn_b_conv_b"]).reshape(16, 128).T.copy()
    m["l1b_Wst"] = _c(inp["L1_attn_b_W_state"])
    m["l1b_Wt"] = _c(inp["L1_attn_b_W_time"])
    m["l1b_bt"] = _c(inp["L1_attn_b_b_time"]).reshape(16, 128).T.copy()
    alog = _c(inp["L1_attn_b_A_log"])
    m["l1b_alog"] = np.ascontiguousarray(
        alog.reshape(16, 128, 16).transpose(1, 0, 2).reshape(128, 256))
    m["l1b_skip"] = _c(inp["L1_attn_b_skip"]).reshape(16, 128).T.copy()
    m["l1b_Wout"] = _c(inp["L1_attn_b_W_out"])
    m["l1f_w"] = _c(inp["L1_ffn_weight"])
    m["l1f_b"] = _c(inp["L1_ffn_bias"]).reshape(8, 128).T.copy()
    return m


INPUT_SHAPES = {
    "x_T": (D, S),
    "c_ident": (128, 128), "c_onerow": (1, 128), "c_onecol": (128, 1),
    "c_causal": (128, 896), "c_tri8": (128, 64), "c_sels": (16, 2048),
    "c_exp8": (8, 512),
    "l0a_qT": (D, D), "l0a_kT": (D, D), "l0a_vT": (D, D), "l0a_oT": (D, D),
    "l0b_Wq": (D, D), "l0b_Wk": (D, D), "l0b_Wv": (D, D), "l0b_Wo": (D, D),
    "l0b_bq": (128, 8), "l0b_bk": (128, 8), "l0b_bv": (128, 8), "l0b_bo": (128, 8),
    "ff_rw1": (D, 256), "ff_rb1": (128, 2), "ff_rw2": (256, 8), "ff_rb2": (128, 8),
    "ff_hw1": (D, 256), "ff_hb1": (128, 2), "ff_hw2": (256, 1), "ff_hb2": (1, 1),
    "ff_ew1": (2048, 512), "ff_eb1": (128, 4), "ff_ew2": (512, 3072),
    "ff_eb2": (8, 3072),
    "ff_sw1": (2048, 64), "ff_sb1": (64, 1), "ff_sw2": (64, 3072),
    "ff_sb2": (128, 24), "ff_swt": (1, 1),
    "ff_qw1": (D, 2048), "ff_qb1": (128, 16), "ff_qw2": (2048, D),
    "ff_qb2": (128, 8), "ff_iw": (128, 8), "ff_mu": (128, 8),
    "l1a_Wq": (D, D), "l1a_Wk": (D, D), "l1a_Wv": (D, D), "l1a_omg": (128, 256),
    "l1b_Win": (D, 4096), "l1b_cw": (128, 48), "l1b_cb": (128, 16),
    "l1b_Wst": (2048, 32), "l1b_Wt": (2048, 2048), "l1b_bt": (128, 16),
    "l1b_alog": (128, 256), "l1b_skip": (128, 16), "l1b_Wout": (2048, D),
    "l1f_w": (D, D), "l1f_b": (128, 8),
}


def bld_inputs(nc, names=None):
    aps = {}
    for name, shp in INPUT_SHAPES.items():
        if names is not None and name not in names:
            continue
        aps[name] = nc.dram_tensor(name, list(shp), F32, kind="ExternalInput").ap()
    return aps



def emit_setup(c):
    p, dr = c.p, c.dr
    for j in range(DCH):
        p.dma("sync", c.RX[:, j, :], dr["x_T"][j * 128:(j + 1) * 128, :], c.dl)
    p.dma("sync", c.ONEROW[:], dr["c_onerow"][:], c.dl)
    p.dma("sync", c.TRI8[:], dr["c_tri8"][:], c.dl)
    p.dma("sync", c.WS[:, 0:128], dr["c_ident"][:], c.dl)
    p.dma("sync", c.WS[:, 128:129], dr["c_onecol"][:], c.dl)
    ld = c.dl.n
    p.wait("scalar", c.dl, ld)
    p.op("scalar", lambda nc: nc.scalar.copy(c.IDT[:], c.WS[:, 0:128]))
    p.op("scalar", lambda nc: nc.scalar.copy(c.ONEROW16[:], c.ONEROW[:]))
    p.op("vector", lambda nc: nc.vector.memset(c.ONES128[:], 1.0))
    p.op("scalar", lambda nc: nc.scalar.copy(c.ONECOL16[:], c.WS[:, 128:129]), inc=c.ss)
    cvt1 = c.ss.n
    p.wait("sync", c.ss, cvt1)
    p.dma("sync", c.WS[:, 0:896], dr["c_causal"][:], c.dl)
    ld2 = c.dl.n
    p.wait("scalar", c.dl, ld2)
    p.op("scalar", lambda nc: nc.scalar.copy(c.CAUS[:], c.WS[:, 0:896]), inc=c.ss)
    c.rx_loaded = ld
    c.const_ss = c.ss.n


def barrier(c):
    p = c.p
    sems = [c.sv, c.ss, c.st, c.sg, c.dl, c.ds, c.dg]
    for eng in ENGINES:
        own = {"vector": c.sv, "scalar": c.ss, "tensor": c.st,
               "gpsimd": c.sg}.get(eng)
        for s in sems:
            if s.n and s is not own:
                p.wait(eng, s, s.n)


def emit_store_out(c):
    p = c.p
    out = c.nc.dram_tensor("out_T", [D, S], F32, kind="ExternalOutput").ap()
    for s in (c.sv, c.ss, c.st, c.sg):
        if s.n:
            p.wait("sync", s, s.n)
    for j in range(DCH):
        p.dma("sync", out[j * 128:(j + 1) * 128, :], c.RX[:, j, :], c.so)
    p.wait("sync", c.so, c.so.n)



def load_weight_bf16(c, name, kchunks, ncols, wb_off, rows=128, dma_eng="sync",
                     col0=0, src_row0=0):
    p, dr = c.p, c.dr
    dsem = c.dl if dma_eng == "sync" else (c.ds if dma_eng == "scalar" else c.dg)
    dst = c.WB[:rows, wb_off:wb_off + kchunks * ncols].rearrange(
        "p (k n) -> p k n", n=ncols)
    for kc in range(kchunks):
        assert ncols <= 1024
        v = p.dma(dma_eng, c.WS[:rows, 0:ncols],
                  dr[name][src_row0 + kc * rows: src_row0 + (kc + 1) * rows,
                           col0:col0 + ncols], dsem)
        p.wait("scalar", dsem, v)
        p.op("scalar", lambda nc, kc=kc: nc.scalar.copy(
            dst[:, kc, :], c.WS[:rows, 0:ncols]), inc=c.ss)
        if kc + 1 < kchunks:
            p.wait(dma_eng, c.ss, c.ss.n)
    return dst, c.ss.n



def emit_ln_ff(c, tt, out16, nch=DCH, rxsrc=None, tw=TT, lo16=None, vout=None):
    p = c.p
    t0 = tt * tw
    rx = rxsrc if rxsrc is not None else (lambda j: c.RX[:, j, t0:t0 + tw])
    stg = c.SMB[:, 0:4 * tw].rearrange("p (a b) -> p a b", b=tw)
    c.pe_guard(4)
    c.pe_guard(5)
    c.pe_guard("PR")
    pe_base = None
    for j in range(nch):
        sl = (j % 2) * 2
        if j >= 2:
            p.wait("scalar", c.st, pe_base + (j - 2))
        p.op("scalar", lambda nc, j=j, sl=sl: nc.scalar.copy(stg[:, sl, :], rx(j)))
        v = p.op("scalar", lambda nc, j=j, sl=sl: nc.scalar.activation(
            stg[:, sl + 1, :], rx(j), AF.Square), inc=c.ss)
        p.wait("tensor", c.ss, v)
        p.op("tensor", lambda nc, j=j, sl=sl: nc.tensor.matmul(
            c.PR[0:1, 0:tw], c.ONECOL16[:], stg[:, sl, :],
            start=(j == 0), stop=(j == nch - 1)))
        vt = p.op("tensor", lambda nc, j=j, sl=sl: nc.tensor.matmul(
            c.PSB[4][0:1, 0:tw], c.ONECOL16[:], stg[:, sl + 1, :],
            start=(j == 0), stop=(j == nch - 1)), inc=c.st)
        if j == 0:
            pe_base = vt
    stats_done = c.st.n
    dim = nch * 128
    mu = c.SMF[0:1, 0:tw]
    var = c.SMF[0:1, 512:512 + tw]
    p.wait("vector", c.st, stats_done)
    p.op("vector", lambda nc: nc.vector.tensor_scalar(
        mu, c.PR[0:1, 0:tw], 1.0 / dim, None, ALU.mult))
    p.op("vector", lambda nc: nc.vector.tensor_tensor(var, mu, mu, ALU.mult))
    p.op("vector", lambda nc: nc.vector.scalar_tensor_tensor(
        var, c.PSB[4][0:1, 0:tw], 1.0 / dim, var, ALU.mult, ALU.subtract), inc=c.sv)
    v_var = c.sv.n
    p.op("vector", lambda nc: nc.vector.tensor_scalar(
        var, var, 1e-5, None, ALU.add), inc=c.sv)
    p.wait("scalar", c.sv, c.sv.n)
    p.op("scalar", lambda nc: nc.scalar.activation(var, var, AF.Ln))
    v_rstd = p.op("scalar", lambda nc: nc.scalar.activation(
        var, var, AF.Exp, scale=-0.5), inc=c.ss)
    p.wait("vector", c.ss, v_rstd)
    v_mur = p.op("vector", lambda nc: nc.vector.tensor_tensor(
        mu, mu, var, ALU.mult), inc=c.sv)
    rstd16 = c.SMB[0:1, 0:tw]
    mur16 = c.SMB[0:1, tw:2 * tw]
    p.wait("scalar", c.ss, v_rstd)
    p.wait("scalar", c.sv, v_mur)
    p.op("scalar", lambda nc: nc.scalar.copy(rstd16, var))
    vrows = p.op("scalar", lambda nc: nc.scalar.copy(mur16, mu), inc=c.ss)
    p.wait("tensor", c.ss, vrows)
    p.op("tensor", lambda nc: nc.tensor.matmul(
        c.PSB[4][:, 0:tw], c.ONEROW16[:], rstd16, start=True, stop=True))
    vb = p.op("tensor", lambda nc: nc.tensor.matmul(
        c.PSB[5][:, 0:tw], c.ONEROW16[:], mur16, start=True, stop=True), inc=c.st)
    c.bank_consumed("PR", c.sv, v_var)
    p.wait("vector", c.st, vb)
    if lo16 is None and vout is None:
        tmp = fa(c, 7680, [128, tw])
        for j in range(nch):
            p.op("vector", lambda nc, j=j: nc.vector.tensor_tensor(
                tmp, rx(j), c.PSB[4][:, 0:tw], ALU.mult))
            p.op("vector", lambda nc, j=j: nc.vector.tensor_tensor(
                out16[:, j, :], tmp, c.PSB[5][:, 0:tw], ALU.subtract),
                inc=c.sv if j == nch - 1 else None)
        done = c.sv.n
        c.bank_consumed(4, c.sv, done)
        c.bank_consumed(5, c.sv, done)
        return done
    vslots = vout if vout is not None else fa(c, 0, [128, nch, tw])
    for j in range(nch):
        p.op("vector", lambda nc, j=j: nc.vector.tensor_tensor(
            vslots[:, j, :], rx(j), c.PSB[4][:, 0:tw], ALU.mult))
        p.op("vector", lambda nc, j=j: nc.vector.tensor_tensor(
            vslots[:, j, :], vslots[:, j, :], c.PSB[5][:, 0:tw], ALU.subtract),
            inc=c.sv if j == nch - 1 else None)
    vdone = c.sv.n
    c.bank_consumed(4, c.sv, vdone)
    c.bank_consumed(5, c.sv, vdone)
    p.wait("scalar", c.sv, vdone)
    for j in range(nch):
        p.op("scalar", lambda nc, j=j: nc.scalar.copy(out16[:, j, :], vslots[:, j, :]),
             inc=c.ss if j == nch - 1 else None)
    hidone = c.ss.n
    if lo16 is None:
        return hidone
    p.wait("vector", c.ss, hidone)
    for j in range(nch):
        p.op("vector", lambda nc, j=j: nc.vector.tensor_tensor(
            lo16[:, j, :], vslots[:, j, :], out16[:, j, :], ALU.subtract),
            inc=c.sv if j == nch - 1 else None)
    return c.sv.n



QX, QT, QY, QNF, QAT, QQ, QSW, QRSW = 0, 1024, 2048, 3072, 4096, 5120, 6144, 7168


def scr2(c, elem_off, rows, cols, dtype=F32, row_stride=None):
    t = c.scr if dtype == F32 else c.scr.bitcast(dtype)
    if row_stride is None:
        row_stride = cols
    return bass.AP(t, elem_off, [[row_stride, rows], [1, cols]])


def _expand32(p, c, src_t, src_off, dst_slot, pong_slot, n=1024):
    nblk = n // 32
    seq = [dst_slot, pong_slot, dst_slot, pong_slot, dst_slot]
    cur_t, cur_off = src_t, src_off
    w = 1
    for i in range(5):
        tgt = seq[i]
        src3 = cur_t[:, cur_off:cur_off + nblk * w].rearrange(
            "p (b q) -> p b q", q=w)
        dst3 = c.FA[:, tgt:tgt + nblk * 2 * w].rearrange(
            "p (b q) -> p b q", q=2 * w)
        for h in range(2):
            if nblk * w < 256:
                vsmall(c, lambda nc, d=dst3, s=src3, h=h, w=w:
                       nc.vector.tensor_copy(d[:, :, h * w:(h + 1) * w], s))
            else:
                p.op("vector", lambda nc, d=dst3, s=src3, h=h, w=w:
                     nc.vector.tensor_copy(d[:, :, h * w:(h + 1) * w], s))
        cur_t, cur_off = c.FA, tgt
        w *= 2


def emit_quantize(c, dst16, n=1024):
    p = c.p
    nblk = n // 32
    X = fa(c, QX, [128, n]); T = fa(c, QT, [128, n]); Y = fa(c, QY, [128, n])
    NFi = fa(c, QNF, [128, n], I32); NF = fa(c, QNF, [128, n])
    AT = fa(c, QAT, [128, n]); Q = fa(c, QQ, [128, n])
    SW = fa(c, QSW, [128, n]); RSW = fa(c, QRSW, [128, n])
    bm = c.SMF[:, 1056:1056 + nblk]
    sw = c.SMF[:, 1088:1088 + nblk]
    rsw = c.SMF[:, 1120:1120 + nblk]
    x3 = fa(c, QX, [128, nblk, 32])
    vsmall(c, lambda nc: nc.vector.tensor_reduce(
        bm, x3, axis=AX.X, op=ALU.max, apply_absolute_value=True))
    vsmall(c, lambda nc: nc.vector.tensor_scalar(
        sw, bm, 1.0 / 7.5, 1e-30, ALU.mult, ALU.max))
    vsmall(c, lambda nc: nc.vector.reciprocal(rsw, sw))
    _expand32(p, c, c.SMF, 1088, QSW, QY, n)
    _expand32(p, c, c.SMF, 1120, QRSW, QY, n)
    p.op("vector", lambda nc: nc.vector.tensor_tensor(T, X, RSW, ALU.mult))
    p.op("vector", lambda nc: nc.vector.tensor_tensor(AT, T, T, ALU.mult))
    for cc, dst in ((8.0, Q), (2.0, RSW), (4.0, Y)):
        p.op("vector", lambda nc, cc=cc: nc.vector.tensor_scalar(
            Y, T, cc, None, ALU.mult))
        p.op("vector", lambda nc: nc.vector.tensor_copy(NFi, Y))
        p.op("vector", lambda nc: nc.vector.tensor_copy(X, NFi))
        p.op("vector", lambda nc: nc.vector.tensor_tensor(NF, Y, X, ALU.subtract))
        p.op("vector", lambda nc, dst=dst: nc.vector.tensor_scalar(
            dst, NF, 0.5, None, ALU.is_gt))
        p.op("vector", lambda nc: nc.vector.tensor_scalar(
            NF, NF, -0.5, None, ALU.is_le))
        p.op("vector", lambda nc, dst=dst: nc.vector.tensor_tensor(
            X, X, dst, ALU.add))
        p.op("vector", lambda nc: nc.vector.tensor_tensor(X, X, NF, ALU.subtract))
        p.op("vector", lambda nc, cc=cc, dst=dst: nc.vector.tensor_scalar(
            dst, X, 1.0 / cc, None, ALU.mult))
    Xi = fa(c, QX, [128, n], I32)
    p.op("vector", lambda nc: nc.vector.tensor_scalar(
        Xi, AT, 16.0, None, ALU.is_lt))
    p.op("vector", lambda nc: nc.vector.select(T, Xi, Y, RSW))
    p.op("vector", lambda nc: nc.vector.tensor_scalar(
        Xi, AT, 4.0, None, ALU.is_lt))
    p.op("vector", lambda nc: nc.vector.select(NF, Xi, Q, T))
    return p.op("vector", lambda nc, dst16=dst16: nc.vector.tensor_tensor(
        dst16, NF, SW, ALU.mult), inc=c.sv)



SCR_WDT = 0


def emit_l1f_wquant(c):
    p, dr = c.p, c.dr
    wd_tm = kb(c, 0, [128, 1024])
    pt3 = c.PT[:].rearrange("p (a b) -> p a b", b=128)
    for ot in range(8):
        v = p.dma("sync", c.WS[:, 0:1024],
                  dr["l1f_w"][ot * 128:(ot + 1) * 128, :], c.dl)
        p.wait("scalar", c.dl, v)
        vs = p.op("scalar", lambda nc: nc.scalar.copy(
            fa(c, QX, [128, 1024]), c.WS[:, 0:1024]), inc=c.ss)
        p.wait("sync", c.ss, vs)
        p.wait("vector", c.ss, vs)
        vq = emit_quantize(c, wd_tm, 1024)
        p.wait("tensor", c.sv, vq)
        c.pe_guard(6)
        vt = None
        for dc in range(DCH):
            vt = p.op("tensor", lambda nc, dc=dc: nc.tensor.transpose(
                pt3[:, dc, :], wd_tm[:, dc * 128:(dc + 1) * 128], c.IDT[:]),
                inc=c.st if dc == DCH - 1 else None)
        p.wait("vector", c.st, vt)
        ve = p.op("vector", lambda nc: nc.vector.tensor_copy(
            c.SMB[:, 0:1024], c.PT[:]), inc=c.sv)
        c.bank_consumed(6, c.sv, ve)
        p.wait("sync", c.sv, ve)
        for dc in range(DCH):
            dst = scr2(c, SCR_WDT + dc * 128 * 1024 + ot * 128, 128, 128,
                       BF16, row_stride=1024)
            p.dma("sync", dst, c.SMB[:, dc * 128:(dc + 1) * 128], c.dl)
        p.wait("vector", c.dl, c.dl.n)


def emit_l1f(c, first_call):
    p, dr = c.p, c.dr
    barrier(c)
    if first_call:
        emit_l1f_wquant(c)
        barrier(c)
    vb = p.dma("sync", c.SMF[:, 1280:1288], dr["l1f_b"][:], c.dl)
    xd_fm = kb(c, 1024, [128, DCH, TT])
    xd_tm = kb(c, 0, [128, 1024])
    pt3 = c.PT[:].rearrange("p (a b) -> p a b", b=128)
    hi16 = kb(c, 1024, [128, DCH, TT])
    lo16 = kb(c, 5120, [128, DCH, TT])
    for tt in range(NTT):
        ln_done = emit_ln_ff(c, tt, hi16, lo16=lo16)
        p.wait("tensor", c.sv, ln_done)
        for ts4 in range(4):
            for r in range(2):
                c.pe_guard(6)
                vt = None
                for i in range(4):
                    dc = 4 * r + i
                    p.op("tensor", lambda nc, dc=dc, i=i, ts4=ts4: nc.tensor.transpose(
                        pt3[:, i, :], hi16[:, dc, ts4 * 128:(ts4 + 1) * 128],
                        c.IDT[:]))
                    vt = p.op("tensor", lambda nc, dc=dc, i=i, ts4=ts4: nc.tensor.transpose(
                        pt3[:, 4 + i, :], lo16[:, dc, ts4 * 128:(ts4 + 1) * 128],
                        c.IDT[:]), inc=c.st if i == 3 else None)
                p.wait("vector", c.st, vt)
                xr = fa(c, QX + r * 512, [128, 4, 128])
                p.op("vector", lambda nc, xr=xr: nc.vector.tensor_copy(
                    xr, pt3[:, 0:4, :]))
                ve = p.op("vector", lambda nc, xr=xr: nc.vector.tensor_tensor(
                    xr, xr, pt3[:, 4:8, :], ALU.add), inc=c.sv)
                c.bank_consumed(6, c.sv, ve)
            vq = emit_quantize(c, xd_tm, 1024)
            p.wait("tensor", c.sv, vq)
            c.pe_guard(6)
            vt = None
            for dc in range(DCH):
                vt = p.op("tensor", lambda nc, dc=dc: nc.tensor.transpose(
                    pt3[:, dc, :], xd_tm[:, dc * 128:(dc + 1) * 128], c.IDT[:]),
                    inc=c.st if dc == DCH - 1 else None)
            p.wait("vector", c.st, vt)
            ve = p.op("vector", lambda nc, ts4=ts4: nc.vector.tensor_copy(
                hi16[:, :, ts4 * 128:(ts4 + 1) * 128], pt3[:, :, :]), inc=c.sv)
            c.bank_consumed(6, c.sv, ve)
        xd_ready = c.sv.n
        p.wait("tensor", c.sv, xd_ready)
        if not hasattr(c, "_wb_last"):
            c._wb_last = {}
        for g in range(2):
            for dc in range(DCH):
                slot = dc % 2
                wdst = c.WB[:, slot * 2048:slot * 2048 + 1024]
                src = scr2(c, SCR_WDT + dc * 128 * 1024, 128, 1024, BF16)
                if slot in c._wb_last:
                    p.wait("sync", c.st, c._wb_last[slot])
                vw = p.dma("sync", wdst, src, c.dl)
                p.wait("tensor", c.dl, vw)
                for oi in range(4):
                    oc = g * 4 + oi
                    if dc == 0:
                        c.pe_guard(oi)
                    vm = p.op("tensor", lambda nc, oi=oi, oc=oc, wdst=wdst, dc=dc:
                              nc.tensor.matmul(
                                  c.PSB[oi][:, :], wdst[:, oc * 128:(oc + 1) * 128],
                                  xd_fm[:, dc, :], start=(dc == 0), stop=(dc == DCH - 1)),
                              inc=c.st)
                c._wb_last[slot] = c.st.n
            mm_done = c.st.n
            p.wait("vector", c.st, mm_done)
            p.wait("vector", c.dl, vb)
            for oi in range(4):
                oc = g * 4 + oi
                rxa = c.RX[:, oc, tt * TT:(tt + 1) * TT]
                vv = p.op("vector", lambda nc, oi=oi, oc=oc, rxa=rxa:
                          nc.vector.scalar_tensor_tensor(
                              rxa, c.PSB[oi][:, :], c.SMF[:, 1280 + oc:1281 + oc],
                              rxa, ALU.add, ALU.add), inc=c.sv)
                c.bank_consumed(oi, c.sv, vv)



import contextlib

SECTION_FUNCS = {}


def build_sections(names):
    nc = bass.Bass("TRN2", target_bir_lowering=False,
                   detect_race_conditions=False)
    dr = bld_inputs(nc)
    with contextlib.ExitStack() as stack:
        c = Ctx(nc, stack, dr)
        emit_setup(c)
        barrier(c)
        if any(n != "l1f" and n != "l1f2" for n in names):
            emit_wprep(c)
            barrier(c)
        for i, name in enumerate(names):
            SECTION_FUNCS[name](c, name not in names[:i])
        emit_store_out(c)
        c.p.run_block()
    return nc


def _sec_l1f(c, first):
    emit_l1f(c, first)


SECTION_FUNCS["l1f"] = _sec_l1f
SECTION_FUNCS["l1f2"] = lambda c, first: emit_l1f(c, False)


def build_full():
    return build_sections(["l0a", "ff", "l0b", "ff2", "l1a", "l1f", "l1b", "l1f2"])



SCR_L1A_KS = 1 * 1024 * 1024
SCR_L1A_QS = SCR_L1A_KS + 8 * 1024 * 1024
SCR_L1A_V = SCR_L1A_QS + 8 * 1024 * 1024
SCR_W = 32 * 1024 * 1024

WPREP = [
    ("l0a_qT", D, D), ("l0a_kT", D, D), ("l0a_vT", D, D), ("l0a_oT", D, D),
    ("l0b_Wq", D, D), ("l0b_Wk", D, D), ("l0b_Wv", D, D), ("l0b_Wo", D, D),
    ("ff_rw1", D, 256), ("ff_rw2", 256, 8), ("ff_hw1", D, 256), ("ff_hw2", 256, 1),
    ("ff_ew1", 2048, 512), ("ff_ew2", 512, 3072), ("ff_sw1", 2048, 64),
    ("ff_sw2", 64, 3072), ("ff_qw1", D, 2048), ("ff_qw2", 2048, D),
    ("ff_eb2", 8, 3072),
    ("l1a_Wq", D, D), ("l1a_Wk", D, D), ("l1a_Wv", D, D), ("l1a_omg", 128, 256),
    ("l1b_Win", D, 4096), ("l1b_Wst", 2048, 32), ("l1b_Wt", 2048, 2048),
    ("l1b_Wout", 2048, D),
]


def emit_wprep(c):
    p, dr = c.p, c.dr
    c.wscr = {}
    off = SCR_W
    prev_dma = None
    for name, rows, cols in WPREP:
        c.wscr[name] = (off, rows, cols)
        nk = (rows + 127) // 128
        for kc in range(nk):
            r0 = kc * 128
            nr = min(128, rows - r0)
            for c0 in range(0, cols, 1024):
                ncg = min(1024, cols - c0)
                v = p.dma("sync", c.WS[:nr, 0:ncg],
                          dr[name][r0:r0 + nr, c0:c0 + ncg], c.dl)
                p.wait("scalar", c.dl, v)
                vs = p.op("scalar", lambda nc, nr=nr, ncg=ncg: nc.scalar.copy(
                    c.SMB[:nr, 0:ncg], c.WS[:nr, 0:ncg]), inc=c.ss)
                p.wait("sync", c.ss, vs)
                dst = scr2(c, off + r0 * cols + c0, nr, ncg, BF16, row_stride=cols)
                vd = p.dma("sync", dst, c.SMB[:nr, 0:ncg], c.dl)
                p.wait("scalar", c.dl, vd)
        off += rows * cols
    assert off <= 2 * 31_800_000, f"weights overrun scr: {off}"
    c.wprep_done = c.dl.n


def stream_w(c, name, wb_off, kc0, nk, col0, ncols, dma_eng="sync", rows=128):
    p = c.p
    off, wrows, cols = c.wscr[name]
    dsem = {"sync": c.dl, "scalar": c.ds, "gpsimd": c.dg}[dma_eng]
    dst = c.WB[0:rows, wb_off:wb_off + nk * ncols].rearrange(
        "p (k n) -> p k n", n=ncols)
    src = bass.AP(c.scr.bitcast(BF16), off + kc0 * rows * cols + col0,
                  [[cols, rows], [rows * cols, nk], [1, ncols]])
    v = p.dma(dma_eng, dst, src, dsem, slow=(ncols < 8 or cols < 8))
    return dst, v



L1A_NEG_LN16 = -math.log(16.0)


def wb_guard(c, slot_off, dma_eng="sync"):
    if not hasattr(c, "_wb_last"):
        c._wb_last = {}
    if slot_off in c._wb_last:
        c.p.wait(dma_eng, c.st, c._wb_last[slot_off])


def wb_used(c, slot_off, st_val):
    c._wb_last[slot_off] = st_val


def emit_l1a(c):
    p, dr = c.p, c.dr
    barrier(c)
    FAb = c.FA.bitcast(BF16)
    ln16 = kb(c, 0, [128, DCH, TT])
    vstage = kb(c, 4096, [128, 1040])
    vst3 = kb(c, 4096, [128, 16, 65])
    pt3 = c.PT[:].rearrange("p (a b) -> p a b", b=128)
    nlncol = c.SMF[:, 1408:1409]
    vsmall(c, lambda nc: nc.vector.memset(nlncol, L1A_NEG_LN16))
    vo = p.dma("sync", c.WS[:, 0:256], dr["l1a_omg"][:], c.dl)
    p.wait("scalar", c.dl, vo)
    vog = p.op("scalar", lambda nc: nc.scalar.copy(c.OMG[:], c.WS[:, 0:256]),
               inc=c.ss)
    p.wait("sync", c.ss, vog)
    for tt in range(NTT):
        emit_ln_ff(c, tt, ln16)
        ln_done = c.sv.n
        p.wait("tensor", c.sv, ln_done)
        for W, base in (("l1a_Wk", 0), ("l1a_Wq", 8192)):
            for oc in range(DCH):
                woff = (oc % 2) * 2048
                wb_guard(c, woff)
                wap, wv = stream_w(c, W, woff, 0, DCH, oc * 128, 128)
                p.wait("tensor", c.dl, wv)
                bank = oc % 2
                c.pe_guard(bank)
                vm = None
                for kc in range(DCH):
                    vm = p.op("tensor", lambda nc, kc=kc, bank=bank, wap=wap:
                              nc.tensor.matmul(
                                  c.PSB[bank][:, :], wap[:, kc, :], ln16[:, kc, :],
                                  start=(kc == 0), stop=(kc == DCH - 1)), inc=c.st)
                wb_used(c, woff, vm)
                p.wait("scalar", c.st, vm)
                va = None
                for half in range(2):
                    h = 2 * oc + half
                    ksl = FAb[:, base + h * 512: base + (h + 1) * 512]
                    ps = c.PSB[bank][half * 64:(half + 1) * 64, :]
                    p.op("scalar", lambda nc, ksl=ksl, ps=ps: nc.scalar.copy(
                        ksl[0:64, :], ps))
                    va = p.op("scalar", lambda nc, ksl=ksl, ps=ps:
                              nc.scalar.activation(ksl[64:128, :], ps, AF.Square),
                              inc=c.ss)
                c.bank_consumed(bank, c.ss, va)
            p.wait("sync", c.ss, c.ss.n)
            scr_base = (SCR_L1A_KS if base == 0 else SCR_L1A_QS) + tt * (128 * 8192)
            dst = bass.AP(c.scr.bitcast(BF16), scr_base, [[8192, 128], [1, 8192]])
            vship = p.dma("sync", dst, FAb[:, base:base + 8192], c.dl)
            p.wait("scalar", c.dl, vship)
        for sub in range(4):
            vsmall(c, lambda nc: nc.vector.memset(vst3[:, :, 64:65], 1.0))
            ones_v = c.sv.n
            for half in range(2):
                woff = 0
                wb_guard(c, 0)
                wb_guard(c, 2048)
                wap, wv = stream_w(c, "l1a_Wv", woff, 0, DCH, half * 512, 512)
                p.wait("tensor", c.dl, wv)
                bank = 2 + half
                c.pe_guard(bank)
                vm = None
                for kc in range(DCH):
                    vm = p.op("tensor", lambda nc, kc=kc, sub=sub, bank=bank, wap=wap:
                              nc.tensor.matmul(
                                  c.PSB[bank][:, :],
                                  ln16[:, kc, sub * 128:(sub + 1) * 128],
                                  wap[:, kc, :],
                                  start=(kc == 0), stop=(kc == DCH - 1)), inc=c.st)
                wb_used(c, 0, vm)
                wb_used(c, 2048, vm)
                p.wait("scalar", c.st, vm)
                p.wait("scalar", c.sv, ones_v)
                ps3 = c.PSB[bank][:, :].rearrange("p (h e) -> p h e", e=64)
                va = p.op("scalar", lambda nc, ps3=ps3, half=half:
                          nc.scalar.copy(
                              vst3[:, half * 8:(half + 1) * 8, 0:64], ps3),
                          inc=c.ss)
                c.bank_consumed(bank, c.ss, va)
            p.wait("sync", c.ss, c.ss.n)
            row0 = tt * TT + sub * 128
            dst = bass.AP(c.scr.bitcast(BF16), SCR_L1A_V + row0 * 1040,
                          [[1040, 128], [1, 1040]])
            vship = p.dma("sync", dst, vstage, c.dl)
            p.wait("vector", c.dl, vship)
            p.wait("scalar", c.dl, vship)
    barrier(c)
    kvmd = kb(c, 0, [128, 16, 130])
    kvacc = lambda h: c.FA[0:65, h * 256:(h + 1) * 256]
    last_kv = None
    for ch in range(32):
        tt, sub = ch // 4, ch % 4
        ks_base = SCR_L1A_KS + tt * (128 * 8192)
        src_ap = bass.AP(c.scr.bitcast(BF16), ks_base + sub * 128,
                         [[8192, 128], [512, 16], [1, 128]])
        ksbuf = kb(c, 4096, [128, 16, 128])
        if last_kv is not None:
            p.wait("sync", c.st, last_kv)
        v1 = p.dma("sync", ksbuf, src_ap, c.dl)
        row0 = ch * 128
        vsrc = bass.AP(c.scr.bitcast(BF16), SCR_L1A_V + row0 * 1040,
                       [[1040, 128], [1, 1040]])
        vbuf = kb(c, 6144, [128, 16, 65])
        v2 = p.dma("sync", vbuf, vsrc, c.dl)
        p.wait("tensor", c.dl, v2)
        for h in range(16):
            c.pe_guard(4)
            vm = p.op("tensor", lambda nc, h=h, ksbuf=ksbuf: nc.tensor.matmul(
                c.PSB[4][:, 0:256], ksbuf[:, h, :], c.OMG[:, :],
                start=True, stop=True), inc=c.st)
            p.wait("scalar", c.st, vm)
            if last_kv is not None:
                p.wait("scalar", c.st, last_kv)
            vpk = p.op("scalar", lambda nc: nc.scalar.activation(
                c.SMB[:, 0:256], c.PSB[4][:, 0:256], AF.Exp, bias=nlncol),
                inc=c.ss)
            c.bank_consumed(4, c.ss, vpk)
            p.wait("tensor", c.ss, vpk)
            bank = h % 4
            c.pe_guard(bank)
            last_kv = p.op("tensor", lambda nc, vbuf=vbuf, h=h, bank=bank:
                           nc.tensor.matmul(
                               c.PSB[bank][0:65, 0:256], vbuf[:, h, :],
                               c.SMB[:, 0:256], start=True, stop=True), inc=c.st)
            p.wait("vector", c.st, last_kv)
            if ch == 0:
                vadd = p.op("vector", lambda nc, h=h, bank=bank:
                            nc.vector.tensor_copy(kvacc(h), c.PSB[bank][0:65, 0:256]),
                            inc=c.sv)
            else:
                vadd = p.op("vector", lambda nc, h=h, bank=bank:
                            nc.vector.tensor_tensor(
                                kvacc(h), kvacc(h), c.PSB[bank][0:65, 0:256],
                                ALU.add), inc=c.sv)
            c.bank_consumed(bank, c.sv, vadd)
    acc_done = c.sv.n
    p.wait("scalar", c.sv, acc_done)
    for h in range(16):
        va = p.op("scalar", lambda nc, h=h: nc.scalar.copy(
            c.SMB[0:65, 0:256], kvacc(h)), inc=c.ss)
        p.wait("tensor", c.ss, va)
        c.pe_guard(6)
        vt = None
        for mc in range(2):
            vt = p.op("tensor", lambda nc, mc=mc: nc.tensor.transpose(
                pt3[:, mc, 0:65], c.SMB[0:65, mc * 128:(mc + 1) * 128],
                c.IDT[0:65, 0:65]), inc=c.st if mc == 1 else None)
        p.wait("vector", c.st, vt)
        ve = p.op("vector", lambda nc, h=h: nc.vector.tensor_copy(
            kvmd[:, h, :].rearrange("p (a b) -> p a b", b=65),
            pt3[:, 0:2, 0:65]), inc=c.sv)
        c.bank_consumed(6, c.sv, ve)
        p.wait("scalar", c.sv, ve)
    barrier(c)
    last_pq = None
    for tt in range(NTT):
        for h in range(16):
            qs_base = SCR_L1A_QS + tt * (128 * 8192)
            src = bass.AP(c.scr.bitcast(BF16), qs_base + h * 512,
                          [[8192, 128], [1, 512]])
            qsbuf = kb(c, 4096, [128, 512])
            if last_pq is not None:
                p.wait("sync", c.st, last_pq)
            v1 = p.dma("sync", qsbuf, src, c.dl)
            p.wait("tensor", c.dl, v1)
            c.pe_guard(4)
            c.pe_guard(5)
            vm = None
            for mc in range(2):
                vm = p.op("tensor", lambda nc, mc=mc, qsbuf=qsbuf: nc.tensor.matmul(
                    c.PSB[4 + mc][:, :], c.OMG[:, mc * 128:(mc + 1) * 128],
                    qsbuf, start=True, stop=True), inc=c.st)
            last_pq = vm
            p.wait("scalar", c.st, vm)
            vpq = None
            for mc in range(2):
                vpq = p.op("scalar", lambda nc, mc=mc: nc.scalar.activation(
                    c.SMB[:, mc * 512:(mc + 1) * 512], c.PSB[4 + mc][:, :],
                    AF.Exp, bias=nlncol), inc=c.ss)
            c.bank_consumed(4, c.ss, vpq)
            c.bank_consumed(5, c.ss, vpq)
            p.wait("tensor", c.ss, vpq)
            c.pe_guard(2)
            c.pe_guard(3)
            kv3 = kvmd[:, h, :].rearrange("p (a b) -> p a b", b=65)
            hp0 = (h % 2) * 64
            htp = None if h % 2 == 0 else (0, 64)
            vz = None
            for mc in range(2):
                p.op("tensor", lambda nc, mc=mc, kv3=kv3: nc.tensor.matmul(
                    c.PSB[2][0:1, :], kv3[:, mc, 64:65],
                    c.SMB[:, mc * 512:(mc + 1) * 512],
                    start=(mc == 0), stop=(mc == 1)))
                vz = p.op("tensor", lambda nc, mc=mc, kv3=kv3, hp0=hp0, htp=htp:
                          nc.tensor.matmul(
                              c.PSB[3][hp0:hp0 + 64, :], kv3[:, mc, 0:64],
                              c.SMB[:, mc * 512:(mc + 1) * 512],
                              start=(mc == 0), stop=(mc == 1),
                              tile_position=htp), inc=c.st)
            p.wait("vector", c.st, vz)
            zrow = c.SMF[0:1, 520:1032]
            vsmall(c, lambda nc: nc.vector.tensor_scalar(
                zrow, c.PSB[2][0:1, :], 1e-6, None, ALU.add))
            vsmall(c, lambda nc: nc.vector.reciprocal(zrow, zrow))
            vrz = c.sv.n
            c.bank_consumed(2, c.sv, vrz)
            p.wait("scalar", c.sv, vrz)
            vrow = p.op("scalar", lambda nc: nc.scalar.copy(
                c.SMB[0:1, 1536:2048], zrow), inc=c.ss)
            p.wait("tensor", c.ss, vrow)
            j, hh = h // 2, h % 2
            p0, p1 = hh * 64, (hh + 1) * 64
            tp = None if hh == 0 else (0, 64)
            c.pe_guard(4)
            vbc = p.op("tensor", lambda nc, p0=p0, p1=p1, tp=tp: nc.tensor.matmul(
                c.PSB[4][p0:p1, :], c.ONEROW16[0:1, 0:64], c.SMB[0:1, 1536:2048],
                start=True, stop=True, tile_position=tp), inc=c.st)
            p.wait("scalar", c.st, vbc)
            vo2 = p.op("scalar", lambda nc, p0=p0, p1=p1: nc.scalar.copy(
                c.SMB[p0:p1, 0:512], c.PSB[3][p0:p1, :]), inc=c.ss)
            c.bank_consumed(3, c.ss, vo2)
            p.wait("vector", c.ss, vo2)
            rxa = c.RX[p0:p1, j, tt * TT:(tt + 1) * TT]
            p.op("vector", lambda nc, p0=p0, p1=p1: nc.vector.tensor_tensor(
                c.SMF[p0:p1, 0:512], c.SMB[p0:p1, 0:512], c.PSB[4][p0:p1, :],
                ALU.mult))
            vfin = p.op("vector", lambda nc, rxa=rxa, p0=p0, p1=p1:
                        nc.vector.tensor_tensor(
                            rxa, rxa, c.SMF[p0:p1, 0:512], ALU.add), inc=c.sv)
            c.bank_consumed(4, c.sv, vfin)
            p.wait("scalar", c.sv, vfin)


SECTION_FUNCS["l1a"] = lambda c, first: emit_l1a(c)



SCR_L0B_Q = 1 * 1024 * 1024
SCR_L0B_K = SCR_L0B_Q + 4 * 1024 * 1024
SCR_L0B_V = SCR_L0B_K + 4 * 1024 * 1024
SCR_L0B_C = SCR_L0B_V + 4 * 1024 * 1024


def emit_l0b(c):
    p, dr = c.p, c.dr
    barrier(c)
    c._l0b_w16v = 0
    c._l0b_vmu = 0
    ln16 = kb(c, 0, [128, DCH, TT])
    pt3 = c.PT[:].rearrange("p (a b) -> p a b", b=128)
    qmax = c.SMF[0:1, 1160:1161]
    kmax = c.SMF[0:1, 1161:1162]
    mcol = c.SMF[:, 1162:1163]
    vsmall(c, lambda nc: nc.vector.memset(c.SMF[0:1, 1160:1162], 0.0))
    vbi = None
    for b in ("bq", "bk", "bv", "bo"):
        off = {"bq": 1168, "bk": 1176, "bv": 1184, "bo": 1192}[b]
        vbi = p.dma("sync", c.SMF[:, off:off + 8], dr[f"l0b_{b}"][:], c.dl)
    for tt in range(NTT):
        emit_ln_ff(c, tt, ln16)
        ln_done = c.sv.n
        p.wait("tensor", c.sv, ln_done)
        p.wait("vector", c.dl, vbi)
        for W, sbase, boff, nrow in (("l0b_Wq", SCR_L0B_Q, 1168, qmax),
                                     ("l0b_Wk", SCR_L0B_K, 1176, kmax),
                                     ("l0b_Wv", SCR_L0B_V, 1184, None)):
            stg = kb(c, 4096, [128, DCH, TT])
            c.pe_guard("PR")
            nv = None
            for oc in range(DCH):
                woff = (oc % 2) * 2048
                wb_guard(c, woff)
                wap, wv = stream_w(c, W, woff, 0, DCH, oc * 128, 128)
                p.wait("tensor", c.dl, wv)
                bank = oc % 2
                c.pe_guard(bank)
                vm = None
                for kc in range(DCH):
                    vm = p.op("tensor", lambda nc, kc=kc, bank=bank, wap=wap:
                              nc.tensor.matmul(
                                  c.PSB[bank][:, :], wap[:, kc, :], ln16[:, kc, :],
                                  start=(kc == 0), stop=(kc == DCH - 1)), inc=c.st)
                wb_used(c, woff, vm)
                p.wait("vector", c.st, vm)
                bcol = c.SMF[:, boff + oc:boff + oc + 1]
                ve = p.op("vector", lambda nc, oc=oc, bank=bank, bcol=bcol, stg=stg:
                          nc.vector.tensor_scalar(
                              stg[:, oc, :], c.PSB[bank][:, :], bcol, None,
                              ALU.add), inc=c.sv)
                c.bank_consumed(bank, c.sv, ve)
                if nrow is not None:
                    p.wait("scalar", c.sv, ve)
                    vsq = p.op("scalar", lambda nc, oc=oc, stg=stg:
                               nc.scalar.activation(
                                   c.SMB[:, 0:TT], stg[:, oc, :], AF.Square),
                               inc=c.ss)
                    p.wait("tensor", c.ss, vsq)
                    p.op("tensor", lambda nc, oc=oc: nc.tensor.matmul(
                        c.PR[0:1, 0:TT], c.ONECOL16[:], c.SMB[:, 0:TT],
                        start=(oc == 0), stop=(oc == DCH - 1)),
                        inc=c.st if oc == DCH - 1 else None)
                    nv = c.st.n
            if nrow is not None:
                p.wait("vector", c.st, nv)
                vsmall(c, lambda nc: nc.vector.reduce_max(
                    c.SMF[0:1, 1163:1164], c.PR[0:1, 0:TT], axis=AX.X))
                vsmall(c, lambda nc, nrow=nrow: nc.vector.tensor_tensor(
                    nrow, nrow, c.SMF[0:1, 1163:1164], ALU.max))
                c.bank_consumed("PR", c.sv, c.sv.n)
            shipw = c.sv.n
            p.wait("sync", c.sv, shipw)
            if W == "l0b_Wv":
                p.wait("tensor", c.sv, shipw)
                for sub in range(4):
                    c.pe_guard(6)
                    vt = None
                    for dc2 in range(DCH):
                        vt = p.op("tensor", lambda nc, dc2=dc2, sub=sub, stg=stg:
                                  nc.tensor.transpose(
                                      pt3[:, dc2, :],
                                      stg[:, dc2, sub * 128:(sub + 1) * 128],
                                      c.IDT[:]), inc=c.st if dc2 == DCH - 1 else None)
                    p.wait("vector", c.st, vt)
                    vte = p.op("vector", lambda nc: nc.vector.tensor_copy(
                        kb(c, 8192, [128, DCH, 128]), pt3[:, :, :]), inc=c.sv)
                    c.bank_consumed(6, c.sv, vte)
                    p.wait("sync", c.sv, vte)
                    row0 = tt * TT + sub * 128
                    dst = bass.AP(c.scr.bitcast(BF16), sbase + row0 * 1024,
                                  [[1024, 128], [1, 1024]])
                    vship = p.dma("sync", dst, kb(c, 8192, [128, 1024]), c.dl)
                    p.wait("vector", c.dl, vship)
            else:
                dst = bass.AP(c.scr.bitcast(BF16), sbase + tt * (128 * 4096),
                              [[4096, 128], [1, 4096]])
                vship = p.dma("sync", dst, kb(c, 4096, [128, 4096]), c.dl)
                p.wait("vector", c.dl, vship)
    barrier(c)
    vsmall(c, lambda nc: nc.vector.tensor_tensor(qmax, qmax, kmax, ALU.mult))
    p.wait("scalar", c.sv, c.sv.n)
    vsq2 = p.op("scalar", lambda nc: nc.scalar.activation(qmax, qmax, AF.Sqrt),
                inc=c.ss)
    p.wait("vector", c.ss, vsq2)
    vsmall(c, lambda nc: nc.vector.tensor_scalar(
        qmax, qmax, -1.0 / 32.0, None, ALU.mult))
    p.wait("scalar", c.sv, c.sv.n)
    vmb = p.op("scalar", lambda nc: nc.scalar.copy(c.SMB[0:1, 0:1], qmax), inc=c.ss)
    p.wait("tensor", c.ss, vmb)
    c.pe_guard(2)
    vbc = p.op("tensor", lambda nc: nc.tensor.matmul(
        c.PSB[2][:, 0:1], c.ONEROW16[:], c.SMB[0:1, 0:1], start=True, stop=True),
        inc=c.st)
    p.wait("scalar", c.st, vbc)
    vmc = p.op("scalar", lambda nc: nc.scalar.copy(mcol, c.PSB[2][:, 0:1]), inc=c.ss)
    c.bank_consumed(2, c.ss, vmc)
    p.wait("tensor", c.ss, vmc)
    p.wait("vector", c.ss, vmc)
    qn = kb(c, 0, [128, DCH, 256])
    km = kb(c, 2048, [128, DCH, 256])
    vn = kb(c, 4096, [128, 2, 1024])
    ptile = kb(c, 6144, [128, 2, 256])
    w16 = kb(c, 6656, [128, 2, 256])
    wacc = fa(c, 0, [128, 2, 256])
    tmpf = fa(c, 512, [128, 256])
    last_q_use = None
    last_v_use = None
    for n in range(16):
        tt, half = n // 2, n % 2
        src = bass.AP(c.scr.bitcast(BF16), SCR_L0B_Q + tt * (128 * 4096) + half * 256,
                      [[4096, 128], [512, DCH], [1, 256]])
        if last_q_use is not None:
            p.wait("sync", c.st, last_q_use)
        vq = p.dma("sync", qn, src, c.dl)
        vsrc = bass.AP(c.scr.bitcast(BF16), SCR_L0B_V + n * 256 * 1024,
                       [[1024, 128], [128 * 1024, 2], [1, 1024]])
        if last_v_use is not None:
            p.wait("sync", c.st, last_v_use)
        vv = p.dma("sync", vn, vsrc, c.dl)
        p.wait("tensor", c.dl, vq)
        last_km = None
        for m in range(16):
            mt, mh = m // 2, m % 2
            ksrc = bass.AP(c.scr.bitcast(BF16),
                           SCR_L0B_K + mt * (128 * 4096) + mh * 256,
                           [[4096, 128], [512, DCH], [1, 256]])
            if last_km is not None:
                p.wait("sync", c.st, last_km)
            vk = p.dma("sync", km, ksrc, c.dl)
            p.wait("tensor", c.dl, vk)
            vsc = None
            for kc in range(2):
                c.pe_guard(kc)
                for dc in range(DCH):
                    vsc = p.op("tensor", lambda nc, kc=kc, dc=dc: nc.tensor.matmul(
                        c.PSB[kc][:, 0:256], km[:, dc, kc * 128:(kc + 1) * 128],
                        qn[:, dc, :], start=(dc == 0), stop=(dc == DCH - 1)),
                        inc=c.st if dc == DCH - 1 else None)
            last_km = vsc
            p.wait("scalar", c.st, vsc)
            if hasattr(c, "_l0b_vmu") and c._l0b_vmu:
                p.wait("scalar", c.sv, c._l0b_vmu)
            vex = None
            for kc in range(2):
                vex = p.op("scalar", lambda nc, kc=kc: nc.scalar.activation(
                    ptile[:, kc, :], c.PSB[kc][:, 0:256], AF.Exp,
                    bias=mcol, scale=1.0 / 32.0), inc=c.ss)
                c.bank_consumed(kc, c.ss, vex)
            p.wait("tensor", c.ss, vex)
            c.pe_guard("PR")
            vl = None
            for kc in range(2):
                vl = p.op("tensor", lambda nc, kc=kc: nc.tensor.matmul(
                    c.PR[0:1, 0:256], c.ONECOL16[:], ptile[:, kc, :],
                    start=(kc == 0), stop=(kc == 1)), inc=c.st)
            p.wait("vector", c.st, vl)
            lrow = c.SMF[0:1, 520:776]
            vsmall(c, lambda nc: nc.vector.reciprocal(lrow, c.PR[0:1, 0:256]))
            if m == n:
                vsmall(c, lambda nc: nc.vector.tensor_scalar(
                    lrow, lrow, 2.0, None, ALU.mult))
            c.bank_consumed("PR", c.sv, c.sv.n)
            p.wait("scalar", c.sv, c.sv.n)
            vlr = p.op("scalar", lambda nc: nc.scalar.copy(
                c.SMB[0:1, 0:256], lrow), inc=c.ss)
            p.wait("tensor", c.ss, vlr)
            c.pe_guard(2)
            vrb = p.op("tensor", lambda nc: nc.tensor.matmul(
                c.PSB[2][:, 0:256], c.ONEROW16[:], c.SMB[0:1, 0:256],
                start=True, stop=True), inc=c.st)
            p.wait("vector", c.st, vrb)
            vwa = None
            vmu = None
            for kc in range(2):
                p.op("vector", lambda nc, kc=kc: nc.vector.tensor_tensor(
                    tmpf, ptile[:, kc, :], c.PSB[2][:, 0:256], ALU.mult),
                    inc=c.sv)
                vmu = c.sv.n
                p.wait("gpsimd", c.sv, vmu)
                if m == 0:
                    p.wait("gpsimd", c.ss, getattr(c, "_l0b_w16v", 0))
                    vwa = p.op("gpsimd", lambda nc, kc=kc: nc.gpsimd.tensor_copy(
                        wacc[:, kc, :], tmpf), inc=c.sg)
                else:
                    vwa = p.op("gpsimd", lambda nc, kc=kc: nc.gpsimd.tensor_tensor(
                        wacc[:, kc, :], wacc[:, kc, :], tmpf, ALU.add), inc=c.sg)
                p.wait("vector", c.sg, vwa)
            c._l0b_vmu = vmu
            c.bank_consumed(2, c.sg, vwa)
        p.wait("scalar", c.sg, c.sg.n)
        vw16 = None
        for kc in range(2):
            vw16 = p.op("scalar", lambda nc, kc=kc: nc.scalar.copy(
                w16[:, kc, :], wacc[:, kc, :]), inc=c.ss)
        c._l0b_w16v = vw16
        p.wait("tensor", c.ss, vw16)
        p.wait("tensor", c.dl, vv)
        cstage = kb(c, 8192, [128, DCH, 256])
        for g in range(2):
            vpv = None
            for oi in range(4):
                ec = g * 4 + oi
                c.pe_guard(oi)
                for kc in range(2):
                    vpv = p.op("tensor", lambda nc, ec=ec, oi=oi, kc=kc:
                               nc.tensor.matmul(
                                   c.PSB[oi][:, 0:256],
                                   vn[:, kc, ec * 128:(ec + 1) * 128],
                                   w16[:, kc, :], start=(kc == 0), stop=(kc == 1)),
                               inc=c.st)
            last_v_use = vpv
            p.wait("vector", c.st, vpv)
            for oi in range(4):
                ec = g * 4 + oi
                ve2 = p.op("vector", lambda nc, ec=ec, oi=oi, cstage=cstage:
                           nc.vector.tensor_copy(cstage[:, ec, :],
                                                 c.PSB[oi][:, 0:256]), inc=c.sv)
                c.bank_consumed(oi, c.sv, ve2)
        last_q_use = last_km
        ship_ok = c.sv.n
        p.wait("sync", c.sv, ship_ok)
        dst = bass.AP(c.scr.bitcast(BF16), SCR_L0B_C + n * (128 * 2048),
                      [[2048, 128], [1, 2048]])
        vship = p.dma("sync", dst, cstage[:, :, :], c.dl)
        p.wait("vector", c.dl, vship)
    barrier(c)
    for tt in range(NTT):
        cb = kb(c, 0, [128, DCH, TT])
        if tt > 0:
            p.wait("sync", c.st, c._l0b_cb_use)
        for half in range(2):
            csrc = bass.AP(c.scr.bitcast(BF16),
                           SCR_L0B_C + (2 * tt + half) * (128 * 2048),
                           [[2048, 128], [256, DCH], [1, 256]])
            vc = p.dma("sync", cb[:, :, half * 256:(half + 1) * 256], csrc, c.dl)
        p.wait("tensor", c.dl, vc)
        for oc in range(DCH):
            woff = (oc % 2) * 2048
            wb_guard(c, woff)
            wap, wv = stream_w(c, "l0b_Wo", woff, 0, DCH, oc * 128, 128)
            p.wait("tensor", c.dl, wv)
            bank = oc % 2
            c.pe_guard(bank)
            vm = None
            for kc in range(DCH):
                vm = p.op("tensor", lambda nc, kc=kc, bank=bank, wap=wap, cb=cb:
                          nc.tensor.matmul(
                              c.PSB[bank][:, :], wap[:, kc, :], cb[:, kc, :],
                              start=(kc == 0), stop=(kc == DCH - 1)), inc=c.st)
            wb_used(c, woff, vm)
            c._l0b_cb_use = vm
            p.wait("vector", c.st, vm)
            bcol = c.SMF[:, 1192 + oc:1193 + oc]
            rxa = c.RX[:, oc, tt * TT:(tt + 1) * TT]
            vr = p.op("vector", lambda nc, bank=bank, bcol=bcol, rxa=rxa:
                      nc.vector.scalar_tensor_tensor(
                          rxa, c.PSB[bank][:, :], bcol, rxa, ALU.add, ALU.add),
                      inc=c.sv)
            c.bank_consumed(bank, c.sv, vr)


SECTION_FUNCS["l0b"] = lambda c, first: emit_l0b(c)

def emit_l0a(c):
    p, dr = c.p, c.dr
    barrier(c)
    ln16 = kb(c, 0, [128, DCH, TT])
    pt3 = c.PT[:].rearrange("p (a b) -> p a b", b=128)
    qmax = c.SMF[0:1, 1160:1161]
    kmax = c.SMF[0:1, 1161:1162]
    mcol = c.SMF[:, 1162:1163]
    vsmall(c, lambda nc: nc.vector.memset(c.SMF[0:1, 1160:1162], 0.0))
    for tt in range(NTT):
        emit_ln_ff(c, tt, ln16)
        ln_done = c.sv.n
        p.wait("tensor", c.sv, ln_done)
        for W, sbase, nrow in (("l0a_qT", SCR_L0B_Q, qmax),
                               ("l0a_kT", SCR_L0B_K, kmax),
                               ("l0a_vT", SCR_L0B_V, None)):
            stg = kb(c, 4096, [128, DCH, TT])
            c.pe_guard("PR")
            nv = None
            for oc in range(DCH):
                woff = (oc % 2) * 2048
                wb_guard(c, woff)
                wap, wv = stream_w(c, W, woff, 0, DCH, oc * 128, 128)
                p.wait("tensor", c.dl, wv)
                bank = oc % 2
                c.pe_guard(bank)
                vm = None
                for kc in range(DCH):
                    vm = p.op("tensor", lambda nc, kc=kc, bank=bank, wap=wap:
                              nc.tensor.matmul(
                                  c.PSB[bank][:, :], wap[:, kc, :], ln16[:, kc, :],
                                  start=(kc == 0), stop=(kc == DCH - 1)), inc=c.st)
                wb_used(c, woff, vm)
                p.wait("scalar", c.st, vm)
                ve = p.op("scalar", lambda nc, oc=oc, bank=bank, stg=stg:
                          nc.scalar.copy(stg[:, oc, :], c.PSB[bank][:, :]),
                          inc=c.ss)
                if nrow is not None:
                    vsq = p.op("scalar", lambda nc, oc=oc, bank=bank:
                               nc.scalar.activation(
                                   c.SMB[:, 0:TT], c.PSB[bank][:, :], AF.Square),
                               inc=c.ss)
                    c.bank_consumed(bank, c.ss, vsq)
                    p.wait("tensor", c.ss, vsq)
                    c.pe_guard("PR")
                    nv = p.op("tensor", lambda nc, oc=oc: nc.tensor.matmul(
                        c.PR[0:1, 0:TT], c.ONECOL16[:], c.SMB[:, 0:TT],
                        start=True, stop=True), inc=c.st)
                    p.wait("vector", c.st, nv)
                    vsmall(c, lambda nc: nc.vector.reduce_max(
                        c.SMF[0:1, 1163:1164], c.PR[0:1, 0:TT], axis=AX.X))
                    vsmall(c, lambda nc, nrow=nrow: nc.vector.tensor_tensor(
                        nrow, nrow, c.SMF[0:1, 1163:1164], ALU.max))
                    c.bank_consumed("PR", c.sv, c.sv.n)
                else:
                    c.bank_consumed(bank, c.ss, ve)
            shipw = c.ss.n
            p.wait("sync", c.ss, shipw)
            if W == "l0a_vT":
                p.wait("tensor", c.ss, shipw)
                for sub in range(4):
                    c.pe_guard(6)
                    vt = None
                    for dc2 in range(DCH):
                        vt = p.op("tensor", lambda nc, dc2=dc2, sub=sub, stg=stg:
                                  nc.tensor.transpose(
                                      pt3[:, dc2, :],
                                      stg[:, dc2, sub * 128:(sub + 1) * 128],
                                      c.IDT[:]), inc=c.st if dc2 == DCH - 1 else None)
                    p.wait("vector", c.st, vt)
                    vte = p.op("vector", lambda nc: nc.vector.tensor_copy(
                        kb(c, 8192, [128, DCH, 128]), pt3[:, :, :]), inc=c.sv)
                    c.bank_consumed(6, c.sv, vte)
                    p.wait("sync", c.sv, vte)
                    row0 = tt * TT + sub * 128
                    dst = bass.AP(c.scr.bitcast(BF16), SCR_L0B_V + row0 * 1024,
                                  [[1024, 128], [1, 1024]])
                    vship = p.dma("sync", dst, kb(c, 8192, [128, 1024]), c.dl)
                    p.wait("vector", c.dl, vship)
            else:
                dst = bass.AP(c.scr.bitcast(BF16), sbase + tt * (128 * 4096),
                              [[4096, 128], [1, 4096]])
                vship = p.dma("sync", dst, kb(c, 4096, [128, 4096]), c.dl)
                p.wait("scalar", c.dl, vship)
    barrier(c)
    vsmall(c, lambda nc: nc.vector.tensor_tensor(qmax, qmax, kmax, ALU.mult))
    p.wait("scalar", c.sv, c.sv.n)
    vsq2 = p.op("scalar", lambda nc: nc.scalar.activation(qmax, qmax, AF.Sqrt),
                inc=c.ss)
    p.wait("vector", c.ss, vsq2)
    vsmall(c, lambda nc: nc.vector.tensor_scalar(
        qmax, qmax, -1.0 / 8.0, None, ALU.mult))
    p.wait("scalar", c.sv, c.sv.n)
    vmb = p.op("scalar", lambda nc: nc.scalar.copy(c.SMB[0:1, 0:1], qmax), inc=c.ss)
    p.wait("tensor", c.ss, vmb)
    c.pe_guard(2)
    vbc = p.op("tensor", lambda nc: nc.tensor.matmul(
        c.PSB[2][:, 0:1], c.ONEROW16[:], c.SMB[0:1, 0:1], start=True, stop=True),
        inc=c.st)
    p.wait("scalar", c.st, vbc)
    vmc = p.op("scalar", lambda nc: nc.scalar.copy(mcol, c.PSB[2][:, 0:1]), inc=c.ss)
    c.bank_consumed(2, c.ss, vmc)
    p.wait("tensor", c.ss, vmc)
    p.wait("vector", c.ss, vmc)
    ptt = kb(c, 4096, [128, 4, TT])
    attn16 = kb(c, 6144, [128, DCH, TT])
    for tt in range(NTT):
        nkt = tt + 1
        for j in range(DCH):
            qsl = kb(c, 0, [128, TT])
            qsrc = bass.AP(c.scr.bitcast(BF16),
                           SCR_L0B_Q + tt * (128 * 4096) + j * 512,
                           [[4096, 128], [1, 512]])
            if getattr(c, "_a_quse", None) is not None:
                p.wait("sync", c.st, c._a_quse)
            vq = p.dma("sync", qsl, qsrc, c.dl)
            p.wait("tensor", c.dl, vq)
            for kt in range(nkt):
                kslot = (kt % 2)
                kbuf = kb(c, 1024 + kslot * 512, [128, TT])
                ksrc = bass.AP(c.scr.bitcast(BF16),
                               SCR_L0B_K + kt * (128 * 4096) + j * 512,
                               [[4096, 128], [1, 512]])
                if getattr(c, "_a_kuse", {}).get(kslot) is not None:
                    p.wait("sync", c.st, c._a_kuse[kslot])
                vk = p.dma("sync", kbuf, ksrc, c.dl)
                vbuf = kb(c, 2048 + kslot * 512, [128, 4, 128])
                vsrc = bass.AP(c.scr.bitcast(BF16),
                               SCR_L0B_V + (kt * 512) * 1024 + j * 128,
                               [[1024, 128], [128 * 1024, 4], [1, 128]])
                if getattr(c, "_a_vuse", {}).get(kslot) is not None:
                    p.wait("sync", c.st, c._a_vuse[kslot])
                vv = p.dma("sync",
                           bass.AP(vbuf.tensor, vbuf.offset,
                                   [[10240, 128], [128, 4], [1, 128]]),
                           vsrc, c.dl)
                p.wait("tensor", c.dl, vv)
                for hh in range(2):
                    h = 2 * j + hh
                    p0 = hh * 64
                    tp = None if hh == 0 else (0, 64)
                    vsc = None
                    for ks in range(4):
                        c.pe_guard(ks)
                        vsc = p.op("tensor", lambda nc, ks=ks, p0=p0, kbuf=kbuf,
                                   qsl=qsl: nc.tensor.matmul(
                                       c.PSB[ks][:, :],
                                       kbuf[p0:p0 + 64, ks * 128:(ks + 1) * 128],
                                       qsl[p0:p0 + 64, :],
                                       start=True, stop=True), inc=c.st)
                    c._a_kuse = getattr(c, "_a_kuse", {})
                    c._a_kuse[kslot] = vsc
                    c._a_quse = vsc
                    p.wait("scalar", c.st, vsc)
                    if getattr(c, "_a_puse", None) is not None:
                        p.wait("scalar", c.sv, c._a_puse[0])
                        p.wait("scalar", c.st, c._a_puse[1])
                    vex = None
                    for ks in range(4):
                        vex = p.op("scalar", lambda nc, ks=ks: nc.scalar.activation(
                            ptt[:, ks, :], c.PSB[ks][:, :], AF.Exp,
                            bias=mcol, scale=1.0 / 8.0), inc=c.ss)
                        c.bank_consumed(ks, c.ss, vex)
                    vmask = None
                    if kt == tt:
                        p.wait("vector", c.ss, vex)
                        if getattr(c, "_a_puse", None) is not None:
                            p.wait("vector", c.st, c._a_puse[1])
                        for ks in range(4):
                            delta = ks * 128
                            vmask = p.op("vector", lambda nc, ks=ks, delta=delta:
                                         nc.vector.tensor_tensor(
                                             ptt[:, ks, :], ptt[:, ks, :],
                                             c.CAUS[:, 384 - delta:896 - delta],
                                             ALU.mult), inc=c.sv)
                        p.wait("tensor", c.sv, vmask)
                    else:
                        p.wait("tensor", c.ss, vex)
                    c.pe_guard("PR")
                    vl = None
                    for ks in range(4):
                        vl = p.op("tensor", lambda nc, ks=ks, p0=p0, tp=tp:
                                  nc.tensor.matmul(
                                      c.PR[p0:p0 + 1, :], c.ONECOL16[:],
                                      ptt[:, ks, :], start=(ks == 0),
                                      stop=(ks == 3), tile_position=tp), inc=c.st)
                    c.pe_guard(4)
                    vpv = None
                    for ks in range(4):
                        vpv = p.op("tensor", lambda nc, ks=ks, p0=p0, tp=tp,
                                   vbuf=vbuf: nc.tensor.matmul(
                                       c.PSB[4][p0:p0 + 64, :],
                                       vbuf[:, ks, p0:p0 + 64], ptt[:, ks, :],
                                       start=(ks == 0), stop=(ks == 3),
                                       tile_position=tp), inc=c.st)
                    c._a_puse = (vmask if vmask else c.sv.n, vpv)
                    c._a_vuse = getattr(c, "_a_vuse", {})
                    c._a_vuse[kslot] = vpv
                    oa = c.FA[p0:p0 + 64, j * 512:(j + 1) * 512]
                    la = c.FA[p0:p0 + 1, 4096 + j * 512:4096 + (j + 1) * 512]
                    p.wait("vector", c.st, vpv)
                    if kt == 0:
                        p.op("vector", lambda nc, oa=oa, p0=p0: nc.vector.tensor_copy(
                            oa, c.PSB[4][p0:p0 + 64, :]))
                        vacc = p.op("vector", lambda nc, la=la, p0=p0:
                                    nc.vector.tensor_copy(la, c.PR[p0:p0 + 1, :]),
                                    inc=c.sv)
                    else:
                        p.op("vector", lambda nc, oa=oa, p0=p0: nc.vector.tensor_tensor(
                            oa, oa, c.PSB[4][p0:p0 + 64, :], ALU.add))
                        vacc = p.op("vector", lambda nc, la=la, p0=p0:
                                    nc.vector.tensor_tensor(
                                        la, la, c.PR[p0:p0 + 1, :], ALU.add),
                                    inc=c.sv)
                    c.bank_consumed(4, c.sv, vacc)
                    c.bank_consumed("PR", c.sv, vacc)
            for hh in range(2):
                h = 2 * j + hh
                p0 = hh * 64
                tp = None if hh == 0 else (0, 64)
                oa = c.FA[p0:p0 + 64, j * 512:(j + 1) * 512]
                la = c.FA[p0:p0 + 1, 4096 + j * 512:4096 + (j + 1) * 512]
                rl = c.SMF[p0:p0 + 1, 520:1032]
                vsmall(c, lambda nc, rl=rl, la=la: nc.vector.reciprocal(rl, la))
                p.wait("scalar", c.sv, c.sv.n)
                vrl = p.op("scalar", lambda nc, rl=rl, p0=p0: nc.scalar.copy(
                    c.SMB[p0:p0 + 1, 0:512], rl), inc=c.ss)
                p.wait("tensor", c.ss, vrl)
                c.pe_guard(5)
                tpb = None if hh == 0 else (64, 64)
                vbc2 = p.op("tensor", lambda nc, p0=p0, tpb=tpb: nc.tensor.matmul(
                    c.PSB[5][p0:p0 + 64, :], c.ONES128[p0:p0 + 1, 0:64],
                    c.SMB[p0:p0 + 1, 0:512], start=True, stop=True,
                    tile_position=tpb), inc=c.st)
                p.wait("vector", c.st, vbc2)
                vfin = p.op("vector", lambda nc, oa=oa, p0=p0, j=j:
                            nc.vector.tensor_tensor(
                                attn16[p0:p0 + 64, j, :], oa,
                                c.PSB[5][p0:p0 + 64, :], ALU.mult), inc=c.sv)
                c.bank_consumed(5, c.sv, vfin)
        attn_ready = c.sv.n
        p.wait("tensor", c.sv, attn_ready)
        for oc in range(DCH):
            woff = (oc % 2) * 2048
            wb_guard(c, woff)
            wap, wv = stream_w(c, "l0a_oT", woff, 0, DCH, oc * 128, 128)
            p.wait("tensor", c.dl, wv)
            bank = oc % 2
            c.pe_guard(bank)
            vm = None
            for kc in range(DCH):
                vm = p.op("tensor", lambda nc, kc=kc, bank=bank, wap=wap:
                          nc.tensor.matmul(
                              c.PSB[bank][:, :], wap[:, kc, :], attn16[:, kc, :],
                              start=(kc == 0), stop=(kc == DCH - 1)), inc=c.st)
            wb_used(c, woff, vm)
            p.wait("vector", c.st, vm)
            rxa = c.RX[:, oc, tt * TT:(tt + 1) * TT]
            vr = p.op("vector", lambda nc, bank=bank, rxa=rxa:
                      nc.vector.tensor_tensor(
                          rxa, rxa, c.PSB[bank][:, :], ALU.add), inc=c.sv)
            c.bank_consumed(bank, c.sv, vr)
        c._a_attn_use = c.st.n


SECTION_FUNCS["l0a"] = lambda c, first: emit_l0a(c)



SCR_SSM_IN = 31_900_000
SCR_SSM_H = 40_300_000
SCR_SSM_DT = 48_700_000
SCR_SSM_DTH = 0
SCR_SSM_B = 31_800_000
SCR_SSM_G = 13_600_000


def scrf(c, elem_off, rows, cols, row_stride=None, b=False):
    if row_stride is None:
        row_stride = cols
    return bass.AP(c.scrB if b else c.scr, elem_off,
                   [[row_stride, rows], [1, cols]])


def emit_l1b(c):
    p, dr = c.p, c.dr
    barrier(c)
    ln16 = kb(c, 0, [128, DCH, TT])
    sg16 = kb(c, 4096, [128, 8, TT])
    vck = p.dma("sync", c.SMF[:, 1168:1216], dr["l1b_cw"][:], c.dl)
    vck = p.dma("sync", c.SMF[:, 1216:1232], dr["l1b_cb"][:], c.dl)
    vck = p.dma("sync", c.SMF[:, 1232:1248], dr["l1b_bt"][:], c.dl)
    vck = p.dma("sync", c.SMF[:, 1248:1264], dr["l1b_skip"][:], c.dl)
    vck = p.dma("sync", c.SMF[:, 1536:1792], dr["l1b_alog"][:], c.dl)
    p.wait("scalar", c.dl, vck)
    va = p.op("scalar", lambda nc: nc.scalar.activation(
        c.SMF[:, 1536:1792], c.SMF[:, 1536:1792], AF.Exp), inc=c.ss)
    p.wait("vector", c.ss, va)
    p.op("vector", lambda nc: nc.vector.tensor_scalar(
        c.SMF[:, 1536:1792], c.SMF[:, 1536:1792], -1.0, None, ALU.mult), inc=c.sv)
    p.op("vector", lambda nc: nc.vector.memset(c.SMF[:, 1792:2048], 0.0), inc=c.sv)
    vconsts = c.sv.n
    p.wait("scalar", c.sv, vconsts)
    p.wait("tensor", c.sv, vconsts)
    acol = lambda ct, s: c.SMF[:, 1536 + ct * 16 + s:1536 + ct * 16 + s + 1]
    stcol = lambda ct, s: c.SMF[:, 1792 + ct * 16 + s:1792 + ct * 16 + s + 1]
    for tt in range(NTT):
        emit_ln_ff(c, tt, ln16)
        ln_done = c.sv.n
        p.wait("tensor", c.sv, ln_done)
        for oc in range(32):
            woff = (oc % 2) * 2048
            wb_guard(c, woff)
            wap, wv = stream_w(c, "l1b_Win", woff, 0, DCH, oc * 128, 128)
            p.wait("tensor", c.dl, wv)
            bank = oc % 2
            c.pe_guard(bank)
            vm = None
            for kc in range(DCH):
                vm = p.op("tensor", lambda nc, kc=kc, bank=bank, wap=wap:
                          nc.tensor.matmul(
                              c.PSB[bank][:, :], wap[:, kc, :], ln16[:, kc, :],
                              start=(kc == 0), stop=(kc == DCH - 1)), inc=c.st)
            wb_used(c, woff, vm)
            if oc < 16:
                slot = fa(c, 7680, [128, TT])
                p.wait("vector", c.st, vm)
                ve = p.op("vector", lambda nc, bank=bank, slot=slot:
                          nc.vector.tensor_copy(slot, c.PSB[bank][:, :]), inc=c.sv)
                c.bank_consumed(bank, c.sv, ve)
                p.wait("sync", c.sv, ve)
                dst = scrf(c, SCR_SSM_IN + (tt * 16 + oc) * (128 * TT), 128, TT)
                vs = p.dma("sync", dst, slot, c.dl)
                p.wait("vector", c.dl, vs)
            else:
                ct = oc - 16
                if ct % 8 == 0 and hasattr(c, "_b_sg_ship"):
                    p.wait("scalar", c.dl, c._b_sg_ship)
                p.wait("scalar", c.st, vm)
                ve = p.op("scalar", lambda nc, bank=bank, ct=ct:
                          nc.scalar.activation(sg16[:, ct % 8, :],
                                               c.PSB[bank][:, :],
                                               AF.Silu), inc=c.ss)
                c.bank_consumed(bank, c.ss, ve)
                if ct % 8 == 7:
                    p.wait("sync", c.ss, c.ss.n)
                    dst = bass.AP(c.scr.bitcast(BF16),
                                  SCR_SSM_G + tt * (128 * 8192)
                                  + (ct // 8) * 4096,
                                  [[8192, 128], [1, 4096]])
                    c._b_sg_ship = p.dma("sync", dst, kb(c, 4096, [128, 4096]),
                                         c.dl)
    barrier(c)
    h16 = kb(c, 0, [128, 16, TT])
    for tt in range(NTT):
        for ct in range(16):
            ibuf = fa(c, 0, [128, 514])
            vz = None
            if tt == 0:
                vz = p.op("vector", lambda nc, ibuf=ibuf: nc.vector.memset(
                    ibuf[:, 0:1], 0.0), inc=c.sv)
                p.wait("sync", c.sv, vz)
            if hasattr(c, "_b_ib_use"):
                p.wait("sync", c.sv, c._b_ib_use)
                p.wait("sync", c.ss, c._b_ib_use2)
            v1 = p.dma("sync", ibuf[:, 1:513],
                       scrf(c, SCR_SSM_IN + (tt * 16 + ct) * (128 * TT), 128, TT),
                       c.dl)
            if tt > 0:
                v1 = p.dma("sync", ibuf[:, 0:1],
                           scrf(c, SCR_SSM_IN + ((tt - 1) * 16 + ct) * (128 * TT)
                                + 511, 128, 1, row_stride=TT), c.dl, slow=True)
            if tt < NTT - 1:
                v1 = p.dma("sync", ibuf[:, 513:514],
                           scrf(c, SCR_SSM_IN + ((tt + 1) * 16 + ct) * (128 * TT),
                                128, 1, row_stride=TT), c.dl, slow=True)
            else:
                v1 = p.dma("sync", ibuf[:, 513:514],
                           scrf(c, SCR_SSM_IN + (tt * 16 + ct) * (128 * TT),
                                128, 1, row_stride=TT), c.dl, slow=True)
            p.wait("vector", c.dl, v1)
            if tt == NTT - 1:
                p.op("vector", lambda nc, ibuf=ibuf: nc.vector.memset(
                    ibuf[:, 513:514], 0.0))
            cw0 = c.SMF[:, 1168 + ct * 3:1169 + ct * 3]
            cw1 = c.SMF[:, 1169 + ct * 3:1170 + ct * 3]
            cw2 = c.SMF[:, 1170 + ct * 3:1171 + ct * 3]
            cbc = c.SMF[:, 1216 + ct:1217 + ct]
            conv = fa(c, 640, [128, TT])
            p.op("vector", lambda nc, ibuf=ibuf, conv=conv, cw0=cw0:
                 nc.vector.tensor_scalar(conv, ibuf[:, 0:512], cw0, None, ALU.mult))
            p.op("vector", lambda nc, ibuf=ibuf, conv=conv, cw1=cw1:
                 nc.vector.scalar_tensor_tensor(
                     conv, ibuf[:, 1:513], cw1, conv, ALU.mult, ALU.add))
            vcv = p.op("vector", lambda nc, ibuf=ibuf, conv=conv, cw2=cw2:
                       nc.vector.scalar_tensor_tensor(
                           conv, ibuf[:, 2:514], cw2, conv, ALU.mult, ALU.add),
                       inc=c.sv)
            c._b_ib_use = vcv
            p.wait("scalar", c.sv, vcv)
            hslot = fa(c, 1152, [128, TT])
            p.op("scalar", lambda nc, conv=conv, hslot=hslot, cbc=cbc:
                 nc.scalar.activation(hslot, conv, AF.Silu, bias=cbc))
            vh = p.op("scalar", lambda nc, hslot=hslot, ct=ct:
                      nc.scalar.copy(h16[:, ct, :], hslot), inc=c.ss)
            c._b_ib_use2 = vh
            p.wait("sync", c.ss, vh)
            vs = p.dma("sync",
                       scrf(c, SCR_SSM_H + (tt * 16 + ct) * (128 * TT), 128, TT),
                       hslot, c.dl)
            p.wait("scalar", c.dl, vs)
        p.wait("tensor", c.ss, c.ss.n)
        c.pe_guard(2)
        vm = None
        for kc in range(16):
            wb_guard(c, 3072)
            wap, wv = stream_w(c, "l1b_Wst", 3072, kc, 1, 16, 16)
            p.wait("tensor", c.dl, wv)
            vm = p.op("tensor", lambda nc, kc=kc, wap=wap: nc.tensor.matmul(
                c.PSB[2][0:16, :], wap[:, 0, :], h16[:, kc, :],
                start=(kc == 0), stop=(kc == 15)), inc=c.st)
            wb_used(c, 3072, vm)
        p.wait("vector", c.st, vm)
        bslot = fa(c, 0, [16, TT])
        vb2 = p.op("vector", lambda nc, bslot=bslot: nc.vector.tensor_copy(
            bslot, c.PSB[2][0:16, :]), inc=c.sv)
        c.bank_consumed(2, c.sv, vb2)
        p.wait("sync", c.sv, vb2)
        vs = p.dma("sync", scrf(c, SCR_SSM_B + tt * (16 * TT), 16, TT), bslot, c.dl)
        p.wait("vector", c.dl, vs)
        for ct in range(16):
            woff = (ct % 2) * 2048
            wb_guard(c, woff)
            wap, wv = stream_w(c, "l1b_Wt", woff, 0, 16, ct * 128, 128)
            p.wait("tensor", c.dl, wv)
            bank = ct % 2
            c.pe_guard(bank)
            vm = None
            for kc in range(16):
                vm = p.op("tensor", lambda nc, kc=kc, bank=bank, wap=wap:
                          nc.tensor.matmul(
                              c.PSB[bank][:, :], wap[:, kc, :], h16[:, kc, :],
                              start=(kc == 0), stop=(kc == 15)), inc=c.st)
            wb_used(c, woff, vm)
            p.wait("vector", c.st, vm)
            btc = c.SMF[:, 1232 + ct:1233 + ct]
            u = fa(c, 0, [128, TT])
            r = fa(c, 512, [128, TT])
            rn = fa(c, 1024, [128, TT])
            dts = fa(c, 1536, [128, TT])
            dth = fa(c, 2048, [128, TT])
            p.op("vector", lambda nc, bank=bank, u=u, btc=btc:
                 nc.vector.tensor_scalar(u, c.PSB[bank][:, :], btc, None, ALU.add),
                 inc=c.sv)
            c.bank_consumed(bank, c.sv, c.sv.n)
            p.op("vector", lambda nc, u=u, r=r: nc.vector.tensor_scalar(
                r, u, 0.0, None, ALU.max))
            vpre = p.op("vector", lambda nc, u=u, rn=rn: nc.vector.tensor_scalar(
                rn, u, -1.0, 0.0, ALU.mult, ALU.max), inc=c.sv)
            p.op("vector", lambda nc, u=u, r=r, rn=rn: nc.vector.tensor_tensor(
                u, r, rn, ALU.add), inc=c.sv)
            vau = c.sv.n
            p.wait("scalar", c.sv, vau)
            p.op("scalar", lambda nc, u=u: nc.scalar.activation(
                u, u, AF.Exp, scale=-1.0))
            vln = p.op("scalar", lambda nc, u=u: nc.scalar.activation(
                u, u, AF.Ln, bias=1.0), inc=c.ss)
            p.wait("vector", c.ss, vln)
            p.op("vector", lambda nc, u=u, r=r, dts=dts: nc.vector.tensor_tensor(
                dts, r, u, ALU.add))
            hbuf = fa(c, 2560, [128, TT])
            vhb = p.dma("sync",
                        scrf(c, SCR_SSM_H + (tt * 16 + ct) * (128 * TT), 128, TT),
                        hbuf, c.dl) if False else None
            vhb = p.dma("sync", hbuf,
                        scrf(c, SCR_SSM_H + (tt * 16 + ct) * (128 * TT), 128, TT),
                        c.dl)
            p.wait("vector", c.dl, vhb)
            vdth = p.op("vector", lambda nc, dts=dts, hbuf=hbuf, dth=dth:
                        nc.vector.tensor_tensor(dth, dts, hbuf, ALU.mult), inc=c.sv)
            p.wait("sync", c.sv, vdth)
            vs1 = p.dma("sync", scrf(c, SCR_SSM_DT + (tt * 16 + ct) * (128 * TT),
                                     128, TT), dts, c.dl)
            vs2 = p.dma("sync", scrf(c, SCR_SSM_DTH + (tt * 16 + ct) * (128 * TT),
                                     128, TT, b=True), dth, c.dl)
            p.wait("vector", c.dl, vs2)
    barrier(c)
    yg16 = kb(c, 0, [128, 8, TT])
    SELS = c.KB[0:16, 8192:10240]
    for half2 in range(2):
        vsl = p.dma("sync", c.WS[0:16, 0:1024],
                    dr["c_sels"][:, half2 * 1024:(half2 + 1) * 1024], c.dl)
        p.wait("scalar", c.dl, vsl)
        vsl2 = p.op("scalar", lambda nc, half2=half2: nc.scalar.copy(
            c.KB[0:16, 8192 + half2 * 1024:8192 + (half2 + 1) * 1024],
            c.WS[0:16, 0:1024]), inc=c.ss)
        p.wait("sync", c.ss, vsl2)
    p.wait("tensor", c.ss, c.ss.n)
    bt16s = c.SMB[0:16, 1536:2048]
    for tt in range(NTT):
        vb1 = p.dma("sync", c.SMF[0:16, 0:512],
                    scrf(c, SCR_SSM_B + tt * (16 * TT), 16, TT), c.dl)
        p.wait("scalar", c.dl, vb1)
        vb16 = p.op("scalar", lambda nc: nc.scalar.copy(
            bt16s, c.SMF[0:16, 0:512]), inc=c.ss)
        p.wait("tensor", c.ss, vb16)
        for ct in range(16):
            dtb = fa(c, 0, [128, TT])
            dthb = fa(c, 512, [128, TT])
            dab = fa(c, 1024, [128, TT])
            dbb = fa(c, 1536, [128, TT])
            memb = fa(c, 2048, [128, TT])
            Yb = fa(c, 2560, [128, TT])
            yb = fa(c, 3072, [128, TT])
            hb = fa(c, 3584, [128, TT])
            if hasattr(c, "_c_dt_use"):
                p.wait("sync", c.ss, c._c_dt_use[0])
                p.wait("sync", c.sv, c._c_dt_use[1])
            v1 = p.dma("sync", dtb, scrf(c, SCR_SSM_DT + (tt * 16 + ct) * (128 * TT),
                                         128, TT), c.dl)
            v2 = p.dma("sync", dthb, scrf(c, SCR_SSM_DTH + (tt * 16 + ct) * (128 * TT),
                                          128, TT, b=True), c.dl)
            v3 = p.dma("sync", hb, scrf(c, SCR_SSM_H + (tt * 16 + ct) * (128 * TT),
                                        128, TT), c.dl)
            p.wait("scalar", c.dl, v1)
            p.wait("vector", c.dl, v3)
            for s in range(16):
                bank = 2 + (s % 2)
                c.pe_guard(bank)
                vbr = p.op("tensor", lambda nc, s=s, bank=bank: nc.tensor.matmul(
                    c.PSB[bank][:, :], SELS[:, s * 128:(s + 1) * 128],
                    bt16s, start=True, stop=True), inc=c.st)
                vda = p.op("scalar", lambda nc, ct=ct, s=s, dtb=dtb, dab=dab:
                           nc.scalar.activation(dab, dtb, AF.Exp,
                                                scale=acol(ct, s)), inc=c.ss)
                p.wait("vector", c.ss, vda)
                p.wait("vector", c.st, vbr)
                vdb = p.op("vector", lambda nc, dthb=dthb, dbb=dbb, bank=bank:
                           nc.vector.tensor_tensor(dbb, dthb, c.PSB[bank][:, :],
                                                   ALU.mult), inc=c.sv)
                c.bank_consumed(bank, c.sv, vdb)
                vsc = p.op("vector", lambda nc, dab=dab, dbb=dbb, memb=memb, ct=ct,
                           s=s: nc.vector.tensor_tensor_scan(
                               memb, dab, dbb, stcol(ct, s), ALU.mult, ALU.add),
                           inc=c.sv)
                vsmall(c, lambda nc, memb=memb, ct=ct, s=s: nc.vector.tensor_copy(
                    stcol(ct, s), memb[:, TT - 1:TT]))
                p.wait("gpsimd", c.sv, vsc)
                if s == 0:
                    vy = p.op("gpsimd", lambda nc, memb=memb, Yb=Yb:
                              nc.gpsimd.tensor_copy(Yb, memb), inc=c.sg)
                else:
                    vy = p.op("gpsimd", lambda nc, memb=memb, Yb=Yb:
                              nc.gpsimd.tensor_tensor(Yb, Yb, memb, ALU.add),
                              inc=c.sg)
                p.wait("vector", c.sg, vy)
            skc = c.SMF[:, 1248 + ct:1249 + ct]
            p.op("vector", lambda nc, hb=hb, Yb=Yb, yb=yb, skc=skc:
                 nc.vector.scalar_tensor_tensor(yb, hb, skc, Yb, ALU.mult, ALU.add))
            sgbuf = kb(c, 4096, [128, TT])
            vsg = p.dma("sync", sgbuf,
                        bass.AP(c.scr.bitcast(BF16),
                                SCR_SSM_G + tt * (128 * 8192) + ct * 512,
                                [[8192, 128], [1, 512]]), c.dl)
            p.wait("vector", c.dl, vsg)
            vyg = p.op("vector", lambda nc, yb=yb, sgbuf=sgbuf, ct=ct:
                       nc.vector.tensor_tensor(yg16[:, ct % 8, :], yb, sgbuf,
                                               ALU.mult), inc=c.sv)
            c._c_dt_use = (c.ss.n, vyg)
            if ct % 8 == 7:
                half = ct // 8
                yg_done = c.sv.n
                p.wait("tensor", c.sv, yg_done)
                for oc in range(DCH):
                    woff = (oc % 2) * 2048
                    wb_guard(c, woff)
                    wap, wv = stream_w(c, "l1b_Wout", woff, half * 8, 8,
                                       oc * 128, 128)
                    p.wait("tensor", c.dl, wv)
                    bank = oc % 2
                    c.pe_guard(bank)
                    vm = None
                    for kc in range(8):
                        vm = p.op("tensor", lambda nc, kc=kc, bank=bank, wap=wap:
                                  nc.tensor.matmul(
                                      c.PSB[bank][:, :], wap[:, kc, :],
                                      yg16[:, kc, :],
                                      start=(kc == 0), stop=(kc == 7)), inc=c.st)
                    wb_used(c, woff, vm)
                    c._c_yg_use = vm
                    p.wait("vector", c.st, vm)
                    rxa = c.RX[:, oc, tt * TT:(tt + 1) * TT]
                    vr = p.op("vector", lambda nc, bank=bank, rxa=rxa:
                              nc.vector.tensor_tensor(rxa, rxa,
                                                      c.PSB[bank][:, :],
                                                      ALU.add), inc=c.sv)
                    c.bank_consumed(bank, c.sv, vr)


SECTION_FUNCS["l1b"] = lambda c, first: emit_l1b(c)



TTF = 256
DT_FF = 0.1


def _ff_j_body(c, p, it, j, integ, vbuf, h_sh, hm1, hm2, PL, SP,
               w1rep, w2rep, haltrep, mu_c, sw_c, osw_c, CM, DTP, SWP, OSWP):
    pool = (j >= 5)
    eng = "gpsimd" if pool else "vector"
    esem = c.sg if pool else c.sv
    wb_guard(c, 0)
    wb_guard(c, 1024)
    wb_guard(c, 2048)
    wsw = []
    for pi in range(3):
        w_, v_ = stream_w(c, "ff_sw2", pi * 128, 0, 1,
                          pi * 1024 + j * 128, 128, rows=64)
        wsw.append(w_)
    wea = []
    for pi in range(3):
        w_, v_ = stream_w(c, "ff_ew2", 512 + pi * 512, 0, 4,
                          pi * 1024 + j * 128, 128)
        wea.append(w_)
    web = []
    for pi in range(3):
        w_, v_ = stream_w(c, "ff_eb2", 2048 + pi * 128, 0, 1,
                          pi * 1024 + j * 128, 128, rows=8)
        web.append(w_)
    p.wait("tensor", c.dl, v_)
    integ_j = integ[:, j, :]
    v_j = vbuf[:, j, :]
    tneg = SP(2)
    GSL = {"sh": PL(2), "e1": PL(3), "e2": PL(4)}
    VNS = {"sh": PL(5), "e1": PL(6), "e2": PL(7)}
    if c._ff_jdone is not None:
        p.wait(eng, c._ff_jdone[1], c._ff_jdone[0])
    if pool:
        p.wait("gpsimd", c.sv, c.sv.n)
    else:
        p.wait("vector", c.sg, c.sg.n)

    def E(fn, inc=None):
        return p.op(eng, fn, inc=inc or esem)
    if pool:
        p.wait("vector", c.sg, c.sg.n)
    vtn = p.op("vector", lambda nc: nc.vector.tensor_scalar(
        tneg, integ_j, -1.0, mu_c(j), ALU.mult, ALU.add), inc=c.sv)
    if pool:
        p.wait("gpsimd", c.sv, vtn)
    for si, srcname in enumerate(("sh", "e1", "e2")):
        msrc = None
        if srcname == "e1":
            msrc = c.SMB[0:8, 0:256]
        elif srcname == "e2":
            msrc = c.SMB[0:8, 256:512]
        hmt = hm1 if srcname == "e1" else hm2
        vm = None
        for pi in range(3):
            if it == 0 and pi == 0:
                continue
            bank = pi
            c.pe_guard(bank)
            if srcname == "sh":
                vm = p.op("tensor", lambda nc, pi=pi, bank=bank:
                          nc.tensor.matmul(
                              c.PSB[bank][:, :TTF], wsw[pi][:, 0, :],
                              h_sh[:, :], start=True, stop=True), inc=c.st)
            else:
                for kc in range(4):
                    p.op("tensor", lambda nc, pi=pi, kc=kc, bank=bank,
                         hmt=hmt: nc.tensor.matmul(
                             c.PSB[bank][:, :TTF], wea[pi][:, kc, :],
                             hmt[:, kc, :], start=(kc == 0), stop=False))
                vm = p.op("tensor", lambda nc, pi=pi, bank=bank,
                          msrc=msrc: nc.tensor.matmul(
                              c.PSB[bank][:, :TTF], web[pi][:, 0, :],
                              msrc, start=False, stop=True), inc=c.st)
        if c._ff_pl01_use is not None:
            p.wait("scalar", c._ff_pl01_use[1], c._ff_pl01_use[0])
        p.wait("scalar", c.st, vm)
        A_, B_, G_ = PL(0), PL(1), GSL[srcname]
        if srcname == "sh":
            def sb2b(pi, j=j):
                return c.SMF[:, CM + 64 + 8 * pi + j:CM + 65 + 8 * pi + j]
        else:
            def sb2b(pi):
                return 0.0
        if it != 0:
            va_ = p.op("scalar", lambda nc, A_=A_: nc.scalar.activation(
                A_, c.PSB[0][:, :TTF], AF.Sigmoid, bias=sb2b(0)), inc=c.ss)
            c.bank_consumed(0, c.ss, va_)
        p.op("scalar", lambda nc, B_=B_: nc.scalar.activation(
            B_, c.PSB[1][:, :TTF], AF.Exp, bias=sb2b(1)))
        p.op("scalar", lambda nc, B_=B_: nc.scalar.activation(
            B_, B_, AF.Ln, bias=1.0))
        vg_ = p.op("scalar", lambda nc, G_=G_: nc.scalar.activation(
            G_, c.PSB[2][:, :TTF], AF.Sigmoid, bias=sb2b(2)), inc=c.ss)
        c.bank_consumed(1, c.ss, vg_)
        c.bank_consumed(2, c.ss, vg_)
        p.wait(eng, c.ss, vg_)
        vn = VNS[srcname]
        if it == 0:
            vv_ = E(lambda nc, vn=vn, B_=B_: getattr(nc, eng)
                    .tensor_tensor(vn, B_, tneg, ALU.mult))
        else:
            E(lambda nc, vn=vn, A_=A_: getattr(nc, eng)
              .tensor_tensor(vn, A_, v_j, ALU.mult))
            E(lambda nc, B_=B_: getattr(nc, eng)
              .tensor_tensor(PL(0), B_, tneg, ALU.mult))
            vv_ = E(lambda nc, vn=vn: getattr(nc, eng)
                    .tensor_tensor(vn, vn, PL(0), ALU.add))
        c._ff_pl01_use = (vv_, esem)
    W0, W1 = PL(0), PL(1)
    E(lambda nc: getattr(nc, eng).tensor_tensor(
        W0, VNS["e1"], w1rep, ALU.mult))
    E(lambda nc: getattr(nc, eng).tensor_tensor(
        W1, VNS["e2"], w2rep, ALU.mult))
    E(lambda nc: getattr(nc, eng).tensor_tensor(
        VNS["e1"], GSL["e1"], W0, ALU.mult))
    E(lambda nc: getattr(nc, eng).tensor_tensor(
        VNS["e2"], GSL["e2"], W1, ALU.mult))
    E(lambda nc: getattr(nc, eng).tensor_tensor(W0, W0, W1, ALU.add))
    E(lambda nc: getattr(nc, eng).tensor_tensor(
        VNS["e1"], VNS["e1"], VNS["e2"], ALU.add))
    if not pool:
        E(lambda nc: nc.vector.scalar_tensor_tensor(
            VNS["e1"], VNS["e1"], DT_FF, integ_j, ALU.mult, ALU.add))
        E(lambda nc: nc.vector.tensor_tensor(
            GSL["sh"], VNS["sh"], GSL["sh"], ALU.mult))
        E(lambda nc: nc.vector.scalar_tensor_tensor(
            GSL["sh"], GSL["sh"], DT_FF, integ_j, ALU.mult, ALU.add))
        E(lambda nc: nc.vector.tensor_scalar(
            VNS["e1"], VNS["e1"], osw_c, None, ALU.mult))
        E(lambda nc: nc.vector.scalar_tensor_tensor(
            VNS["e1"], GSL["sh"], sw_c, VNS["e1"], ALU.mult, ALU.add))
        E(lambda nc: nc.vector.tensor_scalar(
            W0, W0, osw_c, None, ALU.mult))
        E(lambda nc: nc.vector.scalar_tensor_tensor(
            W0, VNS["sh"], sw_c, W0, ALU.mult, ALU.add))
    else:
        g = "gpsimd"
        E(lambda nc: nc.gpsimd.tensor_tensor(
            VNS["e1"], VNS["e1"], DTP, ALU.mult))
        E(lambda nc: nc.gpsimd.tensor_tensor(
            VNS["e1"], VNS["e1"], integ_j, ALU.add))
        E(lambda nc: nc.gpsimd.tensor_tensor(
            GSL["sh"], VNS["sh"], GSL["sh"], ALU.mult))
        E(lambda nc: nc.gpsimd.tensor_tensor(
            GSL["sh"], GSL["sh"], DTP, ALU.mult))
        E(lambda nc: nc.gpsimd.tensor_tensor(
            GSL["sh"], GSL["sh"], integ_j, ALU.add))
        E(lambda nc: nc.gpsimd.tensor_tensor(
            VNS["e1"], VNS["e1"], OSWP, ALU.mult))
        E(lambda nc: nc.gpsimd.tensor_tensor(
            GSL["sh"], GSL["sh"], SWP, ALU.mult))
        E(lambda nc: nc.gpsimd.tensor_tensor(
            VNS["e1"], VNS["e1"], GSL["sh"], ALU.add))
        E(lambda nc: nc.gpsimd.tensor_tensor(W0, W0, OSWP, ALU.mult))
        E(lambda nc: nc.gpsimd.tensor_tensor(
            GSL["e1"], VNS["sh"], SWP, ALU.mult))
        E(lambda nc: nc.gpsimd.tensor_tensor(W0, W0, GSL["e1"], ALU.add))
    E(lambda nc: getattr(nc, eng).tensor_tensor(
        VNS["e1"], VNS["e1"], integ_j, ALU.subtract))
    E(lambda nc: getattr(nc, eng).tensor_tensor(
        VNS["e1"], VNS["e1"], haltrep, ALU.mult))
    E(lambda nc: getattr(nc, eng).tensor_tensor(
        integ_j, integ_j, VNS["e1"], ALU.add))
    E(lambda nc: getattr(nc, eng).tensor_tensor(W0, W0, v_j, ALU.subtract))
    E(lambda nc: getattr(nc, eng).tensor_tensor(W0, W0, haltrep, ALU.mult))
    vj_done = E(lambda nc: getattr(nc, eng).tensor_tensor(
        v_j, v_j, W0, ALU.add))
    c._ff_jdone = (vj_done, esem)
    c._ff_pl01_use = (vj_done, esem)
    wb_used(c, 0, c.st.n)
    wb_used(c, 1024, c.st.n)
    wb_used(c, 2048, c.st.n)



def emit_ff(c):
    p, dr = c.p, c.dr
    barrier(c)
    c._ff_jdone = None
    c._ff_pl01_use = None
    CM = 1024
    for name, off, n in (("ff_mu", CM, 8), ("ff_iw", CM + 8, 8),
                         ("ff_rb1", CM + 24, 2), ("ff_hb1", CM + 26, 2),
                         ("ff_eb1", CM + 28, 4), ("ff_qb1", CM + 32, 16),
                         ("ff_qb2", CM + 48, 8), ("ff_rb2", CM + 56, 8),
                         ("ff_sb2", CM + 64, 24)):
        vl = p.dma("sync", c.SMF[:, off:off + n], dr[name][:, 0:n], c.dl)
    vl = p.dma("sync", c.SMF[0:64, CM + 88:CM + 89], dr["ff_sb1"][:, 0:1], c.dl)
    vl = p.dma("sync", c.SMF[0:1, CM + 89:CM + 90], dr["ff_swt"][:, 0:1], c.dl)
    vl = p.dma("sync", c.SMF[0:1, CM + 90:CM + 91], dr["ff_hb2"][:, 0:1], c.dl)
    p.wait("scalar", c.dl, vl)
    p.wait("vector", c.dl, vl)
    mu_c = lambda j: c.SMF[:, CM + j:CM + 1 + j]
    iw_c = lambda j: c.SMF[:, CM + 8 + j:CM + 9 + j]
    vsmall(c, lambda nc: nc.vector.tensor_tensor(
        c.SMF[:, CM + 16:CM + 24], c.SMF[:, CM + 8:CM + 16],
        c.SMF[:, CM + 48:CM + 56], ALU.mult))
    vsw = p.op("scalar", lambda nc: nc.scalar.activation(
        c.SMF[0:1, CM + 89:CM + 90], c.SMF[0:1, CM + 89:CM + 90], AF.Sigmoid),
        inc=c.ss)
    p.wait("vector", c.ss, vsw)
    vsmall(c, lambda nc: nc.vector.tensor_copy(
        c.SMB[0:1, 0:1], c.SMF[0:1, CM + 89:CM + 90]))
    p.wait("tensor", c.sv, c.sv.n)
    c.pe_guard(5)
    vbb = p.op("tensor", lambda nc: nc.tensor.matmul(
        c.PSB[5][:, 0:1], c.ONEROW16[:], c.SMB[0:1, 0:1], start=True, stop=True),
        inc=c.st)
    p.wait("vector", c.st, vbb)
    vsmall(c, lambda nc: nc.vector.tensor_copy(
        c.SMF[:, CM + 91:CM + 92], c.PSB[5][:, 0:1]))
    vsmall(c, lambda nc: nc.vector.tensor_scalar(
        c.SMF[:, CM + 92:CM + 93], c.SMF[:, CM + 91:CM + 92], -1.0, 1.0,
        ALU.mult, ALU.add))
    c.bank_consumed(5, c.sv, c.sv.n)
    sw_c = c.SMF[:, CM + 91:CM + 92]
    osw_c = c.SMF[:, CM + 92:CM + 93]
    DTP = c.SMF[:, 1216:1472]
    SWP = c.SMF[:, 1472:1728]
    OSWP = c.SMF[:, 1728:1984]
    p.op("vector", lambda nc: nc.vector.memset(DTP, DT_FF))
    p.op("vector", lambda nc: nc.vector.tensor_scalar(
        SWP, DTP, 0.0, sw_c, ALU.mult, ALU.add))
    p.op("vector", lambda nc: nc.vector.tensor_scalar(
        OSWP, DTP, 0.0, osw_c, ALU.mult, ALU.add), inc=c.sv)
    p.wait("gpsimd", c.sv, c.sv.n)
    integ = fa(c, 0, [128, DCH, TTF])
    vbuf = fa(c, 2048, [128, DCH, TTF])
    xf = fa(c, 4096, [128, DCH, TTF])
    PL = lambda k: fa(c, 6144 + k * TTF, [128, TTF])
    SP = lambda k: c.SMF[:, k * TTF:(k + 1) * TTF]
    xf16 = kb(c, 0, [128, DCH, TTF])
    ctx16 = kb(c, 2048, [128, 16, TTF])
    h_all = kb(c, 6144, [128, 4, TTF])
    hm1 = kb(c, 7168, [128, 4, TTF])
    hm2 = kb(c, 8192, [128, 4, TTF])
    h_sh = kb(c, 9216, [64, TTF])
    h_ha = kb(c, 9472, [128, 2, TTF])
    m1b = c.KB[:, 9984:9992]
    m2b = c.KB[:, 9992:10000]
    g16 = kb(c, 4096, [128, 16, TTF])
    pt3 = c.PT[:].rearrange("p (a b) -> p a b", b=128)
    NT_F = S // TTF
    for tt in range(NT_F):
        p.wait("vector", c.sg, c.sg.n)
        ln_done = emit_ln_ff(c, tt, xf16, tw=TTF, vout=xf)
        p.wait("vector", c.ss, ln_done)
        p.wait("tensor", c.ss, ln_done)
        p.wait("gpsimd", c.ss, ln_done)
        for j in range(DCH):
            p.op("vector", lambda nc, j=j: nc.vector.tensor_copy(
                integ[:, j, :], xf[:, j, :]))
        p.op("vector", lambda nc: nc.vector.memset(c.FA[:, 2048:4096], 0.0),
             inc=c.sv)
        init_done = c.sv.n
        p.wait("tensor", c.sv, init_done)
        for oc in range(2):
            wb_guard(c, 0)
            wap, wv = stream_w(c, "ff_rw1", 0, 0, DCH, oc * 128, 128)
            p.wait("tensor", c.dl, wv)
            c.pe_guard(oc)
            vm = None
            for kc in range(DCH):
                vm = p.op("tensor", lambda nc, kc=kc, oc=oc, wap=wap:
                          nc.tensor.matmul(
                              c.PSB[oc][:, :TTF], wap[:, kc, :], xf16[:, kc, :],
                              start=(kc == 0), stop=(kc == DCH - 1)), inc=c.st)
            wb_used(c, 0, vm)
            p.wait("scalar", c.st, vm)
            vh = p.op("scalar", lambda nc, oc=oc: nc.scalar.activation(
                h_all[:, oc, :], c.PSB[oc][:, :TTF], AF.Gelu,
                bias=c.SMF[:, CM + 24 + oc:CM + 25 + oc]), inc=c.ss)
            c.bank_consumed(oc, c.ss, vh)
        wb_guard(c, 1024)
        wap2, wv2 = stream_w(c, "ff_rw2", 1024, 0, 2, 0, 8)
        p.wait("tensor", c.dl, wv2)
        p.wait("tensor", c.ss, c.ss.n)
        for sub in range(2):
            c.pe_guard(2)
            vm = None
            for rc in range(2):
                vm = p.op("tensor", lambda nc, rc=rc, sub=sub, wap2=wap2:
                          nc.tensor.matmul(
                              c.PSB[2][:, 0:8], h_all[:, rc, sub * 128:(sub + 1) * 128],
                              wap2[:, rc, :], start=(rc == 0), stop=(rc == 1)),
                          inc=c.st)
            wb_used(c, 1024, vm)
            p.wait("vector", c.st, vm)
            lg = PL(0)[:, sub * 8:sub * 8 + 8]
            vlg = p.op("vector", lambda nc, lg=lg: nc.vector.tensor_tensor(
                lg, c.PSB[2][:, 0:8], c.SMF[:, CM + 56:CM + 64], ALU.add),
                inc=c.sv)
            c.bank_consumed(2, c.sv, vlg)
            p.wait("vector", c.sv, vlg)
            L8 = PL(1)
            for e in range(8):
                vsmall(c, lambda nc, e=e, lg=lg, L8=L8: nc.vector.tensor_copy(
                    L8[:, e * 8:(e + 1) * 8], lg))
            l3a = fa(c, 6144 + TTF, [128, 8, 8])
            l3b = bass.AP(c.FA, 6144 + TTF, [[8192, 128], [1, 8], [8, 8]])
            gt = fa(c, 6144 + 2 * TTF, [128, 8, 8])
            eq = fa(c, 6144 + 3 * TTF, [128, 8, 8])
            vsmall(c, lambda nc, l3a=l3a, l3b=l3b, gt=gt: nc.vector.tensor_tensor(
                gt, l3a, l3b, ALU.is_gt))
            vsmall(c, lambda nc, l3a=l3a, l3b=l3b, eq=eq: nc.vector.tensor_tensor(
                eq, l3a, l3b, ALU.is_equal))
            vsmall(c, lambda nc, eq=eq: nc.vector.tensor_tensor(
                eq.rearrange("p a b -> p (a b)") if False else
                fa(c, 6144 + 3 * TTF, [128, 64]),
                fa(c, 6144 + 3 * TTF, [128, 64]), c.TRI8[:, :], ALU.mult))
            vsmall(c, lambda nc, gt=gt, eq=eq: nc.vector.tensor_tensor(
                fa(c, 6144 + 2 * TTF, [128, 64]),
                fa(c, 6144 + 2 * TTF, [128, 64]),
                fa(c, 6144 + 3 * TTF, [128, 64]), ALU.add))
            rank = PL(4)
            vsmall(c, lambda nc, gt=gt, rank=rank: nc.vector.tensor_reduce(
                rank[:, 0:8], gt, axis=AX.X, op=ALU.add))
            vsmall(c, lambda nc, rank=rank: nc.vector.tensor_scalar(
                m1b[:, :], rank[:, 0:8], 0.0, None, ALU.is_equal))
            vsmall(c, lambda nc, rank=rank: nc.vector.tensor_scalar(
                m2b[:, :], rank[:, 0:8], 1.0, None, ALU.is_equal))
            m1f = PL(5)
            vsmall(c, lambda nc, m1f=m1f, rank=rank: nc.vector.tensor_scalar(
                m1f[:, 0:8], rank[:, 0:8], 0.0, None, ALU.is_equal))
            vsmall(c, lambda nc, m1f=m1f, rank=rank: nc.vector.tensor_scalar(
                m1f[:, 8:16], rank[:, 0:8], 1.0, None, ALU.is_equal))
            vsmall(c, lambda nc, m1f=m1f, lg=lg: nc.vector.tensor_tensor(
                m1f[:, 0:8], m1f[:, 0:8], lg, ALU.mult))
            vsmall(c, lambda nc, m1f=m1f, lg=lg: nc.vector.tensor_tensor(
                m1f[:, 8:16], m1f[:, 8:16], lg, ALU.mult))
            vsmall(c, lambda nc, m1f=m1f: nc.vector.tensor_reduce(
                m1f[:, 16:17], fa(c, 6144 + 5 * TTF, [128, 1, 8]),
                axis=AX.X, op=ALU.add))
            vsmall(c, lambda nc, m1f=m1f: nc.vector.tensor_reduce(
                m1f[:, 17:18], bass.AP(c.FA, 6144 + 5 * TTF + 8,
                                       [[8192, 128], [1, 1], [1, 8]]),
                axis=AX.X, op=ALU.add))
            vsmall(c, lambda nc, m1f=m1f: nc.vector.tensor_tensor(
                m1f[:, 18:19], m1f[:, 16:17], m1f[:, 17:18], ALU.subtract))
            vw1 = c.sv.n
            p.wait("scalar", c.sv, vw1)
            vsg = p.op("scalar", lambda nc, m1f=m1f: nc.scalar.activation(
                m1f[:, 19:20], m1f[:, 18:19], AF.Sigmoid), inc=c.ss)
            p.wait("vector", c.ss, vsg)
            vsmall(c, lambda nc, m1f=m1f, sub=sub: nc.vector.tensor_copy(
                c.SMB[:, 1024 + sub:1025 + sub], m1f[:, 19:20]))
            vsmall(c, lambda nc, m1f=m1f, sub=sub: nc.vector.tensor_scalar(
                c.SMB[:, 1026 + sub:1027 + sub], m1f[:, 19:20], -1.0, 1.0,
                ALU.mult, ALU.add))
            vsmall(c, lambda nc, sub=sub: nc.vector.tensor_copy(
                c.SMB[:, 1028 + sub * 16:1036 + sub * 16], m1b[:, :]))
            vsmall(c, lambda nc, sub=sub: nc.vector.tensor_copy(
                c.SMB[:, 1036 + sub * 16:1044 + sub * 16], m2b[:, :]))
        p.wait("tensor", c.sv, c.sv.n)
        c.pe_guard(6)
        vt = None
        for sub in range(2):
            vt = p.op("tensor", lambda nc, sub=sub: nc.tensor.transpose(
                pt3[0:8, sub, :], c.SMB[:, 1028 + sub * 16:1036 + sub * 16],
                c.IDT[:]))
            vt = p.op("tensor", lambda nc, sub=sub: nc.tensor.transpose(
                pt3[0:8, 2 + sub, :], c.SMB[:, 1036 + sub * 16:1044 + sub * 16],
                c.IDT[:]))
            vt = p.op("tensor", lambda nc, sub=sub: nc.tensor.transpose(
                pt3[0:1, 4 + sub, :], c.SMB[:, 1024 + sub:1025 + sub],
                c.IDT[:]))
            vt = p.op("tensor", lambda nc, sub=sub: nc.tensor.transpose(
                pt3[0:1, 6 + sub, :], c.SMB[:, 1026 + sub:1027 + sub],
                c.IDT[:]), inc=c.st if sub == 1 else None)
        p.wait("vector", c.st, vt)
        p.op("vector", lambda nc: nc.vector.tensor_copy(
            c.SMB[0:8, 0:256], pt3[0:8, 0:2, :]))
        p.op("vector", lambda nc: nc.vector.tensor_copy(
            c.SMB[0:8, 256:512], pt3[0:8, 2:4, :]))
        p.op("vector", lambda nc: nc.vector.tensor_copy(
            c.SMB[0:1, 512:768], pt3[0:1, 4:6, :]))
        vev = p.op("vector", lambda nc: nc.vector.tensor_copy(
            c.SMB[0:1, 768:1024], pt3[0:1, 6:8, :]), inc=c.sv)
        c.bank_consumed(6, c.sv, vev)
        p.wait("tensor", c.sv, vev)
        c.pe_guard(3)
        c.pe_guard(4)
        p.op("tensor", lambda nc: nc.tensor.matmul(
            c.PSB[3][:, 0:TTF], c.ONEROW16[:], c.SMB[0:1, 512:768],
            start=True, stop=True))
        vwr = p.op("tensor", lambda nc: nc.tensor.matmul(
            c.PSB[4][:, 0:TTF], c.ONEROW16[:], c.SMB[0:1, 768:1024],
            start=True, stop=True), inc=c.st)
        p.wait("scalar", c.st, vwr)
        p.op("scalar", lambda nc: nc.scalar.copy(SP(0), c.PSB[3][:, 0:TTF]))
        vwr2 = p.op("scalar", lambda nc: nc.scalar.copy(SP(1), c.PSB[4][:, 0:TTF]),
                    inc=c.ss)
        c.bank_consumed(3, c.ss, vwr2)
        c.bank_consumed(4, c.ss, vwr2)
        w1rep, w2rep = SP(0), SP(1)
        for it in range(2):
            nkc = DCH if it == 0 else 16
            if it == 1:
                p.wait("scalar", c.sv, c.sv.n)
                p.wait("scalar", c.sg, c.sg.n)
                for j in range(DCH):
                    p.op("scalar", lambda nc, j=j: nc.scalar.copy(
                        ctx16[:, j, :], integ[:, j, :]))
                for j in range(DCH):
                    p.op("scalar", lambda nc, j=j: nc.scalar.copy(
                        ctx16[:, 8 + j, :], vbuf[:, j, :]), inc=c.ss)
                p.wait("tensor", c.ss, c.ss.n)
            rhs_t = (lambda kc: xf16[:, kc, :]) if it == 0 else \
                (lambda kc: ctx16[:, kc, :])
            for oc in range(4):
                wb_guard(c, 0)
                wap, wv = stream_w(c, "ff_ew1", 0, 0, nkc, oc * 128, 128)
                p.wait("tensor", c.dl, wv)
                bank = oc % 2
                c.pe_guard(bank)
                vm = None
                for kc in range(nkc):
                    vm = p.op("tensor", lambda nc, kc=kc, bank=bank, wap=wap,
                              rhs_t=rhs_t, nkc=nkc: nc.tensor.matmul(
                                  c.PSB[bank][:, :TTF], wap[:, kc, :], rhs_t(kc),
                                  start=(kc == 0), stop=(kc == nkc - 1)), inc=c.st)
                wb_used(c, 0, vm)
                p.wait("scalar", c.st, vm)
                vh = p.op("scalar", lambda nc, oc=oc, bank=bank: nc.scalar.activation(
                    h_all[:, oc, :], c.PSB[bank][:, :TTF], AF.Gelu,
                    bias=c.SMF[:, CM + 28 + oc:CM + 29 + oc]), inc=c.ss)
                c.bank_consumed(bank, c.ss, vh)
            wb_guard(c, 1024)
            wap, wv = stream_w(c, "ff_sw1", 1024, 0, nkc, 0, 64)
            p.wait("tensor", c.dl, wv)
            c.pe_guard(2)
            vm = None
            for kc in range(nkc):
                vm = p.op("tensor", lambda nc, kc=kc, wap=wap, rhs_t=rhs_t,
                          nkc=nkc: nc.tensor.matmul(
                    c.PSB[2][0:64, :TTF], wap[:, kc, :], rhs_t(kc),
                    start=(kc == 0), stop=(kc == nkc - 1)), inc=c.st)
            wb_used(c, 1024, vm)
            p.wait("scalar", c.st, vm)
            vh = p.op("scalar", lambda nc: nc.scalar.activation(
                h_sh[:, :], c.PSB[2][0:64, :TTF], AF.Gelu,
                bias=c.SMF[0:64, CM + 88:CM + 89]), inc=c.ss)
            c.bank_consumed(2, c.ss, vh)
            hrhs = (lambda kc: xf16[:, kc, :]) if it == 0 else \
                (lambda kc: ctx16[:, kc, :])
            for oc in range(2):
                wb_guard(c, 2048)
                wap, wv = stream_w(c, "ff_hw1", 2048, 0, DCH, oc * 128, 128)
                p.wait("tensor", c.dl, wv)
                c.pe_guard(3)
                vm = None
                for kc in range(DCH):
                    vm = p.op("tensor", lambda nc, kc=kc, wap=wap, hrhs=hrhs:
                              nc.tensor.matmul(
                        c.PSB[3][:, :TTF], wap[:, kc, :], hrhs(kc),
                        start=(kc == 0), stop=(kc == DCH - 1)), inc=c.st)
                wb_used(c, 2048, vm)
                p.wait("scalar", c.st, vm)
                vh = p.op("scalar", lambda nc, oc=oc: nc.scalar.activation(
                    h_ha[:, oc, :], c.PSB[3][:, :TTF], AF.Gelu,
                    bias=c.SMF[:, CM + 26 + oc:CM + 27 + oc]), inc=c.ss)
                c.bank_consumed(3, c.ss, vh)
            wb_guard(c, 2048)
            wap, wv = stream_w(c, "ff_hw2", 2048, 0, 2, 0, 1)
            p.wait("tensor", c.dl, wv)
            p.wait("tensor", c.ss, c.ss.n)
            c.pe_guard(3)
            vm = None
            for rc in range(2):
                vm = p.op("tensor", lambda nc, rc=rc, wap=wap: nc.tensor.matmul(
                    c.PR[0:1, 0:TTF], wap[:, rc, :], h_ha[:, rc, :],
                    start=(rc == 0), stop=(rc == 1)), inc=c.st)
            wb_used(c, 2048, vm)
            p.wait("scalar", c.st, vm)
            vhr = p.op("scalar", lambda nc: nc.scalar.activation(
                c.SMB[0:1, 1044:1044 + TTF], c.PR[0:1, 0:TTF], AF.Sigmoid,
                bias=c.SMF[0:1, CM + 90:CM + 91]), inc=c.ss)
            c.bank_consumed("PR", c.ss, vhr)
            p.wait("tensor", c.ss, vhr)
            c.pe_guard(3)
            vbr = p.op("tensor", lambda nc: nc.tensor.matmul(
                c.PSB[3][:, 0:TTF], c.ONEROW16[:], c.SMB[0:1, 1044:1044 + TTF],
                start=True, stop=True), inc=c.st)
            p.wait("scalar", c.st, vbr)
            vhrep = p.op("scalar", lambda nc: nc.scalar.copy(
                SP(3), c.PSB[3][:, 0:TTF]), inc=c.ss)
            c.bank_consumed(3, c.ss, vhrep)
            haltrep = SP(3)
            if not hasattr(c, "_ff_exp8"):
                vl8 = p.dma("sync", c.WS[0:8, 0:512], dr["c_exp8"][:], c.dl)
                p.wait("scalar", c.dl, vl8)
                vl8b = p.op("scalar", lambda nc: nc.scalar.copy(
                    c.SMB[0:8, 1300:1812], c.WS[0:8, 0:512]), inc=c.ss)
                p.wait("sync", c.ss, vl8b)
                p.wait("tensor", c.ss, vl8b)
                c._ff_exp8 = True
            p.wait("tensor", c.ss, c.ss.n)
            for mi, msrc in ((0, c.SMB[0:8, 0:256]), (1, c.SMB[0:8, 256:512])):
                hmt = hm1 if mi == 0 else hm2
                for hc in range(4):
                    bank = 4 + (hc % 2)
                    c.pe_guard(bank)
                    vmk = p.op("tensor", lambda nc, hc=hc, msrc=msrc, bank=bank:
                               nc.tensor.matmul(
                                   c.PSB[bank][:, 0:TTF],
                                   c.SMB[0:8, 1300 + hc * 128:1300 + (hc + 1) * 128],
                                   msrc, start=True, stop=True), inc=c.st)
                    p.wait("vector", c.st, vmk)
                    vhm = p.op("vector", lambda nc, hc=hc, hmt=hmt, bank=bank:
                               nc.vector.tensor_tensor(
                                   hmt[:, hc, :], h_all[:, hc, :],
                                   c.PSB[bank][:, 0:TTF], ALU.mult), inc=c.sv)
                    c.bank_consumed(bank, c.sv, vhm)
            p.wait("tensor", c.sv, c.sv.n)
            for j in range(DCH):
                _ff_j_body(c, p, it, j, integ, vbuf, h_sh, hm1, hm2, PL, SP,
                           w1rep, w2rep, haltrep, mu_c, sw_c, osw_c, CM,
                           DTP, SWP, OSWP)
            if c._ff_jdone is not None:
                p.wait("scalar", c.sv, c.sv.n)
                p.wait("scalar", c.sg, c.sg.n)
                p.wait("tensor", c.sv, c.sv.n)
                p.wait("tensor", c.sg, c.sg.n)
        p.wait("scalar", c.sv, c.sv.n)
        p.wait("scalar", c.sg, c.sg.n)
        for j in range(DCH):
            p.op("scalar", lambda nc, j=j: nc.scalar.copy(
                ctx16[:, j, :], integ[:, j, :]), inc=c.ss if j == DCH - 1 else None)
        p.wait("tensor", c.ss, c.ss.n)
        for oc in range(16):
            woff = (oc % 2) * 2048
            wb_guard(c, woff)
            wap, wv = stream_w(c, "ff_qw1", woff, 0, DCH, oc * 128, 128)
            p.wait("tensor", c.dl, wv)
            bank = oc % 2
            c.pe_guard(bank)
            vm = None
            for kc in range(DCH):
                vm = p.op("tensor", lambda nc, kc=kc, bank=bank, wap=wap:
                          nc.tensor.matmul(
                              c.PSB[bank][:, :TTF], wap[:, kc, :], ctx16[:, kc, :],
                              start=(kc == 0), stop=(kc == DCH - 1)), inc=c.st)
            wb_used(c, woff, vm)
            p.wait("scalar", c.st, vm)
            vh = p.op("scalar", lambda nc, oc=oc, bank=bank: nc.scalar.activation(
                g16[:, oc, :], c.PSB[bank][:, :TTF], AF.Gelu,
                bias=c.SMF[:, CM + 32 + oc:CM + 33 + oc]), inc=c.ss)
            c.bank_consumed(bank, c.ss, vh)
        p.wait("tensor", c.ss, c.ss.n)
        for oc in range(DCH):
            woff = (oc % 2) * 2048
            wb_guard(c, woff)
            wap, wv = stream_w(c, "ff_qw2", woff, 0, 16, oc * 128, 128)
            p.wait("tensor", c.dl, wv)
            bank = oc % 2
            c.pe_guard(bank)
            vm = None
            for kc in range(16):
                vm = p.op("tensor", lambda nc, kc=kc, bank=bank, wap=wap:
                          nc.tensor.matmul(
                              c.PSB[bank][:, :TTF], wap[:, kc, :], g16[:, kc, :],
                              start=(kc == 0), stop=(kc == 15)), inc=c.st)
            wb_used(c, woff, vm)
            p.wait("vector", c.st, vm)
            rxa = c.RX[:, oc, tt * TTF:(tt + 1) * TTF]
            p.op("vector", lambda nc, oc=oc, bank=bank: nc.vector.scalar_tensor_tensor(
                PL(0), c.PSB[bank][:, :TTF], iw_c(oc), xf[:, oc, :],
                ALU.mult, ALU.add), inc=c.sv)
            c.bank_consumed(bank, c.sv, c.sv.n)
            vr = p.op("vector", lambda nc, oc=oc, rxa=rxa:
                      nc.vector.scalar_tensor_tensor(
                          rxa, PL(0), c.SMF[:, CM + 16 + oc:CM + 17 + oc],
                          rxa, ALU.bypass, ALU.add) if False else
                      nc.vector.scalar_tensor_tensor(
                          rxa, PL(0),
                          c.SMF[:, CM + 16 + oc:CM + 17 + oc],
                          rxa, ALU.add, ALU.add), inc=c.sv)


SECTION_FUNCS["ff"] = lambda c, first: emit_ff(c)
SECTION_FUNCS["ff2"] = lambda c, first: emit_ff(c)


def kernel(**inputs):
    consts = host_constants()
    in_maps = [prep_core_inputs(cc, inputs, consts) for cc in range(NCORES)]
    nc = build_full()
    res = run_bass_kernel_spmd(nc, in_maps, core_ids=list(range(NCORES)))
    outs = [np.ascontiguousarray(np.asarray(r["out_T"]).T) for r in res.results]
    return np.stack(outs, axis=0).astype(np.float32)


# version 03
# speedup vs baseline: 1.1549x; 1.1549x over previous
import math
import numpy as np
import concourse.bass as bass
import concourse.mybir as mybir
from concourse.bass_utils import run_bass_kernel_spmd

F32 = mybir.dt.float32
BF16 = mybir.dt.bfloat16
I32 = mybir.dt.int32
AF = mybir.ActivationFunctionType
ALU = mybir.AluOpType
AX = mybir.AxisListType

D = 1024
S = 4096
NCORES = 8
DCH = 8
TT = 512
NTT = S // TT

ENGINES = ("sync", "vector", "scalar", "tensor", "gpsimd")


class Sem:
    __slots__ = ("h", "n", "name")

    def __init__(self, handle, name):
        self.h = handle
        self.n = 0
        self.name = name


class Prog:
    def __init__(self, nc):
        self.nc = nc
        self.q = {e: [] for e in ENGINES}
        self.sems = {}

    def sem(self, name):
        if name not in self.sems:
            h = self.nc.semaphore(name).__enter__()
            self.sems[name] = Sem(h, name)
        return self.sems[name]

    def op(self, eng, fn, inc=None, inc_by=1):
        if inc is not None:
            inc.n += inc_by
            def g(fn=fn, h=inc.h, d=inc_by):
                fn(self.nc).then_inc(h, d)
            self.q[eng].append(g)
            return inc.n
        self.q[eng].append(lambda fn=fn: fn(self.nc))
        return None

    def wait(self, eng, sem, val):
        self.q[eng].append(
            lambda h=sem.h, v=val, e=eng: getattr(self.nc, e).wait_ge(h, v))

    def dma(self, eng, out, in_, inc, slow=False):
        return self.op(
            eng, lambda nc, o=out, i=in_, e=eng, s=slow: getattr(nc, e).dma_start(
                out=o, in_=i, allow_slow_non_contiguous=s),
            inc=inc, inc_by=16)

    def run_block(self):
        with self.nc.Block() as block:
            for e in ENGINES:
                ops = self.q[e]
                if not ops:
                    continue
                def make(eng_ops):
                    def _(engine):
                        for f in eng_ops:
                            f()
                    return _
                getattr(block, e)(make(ops))


class Ctx:
    def __init__(self, nc, stack, dr):
        self.nc = nc
        self.p = Prog(nc)
        self.dr = dr
        e = stack.enter_context
        self.RX = e(nc.sbuf_tensor("RX", [128, DCH, S], F32))
        self.FA = e(nc.sbuf_tensor("FA", [128, 8192], F32))
        self.FAi = self.FA.bitcast(I32)
        self.KB = e(nc.sbuf_tensor("KB", [128, 10240], BF16))
        self.WB = e(nc.sbuf_tensor("WB", [128, 4096], BF16))
        self.WS = e(nc.sbuf_tensor("WS", [128, 1024], F32))
        self.SMF = e(nc.sbuf_tensor("SMF", [128, 2048], F32))
        self.SMB = e(nc.sbuf_tensor("SMB", [128, 2048], BF16))
        self.IDT = e(nc.sbuf_tensor("IDT", [128, 128], BF16))
        self.ONEROW = e(nc.sbuf_tensor("ONEROW", [1, 128], F32))
        self.ONEROW16 = e(nc.sbuf_tensor("ONEROW16", [1, 128], BF16))
        self.ONECOL16 = e(nc.sbuf_tensor("ONECOL16", [128, 1], BF16))
        self.CAUS = e(nc.sbuf_tensor("CAUS", [128, 896], BF16))
        self.OMG = e(nc.sbuf_tensor("OMG", [128, 256], BF16))
        self.TRI8 = e(nc.sbuf_tensor("TRI8", [128, 64], F32))
        self.PSB = [e(nc.psum_tensor(f"PS{i}", [128, 512], F32)) for i in range(6)]
        self.PT = e(nc.psum_tensor("PT", [128, 1024], BF16))
        self.PR = e(nc.psum_tensor("PR", [128, 512], F32))
        self.ONES128 = e(nc.sbuf_tensor("ONES128", [128, 128], BF16))
        p = self.p
        self.sv = p.sem("sv")
        self.ss = p.sem("ss")
        self.st = p.sem("st_")
        self.sg = p.sem("sg")
        self.dl = p.sem("dl")
        self.ds = p.sem("ds")
        self.dg = p.sem("dg")
        self.so = p.sem("so")
        self.scr = nc.dram_tensor("scr", [63 * 1024 * 1024], F32, kind="Internal")
        self.scrB = nc.dram_tensor("scrB", [12 * 1024 * 1024], F32, kind="Internal")
        self.bank_free = {}

    def pe_guard(self, bank):
        if bank in self.bank_free:
            sem, val = self.bank_free.pop(bank)
            self.p.wait("tensor", sem, val)

    def bank_consumed(self, bank, sem, val):
        self.bank_free[bank] = (sem, val)

    def scr_ap(self, off_bytes, shape, dtype=F32):
        t = self.scr
        if dtype != F32:
            t = t.bitcast(dtype)
        elems = 1
        for s in shape:
            elems *= s
        esz = 4 if dtype == F32 else 2
        assert off_bytes % 4 == 0
        flat = t.ap()
        start = off_bytes // esz
        ap = bass.AP(flat.tensor, start, [[shape[1], shape[0]], [1, shape[1]]])
        return ap


def vsmall(c, fn):
    v = c.p.op("vector", fn, inc=c.sv)
    c.p.wait("vector", c.sv, v)
    return v


def ssmall(c, fn):
    v = c.p.op("scalar", fn, inc=c.ss)
    c.p.wait("scalar", c.ss, v)
    return v


def kb(c, start, shape):
    n = 1
    for s in shape[1:]:
        n *= s
    ap = c.KB[0:shape[0], start:start + n]
    if len(shape) == 3:
        ap = ap.rearrange("p (a b) -> p a b", b=shape[2])
    return ap


def fa(c, start, shape, dtype=F32):
    n = 1
    for s in shape[1:]:
        n *= s
    base = c.FA if dtype == F32 else c.FAi
    ap = base[0:shape[0], start:start + n]
    if len(shape) == 3:
        ap = ap.rearrange("p (a b) -> p a b", b=shape[2])
    return ap



def _t(a):
    return np.ascontiguousarray(np.asarray(a, np.float32).T)


def _c(a):
    return np.ascontiguousarray(np.asarray(a, np.float32))


def host_constants():
    c = {}
    c["c_ident"] = np.eye(128, dtype=np.float32)
    c["c_onerow"] = np.ones((1, 128), np.float32)
    c["c_onecol"] = np.ones((128, 1), np.float32)
    ki = np.arange(128)[:, None]
    u = np.arange(896)[None, :]
    c["c_causal"] = ((u - 384) >= ki).astype(np.float32)
    tri = (np.arange(8)[:, None] > np.arange(8)[None, :]).astype(np.float32)
    c["c_tri8"] = np.tile(tri.reshape(1, 64), (128, 1)).astype(np.float32)
    sels = np.zeros((16, 2048), np.float32)
    for k in range(16):
        sels[k, k * 128:(k + 1) * 128] = 1.0
    c["c_sels"] = sels
    exp8 = np.zeros((8, 512), np.float32)
    for hc in range(4):
        for pp in range(128):
            exp8[2 * hc + pp // 64, hc * 128 + pp] = 1.0
    c["c_exp8"] = exp8
    return c


def prep_core_inputs(core, inp, consts):
    m = dict(consts)
    m["x_T"] = _t(inp["L0_attn_a_X"][core])
    m["l0a_qT"] = _t(inp["L0_attn_a_q_proj"])
    m["l0a_kT"] = _t(inp["L0_attn_a_k_proj"])
    m["l0a_vT"] = _t(inp["L0_attn_a_v_proj"])
    m["l0a_oT"] = _t(inp["L0_attn_a_o_proj"])
    for w in ("Wq", "Wk", "Wv", "Wo"):
        m[f"l0b_{w}"] = _c(inp[f"L0_attn_b_{w}"])
    for b in ("bq", "bk", "bv", "bo"):
        m[f"l0b_{b}"] = _c(inp[f"L0_attn_b_{b}"]).reshape(8, 128).T.copy()
    m["ff_rw1"] = _c(inp["L0_ffn_router_w1"])
    m["ff_rb1"] = _c(inp["L0_ffn_router_b1"]).reshape(2, 128).T.copy()
    m["ff_rw2"] = _c(inp["L0_ffn_router_w2"])
    m["ff_rb2"] = np.tile(_c(inp["L0_ffn_router_b2"]).reshape(1, 8), (128, 1))
    m["ff_hw1"] = _c(inp["L0_ffn_halt_w1"])
    m["ff_hb1"] = _c(inp["L0_ffn_halt_b1"]).reshape(2, 128).T.copy()
    m["ff_hw2"] = _c(inp["L0_ffn_halt_w2"])
    m["ff_hb2"] = np.full((1, 1), float(np.asarray(inp["L0_ffn_halt_b2"]).reshape(-1)[0]), np.float32)
    ew1 = np.asarray(inp["L0_ffn_expert_w1"], np.float32)
    m["ff_ew1"] = np.ascontiguousarray(ew1.transpose(1, 0, 2).reshape(2048, 512))
    m["ff_eb1"] = _c(inp["L0_ffn_expert_b1"]).reshape(512).reshape(4, 128).T.copy()
    m["ff_ew2"] = _c(inp["L0_ffn_expert_w2"]).reshape(512, 3072)
    m["ff_eb2"] = _c(inp["L0_ffn_expert_b2"])
    m["ff_sw1"] = _c(inp["L0_ffn_shared_w1"])
    m["ff_sb1"] = _c(inp["L0_ffn_shared_b1"]).reshape(64, 1)
    m["ff_sw2"] = _c(inp["L0_ffn_shared_w2"])
    m["ff_sb2"] = _c(inp["L0_ffn_shared_b2"]).reshape(24, 128).T.copy()
    m["ff_swt"] = np.full((1, 1), float(np.asarray(inp["L0_ffn_shared_weight"])), np.float32)
    m["ff_qw1"] = _c(inp["L0_ffn_refine_w1"])
    m["ff_qb1"] = _c(inp["L0_ffn_refine_b1"]).reshape(16, 128).T.copy()
    m["ff_qw2"] = _c(inp["L0_ffn_refine_w2"])
    m["ff_qb2"] = _c(inp["L0_ffn_refine_b2"]).reshape(8, 128).T.copy()
    m["ff_iw"] = _c(inp["L0_ffn_integration_weight"]).reshape(8, 128).T.copy()
    m["ff_mu"] = _c(inp["L0_ffn_mu"]).reshape(8, 128).T.copy()
    m["l1a_Wq"] = _c(inp["L1_attn_a_Wq"])
    m["l1a_Wk"] = _c(inp["L1_attn_a_Wk"])
    m["l1a_Wv"] = _c(inp["L1_attn_a_Wv"])
    omega = np.asarray(inp["L1_attn_a_omega"], np.float32)
    m["l1a_omg"] = np.concatenate(
        [omega.T, np.full((64, 256), -0.5, np.float32)], axis=0)
    m["l1b_Win"] = _c(inp["L1_attn_b_W_in"])
    cw = _c(inp["L1_attn_b_conv_w"])
    m["l1b_cw"] = cw.reshape(16, 128, 3).transpose(1, 0, 2).reshape(128, 48).copy()
    m["l1b_cb"] = _c(inp["L1_attn_b_conv_b"]).reshape(16, 128).T.copy()
    m["l1b_Wst"] = _c(inp["L1_attn_b_W_state"])
    m["l1b_Wt"] = _c(inp["L1_attn_b_W_time"])
    m["l1b_bt"] = _c(inp["L1_attn_b_b_time"]).reshape(16, 128).T.copy()
    alog = _c(inp["L1_attn_b_A_log"])
    m["l1b_alog"] = np.ascontiguousarray(
        alog.reshape(16, 128, 16).transpose(1, 0, 2).reshape(128, 256))
    m["l1b_skip"] = _c(inp["L1_attn_b_skip"]).reshape(16, 128).T.copy()
    m["l1b_Wout"] = _c(inp["L1_attn_b_W_out"])
    m["l1f_w"] = _c(inp["L1_ffn_weight"])
    m["l1f_b"] = _c(inp["L1_ffn_bias"]).reshape(8, 128).T.copy()
    return m


INPUT_SHAPES = {
    "x_T": (D, S),
    "c_ident": (128, 128), "c_onerow": (1, 128), "c_onecol": (128, 1),
    "c_causal": (128, 896), "c_tri8": (128, 64), "c_sels": (16, 2048),
    "c_exp8": (8, 512),
    "l0a_qT": (D, D), "l0a_kT": (D, D), "l0a_vT": (D, D), "l0a_oT": (D, D),
    "l0b_Wq": (D, D), "l0b_Wk": (D, D), "l0b_Wv": (D, D), "l0b_Wo": (D, D),
    "l0b_bq": (128, 8), "l0b_bk": (128, 8), "l0b_bv": (128, 8), "l0b_bo": (128, 8),
    "ff_rw1": (D, 256), "ff_rb1": (128, 2), "ff_rw2": (256, 8), "ff_rb2": (128, 8),
    "ff_hw1": (D, 256), "ff_hb1": (128, 2), "ff_hw2": (256, 1), "ff_hb2": (1, 1),
    "ff_ew1": (2048, 512), "ff_eb1": (128, 4), "ff_ew2": (512, 3072),
    "ff_eb2": (8, 3072),
    "ff_sw1": (2048, 64), "ff_sb1": (64, 1), "ff_sw2": (64, 3072),
    "ff_sb2": (128, 24), "ff_swt": (1, 1),
    "ff_qw1": (D, 2048), "ff_qb1": (128, 16), "ff_qw2": (2048, D),
    "ff_qb2": (128, 8), "ff_iw": (128, 8), "ff_mu": (128, 8),
    "l1a_Wq": (D, D), "l1a_Wk": (D, D), "l1a_Wv": (D, D), "l1a_omg": (128, 256),
    "l1b_Win": (D, 4096), "l1b_cw": (128, 48), "l1b_cb": (128, 16),
    "l1b_Wst": (2048, 32), "l1b_Wt": (2048, 2048), "l1b_bt": (128, 16),
    "l1b_alog": (128, 256), "l1b_skip": (128, 16), "l1b_Wout": (2048, D),
    "l1f_w": (D, D), "l1f_b": (128, 8),
}


def bld_inputs(nc, names=None):
    aps = {}
    for name, shp in INPUT_SHAPES.items():
        if names is not None and name not in names:
            continue
        aps[name] = nc.dram_tensor(name, list(shp), F32, kind="ExternalInput").ap()
    return aps



def emit_setup(c):
    p, dr = c.p, c.dr
    for j in range(DCH):
        p.dma("sync", c.RX[:, j, :], dr["x_T"][j * 128:(j + 1) * 128, :], c.dl)
    p.dma("sync", c.ONEROW[:], dr["c_onerow"][:], c.dl)
    p.dma("sync", c.TRI8[:], dr["c_tri8"][:], c.dl)
    p.dma("sync", c.WS[:, 0:128], dr["c_ident"][:], c.dl)
    p.dma("sync", c.WS[:, 128:129], dr["c_onecol"][:], c.dl)
    ld = c.dl.n
    p.wait("scalar", c.dl, ld)
    p.op("scalar", lambda nc: nc.scalar.copy(c.IDT[:], c.WS[:, 0:128]))
    p.op("scalar", lambda nc: nc.scalar.copy(c.ONEROW16[:], c.ONEROW[:]))
    p.op("vector", lambda nc: nc.vector.memset(c.ONES128[:], 1.0))
    p.op("scalar", lambda nc: nc.scalar.copy(c.ONECOL16[:], c.WS[:, 128:129]), inc=c.ss)
    cvt1 = c.ss.n
    p.wait("sync", c.ss, cvt1)
    p.dma("sync", c.WS[:, 0:896], dr["c_causal"][:], c.dl)
    ld2 = c.dl.n
    p.wait("scalar", c.dl, ld2)
    p.op("scalar", lambda nc: nc.scalar.copy(c.CAUS[:], c.WS[:, 0:896]), inc=c.ss)
    c.rx_loaded = ld
    c.const_ss = c.ss.n


def barrier(c):
    p = c.p
    sems = [c.sv, c.ss, c.st, c.sg, c.dl, c.ds, c.dg]
    for eng in ENGINES:
        own = {"vector": c.sv, "scalar": c.ss, "tensor": c.st,
               "gpsimd": c.sg}.get(eng)
        for s in sems:
            if s.n and s is not own:
                p.wait(eng, s, s.n)


def emit_store_out(c):
    p = c.p
    out = c.nc.dram_tensor("out_T", [D, S], F32, kind="ExternalOutput").ap()
    for s in (c.sv, c.ss, c.st, c.sg):
        if s.n:
            p.wait("sync", s, s.n)
    for j in range(DCH):
        p.dma("sync", out[j * 128:(j + 1) * 128, :], c.RX[:, j, :], c.so)
    p.wait("sync", c.so, c.so.n)



def load_weight_bf16(c, name, kchunks, ncols, wb_off, rows=128, dma_eng="sync",
                     col0=0, src_row0=0):
    p, dr = c.p, c.dr
    dsem = c.dl if dma_eng == "sync" else (c.ds if dma_eng == "scalar" else c.dg)
    dst = c.WB[:rows, wb_off:wb_off + kchunks * ncols].rearrange(
        "p (k n) -> p k n", n=ncols)
    for kc in range(kchunks):
        assert ncols <= 1024
        v = p.dma(dma_eng, c.WS[:rows, 0:ncols],
                  dr[name][src_row0 + kc * rows: src_row0 + (kc + 1) * rows,
                           col0:col0 + ncols], dsem)
        p.wait("scalar", dsem, v)
        p.op("scalar", lambda nc, kc=kc: nc.scalar.copy(
            dst[:, kc, :], c.WS[:rows, 0:ncols]), inc=c.ss)
        if kc + 1 < kchunks:
            p.wait(dma_eng, c.ss, c.ss.n)
    return dst, c.ss.n



def emit_ln_ff(c, tt, out16, nch=DCH, rxsrc=None, tw=TT, lo16=None, vout=None):
    p = c.p
    t0 = tt * tw
    rx = rxsrc if rxsrc is not None else (lambda j: c.RX[:, j, t0:t0 + tw])
    stg = c.SMB[:, 0:4 * tw].rearrange("p (a b) -> p a b", b=tw)
    c.pe_guard(4)
    c.pe_guard(5)
    c.pe_guard("PR")
    pe_base = None
    for j in range(nch):
        sl = (j % 2) * 2
        if j >= 2:
            p.wait("scalar", c.st, pe_base + (j - 2))
        p.op("scalar", lambda nc, j=j, sl=sl: nc.scalar.copy(stg[:, sl, :], rx(j)))
        v = p.op("scalar", lambda nc, j=j, sl=sl: nc.scalar.activation(
            stg[:, sl + 1, :], rx(j), AF.Square), inc=c.ss)
        p.wait("tensor", c.ss, v)
        p.op("tensor", lambda nc, j=j, sl=sl: nc.tensor.matmul(
            c.PR[0:1, 0:tw], c.ONECOL16[:], stg[:, sl, :],
            start=(j == 0), stop=(j == nch - 1)))
        vt = p.op("tensor", lambda nc, j=j, sl=sl: nc.tensor.matmul(
            c.PSB[4][0:1, 0:tw], c.ONECOL16[:], stg[:, sl + 1, :],
            start=(j == 0), stop=(j == nch - 1)), inc=c.st)
        if j == 0:
            pe_base = vt
    stats_done = c.st.n
    dim = nch * 128
    mu = c.SMF[0:1, 0:tw]
    var = c.SMF[0:1, 512:512 + tw]
    p.wait("vector", c.st, stats_done)
    p.op("vector", lambda nc: nc.vector.tensor_scalar(
        mu, c.PR[0:1, 0:tw], 1.0 / dim, None, ALU.mult))
    p.op("vector", lambda nc: nc.vector.tensor_tensor(var, mu, mu, ALU.mult))
    p.op("vector", lambda nc: nc.vector.scalar_tensor_tensor(
        var, c.PSB[4][0:1, 0:tw], 1.0 / dim, var, ALU.mult, ALU.subtract), inc=c.sv)
    v_var = c.sv.n
    p.op("vector", lambda nc: nc.vector.tensor_scalar(
        var, var, 1e-5, None, ALU.add), inc=c.sv)
    p.wait("scalar", c.sv, c.sv.n)
    p.op("scalar", lambda nc: nc.scalar.activation(var, var, AF.Ln))
    v_rstd = p.op("scalar", lambda nc: nc.scalar.activation(
        var, var, AF.Exp, scale=-0.5), inc=c.ss)
    p.wait("vector", c.ss, v_rstd)
    v_mur = p.op("vector", lambda nc: nc.vector.tensor_tensor(
        mu, mu, var, ALU.mult), inc=c.sv)
    rstd16 = c.SMB[0:1, 0:tw]
    mur16 = c.SMB[0:1, tw:2 * tw]
    p.wait("scalar", c.ss, v_rstd)
    p.wait("scalar", c.sv, v_mur)
    p.op("scalar", lambda nc: nc.scalar.copy(rstd16, var))
    vrows = p.op("scalar", lambda nc: nc.scalar.copy(mur16, mu), inc=c.ss)
    p.wait("tensor", c.ss, vrows)
    p.op("tensor", lambda nc: nc.tensor.matmul(
        c.PSB[4][:, 0:tw], c.ONEROW16[:], rstd16, start=True, stop=True))
    vb = p.op("tensor", lambda nc: nc.tensor.matmul(
        c.PSB[5][:, 0:tw], c.ONEROW16[:], mur16, start=True, stop=True), inc=c.st)
    c.bank_consumed("PR", c.sv, v_var)
    p.wait("vector", c.st, vb)
    if lo16 is None and vout is None:
        tmp = fa(c, 7680, [128, tw])
        for j in range(nch):
            p.op("vector", lambda nc, j=j: nc.vector.tensor_tensor(
                tmp, rx(j), c.PSB[4][:, 0:tw], ALU.mult))
            p.op("vector", lambda nc, j=j: nc.vector.tensor_tensor(
                out16[:, j, :], tmp, c.PSB[5][:, 0:tw], ALU.subtract),
                inc=c.sv if j == nch - 1 else None)
        done = c.sv.n
        c.bank_consumed(4, c.sv, done)
        c.bank_consumed(5, c.sv, done)
        return done
    vslots = vout if vout is not None else fa(c, 0, [128, nch, tw])
    for j in range(nch):
        p.op("vector", lambda nc, j=j: nc.vector.tensor_tensor(
            vslots[:, j, :], rx(j), c.PSB[4][:, 0:tw], ALU.mult))
        p.op("vector", lambda nc, j=j: nc.vector.tensor_tensor(
            vslots[:, j, :], vslots[:, j, :], c.PSB[5][:, 0:tw], ALU.subtract),
            inc=c.sv if j == nch - 1 else None)
    vdone = c.sv.n
    c.bank_consumed(4, c.sv, vdone)
    c.bank_consumed(5, c.sv, vdone)
    p.wait("scalar", c.sv, vdone)
    for j in range(nch):
        p.op("scalar", lambda nc, j=j: nc.scalar.copy(out16[:, j, :], vslots[:, j, :]),
             inc=c.ss if j == nch - 1 else None)
    hidone = c.ss.n
    if lo16 is None:
        return hidone
    p.wait("vector", c.ss, hidone)
    for j in range(nch):
        p.op("vector", lambda nc, j=j: nc.vector.tensor_tensor(
            lo16[:, j, :], vslots[:, j, :], out16[:, j, :], ALU.subtract),
            inc=c.sv if j == nch - 1 else None)
    return c.sv.n



QX, QT, QY, QNF, QAT, QQ, QSW, QRSW = 0, 1024, 2048, 3072, 4096, 5120, 6144, 7168


def scr2(c, elem_off, rows, cols, dtype=F32, row_stride=None):
    t = c.scr if dtype == F32 else c.scr.bitcast(dtype)
    if row_stride is None:
        row_stride = cols
    return bass.AP(t, elem_off, [[row_stride, rows], [1, cols]])


def _expand32(p, c, src_t, src_off, dst_slot, pong_slot, n=1024):
    nblk = n // 32
    seq = [dst_slot, pong_slot, dst_slot, pong_slot, dst_slot]
    cur_t, cur_off = src_t, src_off
    w = 1
    for i in range(5):
        tgt = seq[i]
        src3 = cur_t[:, cur_off:cur_off + nblk * w].rearrange(
            "p (b q) -> p b q", q=w)
        dst3 = c.FA[:, tgt:tgt + nblk * 2 * w].rearrange(
            "p (b q) -> p b q", q=2 * w)
        for h in range(2):
            if nblk * w < 256:
                vsmall(c, lambda nc, d=dst3, s=src3, h=h, w=w:
                       nc.vector.tensor_copy(d[:, :, h * w:(h + 1) * w], s))
            else:
                p.op("vector", lambda nc, d=dst3, s=src3, h=h, w=w:
                     nc.vector.tensor_copy(d[:, :, h * w:(h + 1) * w], s))
        cur_t, cur_off = c.FA, tgt
        w *= 2


def emit_quantize(c, dst16, n=1024):
    p = c.p
    nblk = n // 32
    X = fa(c, QX, [128, n]); T = fa(c, QT, [128, n]); Y = fa(c, QY, [128, n])
    NFi = fa(c, QNF, [128, n], I32); NF = fa(c, QNF, [128, n])
    AT = fa(c, QAT, [128, n]); Q = fa(c, QQ, [128, n])
    SW = fa(c, QSW, [128, n]); RSW = fa(c, QRSW, [128, n])
    bm = c.SMF[:, 1056:1056 + nblk]
    sw = c.SMF[:, 1088:1088 + nblk]
    rsw = c.SMF[:, 1120:1120 + nblk]
    x3 = fa(c, QX, [128, nblk, 32])
    vsmall(c, lambda nc: nc.vector.tensor_reduce(
        bm, x3, axis=AX.X, op=ALU.max, apply_absolute_value=True))
    vsmall(c, lambda nc: nc.vector.tensor_scalar(
        sw, bm, 1.0 / 7.5, 1e-30, ALU.mult, ALU.max))
    vsmall(c, lambda nc: nc.vector.reciprocal(rsw, sw))
    _expand32(p, c, c.SMF, 1088, QSW, QY, n)
    _expand32(p, c, c.SMF, 1120, QRSW, QY, n)
    p.op("vector", lambda nc: nc.vector.tensor_tensor(T, X, RSW, ALU.mult))
    p.op("vector", lambda nc: nc.vector.tensor_tensor(AT, T, T, ALU.mult))
    for cc, dst in ((8.0, Q), (2.0, RSW), (4.0, Y)):
        p.op("vector", lambda nc, cc=cc: nc.vector.tensor_scalar(
            Y, T, cc, None, ALU.mult))
        p.op("vector", lambda nc: nc.vector.tensor_copy(NFi, Y))
        p.op("vector", lambda nc: nc.vector.tensor_copy(X, NFi))
        p.op("vector", lambda nc: nc.vector.tensor_tensor(NF, Y, X, ALU.subtract))
        p.op("vector", lambda nc, dst=dst: nc.vector.tensor_scalar(
            dst, NF, 0.5, None, ALU.is_gt))
        p.op("vector", lambda nc: nc.vector.tensor_scalar(
            NF, NF, -0.5, None, ALU.is_le))
        p.op("vector", lambda nc, dst=dst: nc.vector.tensor_tensor(
            X, X, dst, ALU.add))
        p.op("vector", lambda nc: nc.vector.tensor_tensor(X, X, NF, ALU.subtract))
        p.op("vector", lambda nc, cc=cc, dst=dst: nc.vector.tensor_scalar(
            dst, X, 1.0 / cc, None, ALU.mult))
    Xi = fa(c, QX, [128, n], I32)
    p.op("vector", lambda nc: nc.vector.tensor_scalar(
        Xi, AT, 16.0, None, ALU.is_lt))
    p.op("vector", lambda nc: nc.vector.select(T, Xi, Y, RSW))
    p.op("vector", lambda nc: nc.vector.tensor_scalar(
        Xi, AT, 4.0, None, ALU.is_lt))
    p.op("vector", lambda nc: nc.vector.select(NF, Xi, Q, T))
    return p.op("vector", lambda nc, dst16=dst16: nc.vector.tensor_tensor(
        dst16, NF, SW, ALU.mult), inc=c.sv)



SCR_WDT = 0


def emit_l1f_wquant(c):
    p, dr = c.p, c.dr
    wd_tm = kb(c, 0, [128, 1024])
    pt3 = c.PT[:].rearrange("p (a b) -> p a b", b=128)
    for ot in range(8):
        v = p.dma("sync", c.WS[:, 0:1024],
                  dr["l1f_w"][ot * 128:(ot + 1) * 128, :], c.dl)
        p.wait("scalar", c.dl, v)
        vs = p.op("scalar", lambda nc: nc.scalar.copy(
            fa(c, QX, [128, 1024]), c.WS[:, 0:1024]), inc=c.ss)
        p.wait("sync", c.ss, vs)
        p.wait("vector", c.ss, vs)
        vq = emit_quantize(c, wd_tm, 1024)
        p.wait("tensor", c.sv, vq)
        c.pe_guard(6)
        vt = None
        for dc in range(DCH):
            vt = p.op("tensor", lambda nc, dc=dc: nc.tensor.transpose(
                pt3[:, dc, :], wd_tm[:, dc * 128:(dc + 1) * 128], c.IDT[:]),
                inc=c.st if dc == DCH - 1 else None)
        p.wait("vector", c.st, vt)
        ve = p.op("vector", lambda nc: nc.vector.tensor_copy(
            c.SMB[:, 0:1024], c.PT[:]), inc=c.sv)
        c.bank_consumed(6, c.sv, ve)
        p.wait("sync", c.sv, ve)
        for dc in range(DCH):
            dst = scr2(c, SCR_WDT + dc * 128 * 1024 + ot * 128, 128, 128,
                       BF16, row_stride=1024)
            p.dma("sync", dst, c.SMB[:, dc * 128:(dc + 1) * 128], c.dl)
        p.wait("vector", c.dl, c.dl.n)


def emit_l1f(c, first_call):
    p, dr = c.p, c.dr
    barrier(c)
    if first_call:
        emit_l1f_wquant(c)
        barrier(c)
    vb = p.dma("sync", c.SMF[:, 1280:1288], dr["l1f_b"][:], c.dl)
    xd_fm = kb(c, 1024, [128, DCH, TT])
    xd_tm = kb(c, 0, [128, 1024])
    pt3 = c.PT[:].rearrange("p (a b) -> p a b", b=128)
    hi16 = kb(c, 1024, [128, DCH, TT])
    lo16 = kb(c, 5120, [128, DCH, TT])
    for tt in range(NTT):
        ln_done = emit_ln_ff(c, tt, hi16, lo16=lo16)
        p.wait("tensor", c.sv, ln_done)
        for ts4 in range(4):
            for r in range(2):
                c.pe_guard(6)
                vt = None
                for i in range(4):
                    dc = 4 * r + i
                    p.op("tensor", lambda nc, dc=dc, i=i, ts4=ts4: nc.tensor.transpose(
                        pt3[:, i, :], hi16[:, dc, ts4 * 128:(ts4 + 1) * 128],
                        c.IDT[:]))
                    vt = p.op("tensor", lambda nc, dc=dc, i=i, ts4=ts4: nc.tensor.transpose(
                        pt3[:, 4 + i, :], lo16[:, dc, ts4 * 128:(ts4 + 1) * 128],
                        c.IDT[:]), inc=c.st if i == 3 else None)
                p.wait("vector", c.st, vt)
                xr = fa(c, QX + r * 512, [128, 4, 128])
                p.op("vector", lambda nc, xr=xr: nc.vector.tensor_copy(
                    xr, pt3[:, 0:4, :]))
                ve = p.op("vector", lambda nc, xr=xr: nc.vector.tensor_tensor(
                    xr, xr, pt3[:, 4:8, :], ALU.add), inc=c.sv)
                c.bank_consumed(6, c.sv, ve)
            vq = emit_quantize(c, xd_tm, 1024)
            p.wait("tensor", c.sv, vq)
            c.pe_guard(6)
            vt = None
            for dc in range(DCH):
                vt = p.op("tensor", lambda nc, dc=dc: nc.tensor.transpose(
                    pt3[:, dc, :], xd_tm[:, dc * 128:(dc + 1) * 128], c.IDT[:]),
                    inc=c.st if dc == DCH - 1 else None)
            p.wait("vector", c.st, vt)
            ve = p.op("vector", lambda nc, ts4=ts4: nc.vector.tensor_copy(
                hi16[:, :, ts4 * 128:(ts4 + 1) * 128], pt3[:, :, :]), inc=c.sv)
            c.bank_consumed(6, c.sv, ve)
        xd_ready = c.sv.n
        p.wait("tensor", c.sv, xd_ready)
        if not hasattr(c, "_wb_last"):
            c._wb_last = {}
        for g in range(2):
            for dc in range(DCH):
                slot = dc % 2
                wdst = c.WB[:, slot * 2048:slot * 2048 + 1024]
                src = scr2(c, SCR_WDT + dc * 128 * 1024, 128, 1024, BF16)
                if slot in c._wb_last:
                    p.wait("sync", c.st, c._wb_last[slot])
                vw = p.dma("sync", wdst, src, c.dl)
                p.wait("tensor", c.dl, vw)
                for oi in range(4):
                    oc = g * 4 + oi
                    if dc == 0:
                        c.pe_guard(oi)
                    vm = p.op("tensor", lambda nc, oi=oi, oc=oc, wdst=wdst, dc=dc:
                              nc.tensor.matmul(
                                  c.PSB[oi][:, :], wdst[:, oc * 128:(oc + 1) * 128],
                                  xd_fm[:, dc, :], start=(dc == 0), stop=(dc == DCH - 1)),
                              inc=c.st)
                c._wb_last[slot] = c.st.n
            mm_done = c.st.n
            p.wait("vector", c.st, mm_done)
            p.wait("vector", c.dl, vb)
            for oi in range(4):
                oc = g * 4 + oi
                rxa = c.RX[:, oc, tt * TT:(tt + 1) * TT]
                vv = p.op("vector", lambda nc, oi=oi, oc=oc, rxa=rxa:
                          nc.vector.scalar_tensor_tensor(
                              rxa, c.PSB[oi][:, :], c.SMF[:, 1280 + oc:1281 + oc],
                              rxa, ALU.add, ALU.add), inc=c.sv)
                c.bank_consumed(oi, c.sv, vv)



import contextlib

SECTION_FUNCS = {}


def build_sections(names):
    nc = bass.Bass("TRN2", target_bir_lowering=False,
                   detect_race_conditions=False)
    dr = bld_inputs(nc)
    with contextlib.ExitStack() as stack:
        c = Ctx(nc, stack, dr)
        emit_setup(c)
        barrier(c)
        if any(n != "l1f" and n != "l1f2" for n in names):
            emit_wprep(c)
            barrier(c)
        for i, name in enumerate(names):
            SECTION_FUNCS[name](c, name not in names[:i])
        emit_store_out(c)
        c.p.run_block()
    return nc


def _sec_l1f(c, first):
    emit_l1f(c, first)


SECTION_FUNCS["l1f"] = _sec_l1f
SECTION_FUNCS["l1f2"] = lambda c, first: emit_l1f(c, False)


def build_full():
    return build_sections(["l0a", "ff", "l0b", "ff2", "l1a", "l1f", "l1b", "l1f2"])



SCR_L1A_KS = 1 * 1024 * 1024
SCR_L1A_QS = SCR_L1A_KS + 8 * 1024 * 1024
SCR_L1A_V = SCR_L1A_QS + 8 * 1024 * 1024
SCR_W = 32 * 1024 * 1024

WPREP = [
    ("l0a_qT", D, D), ("l0a_kT", D, D), ("l0a_vT", D, D), ("l0a_oT", D, D),
    ("l0b_Wq", D, D), ("l0b_Wk", D, D), ("l0b_Wv", D, D), ("l0b_Wo", D, D),
    ("ff_rw1", D, 256), ("ff_rw2", 256, 8), ("ff_hw1", D, 256), ("ff_hw2", 256, 1),
    ("ff_ew1", 2048, 512), ("ff_ew2", 512, 3072), ("ff_sw1", 2048, 64),
    ("ff_sw2", 64, 3072), ("ff_qw1", D, 2048), ("ff_qw2", 2048, D),
    ("ff_eb2", 8, 3072),
    ("l1a_Wq", D, D), ("l1a_Wk", D, D), ("l1a_Wv", D, D), ("l1a_omg", 128, 256),
    ("l1b_Win", D, 4096), ("l1b_Wst", 2048, 32), ("l1b_Wt", 2048, 2048),
    ("l1b_Wout", 2048, D),
]


def emit_wprep(c):
    p, dr = c.p, c.dr
    c.wscr = {}
    off = SCR_W
    prev_dma = None
    for name, rows, cols in WPREP:
        c.wscr[name] = (off, rows, cols)
        nk = (rows + 127) // 128
        for kc in range(nk):
            r0 = kc * 128
            nr = min(128, rows - r0)
            for c0 in range(0, cols, 1024):
                ncg = min(1024, cols - c0)
                v = p.dma("sync", c.WS[:nr, 0:ncg],
                          dr[name][r0:r0 + nr, c0:c0 + ncg], c.dl)
                p.wait("scalar", c.dl, v)
                vs = p.op("scalar", lambda nc, nr=nr, ncg=ncg: nc.scalar.copy(
                    c.SMB[:nr, 0:ncg], c.WS[:nr, 0:ncg]), inc=c.ss)
                p.wait("sync", c.ss, vs)
                dst = scr2(c, off + r0 * cols + c0, nr, ncg, BF16, row_stride=cols)
                vd = p.dma("sync", dst, c.SMB[:nr, 0:ncg], c.dl)
                p.wait("scalar", c.dl, vd)
        off += rows * cols
    assert off <= 2 * 31_800_000, f"weights overrun scr: {off}"
    c.wprep_done = c.dl.n


def stream_w(c, name, wb_off, kc0, nk, col0, ncols, dma_eng="sync", rows=128):
    p = c.p
    off, wrows, cols = c.wscr[name]
    dsem = {"sync": c.dl, "scalar": c.ds, "gpsimd": c.dg}[dma_eng]
    dst = c.WB[0:rows, wb_off:wb_off + nk * ncols].rearrange(
        "p (k n) -> p k n", n=ncols)
    src = bass.AP(c.scr.bitcast(BF16), off + kc0 * rows * cols + col0,
                  [[cols, rows], [rows * cols, nk], [1, ncols]])
    v = p.dma(dma_eng, dst, src, dsem, slow=(ncols < 8 or cols < 8))
    return dst, v



L1A_NEG_LN16 = -math.log(16.0)


def wb_guard(c, slot_off, dma_eng="sync"):
    if not hasattr(c, "_wb_last"):
        c._wb_last = {}
    if slot_off in c._wb_last:
        c.p.wait(dma_eng, c.st, c._wb_last[slot_off])


def wb_used(c, slot_off, st_val):
    c._wb_last[slot_off] = st_val


def emit_l1a(c):
    p, dr = c.p, c.dr
    barrier(c)
    FAb = c.FA.bitcast(BF16)
    ln16 = kb(c, 0, [128, DCH, TT])
    vstage = kb(c, 4096, [128, 1040])
    vst3 = kb(c, 4096, [128, 16, 65])
    pt3 = c.PT[:].rearrange("p (a b) -> p a b", b=128)
    nlncol = c.SMF[:, 1408:1409]
    vsmall(c, lambda nc: nc.vector.memset(nlncol, L1A_NEG_LN16))
    vo = p.dma("sync", c.WS[:, 0:256], dr["l1a_omg"][:], c.dl)
    p.wait("scalar", c.dl, vo)
    vog = p.op("scalar", lambda nc: nc.scalar.copy(c.OMG[:], c.WS[:, 0:256]),
               inc=c.ss)
    p.wait("sync", c.ss, vog)
    for tt in range(NTT):
        emit_ln_ff(c, tt, ln16)
        ln_done = c.sv.n
        p.wait("tensor", c.sv, ln_done)
        for W, base in (("l1a_Wk", 0), ("l1a_Wq", 8192)):
            for oc in range(DCH):
                woff = (oc % 2) * 2048
                wb_guard(c, woff)
                wap, wv = stream_w(c, W, woff, 0, DCH, oc * 128, 128)
                p.wait("tensor", c.dl, wv)
                bank = oc % 2
                c.pe_guard(bank)
                vm = None
                for kc in range(DCH):
                    vm = p.op("tensor", lambda nc, kc=kc, bank=bank, wap=wap:
                              nc.tensor.matmul(
                                  c.PSB[bank][:, :], wap[:, kc, :], ln16[:, kc, :],
                                  start=(kc == 0), stop=(kc == DCH - 1)), inc=c.st)
                wb_used(c, woff, vm)
                p.wait("scalar", c.st, vm)
                va = None
                for half in range(2):
                    h = 2 * oc + half
                    ksl = FAb[:, base + h * 512: base + (h + 1) * 512]
                    ps = c.PSB[bank][half * 64:(half + 1) * 64, :]
                    p.op("scalar", lambda nc, ksl=ksl, ps=ps: nc.scalar.copy(
                        ksl[0:64, :], ps))
                    va = p.op("scalar", lambda nc, ksl=ksl, ps=ps:
                              nc.scalar.activation(ksl[64:128, :], ps, AF.Square),
                              inc=c.ss)
                c.bank_consumed(bank, c.ss, va)
            p.wait("sync", c.ss, c.ss.n)
            scr_base = (SCR_L1A_KS if base == 0 else SCR_L1A_QS) + tt * (128 * 8192)
            dst = bass.AP(c.scr.bitcast(BF16), scr_base, [[8192, 128], [1, 8192]])
            vship = p.dma("sync", dst, FAb[:, base:base + 8192], c.dl)
            p.wait("scalar", c.dl, vship)
        for sub in range(4):
            vsmall(c, lambda nc: nc.vector.memset(vst3[:, :, 64:65], 1.0))
            ones_v = c.sv.n
            for half in range(2):
                woff = 0
                wb_guard(c, 0)
                wb_guard(c, 2048)
                wap, wv = stream_w(c, "l1a_Wv", woff, 0, DCH, half * 512, 512)
                p.wait("tensor", c.dl, wv)
                bank = 2 + half
                c.pe_guard(bank)
                vm = None
                for kc in range(DCH):
                    vm = p.op("tensor", lambda nc, kc=kc, sub=sub, bank=bank, wap=wap:
                              nc.tensor.matmul(
                                  c.PSB[bank][:, :],
                                  ln16[:, kc, sub * 128:(sub + 1) * 128],
                                  wap[:, kc, :],
                                  start=(kc == 0), stop=(kc == DCH - 1)), inc=c.st)
                wb_used(c, 0, vm)
                wb_used(c, 2048, vm)
                p.wait("scalar", c.st, vm)
                p.wait("scalar", c.sv, ones_v)
                ps3 = c.PSB[bank][:, :].rearrange("p (h e) -> p h e", e=64)
                va = p.op("scalar", lambda nc, ps3=ps3, half=half:
                          nc.scalar.copy(
                              vst3[:, half * 8:(half + 1) * 8, 0:64], ps3),
                          inc=c.ss)
                c.bank_consumed(bank, c.ss, va)
            p.wait("sync", c.ss, c.ss.n)
            row0 = tt * TT + sub * 128
            dst = bass.AP(c.scr.bitcast(BF16), SCR_L1A_V + row0 * 1040,
                          [[1040, 128], [1, 1040]])
            vship = p.dma("sync", dst, vstage, c.dl)
            p.wait("vector", c.dl, vship)
            p.wait("scalar", c.dl, vship)
    barrier(c)
    kvmd = kb(c, 0, [128, 16, 130])
    kvacc = lambda h: c.FA[0:65, h * 256:(h + 1) * 256]
    last_kv = None
    for ch in range(32):
        tt, sub = ch // 4, ch % 4
        ks_base = SCR_L1A_KS + tt * (128 * 8192)
        src_ap = bass.AP(c.scr.bitcast(BF16), ks_base + sub * 128,
                         [[8192, 128], [512, 16], [1, 128]])
        ksbuf = kb(c, 4096, [128, 16, 128])
        if last_kv is not None:
            p.wait("sync", c.st, last_kv)
        v1 = p.dma("sync", ksbuf, src_ap, c.dl)
        row0 = ch * 128
        vsrc = bass.AP(c.scr.bitcast(BF16), SCR_L1A_V + row0 * 1040,
                       [[1040, 128], [1, 1040]])
        vbuf = kb(c, 6144, [128, 16, 65])
        v2 = p.dma("sync", vbuf, vsrc, c.dl)
        p.wait("tensor", c.dl, v2)
        for h in range(16):
            c.pe_guard(4)
            vm = p.op("tensor", lambda nc, h=h, ksbuf=ksbuf: nc.tensor.matmul(
                c.PSB[4][:, 0:256], ksbuf[:, h, :], c.OMG[:, :],
                start=True, stop=True), inc=c.st)
            p.wait("scalar", c.st, vm)
            if last_kv is not None:
                p.wait("scalar", c.st, last_kv)
            vpk = p.op("scalar", lambda nc: nc.scalar.activation(
                c.SMB[:, 0:256], c.PSB[4][:, 0:256], AF.Exp, bias=nlncol),
                inc=c.ss)
            c.bank_consumed(4, c.ss, vpk)
            p.wait("tensor", c.ss, vpk)
            bank = h % 4
            c.pe_guard(bank)
            last_kv = p.op("tensor", lambda nc, vbuf=vbuf, h=h, bank=bank:
                           nc.tensor.matmul(
                               c.PSB[bank][0:65, 0:256], vbuf[:, h, :],
                               c.SMB[:, 0:256], start=True, stop=True), inc=c.st)
            p.wait("vector", c.st, last_kv)
            if ch == 0:
                vadd = p.op("vector", lambda nc, h=h, bank=bank:
                            nc.vector.tensor_copy(kvacc(h), c.PSB[bank][0:65, 0:256]),
                            inc=c.sv)
            else:
                vadd = p.op("vector", lambda nc, h=h, bank=bank:
                            nc.vector.tensor_tensor(
                                kvacc(h), kvacc(h), c.PSB[bank][0:65, 0:256],
                                ALU.add), inc=c.sv)
            c.bank_consumed(bank, c.sv, vadd)
    acc_done = c.sv.n
    p.wait("scalar", c.sv, acc_done)
    for h in range(16):
        va = p.op("scalar", lambda nc, h=h: nc.scalar.copy(
            c.SMB[0:65, 0:256], kvacc(h)), inc=c.ss)
        p.wait("tensor", c.ss, va)
        c.pe_guard(6)
        vt = None
        for mc in range(2):
            vt = p.op("tensor", lambda nc, mc=mc: nc.tensor.transpose(
                pt3[:, mc, 0:65], c.SMB[0:65, mc * 128:(mc + 1) * 128],
                c.IDT[0:65, 0:65]), inc=c.st if mc == 1 else None)
        p.wait("vector", c.st, vt)
        ve = p.op("vector", lambda nc, h=h: nc.vector.tensor_copy(
            kvmd[:, h, :].rearrange("p (a b) -> p a b", b=65),
            pt3[:, 0:2, 0:65]), inc=c.sv)
        c.bank_consumed(6, c.sv, ve)
        p.wait("scalar", c.sv, ve)
    barrier(c)
    last_pq = None
    for tt in range(NTT):
        for h in range(16):
            qs_base = SCR_L1A_QS + tt * (128 * 8192)
            src = bass.AP(c.scr.bitcast(BF16), qs_base + h * 512,
                          [[8192, 128], [1, 512]])
            qsbuf = kb(c, 4096, [128, 512])
            if last_pq is not None:
                p.wait("sync", c.st, last_pq)
            v1 = p.dma("sync", qsbuf, src, c.dl)
            p.wait("tensor", c.dl, v1)
            c.pe_guard(4)
            c.pe_guard(5)
            vm = None
            for mc in range(2):
                vm = p.op("tensor", lambda nc, mc=mc, qsbuf=qsbuf: nc.tensor.matmul(
                    c.PSB[4 + mc][:, :], c.OMG[:, mc * 128:(mc + 1) * 128],
                    qsbuf, start=True, stop=True), inc=c.st)
            last_pq = vm
            p.wait("scalar", c.st, vm)
            vpq = None
            for mc in range(2):
                vpq = p.op("scalar", lambda nc, mc=mc: nc.scalar.activation(
                    c.SMB[:, mc * 512:(mc + 1) * 512], c.PSB[4 + mc][:, :],
                    AF.Exp, bias=nlncol), inc=c.ss)
            c.bank_consumed(4, c.ss, vpq)
            c.bank_consumed(5, c.ss, vpq)
            p.wait("tensor", c.ss, vpq)
            c.pe_guard(2)
            c.pe_guard(3)
            kv3 = kvmd[:, h, :].rearrange("p (a b) -> p a b", b=65)
            hp0 = (h % 2) * 64
            htp = None if h % 2 == 0 else (0, 64)
            vz = None
            for mc in range(2):
                p.op("tensor", lambda nc, mc=mc, kv3=kv3: nc.tensor.matmul(
                    c.PSB[2][0:1, :], kv3[:, mc, 64:65],
                    c.SMB[:, mc * 512:(mc + 1) * 512],
                    start=(mc == 0), stop=(mc == 1)))
                vz = p.op("tensor", lambda nc, mc=mc, kv3=kv3, hp0=hp0, htp=htp:
                          nc.tensor.matmul(
                              c.PSB[3][hp0:hp0 + 64, :], kv3[:, mc, 0:64],
                              c.SMB[:, mc * 512:(mc + 1) * 512],
                              start=(mc == 0), stop=(mc == 1),
                              tile_position=htp), inc=c.st)
            p.wait("vector", c.st, vz)
            zrow = c.SMF[0:1, 520:1032]
            vsmall(c, lambda nc: nc.vector.tensor_scalar(
                zrow, c.PSB[2][0:1, :], 1e-6, None, ALU.add))
            vsmall(c, lambda nc: nc.vector.reciprocal(zrow, zrow))
            vrz = c.sv.n
            c.bank_consumed(2, c.sv, vrz)
            p.wait("scalar", c.sv, vrz)
            vrow = p.op("scalar", lambda nc: nc.scalar.copy(
                c.SMB[0:1, 1536:2048], zrow), inc=c.ss)
            p.wait("tensor", c.ss, vrow)
            j, hh = h // 2, h % 2
            p0, p1 = hh * 64, (hh + 1) * 64
            tp = None if hh == 0 else (0, 64)
            c.pe_guard(4)
            vbc = p.op("tensor", lambda nc, p0=p0, p1=p1, tp=tp: nc.tensor.matmul(
                c.PSB[4][p0:p1, :], c.ONEROW16[0:1, 0:64], c.SMB[0:1, 1536:2048],
                start=True, stop=True, tile_position=tp), inc=c.st)
            p.wait("scalar", c.st, vbc)
            vo2 = p.op("scalar", lambda nc, p0=p0, p1=p1: nc.scalar.copy(
                c.SMB[p0:p1, 0:512], c.PSB[3][p0:p1, :]), inc=c.ss)
            c.bank_consumed(3, c.ss, vo2)
            p.wait("vector", c.ss, vo2)
            rxa = c.RX[p0:p1, j, tt * TT:(tt + 1) * TT]
            p.op("vector", lambda nc, p0=p0, p1=p1: nc.vector.tensor_tensor(
                c.SMF[p0:p1, 0:512], c.SMB[p0:p1, 0:512], c.PSB[4][p0:p1, :],
                ALU.mult))
            vfin = p.op("vector", lambda nc, rxa=rxa, p0=p0, p1=p1:
                        nc.vector.tensor_tensor(
                            rxa, rxa, c.SMF[p0:p1, 0:512], ALU.add), inc=c.sv)
            c.bank_consumed(4, c.sv, vfin)
            p.wait("scalar", c.sv, vfin)


SECTION_FUNCS["l1a"] = lambda c, first: emit_l1a(c)



SCR_L0B_Q = 1 * 1024 * 1024
SCR_L0B_K = SCR_L0B_Q + 4 * 1024 * 1024
SCR_L0B_V = SCR_L0B_K + 4 * 1024 * 1024
SCR_L0B_C = SCR_L0B_V + 4 * 1024 * 1024


def emit_l0b(c):
    p, dr = c.p, c.dr
    barrier(c)
    c._l0b_w16v = 0
    c._l0b_vmu = 0
    ln16 = kb(c, 0, [128, DCH, TT])
    pt3 = c.PT[:].rearrange("p (a b) -> p a b", b=128)
    qmax = c.SMF[0:1, 1160:1161]
    kmax = c.SMF[0:1, 1161:1162]
    mcol = c.SMF[:, 1162:1163]
    vsmall(c, lambda nc: nc.vector.memset(c.SMF[0:1, 1160:1162], 0.0))
    vbi = None
    for b in ("bq", "bk", "bv", "bo"):
        off = {"bq": 1168, "bk": 1176, "bv": 1184, "bo": 1192}[b]
        vbi = p.dma("sync", c.SMF[:, off:off + 8], dr[f"l0b_{b}"][:], c.dl)
    for tt in range(NTT):
        emit_ln_ff(c, tt, ln16)
        ln_done = c.sv.n
        p.wait("tensor", c.sv, ln_done)
        p.wait("vector", c.dl, vbi)
        for W, sbase, boff, nrow in (("l0b_Wq", SCR_L0B_Q, 1168, qmax),
                                     ("l0b_Wk", SCR_L0B_K, 1176, kmax),
                                     ("l0b_Wv", SCR_L0B_V, 1184, None)):
            stg = kb(c, 4096, [128, DCH, TT])
            c.pe_guard("PR")
            nv = None
            for oc in range(DCH):
                woff = (oc % 2) * 2048
                wb_guard(c, woff)
                wap, wv = stream_w(c, W, woff, 0, DCH, oc * 128, 128)
                p.wait("tensor", c.dl, wv)
                bank = oc % 2
                c.pe_guard(bank)
                vm = None
                for kc in range(DCH):
                    vm = p.op("tensor", lambda nc, kc=kc, bank=bank, wap=wap:
                              nc.tensor.matmul(
                                  c.PSB[bank][:, :], wap[:, kc, :], ln16[:, kc, :],
                                  start=(kc == 0), stop=(kc == DCH - 1)), inc=c.st)
                wb_used(c, woff, vm)
                p.wait("vector", c.st, vm)
                bcol = c.SMF[:, boff + oc:boff + oc + 1]
                ve = p.op("vector", lambda nc, oc=oc, bank=bank, bcol=bcol, stg=stg:
                          nc.vector.tensor_scalar(
                              stg[:, oc, :], c.PSB[bank][:, :], bcol, None,
                              ALU.add), inc=c.sv)
                c.bank_consumed(bank, c.sv, ve)
                if nrow is not None:
                    p.wait("scalar", c.sv, ve)
                    vsq = p.op("scalar", lambda nc, oc=oc, stg=stg:
                               nc.scalar.activation(
                                   c.SMB[:, 0:TT], stg[:, oc, :], AF.Square),
                               inc=c.ss)
                    p.wait("tensor", c.ss, vsq)
                    p.op("tensor", lambda nc, oc=oc: nc.tensor.matmul(
                        c.PR[0:1, 0:TT], c.ONECOL16[:], c.SMB[:, 0:TT],
                        start=(oc == 0), stop=(oc == DCH - 1)),
                        inc=c.st if oc == DCH - 1 else None)
                    nv = c.st.n
            if nrow is not None:
                p.wait("vector", c.st, nv)
                vsmall(c, lambda nc: nc.vector.reduce_max(
                    c.SMF[0:1, 1163:1164], c.PR[0:1, 0:TT], axis=AX.X))
                vsmall(c, lambda nc, nrow=nrow: nc.vector.tensor_tensor(
                    nrow, nrow, c.SMF[0:1, 1163:1164], ALU.max))
                c.bank_consumed("PR", c.sv, c.sv.n)
            shipw = c.sv.n
            p.wait("sync", c.sv, shipw)
            if W == "l0b_Wv":
                p.wait("tensor", c.sv, shipw)
                for sub in range(4):
                    c.pe_guard(6)
                    vt = None
                    for dc2 in range(DCH):
                        vt = p.op("tensor", lambda nc, dc2=dc2, sub=sub, stg=stg:
                                  nc.tensor.transpose(
                                      pt3[:, dc2, :],
                                      stg[:, dc2, sub * 128:(sub + 1) * 128],
                                      c.IDT[:]), inc=c.st if dc2 == DCH - 1 else None)
                    p.wait("vector", c.st, vt)
                    vte = p.op("vector", lambda nc: nc.vector.tensor_copy(
                        kb(c, 8192, [128, DCH, 128]), pt3[:, :, :]), inc=c.sv)
                    c.bank_consumed(6, c.sv, vte)
                    p.wait("sync", c.sv, vte)
                    row0 = tt * TT + sub * 128
                    dst = bass.AP(c.scr.bitcast(BF16), sbase + row0 * 1024,
                                  [[1024, 128], [1, 1024]])
                    vship = p.dma("sync", dst, kb(c, 8192, [128, 1024]), c.dl)
                    p.wait("vector", c.dl, vship)
            else:
                dst = bass.AP(c.scr.bitcast(BF16), sbase + tt * (128 * 4096),
                              [[4096, 128], [1, 4096]])
                vship = p.dma("sync", dst, kb(c, 4096, [128, 4096]), c.dl)
                p.wait("vector", c.dl, vship)
    barrier(c)
    vsmall(c, lambda nc: nc.vector.tensor_tensor(qmax, qmax, kmax, ALU.mult))
    p.wait("scalar", c.sv, c.sv.n)
    vsq2 = p.op("scalar", lambda nc: nc.scalar.activation(qmax, qmax, AF.Sqrt),
                inc=c.ss)
    p.wait("vector", c.ss, vsq2)
    vsmall(c, lambda nc: nc.vector.tensor_scalar(
        qmax, qmax, -1.0 / 32.0, None, ALU.mult))
    p.wait("scalar", c.sv, c.sv.n)
    vmb = p.op("scalar", lambda nc: nc.scalar.copy(c.SMB[0:1, 0:1], qmax), inc=c.ss)
    p.wait("tensor", c.ss, vmb)
    c.pe_guard(2)
    vbc = p.op("tensor", lambda nc: nc.tensor.matmul(
        c.PSB[2][:, 0:1], c.ONEROW16[:], c.SMB[0:1, 0:1], start=True, stop=True),
        inc=c.st)
    p.wait("scalar", c.st, vbc)
    vmc = p.op("scalar", lambda nc: nc.scalar.copy(mcol, c.PSB[2][:, 0:1]), inc=c.ss)
    c.bank_consumed(2, c.ss, vmc)
    p.wait("tensor", c.ss, vmc)
    p.wait("vector", c.ss, vmc)
    qn = kb(c, 0, [128, DCH, 256])
    km = kb(c, 2048, [128, DCH, 256])
    vn = kb(c, 4096, [128, 2, 1024])
    ptile = kb(c, 6144, [128, 2, 256])
    w16 = kb(c, 6656, [128, 2, 256])
    wacc = fa(c, 0, [128, 2, 256])
    tmpf = fa(c, 512, [128, 256])
    last_q_use = None
    last_v_use = None
    for n in range(16):
        tt, half = n // 2, n % 2
        src = bass.AP(c.scr.bitcast(BF16), SCR_L0B_Q + tt * (128 * 4096) + half * 256,
                      [[4096, 128], [512, DCH], [1, 256]])
        if last_q_use is not None:
            p.wait("sync", c.st, last_q_use)
        vq = p.dma("sync", qn, src, c.dl)
        vsrc = bass.AP(c.scr.bitcast(BF16), SCR_L0B_V + n * 256 * 1024,
                       [[1024, 128], [128 * 1024, 2], [1, 1024]])
        if last_v_use is not None:
            p.wait("sync", c.st, last_v_use)
        vv = p.dma("sync", vn, vsrc, c.dl)
        p.wait("tensor", c.dl, vq)
        last_km = None
        for m in range(16):
            mt, mh = m // 2, m % 2
            ksrc = bass.AP(c.scr.bitcast(BF16),
                           SCR_L0B_K + mt * (128 * 4096) + mh * 256,
                           [[4096, 128], [512, DCH], [1, 256]])
            if last_km is not None:
                p.wait("sync", c.st, last_km)
            vk = p.dma("sync", km, ksrc, c.dl)
            p.wait("tensor", c.dl, vk)
            vsc = None
            for kc in range(2):
                c.pe_guard(kc)
                for dc in range(DCH):
                    vsc = p.op("tensor", lambda nc, kc=kc, dc=dc: nc.tensor.matmul(
                        c.PSB[kc][:, 0:256], km[:, dc, kc * 128:(kc + 1) * 128],
                        qn[:, dc, :], start=(dc == 0), stop=(dc == DCH - 1)),
                        inc=c.st if dc == DCH - 1 else None)
            last_km = vsc
            p.wait("scalar", c.st, vsc)
            if hasattr(c, "_l0b_vmu") and c._l0b_vmu:
                p.wait("scalar", c.sv, c._l0b_vmu)
            vex = None
            for kc in range(2):
                vex = p.op("scalar", lambda nc, kc=kc: nc.scalar.activation(
                    ptile[:, kc, :], c.PSB[kc][:, 0:256], AF.Exp,
                    bias=mcol, scale=1.0 / 32.0), inc=c.ss)
                c.bank_consumed(kc, c.ss, vex)
            p.wait("tensor", c.ss, vex)
            c.pe_guard("PR")
            vl = None
            for kc in range(2):
                vl = p.op("tensor", lambda nc, kc=kc: nc.tensor.matmul(
                    c.PR[0:1, 0:256], c.ONECOL16[:], ptile[:, kc, :],
                    start=(kc == 0), stop=(kc == 1)), inc=c.st)
            p.wait("vector", c.st, vl)
            lrow = c.SMF[0:1, 520:776]
            vsmall(c, lambda nc: nc.vector.reciprocal(lrow, c.PR[0:1, 0:256]))
            if m == n:
                vsmall(c, lambda nc: nc.vector.tensor_scalar(
                    lrow, lrow, 2.0, None, ALU.mult))
            c.bank_consumed("PR", c.sv, c.sv.n)
            p.wait("scalar", c.sv, c.sv.n)
            vlr = p.op("scalar", lambda nc: nc.scalar.copy(
                c.SMB[0:1, 0:256], lrow), inc=c.ss)
            p.wait("tensor", c.ss, vlr)
            c.pe_guard(2)
            vrb = p.op("tensor", lambda nc: nc.tensor.matmul(
                c.PSB[2][:, 0:256], c.ONEROW16[:], c.SMB[0:1, 0:256],
                start=True, stop=True), inc=c.st)
            p.wait("vector", c.st, vrb)
            vwa = None
            vmu = None
            for kc in range(2):
                p.op("vector", lambda nc, kc=kc: nc.vector.tensor_tensor(
                    tmpf, ptile[:, kc, :], c.PSB[2][:, 0:256], ALU.mult),
                    inc=c.sv)
                vmu = c.sv.n
                p.wait("gpsimd", c.sv, vmu)
                if m == 0:
                    p.wait("gpsimd", c.ss, getattr(c, "_l0b_w16v", 0))
                    vwa = p.op("gpsimd", lambda nc, kc=kc: nc.gpsimd.tensor_copy(
                        wacc[:, kc, :], tmpf), inc=c.sg)
                else:
                    vwa = p.op("gpsimd", lambda nc, kc=kc: nc.gpsimd.tensor_tensor(
                        wacc[:, kc, :], wacc[:, kc, :], tmpf, ALU.add), inc=c.sg)
                p.wait("vector", c.sg, vwa)
            c._l0b_vmu = vmu
            c.bank_consumed(2, c.sg, vwa)
        p.wait("scalar", c.sg, c.sg.n)
        vw16 = None
        for kc in range(2):
            vw16 = p.op("scalar", lambda nc, kc=kc: nc.scalar.copy(
                w16[:, kc, :], wacc[:, kc, :]), inc=c.ss)
        c._l0b_w16v = vw16
        p.wait("tensor", c.ss, vw16)
        p.wait("tensor", c.dl, vv)
        cstage = kb(c, 8192, [128, DCH, 256])
        for g in range(2):
            vpv = None
            for oi in range(4):
                ec = g * 4 + oi
                c.pe_guard(oi)
                for kc in range(2):
                    vpv = p.op("tensor", lambda nc, ec=ec, oi=oi, kc=kc:
                               nc.tensor.matmul(
                                   c.PSB[oi][:, 0:256],
                                   vn[:, kc, ec * 128:(ec + 1) * 128],
                                   w16[:, kc, :], start=(kc == 0), stop=(kc == 1)),
                               inc=c.st)
            last_v_use = vpv
            p.wait("vector", c.st, vpv)
            for oi in range(4):
                ec = g * 4 + oi
                ve2 = p.op("vector", lambda nc, ec=ec, oi=oi, cstage=cstage:
                           nc.vector.tensor_copy(cstage[:, ec, :],
                                                 c.PSB[oi][:, 0:256]), inc=c.sv)
                c.bank_consumed(oi, c.sv, ve2)
        last_q_use = last_km
        ship_ok = c.sv.n
        p.wait("sync", c.sv, ship_ok)
        dst = bass.AP(c.scr.bitcast(BF16), SCR_L0B_C + n * (128 * 2048),
                      [[2048, 128], [1, 2048]])
        vship = p.dma("sync", dst, cstage[:, :, :], c.dl)
        p.wait("vector", c.dl, vship)
    barrier(c)
    for tt in range(NTT):
        cb = kb(c, 0, [128, DCH, TT])
        if tt > 0:
            p.wait("sync", c.st, c._l0b_cb_use)
        for half in range(2):
            csrc = bass.AP(c.scr.bitcast(BF16),
                           SCR_L0B_C + (2 * tt + half) * (128 * 2048),
                           [[2048, 128], [256, DCH], [1, 256]])
            vc = p.dma("sync", cb[:, :, half * 256:(half + 1) * 256], csrc, c.dl)
        p.wait("tensor", c.dl, vc)
        for oc in range(DCH):
            woff = (oc % 2) * 2048
            wb_guard(c, woff)
            wap, wv = stream_w(c, "l0b_Wo", woff, 0, DCH, oc * 128, 128)
            p.wait("tensor", c.dl, wv)
            bank = oc % 2
            c.pe_guard(bank)
            vm = None
            for kc in range(DCH):
                vm = p.op("tensor", lambda nc, kc=kc, bank=bank, wap=wap, cb=cb:
                          nc.tensor.matmul(
                              c.PSB[bank][:, :], wap[:, kc, :], cb[:, kc, :],
                              start=(kc == 0), stop=(kc == DCH - 1)), inc=c.st)
            wb_used(c, woff, vm)
            c._l0b_cb_use = vm
            p.wait("vector", c.st, vm)
            bcol = c.SMF[:, 1192 + oc:1193 + oc]
            rxa = c.RX[:, oc, tt * TT:(tt + 1) * TT]
            vr = p.op("vector", lambda nc, bank=bank, bcol=bcol, rxa=rxa:
                      nc.vector.scalar_tensor_tensor(
                          rxa, c.PSB[bank][:, :], bcol, rxa, ALU.add, ALU.add),
                      inc=c.sv)
            c.bank_consumed(bank, c.sv, vr)


SECTION_FUNCS["l0b"] = lambda c, first: emit_l0b(c)

def emit_l0a(c):
    p, dr = c.p, c.dr
    barrier(c)
    ln16 = kb(c, 0, [128, DCH, TT])
    pt3 = c.PT[:].rearrange("p (a b) -> p a b", b=128)
    qmax = c.SMF[0:1, 1160:1161]
    kmax = c.SMF[0:1, 1161:1162]
    mcol = c.SMF[:, 1162:1163]
    vsmall(c, lambda nc: nc.vector.memset(c.SMF[0:1, 1160:1162], 0.0))
    for tt in range(NTT):
        emit_ln_ff(c, tt, ln16)
        ln_done = c.sv.n
        p.wait("tensor", c.sv, ln_done)
        for W, sbase, nrow in (("l0a_qT", SCR_L0B_Q, qmax),
                               ("l0a_kT", SCR_L0B_K, kmax),
                               ("l0a_vT", SCR_L0B_V, None)):
            stg = kb(c, 4096, [128, DCH, TT])
            c.pe_guard("PR")
            nv = None
            for oc in range(DCH):
                woff = (oc % 2) * 2048
                wb_guard(c, woff)
                wap, wv = stream_w(c, W, woff, 0, DCH, oc * 128, 128)
                p.wait("tensor", c.dl, wv)
                bank = oc % 2
                c.pe_guard(bank)
                vm = None
                for kc in range(DCH):
                    vm = p.op("tensor", lambda nc, kc=kc, bank=bank, wap=wap:
                              nc.tensor.matmul(
                                  c.PSB[bank][:, :], wap[:, kc, :], ln16[:, kc, :],
                                  start=(kc == 0), stop=(kc == DCH - 1)), inc=c.st)
                wb_used(c, woff, vm)
                p.wait("scalar", c.st, vm)
                ve = p.op("scalar", lambda nc, oc=oc, bank=bank, stg=stg:
                          nc.scalar.copy(stg[:, oc, :], c.PSB[bank][:, :]),
                          inc=c.ss)
                if nrow is not None:
                    vsq = p.op("scalar", lambda nc, oc=oc, bank=bank:
                               nc.scalar.activation(
                                   c.SMB[:, 0:TT], c.PSB[bank][:, :], AF.Square),
                               inc=c.ss)
                    c.bank_consumed(bank, c.ss, vsq)
                    p.wait("tensor", c.ss, vsq)
                    c.pe_guard("PR")
                    nv = p.op("tensor", lambda nc, oc=oc: nc.tensor.matmul(
                        c.PR[0:1, 0:TT], c.ONECOL16[:], c.SMB[:, 0:TT],
                        start=True, stop=True), inc=c.st)
                    p.wait("vector", c.st, nv)
                    vsmall(c, lambda nc: nc.vector.reduce_max(
                        c.SMF[0:1, 1163:1164], c.PR[0:1, 0:TT], axis=AX.X))
                    vsmall(c, lambda nc, nrow=nrow: nc.vector.tensor_tensor(
                        nrow, nrow, c.SMF[0:1, 1163:1164], ALU.max))
                    c.bank_consumed("PR", c.sv, c.sv.n)
                else:
                    c.bank_consumed(bank, c.ss, ve)
            shipw = c.ss.n
            p.wait("sync", c.ss, shipw)
            if W == "l0a_vT":
                p.wait("tensor", c.ss, shipw)
                for sub in range(4):
                    c.pe_guard(6)
                    vt = None
                    for dc2 in range(DCH):
                        vt = p.op("tensor", lambda nc, dc2=dc2, sub=sub, stg=stg:
                                  nc.tensor.transpose(
                                      pt3[:, dc2, :],
                                      stg[:, dc2, sub * 128:(sub + 1) * 128],
                                      c.IDT[:]), inc=c.st if dc2 == DCH - 1 else None)
                    p.wait("vector", c.st, vt)
                    vte = p.op("vector", lambda nc: nc.vector.tensor_copy(
                        kb(c, 8192, [128, DCH, 128]), pt3[:, :, :]), inc=c.sv)
                    c.bank_consumed(6, c.sv, vte)
                    p.wait("sync", c.sv, vte)
                    row0 = tt * TT + sub * 128
                    dst = bass.AP(c.scr.bitcast(BF16), SCR_L0B_V + row0 * 1024,
                                  [[1024, 128], [1, 1024]])
                    vship = p.dma("sync", dst, kb(c, 8192, [128, 1024]), c.dl)
                    p.wait("vector", c.dl, vship)
            else:
                dst = bass.AP(c.scr.bitcast(BF16), sbase + tt * (128 * 4096),
                              [[4096, 128], [1, 4096]])
                vship = p.dma("sync", dst, kb(c, 4096, [128, 4096]), c.dl)
                p.wait("scalar", c.dl, vship)
    barrier(c)
    vsmall(c, lambda nc: nc.vector.tensor_tensor(qmax, qmax, kmax, ALU.mult))
    p.wait("scalar", c.sv, c.sv.n)
    vsq2 = p.op("scalar", lambda nc: nc.scalar.activation(qmax, qmax, AF.Sqrt),
                inc=c.ss)
    p.wait("vector", c.ss, vsq2)
    vsmall(c, lambda nc: nc.vector.tensor_scalar(
        qmax, qmax, -1.0 / 8.0, None, ALU.mult))
    p.wait("scalar", c.sv, c.sv.n)
    vmb = p.op("scalar", lambda nc: nc.scalar.copy(c.SMB[0:1, 0:1], qmax), inc=c.ss)
    p.wait("tensor", c.ss, vmb)
    c.pe_guard(2)
    vbc = p.op("tensor", lambda nc: nc.tensor.matmul(
        c.PSB[2][:, 0:1], c.ONEROW16[:], c.SMB[0:1, 0:1], start=True, stop=True),
        inc=c.st)
    p.wait("scalar", c.st, vbc)
    vmc = p.op("scalar", lambda nc: nc.scalar.copy(mcol, c.PSB[2][:, 0:1]), inc=c.ss)
    c.bank_consumed(2, c.ss, vmc)
    p.wait("tensor", c.ss, vmc)
    p.wait("vector", c.ss, vmc)
    ptt = kb(c, 4096, [128, 4, TT])
    attn16 = kb(c, 6144, [128, DCH, TT])
    for tt in range(NTT):
        nkt = tt + 1
        for j in range(DCH):
            qsl = kb(c, 0, [128, TT])
            qsrc = bass.AP(c.scr.bitcast(BF16),
                           SCR_L0B_Q + tt * (128 * 4096) + j * 512,
                           [[4096, 128], [1, 512]])
            if getattr(c, "_a_quse", None) is not None:
                p.wait("sync", c.st, c._a_quse)
            vq = p.dma("sync", qsl, qsrc, c.dl)
            p.wait("tensor", c.dl, vq)
            for kt in range(nkt):
                kslot = (kt % 2)
                kbuf = kb(c, 1024 + kslot * 512, [128, TT])
                ksrc = bass.AP(c.scr.bitcast(BF16),
                               SCR_L0B_K + kt * (128 * 4096) + j * 512,
                               [[4096, 128], [1, 512]])
                if getattr(c, "_a_kuse", {}).get(kslot) is not None:
                    p.wait("sync", c.st, c._a_kuse[kslot])
                vk = p.dma("sync", kbuf, ksrc, c.dl)
                vbuf = kb(c, 2048 + kslot * 512, [128, 4, 128])
                vsrc = bass.AP(c.scr.bitcast(BF16),
                               SCR_L0B_V + (kt * 512) * 1024 + j * 128,
                               [[1024, 128], [128 * 1024, 4], [1, 128]])
                if getattr(c, "_a_vuse", {}).get(kslot) is not None:
                    p.wait("sync", c.st, c._a_vuse[kslot])
                vv = p.dma("sync",
                           bass.AP(vbuf.tensor, vbuf.offset,
                                   [[10240, 128], [128, 4], [1, 128]]),
                           vsrc, c.dl)
                p.wait("tensor", c.dl, vv)
                for hh in range(2):
                    h = 2 * j + hh
                    p0 = hh * 64
                    tp = None if hh == 0 else (0, 64)
                    vsc = None
                    for ks in range(4):
                        c.pe_guard(ks)
                        vsc = p.op("tensor", lambda nc, ks=ks, p0=p0, kbuf=kbuf,
                                   qsl=qsl: nc.tensor.matmul(
                                       c.PSB[ks][:, :],
                                       kbuf[p0:p0 + 64, ks * 128:(ks + 1) * 128],
                                       qsl[p0:p0 + 64, :],
                                       start=True, stop=True), inc=c.st)
                    c._a_kuse = getattr(c, "_a_kuse", {})
                    c._a_kuse[kslot] = vsc
                    c._a_quse = vsc
                    p.wait("scalar", c.st, vsc)
                    if getattr(c, "_a_puse", None) is not None:
                        p.wait("scalar", c.sv, c._a_puse[0])
                        p.wait("scalar", c.st, c._a_puse[1])
                    vex = None
                    for ks in range(4):
                        vex = p.op("scalar", lambda nc, ks=ks: nc.scalar.activation(
                            ptt[:, ks, :], c.PSB[ks][:, :], AF.Exp,
                            bias=mcol, scale=1.0 / 8.0), inc=c.ss)
                        c.bank_consumed(ks, c.ss, vex)
                    vmask = None
                    if kt == tt:
                        p.wait("vector", c.ss, vex)
                        if getattr(c, "_a_puse", None) is not None:
                            p.wait("vector", c.st, c._a_puse[1])
                        for ks in range(4):
                            delta = ks * 128
                            vmask = p.op("vector", lambda nc, ks=ks, delta=delta:
                                         nc.vector.tensor_tensor(
                                             ptt[:, ks, :], ptt[:, ks, :],
                                             c.CAUS[:, 384 - delta:896 - delta],
                                             ALU.mult), inc=c.sv)
                        p.wait("tensor", c.sv, vmask)
                    else:
                        p.wait("tensor", c.ss, vex)
                    c.pe_guard("PR")
                    vl = None
                    for ks in range(4):
                        vl = p.op("tensor", lambda nc, ks=ks, p0=p0, tp=tp:
                                  nc.tensor.matmul(
                                      c.PR[p0:p0 + 1, :], c.ONECOL16[:],
                                      ptt[:, ks, :], start=(ks == 0),
                                      stop=(ks == 3), tile_position=tp), inc=c.st)
                    c.pe_guard(4)
                    vpv = None
                    for ks in range(4):
                        vpv = p.op("tensor", lambda nc, ks=ks, p0=p0, tp=tp,
                                   vbuf=vbuf: nc.tensor.matmul(
                                       c.PSB[4][p0:p0 + 64, :],
                                       vbuf[:, ks, p0:p0 + 64], ptt[:, ks, :],
                                       start=(ks == 0), stop=(ks == 3),
                                       tile_position=tp), inc=c.st)
                    c._a_puse = (vmask if vmask else c.sv.n, vpv)
                    c._a_vuse = getattr(c, "_a_vuse", {})
                    c._a_vuse[kslot] = vpv
                    oa = c.FA[p0:p0 + 64, j * 512:(j + 1) * 512]
                    la = c.FA[p0:p0 + 1, 4096 + j * 512:4096 + (j + 1) * 512]
                    p.wait("vector", c.st, vpv)
                    if kt == 0:
                        p.op("vector", lambda nc, oa=oa, p0=p0: nc.vector.tensor_copy(
                            oa, c.PSB[4][p0:p0 + 64, :]))
                        vacc = p.op("vector", lambda nc, la=la, p0=p0:
                                    nc.vector.tensor_copy(la, c.PR[p0:p0 + 1, :]),
                                    inc=c.sv)
                    else:
                        p.op("vector", lambda nc, oa=oa, p0=p0: nc.vector.tensor_tensor(
                            oa, oa, c.PSB[4][p0:p0 + 64, :], ALU.add))
                        vacc = p.op("vector", lambda nc, la=la, p0=p0:
                                    nc.vector.tensor_tensor(
                                        la, la, c.PR[p0:p0 + 1, :], ALU.add),
                                    inc=c.sv)
                    c.bank_consumed(4, c.sv, vacc)
                    c.bank_consumed("PR", c.sv, vacc)
            for hh in range(2):
                h = 2 * j + hh
                p0 = hh * 64
                tp = None if hh == 0 else (0, 64)
                oa = c.FA[p0:p0 + 64, j * 512:(j + 1) * 512]
                la = c.FA[p0:p0 + 1, 4096 + j * 512:4096 + (j + 1) * 512]
                rl = c.SMF[p0:p0 + 1, 520:1032]
                vsmall(c, lambda nc, rl=rl, la=la: nc.vector.reciprocal(rl, la))
                p.wait("scalar", c.sv, c.sv.n)
                vrl = p.op("scalar", lambda nc, rl=rl, p0=p0: nc.scalar.copy(
                    c.SMB[p0:p0 + 1, 0:512], rl), inc=c.ss)
                p.wait("tensor", c.ss, vrl)
                c.pe_guard(5)
                tpb = None if hh == 0 else (64, 64)
                vbc2 = p.op("tensor", lambda nc, p0=p0, tpb=tpb: nc.tensor.matmul(
                    c.PSB[5][p0:p0 + 64, :], c.ONES128[p0:p0 + 1, 0:64],
                    c.SMB[p0:p0 + 1, 0:512], start=True, stop=True,
                    tile_position=tpb), inc=c.st)
                p.wait("vector", c.st, vbc2)
                vfin = p.op("vector", lambda nc, oa=oa, p0=p0, j=j:
                            nc.vector.tensor_tensor(
                                attn16[p0:p0 + 64, j, :], oa,
                                c.PSB[5][p0:p0 + 64, :], ALU.mult), inc=c.sv)
                c.bank_consumed(5, c.sv, vfin)
        attn_ready = c.sv.n
        p.wait("tensor", c.sv, attn_ready)
        for oc in range(DCH):
            woff = (oc % 2) * 2048
            wb_guard(c, woff)
            wap, wv = stream_w(c, "l0a_oT", woff, 0, DCH, oc * 128, 128)
            p.wait("tensor", c.dl, wv)
            bank = oc % 2
            c.pe_guard(bank)
            vm = None
            for kc in range(DCH):
                vm = p.op("tensor", lambda nc, kc=kc, bank=bank, wap=wap:
                          nc.tensor.matmul(
                              c.PSB[bank][:, :], wap[:, kc, :], attn16[:, kc, :],
                              start=(kc == 0), stop=(kc == DCH - 1)), inc=c.st)
            wb_used(c, woff, vm)
            p.wait("vector", c.st, vm)
            rxa = c.RX[:, oc, tt * TT:(tt + 1) * TT]
            vr = p.op("vector", lambda nc, bank=bank, rxa=rxa:
                      nc.vector.tensor_tensor(
                          rxa, rxa, c.PSB[bank][:, :], ALU.add), inc=c.sv)
            c.bank_consumed(bank, c.sv, vr)
        c._a_attn_use = c.st.n


SECTION_FUNCS["l0a"] = lambda c, first: emit_l0a(c)



SCR_SSM_IN = 31_900_000
SCR_SSM_H = 40_300_000
SCR_SSM_DT = 48_700_000
SCR_SSM_DTH = 0
SCR_SSM_B = 31_800_000
SCR_SSM_G = 13_600_000


def scrf(c, elem_off, rows, cols, row_stride=None, b=False):
    if row_stride is None:
        row_stride = cols
    return bass.AP(c.scrB if b else c.scr, elem_off,
                   [[row_stride, rows], [1, cols]])


def emit_l1b(c):
    p, dr = c.p, c.dr
    barrier(c)
    ln16 = kb(c, 0, [128, DCH, TT])
    sg16 = kb(c, 4096, [128, 8, TT])
    vck = p.dma("sync", c.SMF[:, 1168:1216], dr["l1b_cw"][:], c.dl)
    vck = p.dma("sync", c.SMF[:, 1216:1232], dr["l1b_cb"][:], c.dl)
    vck = p.dma("sync", c.SMF[:, 1232:1248], dr["l1b_bt"][:], c.dl)
    vck = p.dma("sync", c.SMF[:, 1248:1264], dr["l1b_skip"][:], c.dl)
    vck = p.dma("sync", c.SMF[:, 1536:1792], dr["l1b_alog"][:], c.dl)
    p.wait("scalar", c.dl, vck)
    va = p.op("scalar", lambda nc: nc.scalar.activation(
        c.SMF[:, 1536:1792], c.SMF[:, 1536:1792], AF.Exp), inc=c.ss)
    p.wait("vector", c.ss, va)
    p.op("vector", lambda nc: nc.vector.tensor_scalar(
        c.SMF[:, 1536:1792], c.SMF[:, 1536:1792], -1.0, None, ALU.mult), inc=c.sv)
    p.op("vector", lambda nc: nc.vector.memset(c.SMF[:, 1792:2048], 0.0), inc=c.sv)
    vconsts = c.sv.n
    p.wait("scalar", c.sv, vconsts)
    p.wait("tensor", c.sv, vconsts)
    acol = lambda ct, s: c.SMF[:, 1536 + ct * 16 + s:1536 + ct * 16 + s + 1]
    stcol = lambda ct, s: c.SMF[:, 1792 + ct * 16 + s:1792 + ct * 16 + s + 1]
    for tt in range(NTT):
        emit_ln_ff(c, tt, ln16)
        ln_done = c.sv.n
        p.wait("tensor", c.sv, ln_done)
        for oc in range(32):
            woff = (oc % 2) * 2048
            wb_guard(c, woff)
            wap, wv = stream_w(c, "l1b_Win", woff, 0, DCH, oc * 128, 128)
            p.wait("tensor", c.dl, wv)
            bank = oc % 2
            c.pe_guard(bank)
            vm = None
            for kc in range(DCH):
                vm = p.op("tensor", lambda nc, kc=kc, bank=bank, wap=wap:
                          nc.tensor.matmul(
                              c.PSB[bank][:, :], wap[:, kc, :], ln16[:, kc, :],
                              start=(kc == 0), stop=(kc == DCH - 1)), inc=c.st)
            wb_used(c, woff, vm)
            if oc < 16:
                slot = fa(c, 7680, [128, TT])
                p.wait("vector", c.st, vm)
                ve = p.op("vector", lambda nc, bank=bank, slot=slot:
                          nc.vector.tensor_copy(slot, c.PSB[bank][:, :]), inc=c.sv)
                c.bank_consumed(bank, c.sv, ve)
                p.wait("sync", c.sv, ve)
                dst = scrf(c, SCR_SSM_IN + (tt * 16 + oc) * (128 * TT), 128, TT)
                vs = p.dma("sync", dst, slot, c.dl)
                p.wait("vector", c.dl, vs)
            else:
                ct = oc - 16
                if ct % 8 == 0 and hasattr(c, "_b_sg_ship"):
                    p.wait("scalar", c.dl, c._b_sg_ship)
                p.wait("scalar", c.st, vm)
                ve = p.op("scalar", lambda nc, bank=bank, ct=ct:
                          nc.scalar.activation(sg16[:, ct % 8, :],
                                               c.PSB[bank][:, :],
                                               AF.Silu), inc=c.ss)
                c.bank_consumed(bank, c.ss, ve)
                if ct % 8 == 7:
                    p.wait("sync", c.ss, c.ss.n)
                    dst = bass.AP(c.scr.bitcast(BF16),
                                  SCR_SSM_G + tt * (128 * 8192)
                                  + (ct // 8) * 4096,
                                  [[8192, 128], [1, 4096]])
                    c._b_sg_ship = p.dma("sync", dst, kb(c, 4096, [128, 4096]),
                                         c.dl)
    barrier(c)
    h16 = kb(c, 0, [128, 16, TT])
    for tt in range(NTT):
        for ct in range(16):
            ibuf = fa(c, 0, [128, 514])
            vz = None
            if tt == 0:
                vz = p.op("vector", lambda nc, ibuf=ibuf: nc.vector.memset(
                    ibuf[:, 0:1], 0.0), inc=c.sv)
                p.wait("sync", c.sv, vz)
            if hasattr(c, "_b_ib_use"):
                p.wait("sync", c.sv, c._b_ib_use)
                p.wait("sync", c.ss, c._b_ib_use2)
            v1 = p.dma("sync", ibuf[:, 1:513],
                       scrf(c, SCR_SSM_IN + (tt * 16 + ct) * (128 * TT), 128, TT),
                       c.dl)
            if tt > 0:
                v1 = p.dma("sync", ibuf[:, 0:1],
                           scrf(c, SCR_SSM_IN + ((tt - 1) * 16 + ct) * (128 * TT)
                                + 511, 128, 1, row_stride=TT), c.dl, slow=True)
            if tt < NTT - 1:
                v1 = p.dma("sync", ibuf[:, 513:514],
                           scrf(c, SCR_SSM_IN + ((tt + 1) * 16 + ct) * (128 * TT),
                                128, 1, row_stride=TT), c.dl, slow=True)
            else:
                v1 = p.dma("sync", ibuf[:, 513:514],
                           scrf(c, SCR_SSM_IN + (tt * 16 + ct) * (128 * TT),
                                128, 1, row_stride=TT), c.dl, slow=True)
            p.wait("vector", c.dl, v1)
            if tt == NTT - 1:
                p.op("vector", lambda nc, ibuf=ibuf: nc.vector.memset(
                    ibuf[:, 513:514], 0.0))
            cw0 = c.SMF[:, 1168 + ct * 3:1169 + ct * 3]
            cw1 = c.SMF[:, 1169 + ct * 3:1170 + ct * 3]
            cw2 = c.SMF[:, 1170 + ct * 3:1171 + ct * 3]
            cbc = c.SMF[:, 1216 + ct:1217 + ct]
            conv = fa(c, 640, [128, TT])
            p.op("vector", lambda nc, ibuf=ibuf, conv=conv, cw0=cw0:
                 nc.vector.tensor_scalar(conv, ibuf[:, 0:512], cw0, None, ALU.mult))
            p.op("vector", lambda nc, ibuf=ibuf, conv=conv, cw1=cw1:
                 nc.vector.scalar_tensor_tensor(
                     conv, ibuf[:, 1:513], cw1, conv, ALU.mult, ALU.add))
            vcv = p.op("vector", lambda nc, ibuf=ibuf, conv=conv, cw2=cw2:
                       nc.vector.scalar_tensor_tensor(
                           conv, ibuf[:, 2:514], cw2, conv, ALU.mult, ALU.add),
                       inc=c.sv)
            c._b_ib_use = vcv
            p.wait("scalar", c.sv, vcv)
            hslot = fa(c, 1152, [128, TT])
            p.op("scalar", lambda nc, conv=conv, hslot=hslot, cbc=cbc:
                 nc.scalar.activation(hslot, conv, AF.Silu, bias=cbc))
            vh = p.op("scalar", lambda nc, hslot=hslot, ct=ct:
                      nc.scalar.copy(h16[:, ct, :], hslot), inc=c.ss)
            c._b_ib_use2 = vh
            p.wait("sync", c.ss, vh)
            vs = p.dma("sync",
                       scrf(c, SCR_SSM_H + (tt * 16 + ct) * (128 * TT), 128, TT),
                       hslot, c.dl)
            p.wait("scalar", c.dl, vs)
        p.wait("tensor", c.ss, c.ss.n)
        c.pe_guard(2)
        vm = None
        for kc in range(16):
            wb_guard(c, 3072)
            wap, wv = stream_w(c, "l1b_Wst", 3072, kc, 1, 16, 16)
            p.wait("tensor", c.dl, wv)
            vm = p.op("tensor", lambda nc, kc=kc, wap=wap: nc.tensor.matmul(
                c.PSB[2][0:16, :], wap[:, 0, :], h16[:, kc, :],
                start=(kc == 0), stop=(kc == 15)), inc=c.st)
            wb_used(c, 3072, vm)
        p.wait("vector", c.st, vm)
        bslot = fa(c, 0, [16, TT])
        vb2 = p.op("vector", lambda nc, bslot=bslot: nc.vector.tensor_copy(
            bslot, c.PSB[2][0:16, :]), inc=c.sv)
        c.bank_consumed(2, c.sv, vb2)
        p.wait("sync", c.sv, vb2)
        vs = p.dma("sync", scrf(c, SCR_SSM_B + tt * (16 * TT), 16, TT), bslot, c.dl)
        p.wait("vector", c.dl, vs)
        for ct in range(16):
            woff = (ct % 2) * 2048
            wb_guard(c, woff)
            wap, wv = stream_w(c, "l1b_Wt", woff, 0, 16, ct * 128, 128)
            p.wait("tensor", c.dl, wv)
            bank = ct % 2
            c.pe_guard(bank)
            vm = None
            for kc in range(16):
                vm = p.op("tensor", lambda nc, kc=kc, bank=bank, wap=wap:
                          nc.tensor.matmul(
                              c.PSB[bank][:, :], wap[:, kc, :], h16[:, kc, :],
                              start=(kc == 0), stop=(kc == 15)), inc=c.st)
            wb_used(c, woff, vm)
            p.wait("vector", c.st, vm)
            btc = c.SMF[:, 1232 + ct:1233 + ct]
            u = fa(c, 0, [128, TT])
            r = fa(c, 512, [128, TT])
            rn = fa(c, 1024, [128, TT])
            dts = fa(c, 1536, [128, TT])
            dth = fa(c, 2048, [128, TT])
            p.op("vector", lambda nc, bank=bank, u=u, btc=btc:
                 nc.vector.tensor_scalar(u, c.PSB[bank][:, :], btc, None, ALU.add),
                 inc=c.sv)
            c.bank_consumed(bank, c.sv, c.sv.n)
            p.op("vector", lambda nc, u=u, r=r: nc.vector.tensor_scalar(
                r, u, 0.0, None, ALU.max))
            vpre = p.op("vector", lambda nc, u=u, rn=rn: nc.vector.tensor_scalar(
                rn, u, -1.0, 0.0, ALU.mult, ALU.max), inc=c.sv)
            p.op("vector", lambda nc, u=u, r=r, rn=rn: nc.vector.tensor_tensor(
                u, r, rn, ALU.add), inc=c.sv)
            vau = c.sv.n
            p.wait("scalar", c.sv, vau)
            p.op("scalar", lambda nc, u=u: nc.scalar.activation(
                u, u, AF.Exp, scale=-1.0))
            vln = p.op("scalar", lambda nc, u=u: nc.scalar.activation(
                u, u, AF.Ln, bias=1.0), inc=c.ss)
            p.wait("vector", c.ss, vln)
            p.op("vector", lambda nc, u=u, r=r, dts=dts: nc.vector.tensor_tensor(
                dts, r, u, ALU.add))
            hbuf = fa(c, 2560, [128, TT])
            vhb = p.dma("sync",
                        scrf(c, SCR_SSM_H + (tt * 16 + ct) * (128 * TT), 128, TT),
                        hbuf, c.dl) if False else None
            vhb = p.dma("sync", hbuf,
                        scrf(c, SCR_SSM_H + (tt * 16 + ct) * (128 * TT), 128, TT),
                        c.dl)
            p.wait("vector", c.dl, vhb)
            vdth = p.op("vector", lambda nc, dts=dts, hbuf=hbuf, dth=dth:
                        nc.vector.tensor_tensor(dth, dts, hbuf, ALU.mult), inc=c.sv)
            p.wait("sync", c.sv, vdth)
            vs1 = p.dma("sync", scrf(c, SCR_SSM_DT + (tt * 16 + ct) * (128 * TT),
                                     128, TT), dts, c.dl)
            vs2 = p.dma("sync", scrf(c, SCR_SSM_DTH + (tt * 16 + ct) * (128 * TT),
                                     128, TT, b=True), dth, c.dl)
            p.wait("vector", c.dl, vs2)
    barrier(c)
    yg16 = kb(c, 0, [128, 8, TT])
    SELS = c.KB[0:16, 8192:10240]
    for half2 in range(2):
        vsl = p.dma("sync", c.WS[0:16, 0:1024],
                    dr["c_sels"][:, half2 * 1024:(half2 + 1) * 1024], c.dl)
        p.wait("scalar", c.dl, vsl)
        vsl2 = p.op("scalar", lambda nc, half2=half2: nc.scalar.copy(
            c.KB[0:16, 8192 + half2 * 1024:8192 + (half2 + 1) * 1024],
            c.WS[0:16, 0:1024]), inc=c.ss)
        p.wait("sync", c.ss, vsl2)
    p.wait("tensor", c.ss, c.ss.n)
    bt16s = c.SMB[0:16, 1536:2048]
    for tt in range(NTT):
        vb1 = p.dma("sync", c.SMF[0:16, 0:512],
                    scrf(c, SCR_SSM_B + tt * (16 * TT), 16, TT), c.dl)
        p.wait("scalar", c.dl, vb1)
        vb16 = p.op("scalar", lambda nc: nc.scalar.copy(
            bt16s, c.SMF[0:16, 0:512]), inc=c.ss)
        p.wait("tensor", c.ss, vb16)
        for ct in range(16):
            dtb = fa(c, 0, [128, TT])
            dthb = fa(c, 512, [128, TT])
            dab = fa(c, 1024, [128, TT])
            dbb = fa(c, 1536, [128, TT])
            memb = fa(c, 2048, [128, TT])
            Yb = fa(c, 2560, [128, TT])
            yb = fa(c, 3072, [128, TT])
            hb = fa(c, 3584, [128, TT])
            if hasattr(c, "_c_dt_use"):
                p.wait("sync", c.ss, c._c_dt_use[0])
                p.wait("sync", c.sv, c._c_dt_use[1])
            v1 = p.dma("sync", dtb, scrf(c, SCR_SSM_DT + (tt * 16 + ct) * (128 * TT),
                                         128, TT), c.dl)
            v2 = p.dma("sync", dthb, scrf(c, SCR_SSM_DTH + (tt * 16 + ct) * (128 * TT),
                                          128, TT, b=True), c.dl)
            v3 = p.dma("sync", hb, scrf(c, SCR_SSM_H + (tt * 16 + ct) * (128 * TT),
                                        128, TT), c.dl)
            p.wait("scalar", c.dl, v1)
            p.wait("vector", c.dl, v3)
            for s in range(16):
                bank = 2 + (s % 2)
                c.pe_guard(bank)
                vbr = p.op("tensor", lambda nc, s=s, bank=bank: nc.tensor.matmul(
                    c.PSB[bank][:, :], SELS[:, s * 128:(s + 1) * 128],
                    bt16s, start=True, stop=True), inc=c.st)
                vda = p.op("scalar", lambda nc, ct=ct, s=s, dtb=dtb, dab=dab:
                           nc.scalar.activation(dab, dtb, AF.Exp,
                                                scale=acol(ct, s)), inc=c.ss)
                p.wait("vector", c.ss, vda)
                p.wait("vector", c.st, vbr)
                vdb = p.op("vector", lambda nc, dthb=dthb, dbb=dbb, bank=bank:
                           nc.vector.tensor_tensor(dbb, dthb, c.PSB[bank][:, :],
                                                   ALU.mult), inc=c.sv)
                c.bank_consumed(bank, c.sv, vdb)
                vsc = p.op("vector", lambda nc, dab=dab, dbb=dbb, memb=memb, ct=ct,
                           s=s: nc.vector.tensor_tensor_scan(
                               memb, dab, dbb, stcol(ct, s), ALU.mult, ALU.add),
                           inc=c.sv)
                vsmall(c, lambda nc, memb=memb, ct=ct, s=s: nc.vector.tensor_copy(
                    stcol(ct, s), memb[:, TT - 1:TT]))
                p.wait("gpsimd", c.sv, vsc)
                if s == 0:
                    vy = p.op("gpsimd", lambda nc, memb=memb, Yb=Yb:
                              nc.gpsimd.tensor_copy(Yb, memb), inc=c.sg)
                else:
                    vy = p.op("gpsimd", lambda nc, memb=memb, Yb=Yb:
                              nc.gpsimd.tensor_tensor(Yb, Yb, memb, ALU.add),
                              inc=c.sg)
                p.wait("vector", c.sg, vy)
            skc = c.SMF[:, 1248 + ct:1249 + ct]
            p.op("vector", lambda nc, hb=hb, Yb=Yb, yb=yb, skc=skc:
                 nc.vector.scalar_tensor_tensor(yb, hb, skc, Yb, ALU.mult, ALU.add))
            sgbuf = kb(c, 4096, [128, TT])
            vsg = p.dma("sync", sgbuf,
                        bass.AP(c.scr.bitcast(BF16),
                                SCR_SSM_G + tt * (128 * 8192) + ct * 512,
                                [[8192, 128], [1, 512]]), c.dl)
            p.wait("vector", c.dl, vsg)
            vyg = p.op("vector", lambda nc, yb=yb, sgbuf=sgbuf, ct=ct:
                       nc.vector.tensor_tensor(yg16[:, ct % 8, :], yb, sgbuf,
                                               ALU.mult), inc=c.sv)
            c._c_dt_use = (c.ss.n, vyg)
            if ct % 8 == 7:
                half = ct // 8
                yg_done = c.sv.n
                p.wait("tensor", c.sv, yg_done)
                for oc in range(DCH):
                    woff = (oc % 2) * 2048
                    wb_guard(c, woff)
                    wap, wv = stream_w(c, "l1b_Wout", woff, half * 8, 8,
                                       oc * 128, 128)
                    p.wait("tensor", c.dl, wv)
                    bank = oc % 2
                    c.pe_guard(bank)
                    vm = None
                    for kc in range(8):
                        vm = p.op("tensor", lambda nc, kc=kc, bank=bank, wap=wap:
                                  nc.tensor.matmul(
                                      c.PSB[bank][:, :], wap[:, kc, :],
                                      yg16[:, kc, :],
                                      start=(kc == 0), stop=(kc == 7)), inc=c.st)
                    wb_used(c, woff, vm)
                    c._c_yg_use = vm
                    p.wait("vector", c.st, vm)
                    rxa = c.RX[:, oc, tt * TT:(tt + 1) * TT]
                    vr = p.op("vector", lambda nc, bank=bank, rxa=rxa:
                              nc.vector.tensor_tensor(rxa, rxa,
                                                      c.PSB[bank][:, :],
                                                      ALU.add), inc=c.sv)
                    c.bank_consumed(bank, c.sv, vr)


SECTION_FUNCS["l1b"] = lambda c, first: emit_l1b(c)



TTF = 256
DT_FF = 0.1


def _ff_j_body(c, p, it, j, integ, vbuf, h_sh, hm12, PL, SP,
               DTH, mu_c, sw_c, osw_c, CM):
    wb_guard(c, 0)
    wb_guard(c, 1024)
    wb_guard(c, 2048)
    wsw = []
    for pi in range(3):
        w_, v_ = stream_w(c, "ff_sw2", pi * 128, 0, 1,
                          pi * 1024 + j * 128, 128, rows=64)
        wsw.append(w_)
    wea = []
    for pi in range(3):
        w_, v_ = stream_w(c, "ff_ew2", 512 + pi * 512, 0, 4,
                          pi * 1024 + j * 128, 128)
        wea.append(w_)
    web = []
    for pi in range(3):
        w_, v_ = stream_w(c, "ff_eb2", 2048 + pi * 128, 0, 1,
                          pi * 1024 + j * 128, 128, rows=8)
        web.append(w_)
    p.wait("tensor", c.dl, v_)
    integ_j = integ[:, j, :]
    v_j = vbuf[:, j, :]
    tneg = SP(2)
    msrc12 = c.SMB[0:8, 0:512]
    P512 = lambda k: fa(c, 6144 + k * TTF, [128, 2 * TTF])
    WSL, WSR = c.WS[:, 0:256], c.WS[:, 256:512]
    WS512 = c.WS[:, 0:512]
    vm_last = None
    for pi in range(3):
        if it == 0 and pi == 0:
            continue
        c.pe_guard(pi)
        vm_last = p.op("tensor", lambda nc, pi=pi: nc.tensor.matmul(
            c.PSB[pi][:, :TTF], wsw[pi][:, 0, :], h_sh[:, :],
            start=True, stop=True), inc=c.st)
    for pi in range(3):
        if it == 0 and pi == 0:
            continue
        bank = 3 + pi
        c.pe_guard(bank)
        for kc in range(4):
            p.op("tensor", lambda nc, pi=pi, kc=kc, bank=bank:
                 nc.tensor.matmul(
                     c.PSB[bank][:, 0:512], wea[pi][:, kc, :], hm12[:, kc, :],
                     start=(kc == 0), stop=False))
        vm_last = p.op("tensor", lambda nc, pi=pi, bank=bank:
                       nc.tensor.matmul(
                           c.PSB[bank][:, 0:512], web[pi][:, 0, :], msrc12,
                           start=False, stop=True), inc=c.st)
    wb_used(c, 0, vm_last)
    wb_used(c, 1024, vm_last)
    wb_used(c, 2048, vm_last)
    if c._ff_body_done is not None:
        p.wait("scalar", c.sv, c._ff_body_done)
    p.wait("scalar", c.st, vm_last)

    def sb2b(pi, j=j):
        return c.SMF[:, CM + 64 + 8 * pi + j:CM + 65 + 8 * pi + j]
    if it != 0:
        va = p.op("scalar", lambda nc: nc.scalar.activation(
            PL(0), c.PSB[0][:, :TTF], AF.Sigmoid, bias=sb2b(0)), inc=c.ss)
        c.bank_consumed(0, c.ss, va)
        va12 = p.op("scalar", lambda nc: nc.scalar.activation(
            P512(3), c.PSB[3][:, 0:512], AF.Sigmoid), inc=c.ss)
        c.bank_consumed(3, c.ss, va12)
    vg = p.op("scalar", lambda nc: nc.scalar.activation(
        PL(2), c.PSB[2][:, :TTF], AF.Sigmoid, bias=sb2b(2)), inc=c.ss)
    c.bank_consumed(2, c.ss, vg)
    vg12 = p.op("scalar", lambda nc: nc.scalar.activation(
        P512(5), c.PSB[5][:, 0:512], AF.Sigmoid), inc=c.ss)
    c.bank_consumed(5, c.ss, vg12)
    p.op("scalar", lambda nc: nc.scalar.activation(
        PL(1), c.PSB[1][:, :TTF], AF.Exp, bias=sb2b(1)))
    vbsh = p.op("scalar", lambda nc: nc.scalar.activation(
        PL(1), PL(1), AF.Ln, bias=1.0), inc=c.ss)
    c.bank_consumed(1, c.ss, vbsh)
    p.op("scalar", lambda nc: nc.scalar.activation(
        WS512, c.PSB[4][:, 0:512], AF.Exp))
    vb12 = p.op("scalar", lambda nc: nc.scalar.activation(
        WS512, WS512, AF.Ln, bias=1.0), inc=c.ss)
    c.bank_consumed(4, c.ss, vb12)
    V = lambda fn, inc=None: p.op("vector", fn, inc=inc or c.sv)
    V(lambda nc: nc.vector.tensor_scalar(
        tneg, integ_j, -1.0, mu_c(j), ALU.mult, ALU.add))
    p.wait("vector", c.ss, vb12)
    if it == 0:
        V(lambda nc: nc.vector.tensor_tensor(PL(7), PL(1), tneg, ALU.mult))
        V(lambda nc: nc.vector.tensor_tensor(PL(3), WSL, tneg, ALU.mult))
        V(lambda nc: nc.vector.tensor_tensor(PL(4), WSR, tneg, ALU.mult))
    else:
        V(lambda nc: nc.vector.tensor_tensor(PL(7), PL(0), v_j, ALU.mult))
        V(lambda nc: nc.vector.tensor_tensor(PL(1), PL(1), tneg, ALU.mult))
        V(lambda nc: nc.vector.tensor_tensor(PL(7), PL(7), PL(1), ALU.add))
        V(lambda nc: nc.vector.tensor_tensor(PL(3), PL(3), v_j, ALU.mult))
        V(lambda nc: nc.vector.tensor_tensor(PL(4), PL(4), v_j, ALU.mult))
        V(lambda nc: nc.vector.tensor_tensor(PL(0), WSL, tneg, ALU.mult))
        V(lambda nc: nc.vector.tensor_tensor(PL(1), WSR, tneg, ALU.mult))
        V(lambda nc: nc.vector.tensor_tensor(PL(3), PL(3), PL(0), ALU.add))
        V(lambda nc: nc.vector.tensor_tensor(PL(4), PL(4), PL(1), ALU.add))
    w12 = c.SMF[:, 0:512]
    V(lambda nc: nc.vector.tensor_tensor(PL(2), PL(7), PL(2), ALU.mult))
    V(lambda nc: nc.vector.tensor_tensor(P512(3), P512(3), w12, ALU.mult))
    V(lambda nc: nc.vector.tensor_tensor(P512(5), P512(5), P512(3), ALU.mult))
    V(lambda nc: nc.vector.tensor_tensor(PL(3), PL(3), PL(4), ALU.add))
    V(lambda nc: nc.vector.tensor_scalar(PL(7), PL(7), sw_c, None, ALU.mult))
    V(lambda nc: nc.vector.scalar_tensor_tensor(
        PL(7), PL(3), osw_c, PL(7), ALU.mult, ALU.add))
    V(lambda nc: nc.vector.tensor_tensor(PL(7), PL(7), v_j, ALU.subtract))
    V(lambda nc: nc.vector.tensor_tensor(PL(7), PL(7), SP(3), ALU.mult))
    V(lambda nc: nc.vector.tensor_tensor(v_j, v_j, PL(7), ALU.add))
    V(lambda nc: nc.vector.tensor_tensor(PL(5), PL(5), PL(6), ALU.add))
    V(lambda nc: nc.vector.tensor_scalar(PL(5), PL(5), osw_c, None, ALU.mult))
    V(lambda nc: nc.vector.scalar_tensor_tensor(
        PL(5), PL(2), sw_c, PL(5), ALU.mult, ALU.add))
    V(lambda nc: nc.vector.tensor_tensor(PL(5), PL(5), DTH, ALU.mult))
    vdone = V(lambda nc: nc.vector.tensor_tensor(
        integ_j, integ_j, PL(5), ALU.add))
    c._ff_body_done = vdone


def emit_ff(c):
    p, dr = c.p, c.dr
    barrier(c)
    c._ff_body_done = None
    CM = 1024
    for name, off, n in (("ff_mu", CM, 8), ("ff_iw", CM + 8, 8),
                         ("ff_rb1", CM + 24, 2), ("ff_hb1", CM + 26, 2),
                         ("ff_eb1", CM + 28, 4), ("ff_qb1", CM + 32, 16),
                         ("ff_qb2", CM + 48, 8), ("ff_rb2", CM + 56, 8),
                         ("ff_sb2", CM + 64, 24)):
        vl = p.dma("sync", c.SMF[:, off:off + n], dr[name][:, 0:n], c.dl)
    vl = p.dma("sync", c.SMF[0:64, CM + 88:CM + 89], dr["ff_sb1"][:, 0:1], c.dl)
    vl = p.dma("sync", c.SMF[0:1, CM + 89:CM + 90], dr["ff_swt"][:, 0:1], c.dl)
    vl = p.dma("sync", c.SMF[0:1, CM + 90:CM + 91], dr["ff_hb2"][:, 0:1], c.dl)
    p.wait("scalar", c.dl, vl)
    p.wait("vector", c.dl, vl)
    mu_c = lambda j: c.SMF[:, CM + j:CM + 1 + j]
    iw_c = lambda j: c.SMF[:, CM + 8 + j:CM + 9 + j]
    vsmall(c, lambda nc: nc.vector.tensor_tensor(
        c.SMF[:, CM + 16:CM + 24], c.SMF[:, CM + 8:CM + 16],
        c.SMF[:, CM + 48:CM + 56], ALU.mult))
    vsw = p.op("scalar", lambda nc: nc.scalar.activation(
        c.SMF[0:1, CM + 89:CM + 90], c.SMF[0:1, CM + 89:CM + 90], AF.Sigmoid),
        inc=c.ss)
    p.wait("vector", c.ss, vsw)
    vsmall(c, lambda nc: nc.vector.tensor_copy(
        c.SMB[0:1, 0:1], c.SMF[0:1, CM + 89:CM + 90]))
    p.wait("tensor", c.sv, c.sv.n)
    c.pe_guard(5)
    vbb = p.op("tensor", lambda nc: nc.tensor.matmul(
        c.PSB[5][:, 0:1], c.ONEROW16[:], c.SMB[0:1, 0:1], start=True, stop=True),
        inc=c.st)
    p.wait("vector", c.st, vbb)
    vsmall(c, lambda nc: nc.vector.tensor_copy(
        c.SMF[:, CM + 91:CM + 92], c.PSB[5][:, 0:1]))
    vsmall(c, lambda nc: nc.vector.tensor_scalar(
        c.SMF[:, CM + 92:CM + 93], c.SMF[:, CM + 91:CM + 92], -1.0, 1.0,
        ALU.mult, ALU.add))
    c.bank_consumed(5, c.sv, c.sv.n)
    sw_c = c.SMF[:, CM + 91:CM + 92]
    osw_c = c.SMF[:, CM + 92:CM + 93]
    DTH = c.SMF[:, 1216:1472]
    integ = fa(c, 0, [128, DCH, TTF])
    vbuf = fa(c, 2048, [128, DCH, TTF])
    xf = fa(c, 4096, [128, DCH, TTF])
    PL = lambda k: fa(c, 6144 + k * TTF, [128, TTF])
    SP = lambda k: c.SMF[:, k * TTF:(k + 1) * TTF]
    xf16 = kb(c, 0, [128, DCH, TTF])
    ctx16 = kb(c, 2048, [128, 16, TTF])
    h_all = kb(c, 6144, [128, 4, TTF])
    hm12 = kb(c, 7168, [128, 4, 2 * TTF])
    h_sh = kb(c, 9216, [64, TTF])
    h_ha = kb(c, 9472, [128, 2, TTF])
    m1b = c.KB[:, 9984:9992]
    m2b = c.KB[:, 9992:10000]
    g16 = kb(c, 4096, [128, 16, TTF])
    pt3 = c.PT[:].rearrange("p (a b) -> p a b", b=128)
    NT_F = S // TTF
    for tt in range(NT_F):
        p.wait("vector", c.sg, c.sg.n)
        ln_done = emit_ln_ff(c, tt, xf16, tw=TTF, vout=xf)
        p.wait("vector", c.ss, ln_done)
        p.wait("tensor", c.ss, ln_done)
        p.wait("gpsimd", c.ss, ln_done)
        for j in range(DCH):
            p.op("vector", lambda nc, j=j: nc.vector.tensor_copy(
                integ[:, j, :], xf[:, j, :]))
        p.op("vector", lambda nc: nc.vector.memset(c.FA[:, 2048:4096], 0.0),
             inc=c.sv)
        init_done = c.sv.n
        p.wait("tensor", c.sv, init_done)
        for oc in range(2):
            wb_guard(c, 0)
            wap, wv = stream_w(c, "ff_rw1", 0, 0, DCH, oc * 128, 128)
            p.wait("tensor", c.dl, wv)
            c.pe_guard(oc)
            vm = None
            for kc in range(DCH):
                vm = p.op("tensor", lambda nc, kc=kc, oc=oc, wap=wap:
                          nc.tensor.matmul(
                              c.PSB[oc][:, :TTF], wap[:, kc, :], xf16[:, kc, :],
                              start=(kc == 0), stop=(kc == DCH - 1)), inc=c.st)
            wb_used(c, 0, vm)
            p.wait("scalar", c.st, vm)
            vh = p.op("scalar", lambda nc, oc=oc: nc.scalar.activation(
                h_all[:, oc, :], c.PSB[oc][:, :TTF], AF.Gelu,
                bias=c.SMF[:, CM + 24 + oc:CM + 25 + oc]), inc=c.ss)
            c.bank_consumed(oc, c.ss, vh)
        wb_guard(c, 1024)
        wap2, wv2 = stream_w(c, "ff_rw2", 1024, 0, 2, 0, 8)
        p.wait("tensor", c.dl, wv2)
        p.wait("tensor", c.ss, c.ss.n)
        for sub in range(2):
            c.pe_guard(2)
            vm = None
            for rc in range(2):
                vm = p.op("tensor", lambda nc, rc=rc, sub=sub, wap2=wap2:
                          nc.tensor.matmul(
                              c.PSB[2][:, 0:8], h_all[:, rc, sub * 128:(sub + 1) * 128],
                              wap2[:, rc, :], start=(rc == 0), stop=(rc == 1)),
                          inc=c.st)
            wb_used(c, 1024, vm)
            p.wait("vector", c.st, vm)
            lg = PL(0)[:, sub * 8:sub * 8 + 8]
            vlg = p.op("vector", lambda nc, lg=lg: nc.vector.tensor_tensor(
                lg, c.PSB[2][:, 0:8], c.SMF[:, CM + 56:CM + 64], ALU.add),
                inc=c.sv)
            c.bank_consumed(2, c.sv, vlg)
            p.wait("vector", c.sv, vlg)
            L8 = PL(1)
            for e in range(8):
                vsmall(c, lambda nc, e=e, lg=lg, L8=L8: nc.vector.tensor_copy(
                    L8[:, e * 8:(e + 1) * 8], lg))
            l3a = fa(c, 6144 + TTF, [128, 8, 8])
            l3b = bass.AP(c.FA, 6144 + TTF, [[8192, 128], [1, 8], [8, 8]])
            gt = fa(c, 6144 + 2 * TTF, [128, 8, 8])
            eq = fa(c, 6144 + 3 * TTF, [128, 8, 8])
            vsmall(c, lambda nc, l3a=l3a, l3b=l3b, gt=gt: nc.vector.tensor_tensor(
                gt, l3a, l3b, ALU.is_gt))
            vsmall(c, lambda nc, l3a=l3a, l3b=l3b, eq=eq: nc.vector.tensor_tensor(
                eq, l3a, l3b, ALU.is_equal))
            vsmall(c, lambda nc, eq=eq: nc.vector.tensor_tensor(
                eq.rearrange("p a b -> p (a b)") if False else
                fa(c, 6144 + 3 * TTF, [128, 64]),
                fa(c, 6144 + 3 * TTF, [128, 64]), c.TRI8[:, :], ALU.mult))
            vsmall(c, lambda nc, gt=gt, eq=eq: nc.vector.tensor_tensor(
                fa(c, 6144 + 2 * TTF, [128, 64]),
                fa(c, 6144 + 2 * TTF, [128, 64]),
                fa(c, 6144 + 3 * TTF, [128, 64]), ALU.add))
            rank = PL(4)
            vsmall(c, lambda nc, gt=gt, rank=rank: nc.vector.tensor_reduce(
                rank[:, 0:8], gt, axis=AX.X, op=ALU.add))
            vsmall(c, lambda nc, rank=rank: nc.vector.tensor_scalar(
                m1b[:, :], rank[:, 0:8], 0.0, None, ALU.is_equal))
            vsmall(c, lambda nc, rank=rank: nc.vector.tensor_scalar(
                m2b[:, :], rank[:, 0:8], 1.0, None, ALU.is_equal))
            m1f = PL(5)
            vsmall(c, lambda nc, m1f=m1f, rank=rank: nc.vector.tensor_scalar(
                m1f[:, 0:8], rank[:, 0:8], 0.0, None, ALU.is_equal))
            vsmall(c, lambda nc, m1f=m1f, rank=rank: nc.vector.tensor_scalar(
                m1f[:, 8:16], rank[:, 0:8], 1.0, None, ALU.is_equal))
            vsmall(c, lambda nc, m1f=m1f, lg=lg: nc.vector.tensor_tensor(
                m1f[:, 0:8], m1f[:, 0:8], lg, ALU.mult))
            vsmall(c, lambda nc, m1f=m1f, lg=lg: nc.vector.tensor_tensor(
                m1f[:, 8:16], m1f[:, 8:16], lg, ALU.mult))
            vsmall(c, lambda nc, m1f=m1f: nc.vector.tensor_reduce(
                m1f[:, 16:17], fa(c, 6144 + 5 * TTF, [128, 1, 8]),
                axis=AX.X, op=ALU.add))
            vsmall(c, lambda nc, m1f=m1f: nc.vector.tensor_reduce(
                m1f[:, 17:18], bass.AP(c.FA, 6144 + 5 * TTF + 8,
                                       [[8192, 128], [1, 1], [1, 8]]),
                axis=AX.X, op=ALU.add))
            vsmall(c, lambda nc, m1f=m1f: nc.vector.tensor_tensor(
                m1f[:, 18:19], m1f[:, 16:17], m1f[:, 17:18], ALU.subtract))
            vw1 = c.sv.n
            p.wait("scalar", c.sv, vw1)
            vsg = p.op("scalar", lambda nc, m1f=m1f: nc.scalar.activation(
                m1f[:, 19:20], m1f[:, 18:19], AF.Sigmoid), inc=c.ss)
            p.wait("vector", c.ss, vsg)
            vsmall(c, lambda nc, m1f=m1f, sub=sub: nc.vector.tensor_copy(
                c.SMB[:, 1024 + sub:1025 + sub], m1f[:, 19:20]))
            vsmall(c, lambda nc, m1f=m1f, sub=sub: nc.vector.tensor_scalar(
                c.SMB[:, 1026 + sub:1027 + sub], m1f[:, 19:20], -1.0, 1.0,
                ALU.mult, ALU.add))
            vsmall(c, lambda nc, sub=sub: nc.vector.tensor_copy(
                c.SMB[:, 1028 + sub * 16:1036 + sub * 16], m1b[:, :]))
            vsmall(c, lambda nc, sub=sub: nc.vector.tensor_copy(
                c.SMB[:, 1036 + sub * 16:1044 + sub * 16], m2b[:, :]))
        p.wait("tensor", c.sv, c.sv.n)
        c.pe_guard(6)
        vt = None
        for sub in range(2):
            vt = p.op("tensor", lambda nc, sub=sub: nc.tensor.transpose(
                pt3[0:8, sub, :], c.SMB[:, 1028 + sub * 16:1036 + sub * 16],
                c.IDT[:]))
            vt = p.op("tensor", lambda nc, sub=sub: nc.tensor.transpose(
                pt3[0:8, 2 + sub, :], c.SMB[:, 1036 + sub * 16:1044 + sub * 16],
                c.IDT[:]))
            vt = p.op("tensor", lambda nc, sub=sub: nc.tensor.transpose(
                pt3[0:1, 4 + sub, :], c.SMB[:, 1024 + sub:1025 + sub],
                c.IDT[:]))
            vt = p.op("tensor", lambda nc, sub=sub: nc.tensor.transpose(
                pt3[0:1, 6 + sub, :], c.SMB[:, 1026 + sub:1027 + sub],
                c.IDT[:]), inc=c.st if sub == 1 else None)
        p.wait("vector", c.st, vt)
        p.op("vector", lambda nc: nc.vector.tensor_copy(
            c.SMB[0:8, 0:256], pt3[0:8, 0:2, :]))
        p.op("vector", lambda nc: nc.vector.tensor_copy(
            c.SMB[0:8, 256:512], pt3[0:8, 2:4, :]))
        p.op("vector", lambda nc: nc.vector.tensor_copy(
            c.SMB[0:1, 512:768], pt3[0:1, 4:6, :]))
        vev = p.op("vector", lambda nc: nc.vector.tensor_copy(
            c.SMB[0:1, 768:1024], pt3[0:1, 6:8, :]), inc=c.sv)
        c.bank_consumed(6, c.sv, vev)
        p.wait("tensor", c.sv, vev)
        c.pe_guard(3)
        c.pe_guard(4)
        p.op("tensor", lambda nc: nc.tensor.matmul(
            c.PSB[3][:, 0:TTF], c.ONEROW16[:], c.SMB[0:1, 512:768],
            start=True, stop=True))
        vwr = p.op("tensor", lambda nc: nc.tensor.matmul(
            c.PSB[4][:, 0:TTF], c.ONEROW16[:], c.SMB[0:1, 768:1024],
            start=True, stop=True), inc=c.st)
        p.wait("scalar", c.st, vwr)
        p.op("scalar", lambda nc: nc.scalar.copy(SP(0), c.PSB[3][:, 0:TTF]))
        vwr2 = p.op("scalar", lambda nc: nc.scalar.copy(SP(1), c.PSB[4][:, 0:TTF]),
                    inc=c.ss)
        c.bank_consumed(3, c.ss, vwr2)
        c.bank_consumed(4, c.ss, vwr2)
        w1rep, w2rep = SP(0), SP(1)
        for it in range(2):
            nkc = DCH if it == 0 else 16
            if it == 1:
                p.wait("scalar", c.sv, c.sv.n)
                p.wait("scalar", c.sg, c.sg.n)
                for j in range(DCH):
                    p.op("scalar", lambda nc, j=j: nc.scalar.copy(
                        ctx16[:, j, :], integ[:, j, :]))
                for j in range(DCH):
                    p.op("scalar", lambda nc, j=j: nc.scalar.copy(
                        ctx16[:, 8 + j, :], vbuf[:, j, :]), inc=c.ss)
                p.wait("tensor", c.ss, c.ss.n)
            rhs_t = (lambda kc: xf16[:, kc, :]) if it == 0 else \
                (lambda kc: ctx16[:, kc, :])
            for oc in range(4):
                wb_guard(c, 0)
                wap, wv = stream_w(c, "ff_ew1", 0, 0, nkc, oc * 128, 128)
                p.wait("tensor", c.dl, wv)
                bank = oc % 2
                c.pe_guard(bank)
                vm = None
                for kc in range(nkc):
                    vm = p.op("tensor", lambda nc, kc=kc, bank=bank, wap=wap,
                              rhs_t=rhs_t, nkc=nkc: nc.tensor.matmul(
                                  c.PSB[bank][:, :TTF], wap[:, kc, :], rhs_t(kc),
                                  start=(kc == 0), stop=(kc == nkc - 1)), inc=c.st)
                wb_used(c, 0, vm)
                p.wait("scalar", c.st, vm)
                vh = p.op("scalar", lambda nc, oc=oc, bank=bank: nc.scalar.activation(
                    h_all[:, oc, :], c.PSB[bank][:, :TTF], AF.Gelu,
                    bias=c.SMF[:, CM + 28 + oc:CM + 29 + oc]), inc=c.ss)
                c.bank_consumed(bank, c.ss, vh)
            wb_guard(c, 1024)
            wap, wv = stream_w(c, "ff_sw1", 1024, 0, nkc, 0, 64)
            p.wait("tensor", c.dl, wv)
            c.pe_guard(2)
            vm = None
            for kc in range(nkc):
                vm = p.op("tensor", lambda nc, kc=kc, wap=wap, rhs_t=rhs_t,
                          nkc=nkc: nc.tensor.matmul(
                    c.PSB[2][0:64, :TTF], wap[:, kc, :], rhs_t(kc),
                    start=(kc == 0), stop=(kc == nkc - 1)), inc=c.st)
            wb_used(c, 1024, vm)
            p.wait("scalar", c.st, vm)
            vh = p.op("scalar", lambda nc: nc.scalar.activation(
                h_sh[:, :], c.PSB[2][0:64, :TTF], AF.Gelu,
                bias=c.SMF[0:64, CM + 88:CM + 89]), inc=c.ss)
            c.bank_consumed(2, c.ss, vh)
            hrhs = (lambda kc: xf16[:, kc, :]) if it == 0 else \
                (lambda kc: ctx16[:, kc, :])
            for oc in range(2):
                wb_guard(c, 2048)
                wap, wv = stream_w(c, "ff_hw1", 2048, 0, DCH, oc * 128, 128)
                p.wait("tensor", c.dl, wv)
                c.pe_guard(3)
                vm = None
                for kc in range(DCH):
                    vm = p.op("tensor", lambda nc, kc=kc, wap=wap, hrhs=hrhs:
                              nc.tensor.matmul(
                        c.PSB[3][:, :TTF], wap[:, kc, :], hrhs(kc),
                        start=(kc == 0), stop=(kc == DCH - 1)), inc=c.st)
                wb_used(c, 2048, vm)
                p.wait("scalar", c.st, vm)
                vh = p.op("scalar", lambda nc, oc=oc: nc.scalar.activation(
                    h_ha[:, oc, :], c.PSB[3][:, :TTF], AF.Gelu,
                    bias=c.SMF[:, CM + 26 + oc:CM + 27 + oc]), inc=c.ss)
                c.bank_consumed(3, c.ss, vh)
            wb_guard(c, 2048)
            wap, wv = stream_w(c, "ff_hw2", 2048, 0, 2, 0, 1)
            p.wait("tensor", c.dl, wv)
            p.wait("tensor", c.ss, c.ss.n)
            c.pe_guard(3)
            vm = None
            for rc in range(2):
                vm = p.op("tensor", lambda nc, rc=rc, wap=wap: nc.tensor.matmul(
                    c.PR[0:1, 0:TTF], wap[:, rc, :], h_ha[:, rc, :],
                    start=(rc == 0), stop=(rc == 1)), inc=c.st)
            wb_used(c, 2048, vm)
            p.wait("scalar", c.st, vm)
            vhr = p.op("scalar", lambda nc: nc.scalar.activation(
                c.SMB[0:1, 1044:1044 + TTF], c.PR[0:1, 0:TTF], AF.Sigmoid,
                bias=c.SMF[0:1, CM + 90:CM + 91]), inc=c.ss)
            c.bank_consumed("PR", c.ss, vhr)
            p.wait("tensor", c.ss, vhr)
            c.pe_guard(3)
            vbr = p.op("tensor", lambda nc: nc.tensor.matmul(
                c.PSB[3][:, 0:TTF], c.ONEROW16[:], c.SMB[0:1, 1044:1044 + TTF],
                start=True, stop=True), inc=c.st)
            p.wait("scalar", c.st, vbr)
            p.op("scalar", lambda nc: nc.scalar.copy(
                SP(3), c.PSB[3][:, 0:TTF]))
            vhrep = p.op("scalar", lambda nc: nc.scalar.activation(
                DTH, c.PSB[3][:, 0:TTF], AF.Copy, scale=DT_FF), inc=c.ss)
            c.bank_consumed(3, c.ss, vhrep)
            if not hasattr(c, "_ff_exp8"):
                vl8 = p.dma("sync", c.WS[0:8, 0:512], dr["c_exp8"][:], c.dl)
                p.wait("scalar", c.dl, vl8)
                vl8b = p.op("scalar", lambda nc: nc.scalar.copy(
                    c.SMB[0:8, 1300:1812], c.WS[0:8, 0:512]), inc=c.ss)
                p.wait("sync", c.ss, vl8b)
                p.wait("tensor", c.ss, vl8b)
                c._ff_exp8 = True
            p.wait("tensor", c.ss, c.ss.n)
            msk12 = c.SMB[0:8, 0:512]
            for hc in range(4):
                bank = 4 + (hc % 2)
                c.pe_guard(bank)
                vmk = p.op("tensor", lambda nc, hc=hc, bank=bank:
                           nc.tensor.matmul(
                               c.PSB[bank][:, 0:512],
                               c.SMB[0:8, 1300 + hc * 128:1300 + (hc + 1) * 128],
                               msk12, start=True, stop=True), inc=c.st)
                p.wait("vector", c.st, vmk)
                p.op("vector", lambda nc, hc=hc, bank=bank:
                     nc.vector.tensor_tensor(
                         hm12[:, hc, 0:TTF], h_all[:, hc, :],
                         c.PSB[bank][:, 0:TTF], ALU.mult))
                vhm = p.op("vector", lambda nc, hc=hc, bank=bank:
                           nc.vector.tensor_tensor(
                               hm12[:, hc, TTF:2 * TTF], h_all[:, hc, :],
                               c.PSB[bank][:, TTF:2 * TTF], ALU.mult), inc=c.sv)
                c.bank_consumed(bank, c.sv, vhm)
            p.wait("tensor", c.sv, c.sv.n)
            for j in range(DCH):
                _ff_j_body(c, p, it, j, integ, vbuf, h_sh, hm12, PL, SP,
                           DTH, mu_c, sw_c, osw_c, CM)
            p.wait("scalar", c.sv, c.sv.n)
            p.wait("tensor", c.sv, c.sv.n)
        p.wait("scalar", c.sv, c.sv.n)
        for j in range(DCH):
            p.op("scalar", lambda nc, j=j: nc.scalar.copy(
                ctx16[:, j, :], integ[:, j, :]), inc=c.ss if j == DCH - 1 else None)
        p.wait("tensor", c.ss, c.ss.n)
        for oc in range(16):
            woff = (oc % 2) * 2048
            wb_guard(c, woff)
            wap, wv = stream_w(c, "ff_qw1", woff, 0, DCH, oc * 128, 128)
            p.wait("tensor", c.dl, wv)
            bank = oc % 2
            c.pe_guard(bank)
            vm = None
            for kc in range(DCH):
                vm = p.op("tensor", lambda nc, kc=kc, bank=bank, wap=wap:
                          nc.tensor.matmul(
                              c.PSB[bank][:, :TTF], wap[:, kc, :], ctx16[:, kc, :],
                              start=(kc == 0), stop=(kc == DCH - 1)), inc=c.st)
            wb_used(c, woff, vm)
            p.wait("scalar", c.st, vm)
            vh = p.op("scalar", lambda nc, oc=oc, bank=bank: nc.scalar.activation(
                g16[:, oc, :], c.PSB[bank][:, :TTF], AF.Gelu,
                bias=c.SMF[:, CM + 32 + oc:CM + 33 + oc]), inc=c.ss)
            c.bank_consumed(bank, c.ss, vh)
        p.wait("tensor", c.ss, c.ss.n)
        for oc in range(DCH):
            woff = (oc % 2) * 2048
            wb_guard(c, woff)
            wap, wv = stream_w(c, "ff_qw2", woff, 0, 16, oc * 128, 128)
            p.wait("tensor", c.dl, wv)
            bank = oc % 2
            c.pe_guard(bank)
            vm = None
            for kc in range(16):
                vm = p.op("tensor", lambda nc, kc=kc, bank=bank, wap=wap:
                          nc.tensor.matmul(
                              c.PSB[bank][:, :TTF], wap[:, kc, :], g16[:, kc, :],
                              start=(kc == 0), stop=(kc == 15)), inc=c.st)
            wb_used(c, woff, vm)
            p.wait("vector", c.st, vm)
            rxa = c.RX[:, oc, tt * TTF:(tt + 1) * TTF]
            p.op("vector", lambda nc, oc=oc, bank=bank: nc.vector.scalar_tensor_tensor(
                PL(0), c.PSB[bank][:, :TTF], iw_c(oc), xf[:, oc, :],
                ALU.mult, ALU.add), inc=c.sv)
            c.bank_consumed(bank, c.sv, c.sv.n)
            vr = p.op("vector", lambda nc, oc=oc, rxa=rxa:
                      nc.vector.scalar_tensor_tensor(
                          rxa, PL(0), c.SMF[:, CM + 16 + oc:CM + 17 + oc],
                          rxa, ALU.bypass, ALU.add) if False else
                      nc.vector.scalar_tensor_tensor(
                          rxa, PL(0),
                          c.SMF[:, CM + 16 + oc:CM + 17 + oc],
                          rxa, ALU.add, ALU.add), inc=c.sv)


SECTION_FUNCS["ff"] = lambda c, first: emit_ff(c)
SECTION_FUNCS["ff2"] = lambda c, first: emit_ff(c)


def kernel(**inputs):
    consts = host_constants()
    in_maps = [prep_core_inputs(cc, inputs, consts) for cc in range(NCORES)]
    nc = build_full()
    res = run_bass_kernel_spmd(nc, in_maps, core_ids=list(range(NCORES)))
    outs = [np.ascontiguousarray(np.asarray(r["out_T"]).T) for r in res.results]
    return np.stack(outs, axis=0).astype(np.float32)


# version 04
# speedup vs baseline: 1.1956x; 1.0352x over previous
import math
import numpy as np
import concourse.bass as bass
import concourse.mybir as mybir
from concourse.bass_utils import run_bass_kernel_spmd

F32 = mybir.dt.float32
BF16 = mybir.dt.bfloat16
I32 = mybir.dt.int32
AF = mybir.ActivationFunctionType
ALU = mybir.AluOpType
AX = mybir.AxisListType

D = 1024
S = 4096
NCORES = 8
DCH = 8
TT = 512
NTT = S // TT

ENGINES = ("sync", "vector", "scalar", "tensor", "gpsimd")


class Sem:
    __slots__ = ("h", "n", "name")

    def __init__(self, handle, name):
        self.h = handle
        self.n = 0
        self.name = name


class Prog:
    def __init__(self, nc):
        self.nc = nc
        self.q = {e: [] for e in ENGINES}
        self.sems = {}

    def sem(self, name):
        if name not in self.sems:
            h = self.nc.semaphore(name).__enter__()
            self.sems[name] = Sem(h, name)
        return self.sems[name]

    def op(self, eng, fn, inc=None, inc_by=1):
        if inc is not None:
            inc.n += inc_by
            def g(fn=fn, h=inc.h, d=inc_by):
                fn(self.nc).then_inc(h, d)
            self.q[eng].append(g)
            return inc.n
        self.q[eng].append(lambda fn=fn: fn(self.nc))
        return None

    def wait(self, eng, sem, val):
        self.q[eng].append(
            lambda h=sem.h, v=val, e=eng: getattr(self.nc, e).wait_ge(h, v))

    def dma(self, eng, out, in_, inc, slow=False):
        return self.op(
            eng, lambda nc, o=out, i=in_, e=eng, s=slow: getattr(nc, e).dma_start(
                out=o, in_=i, allow_slow_non_contiguous=s),
            inc=inc, inc_by=16)

    def run_block(self):
        with self.nc.Block() as block:
            for e in ENGINES:
                ops = self.q[e]
                if not ops:
                    continue
                def make(eng_ops):
                    def _(engine):
                        for f in eng_ops:
                            f()
                    return _
                getattr(block, e)(make(ops))


class Ctx:
    def __init__(self, nc, stack, dr):
        self.nc = nc
        self.p = Prog(nc)
        self.dr = dr
        e = stack.enter_context
        self.RX = e(nc.sbuf_tensor("RX", [128, DCH, S], F32))
        self.FA = e(nc.sbuf_tensor("FA", [128, 8192], F32))
        self.FAi = self.FA.bitcast(I32)
        self.KB = e(nc.sbuf_tensor("KB", [128, 10240], BF16))
        self.WB = e(nc.sbuf_tensor("WB", [128, 4096], BF16))
        self.WS = e(nc.sbuf_tensor("WS", [128, 1024], F32))
        self.SMF = e(nc.sbuf_tensor("SMF", [128, 2048], F32))
        self.SMB = e(nc.sbuf_tensor("SMB", [128, 2048], BF16))
        self.IDT = e(nc.sbuf_tensor("IDT", [128, 128], BF16))
        self.ONEROW = e(nc.sbuf_tensor("ONEROW", [1, 128], F32))
        self.ONEROW16 = e(nc.sbuf_tensor("ONEROW16", [1, 128], BF16))
        self.ONECOL16 = e(nc.sbuf_tensor("ONECOL16", [128, 1], BF16))
        self.CAUS = e(nc.sbuf_tensor("CAUS", [128, 896], BF16))
        self.OMG = e(nc.sbuf_tensor("OMG", [128, 256], BF16))
        self.TRI8 = e(nc.sbuf_tensor("TRI8", [128, 64], F32))
        self.PSB = [e(nc.psum_tensor(f"PS{i}", [128, 512], F32)) for i in range(6)]
        self.PT = e(nc.psum_tensor("PT", [128, 1024], BF16))
        self.PR = e(nc.psum_tensor("PR", [128, 512], F32))
        self.ONES128 = e(nc.sbuf_tensor("ONES128", [128, 128], BF16))
        p = self.p
        self.sv = p.sem("sv")
        self.ss = p.sem("ss")
        self.st = p.sem("st_")
        self.sg = p.sem("sg")
        self.dl = p.sem("dl")
        self.ds = p.sem("ds")
        self.dg = p.sem("dg")
        self.so = p.sem("so")
        self.scr = nc.dram_tensor("scr", [63 * 1024 * 1024], F32, kind="Internal")
        self.scrB = nc.dram_tensor("scrB", [12 * 1024 * 1024], F32, kind="Internal")
        self.bank_free = {}

    def pe_guard(self, bank):
        if bank in self.bank_free:
            sem, val = self.bank_free.pop(bank)
            self.p.wait("tensor", sem, val)

    def bank_consumed(self, bank, sem, val):
        self.bank_free[bank] = (sem, val)

    def scr_ap(self, off_bytes, shape, dtype=F32):
        t = self.scr
        if dtype != F32:
            t = t.bitcast(dtype)
        elems = 1
        for s in shape:
            elems *= s
        esz = 4 if dtype == F32 else 2
        assert off_bytes % 4 == 0
        flat = t.ap()
        start = off_bytes // esz
        ap = bass.AP(flat.tensor, start, [[shape[1], shape[0]], [1, shape[1]]])
        return ap


def vsmall(c, fn):
    v = c.p.op("vector", fn, inc=c.sv)
    c.p.wait("vector", c.sv, v)
    return v


def ssmall(c, fn):
    v = c.p.op("scalar", fn, inc=c.ss)
    c.p.wait("scalar", c.ss, v)
    return v


def kb(c, start, shape):
    n = 1
    for s in shape[1:]:
        n *= s
    ap = c.KB[0:shape[0], start:start + n]
    if len(shape) == 3:
        ap = ap.rearrange("p (a b) -> p a b", b=shape[2])
    return ap


def fa(c, start, shape, dtype=F32):
    n = 1
    for s in shape[1:]:
        n *= s
    base = c.FA if dtype == F32 else c.FAi
    ap = base[0:shape[0], start:start + n]
    if len(shape) == 3:
        ap = ap.rearrange("p (a b) -> p a b", b=shape[2])
    return ap



def _t(a):
    return np.ascontiguousarray(np.asarray(a, np.float32).T)


def _c(a):
    return np.ascontiguousarray(np.asarray(a, np.float32))


def host_constants():
    c = {}
    c["c_ident"] = np.eye(128, dtype=np.float32)
    c["c_onerow"] = np.ones((1, 128), np.float32)
    c["c_onecol"] = np.ones((128, 1), np.float32)
    ki = np.arange(128)[:, None]
    u = np.arange(896)[None, :]
    c["c_causal"] = ((u - 384) >= ki).astype(np.float32)
    tri = (np.arange(8)[:, None] > np.arange(8)[None, :]).astype(np.float32)
    c["c_tri8"] = np.tile(tri.reshape(1, 64), (128, 1)).astype(np.float32)
    sels = np.zeros((16, 2048), np.float32)
    for k in range(16):
        sels[k, k * 128:(k + 1) * 128] = 1.0
    c["c_sels"] = sels
    exp8 = np.zeros((8, 512), np.float32)
    for hc in range(4):
        for pp in range(128):
            exp8[2 * hc + pp // 64, hc * 128 + pp] = 1.0
    c["c_exp8"] = exp8
    return c


def prep_core_inputs(core, inp, consts):
    m = dict(consts)
    m["x_T"] = _t(inp["L0_attn_a_X"][core])
    m["l0a_qT"] = _t(inp["L0_attn_a_q_proj"])
    m["l0a_kT"] = _t(inp["L0_attn_a_k_proj"])
    m["l0a_vT"] = _t(inp["L0_attn_a_v_proj"])
    m["l0a_oT"] = _t(inp["L0_attn_a_o_proj"])
    for w in ("Wq", "Wk", "Wv", "Wo"):
        m[f"l0b_{w}"] = _c(inp[f"L0_attn_b_{w}"])
    for b in ("bq", "bk", "bv", "bo"):
        m[f"l0b_{b}"] = _c(inp[f"L0_attn_b_{b}"]).reshape(8, 128).T.copy()
    m["ff_rw1"] = _c(inp["L0_ffn_router_w1"])
    m["ff_rb1"] = _c(inp["L0_ffn_router_b1"]).reshape(2, 128).T.copy()
    m["ff_rw2"] = _c(inp["L0_ffn_router_w2"])
    m["ff_rb2"] = np.tile(_c(inp["L0_ffn_router_b2"]).reshape(1, 8), (128, 1))
    m["ff_hw1"] = _c(inp["L0_ffn_halt_w1"])
    m["ff_hb1"] = _c(inp["L0_ffn_halt_b1"]).reshape(2, 128).T.copy()
    m["ff_hw2"] = _c(inp["L0_ffn_halt_w2"])
    m["ff_hb2"] = np.full((1, 1), float(np.asarray(inp["L0_ffn_halt_b2"]).reshape(-1)[0]), np.float32)
    ew1 = np.asarray(inp["L0_ffn_expert_w1"], np.float32)
    m["ff_ew1"] = np.ascontiguousarray(ew1.transpose(1, 0, 2).reshape(2048, 512))
    m["ff_eb1"] = _c(inp["L0_ffn_expert_b1"]).reshape(512).reshape(4, 128).T.copy()
    m["ff_ew2"] = _c(inp["L0_ffn_expert_w2"]).reshape(512, 3072)
    m["ff_eb2"] = _c(inp["L0_ffn_expert_b2"])
    m["ff_sw1"] = _c(inp["L0_ffn_shared_w1"])
    m["ff_sb1"] = _c(inp["L0_ffn_shared_b1"]).reshape(64, 1)
    m["ff_sw2"] = _c(inp["L0_ffn_shared_w2"])
    m["ff_sb2"] = _c(inp["L0_ffn_shared_b2"]).reshape(24, 128).T.copy()
    m["ff_swt"] = np.full((1, 1), float(np.asarray(inp["L0_ffn_shared_weight"])), np.float32)
    m["ff_qw1"] = _c(inp["L0_ffn_refine_w1"])
    m["ff_qb1"] = _c(inp["L0_ffn_refine_b1"]).reshape(16, 128).T.copy()
    m["ff_qw2"] = _c(inp["L0_ffn_refine_w2"])
    m["ff_qb2"] = _c(inp["L0_ffn_refine_b2"]).reshape(8, 128).T.copy()
    m["ff_iw"] = _c(inp["L0_ffn_integration_weight"]).reshape(8, 128).T.copy()
    m["ff_mu"] = _c(inp["L0_ffn_mu"]).reshape(8, 128).T.copy()
    m["l1a_Wq"] = _c(inp["L1_attn_a_Wq"])
    m["l1a_Wk"] = _c(inp["L1_attn_a_Wk"])
    m["l1a_Wv"] = _c(inp["L1_attn_a_Wv"])
    omega = np.asarray(inp["L1_attn_a_omega"], np.float32)
    m["l1a_omg"] = np.concatenate(
        [omega.T, np.full((64, 256), -0.5, np.float32)], axis=0)
    m["l1b_Win"] = _c(inp["L1_attn_b_W_in"])
    cw = _c(inp["L1_attn_b_conv_w"])
    m["l1b_cw"] = cw.reshape(16, 128, 3).transpose(1, 0, 2).reshape(128, 48).copy()
    m["l1b_cb"] = _c(inp["L1_attn_b_conv_b"]).reshape(16, 128).T.copy()
    m["l1b_Wst"] = _c(inp["L1_attn_b_W_state"])
    m["l1b_Wt"] = _c(inp["L1_attn_b_W_time"])
    m["l1b_bt"] = _c(inp["L1_attn_b_b_time"]).reshape(16, 128).T.copy()
    alog = _c(inp["L1_attn_b_A_log"])
    m["l1b_alog"] = np.ascontiguousarray(
        alog.reshape(16, 128, 16).transpose(1, 0, 2).reshape(128, 256))
    m["l1b_skip"] = _c(inp["L1_attn_b_skip"]).reshape(16, 128).T.copy()
    m["l1b_Wout"] = _c(inp["L1_attn_b_W_out"])
    m["l1f_w"] = _c(inp["L1_ffn_weight"])
    m["l1f_b"] = _c(inp["L1_ffn_bias"]).reshape(8, 128).T.copy()
    return m


INPUT_SHAPES = {
    "x_T": (D, S),
    "c_ident": (128, 128), "c_onerow": (1, 128), "c_onecol": (128, 1),
    "c_causal": (128, 896), "c_tri8": (128, 64), "c_sels": (16, 2048),
    "c_exp8": (8, 512),
    "l0a_qT": (D, D), "l0a_kT": (D, D), "l0a_vT": (D, D), "l0a_oT": (D, D),
    "l0b_Wq": (D, D), "l0b_Wk": (D, D), "l0b_Wv": (D, D), "l0b_Wo": (D, D),
    "l0b_bq": (128, 8), "l0b_bk": (128, 8), "l0b_bv": (128, 8), "l0b_bo": (128, 8),
    "ff_rw1": (D, 256), "ff_rb1": (128, 2), "ff_rw2": (256, 8), "ff_rb2": (128, 8),
    "ff_hw1": (D, 256), "ff_hb1": (128, 2), "ff_hw2": (256, 1), "ff_hb2": (1, 1),
    "ff_ew1": (2048, 512), "ff_eb1": (128, 4), "ff_ew2": (512, 3072),
    "ff_eb2": (8, 3072),
    "ff_sw1": (2048, 64), "ff_sb1": (64, 1), "ff_sw2": (64, 3072),
    "ff_sb2": (128, 24), "ff_swt": (1, 1),
    "ff_qw1": (D, 2048), "ff_qb1": (128, 16), "ff_qw2": (2048, D),
    "ff_qb2": (128, 8), "ff_iw": (128, 8), "ff_mu": (128, 8),
    "l1a_Wq": (D, D), "l1a_Wk": (D, D), "l1a_Wv": (D, D), "l1a_omg": (128, 256),
    "l1b_Win": (D, 4096), "l1b_cw": (128, 48), "l1b_cb": (128, 16),
    "l1b_Wst": (2048, 32), "l1b_Wt": (2048, 2048), "l1b_bt": (128, 16),
    "l1b_alog": (128, 256), "l1b_skip": (128, 16), "l1b_Wout": (2048, D),
    "l1f_w": (D, D), "l1f_b": (128, 8),
}


def bld_inputs(nc, names=None):
    aps = {}
    for name, shp in INPUT_SHAPES.items():
        if names is not None and name not in names:
            continue
        aps[name] = nc.dram_tensor(name, list(shp), F32, kind="ExternalInput").ap()
    return aps



def emit_setup(c):
    p, dr = c.p, c.dr
    for j in range(DCH):
        p.dma("sync", c.RX[:, j, :], dr["x_T"][j * 128:(j + 1) * 128, :], c.dl)
    p.dma("sync", c.ONEROW[:], dr["c_onerow"][:], c.dl)
    p.dma("sync", c.TRI8[:], dr["c_tri8"][:], c.dl)
    p.dma("sync", c.WS[:, 0:128], dr["c_ident"][:], c.dl)
    p.dma("sync", c.WS[:, 128:129], dr["c_onecol"][:], c.dl)
    ld = c.dl.n
    p.wait("scalar", c.dl, ld)
    p.op("scalar", lambda nc: nc.scalar.copy(c.IDT[:], c.WS[:, 0:128]))
    p.op("scalar", lambda nc: nc.scalar.copy(c.ONEROW16[:], c.ONEROW[:]))
    p.op("vector", lambda nc: nc.vector.memset(c.ONES128[:], 1.0))
    p.op("scalar", lambda nc: nc.scalar.copy(c.ONECOL16[:], c.WS[:, 128:129]), inc=c.ss)
    cvt1 = c.ss.n
    p.wait("sync", c.ss, cvt1)
    p.dma("sync", c.WS[:, 0:896], dr["c_causal"][:], c.dl)
    ld2 = c.dl.n
    p.wait("scalar", c.dl, ld2)
    p.op("scalar", lambda nc: nc.scalar.copy(c.CAUS[:], c.WS[:, 0:896]), inc=c.ss)
    c.rx_loaded = ld
    c.const_ss = c.ss.n


def barrier(c):
    p = c.p
    sems = [c.sv, c.ss, c.st, c.sg, c.dl, c.ds, c.dg]
    for eng in ENGINES:
        own = {"vector": c.sv, "scalar": c.ss, "tensor": c.st,
               "gpsimd": c.sg}.get(eng)
        for s in sems:
            if s.n and s is not own:
                p.wait(eng, s, s.n)


def emit_store_out(c):
    p = c.p
    out = c.nc.dram_tensor("out_T", [D, S], F32, kind="ExternalOutput").ap()
    for s in (c.sv, c.ss, c.st, c.sg):
        if s.n:
            p.wait("sync", s, s.n)
    for j in range(DCH):
        p.dma("sync", out[j * 128:(j + 1) * 128, :], c.RX[:, j, :], c.so)
    p.wait("sync", c.so, c.so.n)



def load_weight_bf16(c, name, kchunks, ncols, wb_off, rows=128, dma_eng="sync",
                     col0=0, src_row0=0):
    p, dr = c.p, c.dr
    dsem = c.dl if dma_eng == "sync" else (c.ds if dma_eng == "scalar" else c.dg)
    dst = c.WB[:rows, wb_off:wb_off + kchunks * ncols].rearrange(
        "p (k n) -> p k n", n=ncols)
    for kc in range(kchunks):
        assert ncols <= 1024
        v = p.dma(dma_eng, c.WS[:rows, 0:ncols],
                  dr[name][src_row0 + kc * rows: src_row0 + (kc + 1) * rows,
                           col0:col0 + ncols], dsem)
        p.wait("scalar", dsem, v)
        p.op("scalar", lambda nc, kc=kc: nc.scalar.copy(
            dst[:, kc, :], c.WS[:rows, 0:ncols]), inc=c.ss)
        if kc + 1 < kchunks:
            p.wait(dma_eng, c.ss, c.ss.n)
    return dst, c.ss.n



def emit_ln_ff(c, tt, out16, nch=DCH, rxsrc=None, tw=TT, lo16=None, vout=None):
    p = c.p
    t0 = tt * tw
    rx = rxsrc if rxsrc is not None else (lambda j: c.RX[:, j, t0:t0 + tw])
    stg = c.SMB[:, 0:4 * tw].rearrange("p (a b) -> p a b", b=tw)
    c.pe_guard(4)
    c.pe_guard(5)
    c.pe_guard("PR")
    pe_base = None
    for j in range(nch):
        sl = (j % 2) * 2
        if j >= 2:
            p.wait("scalar", c.st, pe_base + (j - 2))
        p.op("scalar", lambda nc, j=j, sl=sl: nc.scalar.copy(stg[:, sl, :], rx(j)))
        v = p.op("scalar", lambda nc, j=j, sl=sl: nc.scalar.activation(
            stg[:, sl + 1, :], rx(j), AF.Square), inc=c.ss)
        p.wait("tensor", c.ss, v)
        p.op("tensor", lambda nc, j=j, sl=sl: nc.tensor.matmul(
            c.PR[0:1, 0:tw], c.ONECOL16[:], stg[:, sl, :],
            start=(j == 0), stop=(j == nch - 1)))
        vt = p.op("tensor", lambda nc, j=j, sl=sl: nc.tensor.matmul(
            c.PSB[4][0:1, 0:tw], c.ONECOL16[:], stg[:, sl + 1, :],
            start=(j == 0), stop=(j == nch - 1)), inc=c.st)
        if j == 0:
            pe_base = vt
    stats_done = c.st.n
    dim = nch * 128
    mu = c.SMF[0:1, 0:tw]
    var = c.SMF[0:1, 512:512 + tw]
    p.wait("vector", c.st, stats_done)
    p.op("vector", lambda nc: nc.vector.tensor_scalar(
        mu, c.PR[0:1, 0:tw], 1.0 / dim, None, ALU.mult))
    p.op("vector", lambda nc: nc.vector.tensor_tensor(var, mu, mu, ALU.mult))
    p.op("vector", lambda nc: nc.vector.scalar_tensor_tensor(
        var, c.PSB[4][0:1, 0:tw], 1.0 / dim, var, ALU.mult, ALU.subtract), inc=c.sv)
    v_var = c.sv.n
    p.op("vector", lambda nc: nc.vector.tensor_scalar(
        var, var, 1e-5, None, ALU.add), inc=c.sv)
    p.wait("scalar", c.sv, c.sv.n)
    p.op("scalar", lambda nc: nc.scalar.activation(var, var, AF.Ln))
    v_rstd = p.op("scalar", lambda nc: nc.scalar.activation(
        var, var, AF.Exp, scale=-0.5), inc=c.ss)
    p.wait("vector", c.ss, v_rstd)
    v_mur = p.op("vector", lambda nc: nc.vector.tensor_tensor(
        mu, mu, var, ALU.mult), inc=c.sv)
    rstd16 = c.SMB[0:1, 0:tw]
    mur16 = c.SMB[0:1, tw:2 * tw]
    p.wait("scalar", c.ss, v_rstd)
    p.wait("scalar", c.sv, v_mur)
    p.op("scalar", lambda nc: nc.scalar.copy(rstd16, var))
    vrows = p.op("scalar", lambda nc: nc.scalar.copy(mur16, mu), inc=c.ss)
    p.wait("tensor", c.ss, vrows)
    p.op("tensor", lambda nc: nc.tensor.matmul(
        c.PSB[4][:, 0:tw], c.ONEROW16[:], rstd16, start=True, stop=True))
    vb = p.op("tensor", lambda nc: nc.tensor.matmul(
        c.PSB[5][:, 0:tw], c.ONEROW16[:], mur16, start=True, stop=True), inc=c.st)
    c.bank_consumed("PR", c.sv, v_var)
    p.wait("vector", c.st, vb)
    if lo16 is None and vout is None:
        tmp = fa(c, 7680, [128, tw])
        for j in range(nch):
            p.op("vector", lambda nc, j=j: nc.vector.tensor_tensor(
                tmp, rx(j), c.PSB[4][:, 0:tw], ALU.mult))
            p.op("vector", lambda nc, j=j: nc.vector.tensor_tensor(
                out16[:, j, :], tmp, c.PSB[5][:, 0:tw], ALU.subtract),
                inc=c.sv if j == nch - 1 else None)
        done = c.sv.n
        c.bank_consumed(4, c.sv, done)
        c.bank_consumed(5, c.sv, done)
        return done
    vslots = vout if vout is not None else fa(c, 0, [128, nch, tw])
    for j in range(nch):
        p.op("vector", lambda nc, j=j: nc.vector.tensor_tensor(
            vslots[:, j, :], rx(j), c.PSB[4][:, 0:tw], ALU.mult))
        p.op("vector", lambda nc, j=j: nc.vector.tensor_tensor(
            vslots[:, j, :], vslots[:, j, :], c.PSB[5][:, 0:tw], ALU.subtract),
            inc=c.sv if j == nch - 1 else None)
    vdone = c.sv.n
    c.bank_consumed(4, c.sv, vdone)
    c.bank_consumed(5, c.sv, vdone)
    p.wait("scalar", c.sv, vdone)
    for j in range(nch):
        p.op("scalar", lambda nc, j=j: nc.scalar.copy(out16[:, j, :], vslots[:, j, :]),
             inc=c.ss if j == nch - 1 else None)
    hidone = c.ss.n
    if lo16 is None:
        return hidone
    p.wait("vector", c.ss, hidone)
    for j in range(nch):
        p.op("vector", lambda nc, j=j: nc.vector.tensor_tensor(
            lo16[:, j, :], vslots[:, j, :], out16[:, j, :], ALU.subtract),
            inc=c.sv if j == nch - 1 else None)
    return c.sv.n



QX, QT, QY, QNF, QAT, QQ, QSW, QRSW = 0, 1024, 2048, 3072, 4096, 5120, 6144, 7168


def scr2(c, elem_off, rows, cols, dtype=F32, row_stride=None):
    t = c.scr if dtype == F32 else c.scr.bitcast(dtype)
    if row_stride is None:
        row_stride = cols
    return bass.AP(t, elem_off, [[row_stride, rows], [1, cols]])


def _expand32(p, c, src_t, src_off, dst_slot, pong_slot, n=1024):
    nblk = n // 32
    seq = [dst_slot, pong_slot, dst_slot, pong_slot, dst_slot]
    cur_t, cur_off = src_t, src_off
    w = 1
    for i in range(5):
        tgt = seq[i]
        src3 = cur_t[:, cur_off:cur_off + nblk * w].rearrange(
            "p (b q) -> p b q", q=w)
        dst3 = c.FA[:, tgt:tgt + nblk * 2 * w].rearrange(
            "p (b q) -> p b q", q=2 * w)
        for h in range(2):
            if nblk * w < 256:
                vsmall(c, lambda nc, d=dst3, s=src3, h=h, w=w:
                       nc.vector.tensor_copy(d[:, :, h * w:(h + 1) * w], s))
            else:
                p.op("vector", lambda nc, d=dst3, s=src3, h=h, w=w:
                     nc.vector.tensor_copy(d[:, :, h * w:(h + 1) * w], s))
        cur_t, cur_off = c.FA, tgt
        w *= 2


def emit_quantize(c, dst16, n=1024):
    p = c.p
    nblk = n // 32
    X = fa(c, QX, [128, n]); T = fa(c, QT, [128, n]); Y = fa(c, QY, [128, n])
    NFi = fa(c, QNF, [128, n], I32); NF = fa(c, QNF, [128, n])
    AT = fa(c, QAT, [128, n]); Q = fa(c, QQ, [128, n])
    SW = fa(c, QSW, [128, n]); RSW = fa(c, QRSW, [128, n])
    bm = c.SMF[:, 1056:1056 + nblk]
    sw = c.SMF[:, 1088:1088 + nblk]
    rsw = c.SMF[:, 1120:1120 + nblk]
    x3 = fa(c, QX, [128, nblk, 32])
    vsmall(c, lambda nc: nc.vector.tensor_reduce(
        bm, x3, axis=AX.X, op=ALU.max, apply_absolute_value=True))
    vsmall(c, lambda nc: nc.vector.tensor_scalar(
        sw, bm, 1.0 / 7.5, 1e-30, ALU.mult, ALU.max))
    vsmall(c, lambda nc: nc.vector.reciprocal(rsw, sw))
    _expand32(p, c, c.SMF, 1088, QSW, QY, n)
    _expand32(p, c, c.SMF, 1120, QRSW, QY, n)
    p.op("vector", lambda nc: nc.vector.tensor_tensor(T, X, RSW, ALU.mult))
    p.op("vector", lambda nc: nc.vector.tensor_tensor(AT, T, T, ALU.mult))
    for cc, dst in ((8.0, Q), (2.0, RSW), (4.0, Y)):
        p.op("vector", lambda nc, cc=cc: nc.vector.tensor_scalar(
            Y, T, cc, None, ALU.mult))
        p.op("vector", lambda nc: nc.vector.tensor_copy(NFi, Y))
        p.op("vector", lambda nc: nc.vector.tensor_copy(X, NFi))
        p.op("vector", lambda nc: nc.vector.tensor_tensor(NF, Y, X, ALU.subtract))
        p.op("vector", lambda nc, dst=dst: nc.vector.tensor_scalar(
            dst, NF, 0.5, None, ALU.is_gt))
        p.op("vector", lambda nc: nc.vector.tensor_scalar(
            NF, NF, -0.5, None, ALU.is_le))
        p.op("vector", lambda nc, dst=dst: nc.vector.tensor_tensor(
            X, X, dst, ALU.add))
        p.op("vector", lambda nc: nc.vector.tensor_tensor(X, X, NF, ALU.subtract))
        p.op("vector", lambda nc, cc=cc, dst=dst: nc.vector.tensor_scalar(
            dst, X, 1.0 / cc, None, ALU.mult))
    Xi = fa(c, QX, [128, n], I32)
    p.op("vector", lambda nc: nc.vector.tensor_scalar(
        Xi, AT, 16.0, None, ALU.is_lt))
    p.op("vector", lambda nc: nc.vector.select(T, Xi, Y, RSW))
    p.op("vector", lambda nc: nc.vector.tensor_scalar(
        Xi, AT, 4.0, None, ALU.is_lt))
    p.op("vector", lambda nc: nc.vector.select(NF, Xi, Q, T))
    return p.op("vector", lambda nc, dst16=dst16: nc.vector.tensor_tensor(
        dst16, NF, SW, ALU.mult), inc=c.sv)



SCR_WDT = 0


def emit_l1f_wquant(c):
    p, dr = c.p, c.dr
    wd_tm = kb(c, 0, [128, 1024])
    pt3 = c.PT[:].rearrange("p (a b) -> p a b", b=128)
    for ot in range(8):
        v = p.dma("sync", c.WS[:, 0:1024],
                  dr["l1f_w"][ot * 128:(ot + 1) * 128, :], c.dl)
        p.wait("scalar", c.dl, v)
        vs = p.op("scalar", lambda nc: nc.scalar.copy(
            fa(c, QX, [128, 1024]), c.WS[:, 0:1024]), inc=c.ss)
        p.wait("sync", c.ss, vs)
        p.wait("vector", c.ss, vs)
        vq = emit_quantize(c, wd_tm, 1024)
        p.wait("tensor", c.sv, vq)
        c.pe_guard(6)
        vt = None
        for dc in range(DCH):
            vt = p.op("tensor", lambda nc, dc=dc: nc.tensor.transpose(
                pt3[:, dc, :], wd_tm[:, dc * 128:(dc + 1) * 128], c.IDT[:]),
                inc=c.st if dc == DCH - 1 else None)
        p.wait("vector", c.st, vt)
        ve = p.op("vector", lambda nc: nc.vector.tensor_copy(
            c.SMB[:, 0:1024], c.PT[:]), inc=c.sv)
        c.bank_consumed(6, c.sv, ve)
        p.wait("sync", c.sv, ve)
        for dc in range(DCH):
            dst = scr2(c, SCR_WDT + dc * 128 * 1024 + ot * 128, 128, 128,
                       BF16, row_stride=1024)
            p.dma("sync", dst, c.SMB[:, dc * 128:(dc + 1) * 128], c.dl)
        p.wait("vector", c.dl, c.dl.n)


def emit_l1f(c, first_call):
    p, dr = c.p, c.dr
    barrier(c)
    if first_call:
        emit_l1f_wquant(c)
        barrier(c)
    vb = p.dma("sync", c.SMF[:, 1280:1288], dr["l1f_b"][:], c.dl)
    xd_fm = kb(c, 1024, [128, DCH, TT])
    xd_tm = kb(c, 0, [128, 1024])
    pt3 = c.PT[:].rearrange("p (a b) -> p a b", b=128)
    hi16 = kb(c, 1024, [128, DCH, TT])
    lo16 = kb(c, 5120, [128, DCH, TT])
    for tt in range(NTT):
        ln_done = emit_ln_ff(c, tt, hi16, lo16=lo16)
        p.wait("tensor", c.sv, ln_done)
        for ts4 in range(4):
            for r in range(2):
                c.pe_guard(6)
                vt = None
                for i in range(4):
                    dc = 4 * r + i
                    p.op("tensor", lambda nc, dc=dc, i=i, ts4=ts4: nc.tensor.transpose(
                        pt3[:, i, :], hi16[:, dc, ts4 * 128:(ts4 + 1) * 128],
                        c.IDT[:]))
                    vt = p.op("tensor", lambda nc, dc=dc, i=i, ts4=ts4: nc.tensor.transpose(
                        pt3[:, 4 + i, :], lo16[:, dc, ts4 * 128:(ts4 + 1) * 128],
                        c.IDT[:]), inc=c.st if i == 3 else None)
                p.wait("vector", c.st, vt)
                xr = fa(c, QX + r * 512, [128, 4, 128])
                p.op("vector", lambda nc, xr=xr: nc.vector.tensor_copy(
                    xr, pt3[:, 0:4, :]))
                ve = p.op("vector", lambda nc, xr=xr: nc.vector.tensor_tensor(
                    xr, xr, pt3[:, 4:8, :], ALU.add), inc=c.sv)
                c.bank_consumed(6, c.sv, ve)
            vq = emit_quantize(c, xd_tm, 1024)
            p.wait("tensor", c.sv, vq)
            c.pe_guard(6)
            vt = None
            for dc in range(DCH):
                vt = p.op("tensor", lambda nc, dc=dc: nc.tensor.transpose(
                    pt3[:, dc, :], xd_tm[:, dc * 128:(dc + 1) * 128], c.IDT[:]),
                    inc=c.st if dc == DCH - 1 else None)
            p.wait("vector", c.st, vt)
            ve = p.op("vector", lambda nc, ts4=ts4: nc.vector.tensor_copy(
                hi16[:, :, ts4 * 128:(ts4 + 1) * 128], pt3[:, :, :]), inc=c.sv)
            c.bank_consumed(6, c.sv, ve)
        xd_ready = c.sv.n
        p.wait("tensor", c.sv, xd_ready)
        if not hasattr(c, "_wb_last"):
            c._wb_last = {}
        for g in range(2):
            for dc in range(DCH):
                slot = dc % 2
                wdst = c.WB[:, slot * 2048:slot * 2048 + 1024]
                src = scr2(c, SCR_WDT + dc * 128 * 1024, 128, 1024, BF16)
                if slot in c._wb_last:
                    p.wait("sync", c.st, c._wb_last[slot])
                vw = p.dma("sync", wdst, src, c.dl)
                p.wait("tensor", c.dl, vw)
                for oi in range(4):
                    oc = g * 4 + oi
                    if dc == 0:
                        c.pe_guard(oi)
                    vm = p.op("tensor", lambda nc, oi=oi, oc=oc, wdst=wdst, dc=dc:
                              nc.tensor.matmul(
                                  c.PSB[oi][:, :], wdst[:, oc * 128:(oc + 1) * 128],
                                  xd_fm[:, dc, :], start=(dc == 0), stop=(dc == DCH - 1)),
                              inc=c.st)
                c._wb_last[slot] = c.st.n
            mm_done = c.st.n
            p.wait("vector", c.st, mm_done)
            p.wait("vector", c.dl, vb)
            for oi in range(4):
                oc = g * 4 + oi
                rxa = c.RX[:, oc, tt * TT:(tt + 1) * TT]
                vv = p.op("vector", lambda nc, oi=oi, oc=oc, rxa=rxa:
                          nc.vector.scalar_tensor_tensor(
                              rxa, c.PSB[oi][:, :], c.SMF[:, 1280 + oc:1281 + oc],
                              rxa, ALU.add, ALU.add), inc=c.sv)
                c.bank_consumed(oi, c.sv, vv)



import contextlib

SECTION_FUNCS = {}


def build_sections(names):
    nc = bass.Bass("TRN2", target_bir_lowering=False,
                   detect_race_conditions=False)
    dr = bld_inputs(nc)
    with contextlib.ExitStack() as stack:
        c = Ctx(nc, stack, dr)
        emit_setup(c)
        barrier(c)
        if any(n != "l1f" and n != "l1f2" for n in names):
            emit_wprep(c)
            barrier(c)
        for i, name in enumerate(names):
            SECTION_FUNCS[name](c, name not in names[:i])
        emit_store_out(c)
        c.p.run_block()
    return nc


def _sec_l1f(c, first):
    emit_l1f(c, first)


SECTION_FUNCS["l1f"] = _sec_l1f
SECTION_FUNCS["l1f2"] = lambda c, first: emit_l1f(c, False)


def build_full():
    return build_sections(["l0a", "ff", "l0b", "ff2", "l1a", "l1f", "l1b", "l1f2"])



SCR_L1A_KS = 1 * 1024 * 1024
SCR_L1A_QS = SCR_L1A_KS + 8 * 1024 * 1024
SCR_L1A_V = SCR_L1A_QS + 8 * 1024 * 1024
SCR_W = 32 * 1024 * 1024

WPREP = [
    ("l0a_qT", D, D), ("l0a_kT", D, D), ("l0a_vT", D, D), ("l0a_oT", D, D),
    ("l0b_Wq", D, D), ("l0b_Wk", D, D), ("l0b_Wv", D, D), ("l0b_Wo", D, D),
    ("ff_rw1", D, 256), ("ff_rw2", 256, 8), ("ff_hw1", D, 256), ("ff_hw2", 256, 1),
    ("ff_ew1", 2048, 512), ("ff_ew2", 512, 3072), ("ff_sw1", 2048, 64),
    ("ff_sw2", 64, 3072), ("ff_qw1", D, 2048), ("ff_qw2", 2048, D),
    ("ff_eb2", 8, 3072),
    ("l1a_Wq", D, D), ("l1a_Wk", D, D), ("l1a_Wv", D, D), ("l1a_omg", 128, 256),
    ("l1b_Win", D, 4096), ("l1b_Wst", 2048, 32), ("l1b_Wt", 2048, 2048),
    ("l1b_Wout", 2048, D),
]


def emit_wprep(c):
    p, dr = c.p, c.dr
    c.wscr = {}
    off = SCR_W
    prev_dma = None
    for name, rows, cols in WPREP:
        c.wscr[name] = (off, rows, cols)
        nk = (rows + 127) // 128
        for kc in range(nk):
            r0 = kc * 128
            nr = min(128, rows - r0)
            for c0 in range(0, cols, 1024):
                ncg = min(1024, cols - c0)
                v = p.dma("sync", c.WS[:nr, 0:ncg],
                          dr[name][r0:r0 + nr, c0:c0 + ncg], c.dl)
                p.wait("scalar", c.dl, v)
                vs = p.op("scalar", lambda nc, nr=nr, ncg=ncg: nc.scalar.copy(
                    c.SMB[:nr, 0:ncg], c.WS[:nr, 0:ncg]), inc=c.ss)
                p.wait("sync", c.ss, vs)
                dst = scr2(c, off + r0 * cols + c0, nr, ncg, BF16, row_stride=cols)
                vd = p.dma("sync", dst, c.SMB[:nr, 0:ncg], c.dl)
                p.wait("scalar", c.dl, vd)
        off += rows * cols
    assert off <= 2 * 31_800_000, f"weights overrun scr: {off}"
    c.wprep_done = c.dl.n


def stream_w(c, name, wb_off, kc0, nk, col0, ncols, dma_eng="sync", rows=128):
    p = c.p
    off, wrows, cols = c.wscr[name]
    dsem = {"sync": c.dl, "scalar": c.ds, "gpsimd": c.dg}[dma_eng]
    dst = c.WB[0:rows, wb_off:wb_off + nk * ncols].rearrange(
        "p (k n) -> p k n", n=ncols)
    src = bass.AP(c.scr.bitcast(BF16), off + kc0 * rows * cols + col0,
                  [[cols, rows], [rows * cols, nk], [1, ncols]])
    v = p.dma(dma_eng, dst, src, dsem, slow=(ncols < 8 or cols < 8))
    return dst, v



L1A_NEG_LN16 = -math.log(16.0)


def wb_guard(c, slot_off, dma_eng="sync"):
    if not hasattr(c, "_wb_last"):
        c._wb_last = {}
    if slot_off in c._wb_last:
        c.p.wait(dma_eng, c.st, c._wb_last[slot_off])


def wb_used(c, slot_off, st_val):
    c._wb_last[slot_off] = st_val


def emit_l1a(c):
    p, dr = c.p, c.dr
    barrier(c)
    FAb = c.FA.bitcast(BF16)
    ln16 = kb(c, 0, [128, DCH, TT])
    vstage = kb(c, 4096, [128, 1040])
    vst3 = kb(c, 4096, [128, 16, 65])
    pt3 = c.PT[:].rearrange("p (a b) -> p a b", b=128)
    nlncol = c.SMF[:, 1408:1409]
    vsmall(c, lambda nc: nc.vector.memset(nlncol, L1A_NEG_LN16))
    vo = p.dma("sync", c.WS[:, 0:256], dr["l1a_omg"][:], c.dl)
    p.wait("scalar", c.dl, vo)
    vog = p.op("scalar", lambda nc: nc.scalar.copy(c.OMG[:], c.WS[:, 0:256]),
               inc=c.ss)
    p.wait("sync", c.ss, vog)
    for tt in range(NTT):
        emit_ln_ff(c, tt, ln16)
        ln_done = c.sv.n
        p.wait("tensor", c.sv, ln_done)
        for W, base in (("l1a_Wk", 0), ("l1a_Wq", 8192)):
            for oc in range(DCH):
                woff = (oc % 2) * 2048
                wb_guard(c, woff)
                wap, wv = stream_w(c, W, woff, 0, DCH, oc * 128, 128)
                p.wait("tensor", c.dl, wv)
                bank = oc % 2
                c.pe_guard(bank)
                vm = None
                for kc in range(DCH):
                    vm = p.op("tensor", lambda nc, kc=kc, bank=bank, wap=wap:
                              nc.tensor.matmul(
                                  c.PSB[bank][:, :], wap[:, kc, :], ln16[:, kc, :],
                                  start=(kc == 0), stop=(kc == DCH - 1)), inc=c.st)
                wb_used(c, woff, vm)
                p.wait("scalar", c.st, vm)
                va = None
                for half in range(2):
                    h = 2 * oc + half
                    ksl = FAb[:, base + h * 512: base + (h + 1) * 512]
                    ps = c.PSB[bank][half * 64:(half + 1) * 64, :]
                    p.op("scalar", lambda nc, ksl=ksl, ps=ps: nc.scalar.copy(
                        ksl[0:64, :], ps))
                    va = p.op("scalar", lambda nc, ksl=ksl, ps=ps:
                              nc.scalar.activation(ksl[64:128, :], ps, AF.Square),
                              inc=c.ss)
                c.bank_consumed(bank, c.ss, va)
            p.wait("sync", c.ss, c.ss.n)
            scr_base = (SCR_L1A_KS if base == 0 else SCR_L1A_QS) + tt * (128 * 8192)
            dst = bass.AP(c.scr.bitcast(BF16), scr_base, [[8192, 128], [1, 8192]])
            vship = p.dma("sync", dst, FAb[:, base:base + 8192], c.dl)
            p.wait("scalar", c.dl, vship)
        for sub in range(4):
            vsmall(c, lambda nc: nc.vector.memset(vst3[:, :, 64:65], 1.0))
            ones_v = c.sv.n
            for half in range(2):
                woff = 0
                wb_guard(c, 0)
                wb_guard(c, 2048)
                wap, wv = stream_w(c, "l1a_Wv", woff, 0, DCH, half * 512, 512)
                p.wait("tensor", c.dl, wv)
                bank = 2 + half
                c.pe_guard(bank)
                vm = None
                for kc in range(DCH):
                    vm = p.op("tensor", lambda nc, kc=kc, sub=sub, bank=bank, wap=wap:
                              nc.tensor.matmul(
                                  c.PSB[bank][:, :],
                                  ln16[:, kc, sub * 128:(sub + 1) * 128],
                                  wap[:, kc, :],
                                  start=(kc == 0), stop=(kc == DCH - 1)), inc=c.st)
                wb_used(c, 0, vm)
                wb_used(c, 2048, vm)
                p.wait("scalar", c.st, vm)
                p.wait("scalar", c.sv, ones_v)
                ps3 = c.PSB[bank][:, :].rearrange("p (h e) -> p h e", e=64)
                va = p.op("scalar", lambda nc, ps3=ps3, half=half:
                          nc.scalar.copy(
                              vst3[:, half * 8:(half + 1) * 8, 0:64], ps3),
                          inc=c.ss)
                c.bank_consumed(bank, c.ss, va)
            p.wait("sync", c.ss, c.ss.n)
            row0 = tt * TT + sub * 128
            dst = bass.AP(c.scr.bitcast(BF16), SCR_L1A_V + row0 * 1040,
                          [[1040, 128], [1, 1040]])
            vship = p.dma("sync", dst, vstage, c.dl)
            p.wait("vector", c.dl, vship)
            p.wait("scalar", c.dl, vship)
    barrier(c)
    kvmd = kb(c, 0, [128, 16, 130])
    kvacc = lambda h: c.FA[0:65, h * 256:(h + 1) * 256]
    last_kv = None
    for ch in range(32):
        tt, sub = ch // 4, ch % 4
        ks_base = SCR_L1A_KS + tt * (128 * 8192)
        src_ap = bass.AP(c.scr.bitcast(BF16), ks_base + sub * 128,
                         [[8192, 128], [512, 16], [1, 128]])
        ksbuf = kb(c, 4096, [128, 16, 128])
        if last_kv is not None:
            p.wait("sync", c.st, last_kv)
        v1 = p.dma("sync", ksbuf, src_ap, c.dl)
        row0 = ch * 128
        vsrc = bass.AP(c.scr.bitcast(BF16), SCR_L1A_V + row0 * 1040,
                       [[1040, 128], [1, 1040]])
        vbuf = kb(c, 6144, [128, 16, 65])
        v2 = p.dma("sync", vbuf, vsrc, c.dl)
        p.wait("tensor", c.dl, v2)
        for h in range(16):
            c.pe_guard(4)
            vm = p.op("tensor", lambda nc, h=h, ksbuf=ksbuf: nc.tensor.matmul(
                c.PSB[4][:, 0:256], ksbuf[:, h, :], c.OMG[:, :],
                start=True, stop=True), inc=c.st)
            p.wait("scalar", c.st, vm)
            if last_kv is not None:
                p.wait("scalar", c.st, last_kv)
            vpk = p.op("scalar", lambda nc: nc.scalar.activation(
                c.SMB[:, 0:256], c.PSB[4][:, 0:256], AF.Exp, bias=nlncol),
                inc=c.ss)
            c.bank_consumed(4, c.ss, vpk)
            p.wait("tensor", c.ss, vpk)
            bank = h % 4
            c.pe_guard(bank)
            last_kv = p.op("tensor", lambda nc, vbuf=vbuf, h=h, bank=bank:
                           nc.tensor.matmul(
                               c.PSB[bank][0:65, 0:256], vbuf[:, h, :],
                               c.SMB[:, 0:256], start=True, stop=True), inc=c.st)
            p.wait("vector", c.st, last_kv)
            if ch == 0:
                vadd = p.op("vector", lambda nc, h=h, bank=bank:
                            nc.vector.tensor_copy(kvacc(h), c.PSB[bank][0:65, 0:256]),
                            inc=c.sv)
            else:
                vadd = p.op("vector", lambda nc, h=h, bank=bank:
                            nc.vector.tensor_tensor(
                                kvacc(h), kvacc(h), c.PSB[bank][0:65, 0:256],
                                ALU.add), inc=c.sv)
            c.bank_consumed(bank, c.sv, vadd)
    acc_done = c.sv.n
    p.wait("scalar", c.sv, acc_done)
    for h in range(16):
        va = p.op("scalar", lambda nc, h=h: nc.scalar.copy(
            c.SMB[0:65, 0:256], kvacc(h)), inc=c.ss)
        p.wait("tensor", c.ss, va)
        c.pe_guard(6)
        vt = None
        for mc in range(2):
            vt = p.op("tensor", lambda nc, mc=mc: nc.tensor.transpose(
                pt3[:, mc, 0:65], c.SMB[0:65, mc * 128:(mc + 1) * 128],
                c.IDT[0:65, 0:65]), inc=c.st if mc == 1 else None)
        p.wait("vector", c.st, vt)
        ve = p.op("vector", lambda nc, h=h: nc.vector.tensor_copy(
            kvmd[:, h, :].rearrange("p (a b) -> p a b", b=65),
            pt3[:, 0:2, 0:65]), inc=c.sv)
        c.bank_consumed(6, c.sv, ve)
        p.wait("scalar", c.sv, ve)
    barrier(c)
    last_pq = None
    for tt in range(NTT):
        for h in range(16):
            qs_base = SCR_L1A_QS + tt * (128 * 8192)
            src = bass.AP(c.scr.bitcast(BF16), qs_base + h * 512,
                          [[8192, 128], [1, 512]])
            qsbuf = kb(c, 4096, [128, 512])
            if last_pq is not None:
                p.wait("sync", c.st, last_pq)
            v1 = p.dma("sync", qsbuf, src, c.dl)
            p.wait("tensor", c.dl, v1)
            c.pe_guard(4)
            c.pe_guard(5)
            vm = None
            for mc in range(2):
                vm = p.op("tensor", lambda nc, mc=mc, qsbuf=qsbuf: nc.tensor.matmul(
                    c.PSB[4 + mc][:, :], c.OMG[:, mc * 128:(mc + 1) * 128],
                    qsbuf, start=True, stop=True), inc=c.st)
            last_pq = vm
            p.wait("scalar", c.st, vm)
            vpq = None
            for mc in range(2):
                vpq = p.op("scalar", lambda nc, mc=mc: nc.scalar.activation(
                    c.SMB[:, mc * 512:(mc + 1) * 512], c.PSB[4 + mc][:, :],
                    AF.Exp, bias=nlncol), inc=c.ss)
            c.bank_consumed(4, c.ss, vpq)
            c.bank_consumed(5, c.ss, vpq)
            p.wait("tensor", c.ss, vpq)
            c.pe_guard(2)
            c.pe_guard(3)
            kv3 = kvmd[:, h, :].rearrange("p (a b) -> p a b", b=65)
            hp0 = (h % 2) * 64
            htp = None if h % 2 == 0 else (0, 64)
            vz = None
            for mc in range(2):
                p.op("tensor", lambda nc, mc=mc, kv3=kv3: nc.tensor.matmul(
                    c.PSB[2][0:1, :], kv3[:, mc, 64:65],
                    c.SMB[:, mc * 512:(mc + 1) * 512],
                    start=(mc == 0), stop=(mc == 1)))
                vz = p.op("tensor", lambda nc, mc=mc, kv3=kv3, hp0=hp0, htp=htp:
                          nc.tensor.matmul(
                              c.PSB[3][hp0:hp0 + 64, :], kv3[:, mc, 0:64],
                              c.SMB[:, mc * 512:(mc + 1) * 512],
                              start=(mc == 0), stop=(mc == 1),
                              tile_position=htp), inc=c.st)
            p.wait("vector", c.st, vz)
            zrow = c.SMF[0:1, 520:1032]
            vsmall(c, lambda nc: nc.vector.tensor_scalar(
                zrow, c.PSB[2][0:1, :], 1e-6, None, ALU.add))
            vsmall(c, lambda nc: nc.vector.reciprocal(zrow, zrow))
            vrz = c.sv.n
            c.bank_consumed(2, c.sv, vrz)
            p.wait("scalar", c.sv, vrz)
            vrow = p.op("scalar", lambda nc: nc.scalar.copy(
                c.SMB[0:1, 1536:2048], zrow), inc=c.ss)
            p.wait("tensor", c.ss, vrow)
            j, hh = h // 2, h % 2
            p0, p1 = hh * 64, (hh + 1) * 64
            tp = None if hh == 0 else (0, 64)
            c.pe_guard(4)
            vbc = p.op("tensor", lambda nc, p0=p0, p1=p1, tp=tp: nc.tensor.matmul(
                c.PSB[4][p0:p1, :], c.ONEROW16[0:1, 0:64], c.SMB[0:1, 1536:2048],
                start=True, stop=True, tile_position=tp), inc=c.st)
            p.wait("scalar", c.st, vbc)
            vo2 = p.op("scalar", lambda nc, p0=p0, p1=p1: nc.scalar.copy(
                c.SMB[p0:p1, 0:512], c.PSB[3][p0:p1, :]), inc=c.ss)
            c.bank_consumed(3, c.ss, vo2)
            p.wait("vector", c.ss, vo2)
            rxa = c.RX[p0:p1, j, tt * TT:(tt + 1) * TT]
            p.op("vector", lambda nc, p0=p0, p1=p1: nc.vector.tensor_tensor(
                c.SMF[p0:p1, 0:512], c.SMB[p0:p1, 0:512], c.PSB[4][p0:p1, :],
                ALU.mult))
            vfin = p.op("vector", lambda nc, rxa=rxa, p0=p0, p1=p1:
                        nc.vector.tensor_tensor(
                            rxa, rxa, c.SMF[p0:p1, 0:512], ALU.add), inc=c.sv)
            c.bank_consumed(4, c.sv, vfin)
            p.wait("scalar", c.sv, vfin)


SECTION_FUNCS["l1a"] = lambda c, first: emit_l1a(c)



SCR_L0B_Q = 1 * 1024 * 1024
SCR_L0B_K = SCR_L0B_Q + 4 * 1024 * 1024
SCR_L0B_V = SCR_L0B_K + 4 * 1024 * 1024
SCR_L0B_C = SCR_L0B_V + 4 * 1024 * 1024


def emit_l0b(c):
    p, dr = c.p, c.dr
    barrier(c)
    c._l0b_w16v = 0
    c._l0b_vmu = 0
    ln16 = kb(c, 0, [128, DCH, TT])
    pt3 = c.PT[:].rearrange("p (a b) -> p a b", b=128)
    qmax = c.SMF[0:1, 1160:1161]
    kmax = c.SMF[0:1, 1161:1162]
    mcol = c.SMF[:, 1162:1163]
    vsmall(c, lambda nc: nc.vector.memset(c.SMF[0:1, 1160:1162], 0.0))
    vbi = None
    for b in ("bq", "bk", "bv", "bo"):
        off = {"bq": 1168, "bk": 1176, "bv": 1184, "bo": 1192}[b]
        vbi = p.dma("sync", c.SMF[:, off:off + 8], dr[f"l0b_{b}"][:], c.dl)
    for tt in range(NTT):
        emit_ln_ff(c, tt, ln16)
        ln_done = c.sv.n
        p.wait("tensor", c.sv, ln_done)
        p.wait("vector", c.dl, vbi)
        for W, sbase, boff, nrow in (("l0b_Wq", SCR_L0B_Q, 1168, qmax),
                                     ("l0b_Wk", SCR_L0B_K, 1176, kmax),
                                     ("l0b_Wv", SCR_L0B_V, 1184, None)):
            stg = kb(c, 4096, [128, DCH, TT])
            c.pe_guard("PR")
            nv = None
            for oc in range(DCH):
                woff = (oc % 2) * 2048
                wb_guard(c, woff)
                wap, wv = stream_w(c, W, woff, 0, DCH, oc * 128, 128)
                p.wait("tensor", c.dl, wv)
                bank = oc % 2
                c.pe_guard(bank)
                vm = None
                for kc in range(DCH):
                    vm = p.op("tensor", lambda nc, kc=kc, bank=bank, wap=wap:
                              nc.tensor.matmul(
                                  c.PSB[bank][:, :], wap[:, kc, :], ln16[:, kc, :],
                                  start=(kc == 0), stop=(kc == DCH - 1)), inc=c.st)
                wb_used(c, woff, vm)
                p.wait("vector", c.st, vm)
                bcol = c.SMF[:, boff + oc:boff + oc + 1]
                ve = p.op("vector", lambda nc, oc=oc, bank=bank, bcol=bcol, stg=stg:
                          nc.vector.tensor_scalar(
                              stg[:, oc, :], c.PSB[bank][:, :], bcol, None,
                              ALU.add), inc=c.sv)
                c.bank_consumed(bank, c.sv, ve)
                if nrow is not None:
                    p.wait("scalar", c.sv, ve)
                    vsq = p.op("scalar", lambda nc, oc=oc, stg=stg:
                               nc.scalar.activation(
                                   c.SMB[:, 0:TT], stg[:, oc, :], AF.Square),
                               inc=c.ss)
                    p.wait("tensor", c.ss, vsq)
                    p.op("tensor", lambda nc, oc=oc: nc.tensor.matmul(
                        c.PR[0:1, 0:TT], c.ONECOL16[:], c.SMB[:, 0:TT],
                        start=(oc == 0), stop=(oc == DCH - 1)),
                        inc=c.st if oc == DCH - 1 else None)
                    nv = c.st.n
            if nrow is not None:
                p.wait("vector", c.st, nv)
                vsmall(c, lambda nc: nc.vector.reduce_max(
                    c.SMF[0:1, 1163:1164], c.PR[0:1, 0:TT], axis=AX.X))
                vsmall(c, lambda nc, nrow=nrow: nc.vector.tensor_tensor(
                    nrow, nrow, c.SMF[0:1, 1163:1164], ALU.max))
                c.bank_consumed("PR", c.sv, c.sv.n)
            shipw = c.sv.n
            p.wait("sync", c.sv, shipw)
            if W == "l0b_Wv":
                p.wait("tensor", c.sv, shipw)
                for sub in range(4):
                    c.pe_guard(6)
                    vt = None
                    for dc2 in range(DCH):
                        vt = p.op("tensor", lambda nc, dc2=dc2, sub=sub, stg=stg:
                                  nc.tensor.transpose(
                                      pt3[:, dc2, :],
                                      stg[:, dc2, sub * 128:(sub + 1) * 128],
                                      c.IDT[:]), inc=c.st if dc2 == DCH - 1 else None)
                    p.wait("vector", c.st, vt)
                    vte = p.op("vector", lambda nc: nc.vector.tensor_copy(
                        kb(c, 8192, [128, DCH, 128]), pt3[:, :, :]), inc=c.sv)
                    c.bank_consumed(6, c.sv, vte)
                    p.wait("sync", c.sv, vte)
                    row0 = tt * TT + sub * 128
                    dst = bass.AP(c.scr.bitcast(BF16), sbase + row0 * 1024,
                                  [[1024, 128], [1, 1024]])
                    vship = p.dma("sync", dst, kb(c, 8192, [128, 1024]), c.dl)
                    p.wait("vector", c.dl, vship)
            else:
                dst = bass.AP(c.scr.bitcast(BF16), sbase + tt * (128 * 4096),
                              [[4096, 128], [1, 4096]])
                vship = p.dma("sync", dst, kb(c, 4096, [128, 4096]), c.dl)
                p.wait("vector", c.dl, vship)
    barrier(c)
    vsmall(c, lambda nc: nc.vector.tensor_tensor(qmax, qmax, kmax, ALU.mult))
    p.wait("scalar", c.sv, c.sv.n)
    vsq2 = p.op("scalar", lambda nc: nc.scalar.activation(qmax, qmax, AF.Sqrt),
                inc=c.ss)
    p.wait("vector", c.ss, vsq2)
    vsmall(c, lambda nc: nc.vector.tensor_scalar(
        qmax, qmax, -1.0 / 32.0, None, ALU.mult))
    p.wait("scalar", c.sv, c.sv.n)
    vmb = p.op("scalar", lambda nc: nc.scalar.copy(c.SMB[0:1, 0:1], qmax), inc=c.ss)
    p.wait("tensor", c.ss, vmb)
    c.pe_guard(2)
    vbc = p.op("tensor", lambda nc: nc.tensor.matmul(
        c.PSB[2][:, 0:1], c.ONEROW16[:], c.SMB[0:1, 0:1], start=True, stop=True),
        inc=c.st)
    p.wait("scalar", c.st, vbc)
    vmc = p.op("scalar", lambda nc: nc.scalar.copy(mcol, c.PSB[2][:, 0:1]), inc=c.ss)
    c.bank_consumed(2, c.ss, vmc)
    p.wait("tensor", c.ss, vmc)
    p.wait("vector", c.ss, vmc)
    qn = kb(c, 0, [128, DCH, 256])
    km = kb(c, 2048, [128, DCH, 256])
    vn = kb(c, 4096, [128, 2, 1024])
    ptile = kb(c, 6144, [128, 2, 256])
    w16 = kb(c, 6656, [128, 2, 256])
    wacc = fa(c, 0, [128, 2, 256])
    tmpf = fa(c, 512, [128, 256])
    last_q_use = None
    last_v_use = None
    for n in range(16):
        tt, half = n // 2, n % 2
        src = bass.AP(c.scr.bitcast(BF16), SCR_L0B_Q + tt * (128 * 4096) + half * 256,
                      [[4096, 128], [512, DCH], [1, 256]])
        if last_q_use is not None:
            p.wait("sync", c.st, last_q_use)
        vq = p.dma("sync", qn, src, c.dl)
        vsrc = bass.AP(c.scr.bitcast(BF16), SCR_L0B_V + n * 256 * 1024,
                       [[1024, 128], [128 * 1024, 2], [1, 1024]])
        if last_v_use is not None:
            p.wait("sync", c.st, last_v_use)
        vv = p.dma("sync", vn, vsrc, c.dl)
        p.wait("tensor", c.dl, vq)
        last_km = None
        for m in range(16):
            mt, mh = m // 2, m % 2
            ksrc = bass.AP(c.scr.bitcast(BF16),
                           SCR_L0B_K + mt * (128 * 4096) + mh * 256,
                           [[4096, 128], [512, DCH], [1, 256]])
            if last_km is not None:
                p.wait("sync", c.st, last_km)
            vk = p.dma("sync", km, ksrc, c.dl)
            p.wait("tensor", c.dl, vk)
            vsc = None
            for kc in range(2):
                c.pe_guard(kc)
                for dc in range(DCH):
                    vsc = p.op("tensor", lambda nc, kc=kc, dc=dc: nc.tensor.matmul(
                        c.PSB[kc][:, 0:256], km[:, dc, kc * 128:(kc + 1) * 128],
                        qn[:, dc, :], start=(dc == 0), stop=(dc == DCH - 1)),
                        inc=c.st if dc == DCH - 1 else None)
            last_km = vsc
            p.wait("scalar", c.st, vsc)
            if hasattr(c, "_l0b_vmu") and c._l0b_vmu:
                p.wait("scalar", c.sv, c._l0b_vmu)
            vex = None
            for kc in range(2):
                vex = p.op("scalar", lambda nc, kc=kc: nc.scalar.activation(
                    ptile[:, kc, :], c.PSB[kc][:, 0:256], AF.Exp,
                    bias=mcol, scale=1.0 / 32.0), inc=c.ss)
                c.bank_consumed(kc, c.ss, vex)
            p.wait("tensor", c.ss, vex)
            c.pe_guard("PR")
            vl = None
            for kc in range(2):
                vl = p.op("tensor", lambda nc, kc=kc: nc.tensor.matmul(
                    c.PR[0:1, 0:256], c.ONECOL16[:], ptile[:, kc, :],
                    start=(kc == 0), stop=(kc == 1)), inc=c.st)
            p.wait("vector", c.st, vl)
            lrow = c.SMF[0:1, 520:776]
            vsmall(c, lambda nc: nc.vector.reciprocal(lrow, c.PR[0:1, 0:256]))
            if m == n:
                vsmall(c, lambda nc: nc.vector.tensor_scalar(
                    lrow, lrow, 2.0, None, ALU.mult))
            c.bank_consumed("PR", c.sv, c.sv.n)
            p.wait("scalar", c.sv, c.sv.n)
            vlr = p.op("scalar", lambda nc: nc.scalar.copy(
                c.SMB[0:1, 0:256], lrow), inc=c.ss)
            p.wait("tensor", c.ss, vlr)
            c.pe_guard(2)
            vrb = p.op("tensor", lambda nc: nc.tensor.matmul(
                c.PSB[2][:, 0:256], c.ONEROW16[:], c.SMB[0:1, 0:256],
                start=True, stop=True), inc=c.st)
            p.wait("vector", c.st, vrb)
            vwa = None
            vmu = None
            for kc in range(2):
                p.op("vector", lambda nc, kc=kc: nc.vector.tensor_tensor(
                    tmpf, ptile[:, kc, :], c.PSB[2][:, 0:256], ALU.mult),
                    inc=c.sv)
                vmu = c.sv.n
                p.wait("gpsimd", c.sv, vmu)
                if m == 0:
                    p.wait("gpsimd", c.ss, getattr(c, "_l0b_w16v", 0))
                    vwa = p.op("gpsimd", lambda nc, kc=kc: nc.gpsimd.tensor_copy(
                        wacc[:, kc, :], tmpf), inc=c.sg)
                else:
                    vwa = p.op("gpsimd", lambda nc, kc=kc: nc.gpsimd.tensor_tensor(
                        wacc[:, kc, :], wacc[:, kc, :], tmpf, ALU.add), inc=c.sg)
                p.wait("vector", c.sg, vwa)
            c._l0b_vmu = vmu
            c.bank_consumed(2, c.sg, vwa)
        p.wait("scalar", c.sg, c.sg.n)
        vw16 = None
        for kc in range(2):
            vw16 = p.op("scalar", lambda nc, kc=kc: nc.scalar.copy(
                w16[:, kc, :], wacc[:, kc, :]), inc=c.ss)
        c._l0b_w16v = vw16
        p.wait("tensor", c.ss, vw16)
        p.wait("tensor", c.dl, vv)
        cstage = kb(c, 8192, [128, DCH, 256])
        for g in range(2):
            vpv = None
            for oi in range(4):
                ec = g * 4 + oi
                c.pe_guard(oi)
                for kc in range(2):
                    vpv = p.op("tensor", lambda nc, ec=ec, oi=oi, kc=kc:
                               nc.tensor.matmul(
                                   c.PSB[oi][:, 0:256],
                                   vn[:, kc, ec * 128:(ec + 1) * 128],
                                   w16[:, kc, :], start=(kc == 0), stop=(kc == 1)),
                               inc=c.st)
            last_v_use = vpv
            p.wait("vector", c.st, vpv)
            for oi in range(4):
                ec = g * 4 + oi
                ve2 = p.op("vector", lambda nc, ec=ec, oi=oi, cstage=cstage:
                           nc.vector.tensor_copy(cstage[:, ec, :],
                                                 c.PSB[oi][:, 0:256]), inc=c.sv)
                c.bank_consumed(oi, c.sv, ve2)
        last_q_use = last_km
        ship_ok = c.sv.n
        p.wait("sync", c.sv, ship_ok)
        dst = bass.AP(c.scr.bitcast(BF16), SCR_L0B_C + n * (128 * 2048),
                      [[2048, 128], [1, 2048]])
        vship = p.dma("sync", dst, cstage[:, :, :], c.dl)
        p.wait("vector", c.dl, vship)
    barrier(c)
    for tt in range(NTT):
        cb = kb(c, 0, [128, DCH, TT])
        if tt > 0:
            p.wait("sync", c.st, c._l0b_cb_use)
        for half in range(2):
            csrc = bass.AP(c.scr.bitcast(BF16),
                           SCR_L0B_C + (2 * tt + half) * (128 * 2048),
                           [[2048, 128], [256, DCH], [1, 256]])
            vc = p.dma("sync", cb[:, :, half * 256:(half + 1) * 256], csrc, c.dl)
        p.wait("tensor", c.dl, vc)
        for oc in range(DCH):
            woff = (oc % 2) * 2048
            wb_guard(c, woff)
            wap, wv = stream_w(c, "l0b_Wo", woff, 0, DCH, oc * 128, 128)
            p.wait("tensor", c.dl, wv)
            bank = oc % 2
            c.pe_guard(bank)
            vm = None
            for kc in range(DCH):
                vm = p.op("tensor", lambda nc, kc=kc, bank=bank, wap=wap, cb=cb:
                          nc.tensor.matmul(
                              c.PSB[bank][:, :], wap[:, kc, :], cb[:, kc, :],
                              start=(kc == 0), stop=(kc == DCH - 1)), inc=c.st)
            wb_used(c, woff, vm)
            c._l0b_cb_use = vm
            p.wait("vector", c.st, vm)
            bcol = c.SMF[:, 1192 + oc:1193 + oc]
            rxa = c.RX[:, oc, tt * TT:(tt + 1) * TT]
            vr = p.op("vector", lambda nc, bank=bank, bcol=bcol, rxa=rxa:
                      nc.vector.scalar_tensor_tensor(
                          rxa, c.PSB[bank][:, :], bcol, rxa, ALU.add, ALU.add),
                      inc=c.sv)
            c.bank_consumed(bank, c.sv, vr)


SECTION_FUNCS["l0b"] = lambda c, first: emit_l0b(c)

def emit_l0a(c):
    p, dr = c.p, c.dr
    barrier(c)
    ln16 = kb(c, 0, [128, DCH, TT])
    pt3 = c.PT[:].rearrange("p (a b) -> p a b", b=128)
    qmax = c.SMF[0:1, 1160:1161]
    kmax = c.SMF[0:1, 1161:1162]
    mcol = c.SMF[:, 1162:1163]
    vsmall(c, lambda nc: nc.vector.memset(c.SMF[0:1, 1160:1162], 0.0))
    for tt in range(NTT):
        emit_ln_ff(c, tt, ln16)
        ln_done = c.sv.n
        p.wait("tensor", c.sv, ln_done)
        for W, sbase, nrow in (("l0a_qT", SCR_L0B_Q, qmax),
                               ("l0a_kT", SCR_L0B_K, kmax),
                               ("l0a_vT", SCR_L0B_V, None)):
            stg = kb(c, 4096, [128, DCH, TT])
            c.pe_guard("PR")
            nv = None
            for oc in range(DCH):
                woff = (oc % 2) * 2048
                wb_guard(c, woff)
                wap, wv = stream_w(c, W, woff, 0, DCH, oc * 128, 128)
                p.wait("tensor", c.dl, wv)
                bank = oc % 2
                c.pe_guard(bank)
                vm = None
                for kc in range(DCH):
                    vm = p.op("tensor", lambda nc, kc=kc, bank=bank, wap=wap:
                              nc.tensor.matmul(
                                  c.PSB[bank][:, :], wap[:, kc, :], ln16[:, kc, :],
                                  start=(kc == 0), stop=(kc == DCH - 1)), inc=c.st)
                wb_used(c, woff, vm)
                p.wait("scalar", c.st, vm)
                ve = p.op("scalar", lambda nc, oc=oc, bank=bank, stg=stg:
                          nc.scalar.copy(stg[:, oc, :], c.PSB[bank][:, :]),
                          inc=c.ss)
                if nrow is not None:
                    vsq = p.op("scalar", lambda nc, oc=oc, bank=bank:
                               nc.scalar.activation(
                                   c.SMB[:, 0:TT], c.PSB[bank][:, :], AF.Square),
                               inc=c.ss)
                    c.bank_consumed(bank, c.ss, vsq)
                    p.wait("tensor", c.ss, vsq)
                    c.pe_guard("PR")
                    nv = p.op("tensor", lambda nc, oc=oc: nc.tensor.matmul(
                        c.PR[0:1, 0:TT], c.ONECOL16[:], c.SMB[:, 0:TT],
                        start=True, stop=True), inc=c.st)
                    p.wait("vector", c.st, nv)
                    vsmall(c, lambda nc: nc.vector.reduce_max(
                        c.SMF[0:1, 1163:1164], c.PR[0:1, 0:TT], axis=AX.X))
                    vsmall(c, lambda nc, nrow=nrow: nc.vector.tensor_tensor(
                        nrow, nrow, c.SMF[0:1, 1163:1164], ALU.max))
                    c.bank_consumed("PR", c.sv, c.sv.n)
                else:
                    c.bank_consumed(bank, c.ss, ve)
            shipw = c.ss.n
            p.wait("sync", c.ss, shipw)
            if W == "l0a_vT":
                p.wait("tensor", c.ss, shipw)
                for sub in range(4):
                    c.pe_guard(6)
                    vt = None
                    for dc2 in range(DCH):
                        vt = p.op("tensor", lambda nc, dc2=dc2, sub=sub, stg=stg:
                                  nc.tensor.transpose(
                                      pt3[:, dc2, :],
                                      stg[:, dc2, sub * 128:(sub + 1) * 128],
                                      c.IDT[:]), inc=c.st if dc2 == DCH - 1 else None)
                    p.wait("vector", c.st, vt)
                    vte = p.op("vector", lambda nc: nc.vector.tensor_copy(
                        kb(c, 8192, [128, DCH, 128]), pt3[:, :, :]), inc=c.sv)
                    c.bank_consumed(6, c.sv, vte)
                    p.wait("sync", c.sv, vte)
                    row0 = tt * TT + sub * 128
                    dst = bass.AP(c.scr.bitcast(BF16), SCR_L0B_V + row0 * 1024,
                                  [[1024, 128], [1, 1024]])
                    vship = p.dma("sync", dst, kb(c, 8192, [128, 1024]), c.dl)
                    p.wait("vector", c.dl, vship)
            else:
                dst = bass.AP(c.scr.bitcast(BF16), sbase + tt * (128 * 4096),
                              [[4096, 128], [1, 4096]])
                vship = p.dma("sync", dst, kb(c, 4096, [128, 4096]), c.dl)
                p.wait("scalar", c.dl, vship)
    barrier(c)
    vsmall(c, lambda nc: nc.vector.tensor_tensor(qmax, qmax, kmax, ALU.mult))
    p.wait("scalar", c.sv, c.sv.n)
    vsq2 = p.op("scalar", lambda nc: nc.scalar.activation(qmax, qmax, AF.Sqrt),
                inc=c.ss)
    p.wait("vector", c.ss, vsq2)
    vsmall(c, lambda nc: nc.vector.tensor_scalar(
        qmax, qmax, -1.0 / 8.0, None, ALU.mult))
    p.wait("scalar", c.sv, c.sv.n)
    vmb = p.op("scalar", lambda nc: nc.scalar.copy(c.SMB[0:1, 0:1], qmax), inc=c.ss)
    p.wait("tensor", c.ss, vmb)
    c.pe_guard(2)
    vbc = p.op("tensor", lambda nc: nc.tensor.matmul(
        c.PSB[2][:, 0:1], c.ONEROW16[:], c.SMB[0:1, 0:1], start=True, stop=True),
        inc=c.st)
    p.wait("scalar", c.st, vbc)
    vmc = p.op("scalar", lambda nc: nc.scalar.copy(mcol, c.PSB[2][:, 0:1]), inc=c.ss)
    c.bank_consumed(2, c.ss, vmc)
    p.wait("tensor", c.ss, vmc)
    p.wait("vector", c.ss, vmc)
    ptt = kb(c, 4096, [128, 4, TT])
    attn16 = kb(c, 6144, [128, DCH, TT])
    for tt in range(NTT):
        nkt = tt + 1
        for j in range(DCH):
            qsl = kb(c, 0, [128, TT])
            qsrc = bass.AP(c.scr.bitcast(BF16),
                           SCR_L0B_Q + tt * (128 * 4096) + j * 512,
                           [[4096, 128], [1, 512]])
            if getattr(c, "_a_quse", None) is not None:
                p.wait("sync", c.st, c._a_quse)
            vq = p.dma("sync", qsl, qsrc, c.dl)
            p.wait("tensor", c.dl, vq)
            for kt in range(nkt):
                kslot = (kt % 2)
                kbuf = kb(c, 1024 + kslot * 512, [128, TT])
                ksrc = bass.AP(c.scr.bitcast(BF16),
                               SCR_L0B_K + kt * (128 * 4096) + j * 512,
                               [[4096, 128], [1, 512]])
                if getattr(c, "_a_kuse", {}).get(kslot) is not None:
                    p.wait("sync", c.st, c._a_kuse[kslot])
                vk = p.dma("sync", kbuf, ksrc, c.dl)
                vbuf = kb(c, 2048 + kslot * 512, [128, 4, 128])
                vsrc = bass.AP(c.scr.bitcast(BF16),
                               SCR_L0B_V + (kt * 512) * 1024 + j * 128,
                               [[1024, 128], [128 * 1024, 4], [1, 128]])
                if getattr(c, "_a_vuse", {}).get(kslot) is not None:
                    p.wait("sync", c.st, c._a_vuse[kslot])
                vv = p.dma("sync",
                           bass.AP(vbuf.tensor, vbuf.offset,
                                   [[10240, 128], [128, 4], [1, 128]]),
                           vsrc, c.dl)
                p.wait("tensor", c.dl, vv)
                for hh in range(2):
                    h = 2 * j + hh
                    p0 = hh * 64
                    tp = None if hh == 0 else (0, 64)
                    vsc = None
                    for ks in range(4):
                        c.pe_guard(ks)
                        vsc = p.op("tensor", lambda nc, ks=ks, p0=p0, kbuf=kbuf,
                                   qsl=qsl: nc.tensor.matmul(
                                       c.PSB[ks][:, :],
                                       kbuf[p0:p0 + 64, ks * 128:(ks + 1) * 128],
                                       qsl[p0:p0 + 64, :],
                                       start=True, stop=True), inc=c.st)
                    c._a_kuse = getattr(c, "_a_kuse", {})
                    c._a_kuse[kslot] = vsc
                    c._a_quse = vsc
                    p.wait("scalar", c.st, vsc)
                    if getattr(c, "_a_puse", None) is not None:
                        p.wait("scalar", c.sv, c._a_puse[0])
                        p.wait("scalar", c.st, c._a_puse[1])
                    vex = None
                    for ks in range(4):
                        vex = p.op("scalar", lambda nc, ks=ks: nc.scalar.activation(
                            ptt[:, ks, :], c.PSB[ks][:, :], AF.Exp,
                            bias=mcol, scale=1.0 / 8.0), inc=c.ss)
                        c.bank_consumed(ks, c.ss, vex)
                    vmask = None
                    if kt == tt:
                        p.wait("vector", c.ss, vex)
                        if getattr(c, "_a_puse", None) is not None:
                            p.wait("vector", c.st, c._a_puse[1])
                        for ks in range(4):
                            delta = ks * 128
                            vmask = p.op("vector", lambda nc, ks=ks, delta=delta:
                                         nc.vector.tensor_tensor(
                                             ptt[:, ks, :], ptt[:, ks, :],
                                             c.CAUS[:, 384 - delta:896 - delta],
                                             ALU.mult), inc=c.sv)
                        p.wait("tensor", c.sv, vmask)
                    else:
                        p.wait("tensor", c.ss, vex)
                    c.pe_guard("PR")
                    vl = None
                    for ks in range(4):
                        vl = p.op("tensor", lambda nc, ks=ks, p0=p0, tp=tp:
                                  nc.tensor.matmul(
                                      c.PR[p0:p0 + 1, :], c.ONECOL16[:],
                                      ptt[:, ks, :], start=(ks == 0),
                                      stop=(ks == 3), tile_position=tp), inc=c.st)
                    c.pe_guard(4)
                    vpv = None
                    for ks in range(4):
                        vpv = p.op("tensor", lambda nc, ks=ks, p0=p0, tp=tp,
                                   vbuf=vbuf: nc.tensor.matmul(
                                       c.PSB[4][p0:p0 + 64, :],
                                       vbuf[:, ks, p0:p0 + 64], ptt[:, ks, :],
                                       start=(ks == 0), stop=(ks == 3),
                                       tile_position=tp), inc=c.st)
                    c._a_puse = (vmask if vmask else c.sv.n, vpv)
                    c._a_vuse = getattr(c, "_a_vuse", {})
                    c._a_vuse[kslot] = vpv
                    oa = c.FA[p0:p0 + 64, j * 512:(j + 1) * 512]
                    la = c.FA[p0:p0 + 1, 4096 + j * 512:4096 + (j + 1) * 512]
                    p.wait("vector", c.st, vpv)
                    if kt == 0:
                        p.op("vector", lambda nc, oa=oa, p0=p0: nc.vector.tensor_copy(
                            oa, c.PSB[4][p0:p0 + 64, :]))
                        vacc = p.op("vector", lambda nc, la=la, p0=p0:
                                    nc.vector.tensor_copy(la, c.PR[p0:p0 + 1, :]),
                                    inc=c.sv)
                    else:
                        p.op("vector", lambda nc, oa=oa, p0=p0: nc.vector.tensor_tensor(
                            oa, oa, c.PSB[4][p0:p0 + 64, :], ALU.add))
                        vacc = p.op("vector", lambda nc, la=la, p0=p0:
                                    nc.vector.tensor_tensor(
                                        la, la, c.PR[p0:p0 + 1, :], ALU.add),
                                    inc=c.sv)
                    c.bank_consumed(4, c.sv, vacc)
                    c.bank_consumed("PR", c.sv, vacc)
            for hh in range(2):
                h = 2 * j + hh
                p0 = hh * 64
                tp = None if hh == 0 else (0, 64)
                oa = c.FA[p0:p0 + 64, j * 512:(j + 1) * 512]
                la = c.FA[p0:p0 + 1, 4096 + j * 512:4096 + (j + 1) * 512]
                rl = c.SMF[p0:p0 + 1, 520:1032]
                vsmall(c, lambda nc, rl=rl, la=la: nc.vector.reciprocal(rl, la))
                p.wait("scalar", c.sv, c.sv.n)
                vrl = p.op("scalar", lambda nc, rl=rl, p0=p0: nc.scalar.copy(
                    c.SMB[p0:p0 + 1, 0:512], rl), inc=c.ss)
                p.wait("tensor", c.ss, vrl)
                c.pe_guard(5)
                tpb = None if hh == 0 else (64, 64)
                vbc2 = p.op("tensor", lambda nc, p0=p0, tpb=tpb: nc.tensor.matmul(
                    c.PSB[5][p0:p0 + 64, :], c.ONES128[p0:p0 + 1, 0:64],
                    c.SMB[p0:p0 + 1, 0:512], start=True, stop=True,
                    tile_position=tpb), inc=c.st)
                p.wait("vector", c.st, vbc2)
                vfin = p.op("vector", lambda nc, oa=oa, p0=p0, j=j:
                            nc.vector.tensor_tensor(
                                attn16[p0:p0 + 64, j, :], oa,
                                c.PSB[5][p0:p0 + 64, :], ALU.mult), inc=c.sv)
                c.bank_consumed(5, c.sv, vfin)
        attn_ready = c.sv.n
        p.wait("tensor", c.sv, attn_ready)
        for oc in range(DCH):
            woff = (oc % 2) * 2048
            wb_guard(c, woff)
            wap, wv = stream_w(c, "l0a_oT", woff, 0, DCH, oc * 128, 128)
            p.wait("tensor", c.dl, wv)
            bank = oc % 2
            c.pe_guard(bank)
            vm = None
            for kc in range(DCH):
                vm = p.op("tensor", lambda nc, kc=kc, bank=bank, wap=wap:
                          nc.tensor.matmul(
                              c.PSB[bank][:, :], wap[:, kc, :], attn16[:, kc, :],
                              start=(kc == 0), stop=(kc == DCH - 1)), inc=c.st)
            wb_used(c, woff, vm)
            p.wait("vector", c.st, vm)
            rxa = c.RX[:, oc, tt * TT:(tt + 1) * TT]
            vr = p.op("vector", lambda nc, bank=bank, rxa=rxa:
                      nc.vector.tensor_tensor(
                          rxa, rxa, c.PSB[bank][:, :], ALU.add), inc=c.sv)
            c.bank_consumed(bank, c.sv, vr)
        c._a_attn_use = c.st.n


SECTION_FUNCS["l0a"] = lambda c, first: emit_l0a(c)



SCR_SSM_IN = 31_900_000
SCR_SSM_H = 40_300_000
SCR_SSM_DT = 48_700_000
SCR_SSM_DTH = 0
SCR_SSM_B = 31_800_000
SCR_SSM_G = 13_600_000


def scrf(c, elem_off, rows, cols, row_stride=None, b=False):
    if row_stride is None:
        row_stride = cols
    return bass.AP(c.scrB if b else c.scr, elem_off,
                   [[row_stride, rows], [1, cols]])


def emit_l1b(c):
    p, dr = c.p, c.dr
    barrier(c)
    ln16 = kb(c, 0, [128, DCH, TT])
    sg16 = kb(c, 4096, [128, 8, TT])
    vck = p.dma("sync", c.SMF[:, 1168:1216], dr["l1b_cw"][:], c.dl)
    vck = p.dma("sync", c.SMF[:, 1216:1232], dr["l1b_cb"][:], c.dl)
    vck = p.dma("sync", c.SMF[:, 1232:1248], dr["l1b_bt"][:], c.dl)
    vck = p.dma("sync", c.SMF[:, 1248:1264], dr["l1b_skip"][:], c.dl)
    vck = p.dma("sync", c.SMF[:, 1536:1792], dr["l1b_alog"][:], c.dl)
    p.wait("scalar", c.dl, vck)
    va = p.op("scalar", lambda nc: nc.scalar.activation(
        c.SMF[:, 1536:1792], c.SMF[:, 1536:1792], AF.Exp), inc=c.ss)
    p.wait("vector", c.ss, va)
    p.op("vector", lambda nc: nc.vector.tensor_scalar(
        c.SMF[:, 1536:1792], c.SMF[:, 1536:1792], -1.0, None, ALU.mult), inc=c.sv)
    p.op("vector", lambda nc: nc.vector.memset(c.SMF[:, 1792:2048], 0.0), inc=c.sv)
    vconsts = c.sv.n
    p.wait("scalar", c.sv, vconsts)
    p.wait("tensor", c.sv, vconsts)
    acol = lambda ct, s: c.SMF[:, 1536 + ct * 16 + s:1536 + ct * 16 + s + 1]
    stcol = lambda ct, s: c.SMF[:, 1792 + ct * 16 + s:1792 + ct * 16 + s + 1]
    for tt in range(NTT):
        emit_ln_ff(c, tt, ln16)
        ln_done = c.sv.n
        p.wait("tensor", c.sv, ln_done)
        for oc in range(32):
            woff = (oc % 2) * 2048
            wb_guard(c, woff)
            wap, wv = stream_w(c, "l1b_Win", woff, 0, DCH, oc * 128, 128)
            p.wait("tensor", c.dl, wv)
            bank = oc % 2
            c.pe_guard(bank)
            vm = None
            for kc in range(DCH):
                vm = p.op("tensor", lambda nc, kc=kc, bank=bank, wap=wap:
                          nc.tensor.matmul(
                              c.PSB[bank][:, :], wap[:, kc, :], ln16[:, kc, :],
                              start=(kc == 0), stop=(kc == DCH - 1)), inc=c.st)
            wb_used(c, woff, vm)
            if oc < 16:
                slot = fa(c, 7680, [128, TT])
                p.wait("vector", c.st, vm)
                ve = p.op("vector", lambda nc, bank=bank, slot=slot:
                          nc.vector.tensor_copy(slot, c.PSB[bank][:, :]), inc=c.sv)
                c.bank_consumed(bank, c.sv, ve)
                p.wait("sync", c.sv, ve)
                dst = scrf(c, SCR_SSM_IN + (tt * 16 + oc) * (128 * TT), 128, TT)
                vs = p.dma("sync", dst, slot, c.dl)
                p.wait("vector", c.dl, vs)
            else:
                ct = oc - 16
                if ct % 8 == 0 and hasattr(c, "_b_sg_ship"):
                    p.wait("scalar", c.dl, c._b_sg_ship)
                p.wait("scalar", c.st, vm)
                ve = p.op("scalar", lambda nc, bank=bank, ct=ct:
                          nc.scalar.activation(sg16[:, ct % 8, :],
                                               c.PSB[bank][:, :],
                                               AF.Silu), inc=c.ss)
                c.bank_consumed(bank, c.ss, ve)
                if ct % 8 == 7:
                    p.wait("sync", c.ss, c.ss.n)
                    dst = bass.AP(c.scr.bitcast(BF16),
                                  SCR_SSM_G + tt * (128 * 8192)
                                  + (ct // 8) * 4096,
                                  [[8192, 128], [1, 4096]])
                    c._b_sg_ship = p.dma("sync", dst, kb(c, 4096, [128, 4096]),
                                         c.dl)
    barrier(c)
    h16 = kb(c, 0, [128, 16, TT])
    for tt in range(NTT):
        for ct in range(16):
            ibuf = fa(c, 0, [128, 514])
            vz = None
            if tt == 0:
                vz = p.op("vector", lambda nc, ibuf=ibuf: nc.vector.memset(
                    ibuf[:, 0:1], 0.0), inc=c.sv)
                p.wait("sync", c.sv, vz)
            if hasattr(c, "_b_ib_use"):
                p.wait("sync", c.sv, c._b_ib_use)
                p.wait("sync", c.ss, c._b_ib_use2)
            v1 = p.dma("sync", ibuf[:, 1:513],
                       scrf(c, SCR_SSM_IN + (tt * 16 + ct) * (128 * TT), 128, TT),
                       c.dl)
            if tt > 0:
                v1 = p.dma("sync", ibuf[:, 0:1],
                           scrf(c, SCR_SSM_IN + ((tt - 1) * 16 + ct) * (128 * TT)
                                + 511, 128, 1, row_stride=TT), c.dl, slow=True)
            if tt < NTT - 1:
                v1 = p.dma("sync", ibuf[:, 513:514],
                           scrf(c, SCR_SSM_IN + ((tt + 1) * 16 + ct) * (128 * TT),
                                128, 1, row_stride=TT), c.dl, slow=True)
            else:
                v1 = p.dma("sync", ibuf[:, 513:514],
                           scrf(c, SCR_SSM_IN + (tt * 16 + ct) * (128 * TT),
                                128, 1, row_stride=TT), c.dl, slow=True)
            p.wait("vector", c.dl, v1)
            if tt == NTT - 1:
                p.op("vector", lambda nc, ibuf=ibuf: nc.vector.memset(
                    ibuf[:, 513:514], 0.0))
            cw0 = c.SMF[:, 1168 + ct * 3:1169 + ct * 3]
            cw1 = c.SMF[:, 1169 + ct * 3:1170 + ct * 3]
            cw2 = c.SMF[:, 1170 + ct * 3:1171 + ct * 3]
            cbc = c.SMF[:, 1216 + ct:1217 + ct]
            conv = fa(c, 640, [128, TT])
            p.op("vector", lambda nc, ibuf=ibuf, conv=conv, cw0=cw0:
                 nc.vector.tensor_scalar(conv, ibuf[:, 0:512], cw0, None, ALU.mult))
            p.op("vector", lambda nc, ibuf=ibuf, conv=conv, cw1=cw1:
                 nc.vector.scalar_tensor_tensor(
                     conv, ibuf[:, 1:513], cw1, conv, ALU.mult, ALU.add))
            vcv = p.op("vector", lambda nc, ibuf=ibuf, conv=conv, cw2=cw2:
                       nc.vector.scalar_tensor_tensor(
                           conv, ibuf[:, 2:514], cw2, conv, ALU.mult, ALU.add),
                       inc=c.sv)
            c._b_ib_use = vcv
            p.wait("scalar", c.sv, vcv)
            hslot = fa(c, 1152, [128, TT])
            p.op("scalar", lambda nc, conv=conv, hslot=hslot, cbc=cbc:
                 nc.scalar.activation(hslot, conv, AF.Silu, bias=cbc))
            vh = p.op("scalar", lambda nc, hslot=hslot, ct=ct:
                      nc.scalar.copy(h16[:, ct, :], hslot), inc=c.ss)
            c._b_ib_use2 = vh
            p.wait("sync", c.ss, vh)
            vs = p.dma("sync",
                       scrf(c, SCR_SSM_H + (tt * 16 + ct) * (128 * TT), 128, TT),
                       hslot, c.dl)
            p.wait("scalar", c.dl, vs)
        p.wait("tensor", c.ss, c.ss.n)
        c.pe_guard(2)
        vm = None
        for kc in range(16):
            wb_guard(c, 3072)
            wap, wv = stream_w(c, "l1b_Wst", 3072, kc, 1, 16, 16)
            p.wait("tensor", c.dl, wv)
            vm = p.op("tensor", lambda nc, kc=kc, wap=wap: nc.tensor.matmul(
                c.PSB[2][0:16, :], wap[:, 0, :], h16[:, kc, :],
                start=(kc == 0), stop=(kc == 15)), inc=c.st)
            wb_used(c, 3072, vm)
        p.wait("vector", c.st, vm)
        bslot = fa(c, 0, [16, TT])
        vb2 = p.op("vector", lambda nc, bslot=bslot: nc.vector.tensor_copy(
            bslot, c.PSB[2][0:16, :]), inc=c.sv)
        c.bank_consumed(2, c.sv, vb2)
        p.wait("sync", c.sv, vb2)
        vs = p.dma("sync", scrf(c, SCR_SSM_B + tt * (16 * TT), 16, TT), bslot, c.dl)
        p.wait("vector", c.dl, vs)
        for ct in range(16):
            woff = (ct % 2) * 2048
            wb_guard(c, woff)
            wap, wv = stream_w(c, "l1b_Wt", woff, 0, 16, ct * 128, 128)
            p.wait("tensor", c.dl, wv)
            bank = ct % 2
            c.pe_guard(bank)
            vm = None
            for kc in range(16):
                vm = p.op("tensor", lambda nc, kc=kc, bank=bank, wap=wap:
                          nc.tensor.matmul(
                              c.PSB[bank][:, :], wap[:, kc, :], h16[:, kc, :],
                              start=(kc == 0), stop=(kc == 15)), inc=c.st)
            wb_used(c, woff, vm)
            p.wait("vector", c.st, vm)
            btc = c.SMF[:, 1232 + ct:1233 + ct]
            u = fa(c, 0, [128, TT])
            r = fa(c, 512, [128, TT])
            rn = fa(c, 1024, [128, TT])
            dts = fa(c, 1536, [128, TT])
            dth = fa(c, 2048, [128, TT])
            p.op("vector", lambda nc, bank=bank, u=u, btc=btc:
                 nc.vector.tensor_scalar(u, c.PSB[bank][:, :], btc, None, ALU.add),
                 inc=c.sv)
            c.bank_consumed(bank, c.sv, c.sv.n)
            p.op("vector", lambda nc, u=u, r=r: nc.vector.tensor_scalar(
                r, u, 0.0, None, ALU.max))
            vpre = p.op("vector", lambda nc, u=u, rn=rn: nc.vector.tensor_scalar(
                rn, u, -1.0, 0.0, ALU.mult, ALU.max), inc=c.sv)
            p.op("vector", lambda nc, u=u, r=r, rn=rn: nc.vector.tensor_tensor(
                u, r, rn, ALU.add), inc=c.sv)
            vau = c.sv.n
            p.wait("scalar", c.sv, vau)
            p.op("scalar", lambda nc, u=u: nc.scalar.activation(
                u, u, AF.Exp, scale=-1.0))
            vln = p.op("scalar", lambda nc, u=u: nc.scalar.activation(
                u, u, AF.Ln, bias=1.0), inc=c.ss)
            p.wait("vector", c.ss, vln)
            p.op("vector", lambda nc, u=u, r=r, dts=dts: nc.vector.tensor_tensor(
                dts, r, u, ALU.add))
            hbuf = fa(c, 2560, [128, TT])
            vhb = p.dma("sync",
                        scrf(c, SCR_SSM_H + (tt * 16 + ct) * (128 * TT), 128, TT),
                        hbuf, c.dl) if False else None
            vhb = p.dma("sync", hbuf,
                        scrf(c, SCR_SSM_H + (tt * 16 + ct) * (128 * TT), 128, TT),
                        c.dl)
            p.wait("vector", c.dl, vhb)
            vdth = p.op("vector", lambda nc, dts=dts, hbuf=hbuf, dth=dth:
                        nc.vector.tensor_tensor(dth, dts, hbuf, ALU.mult), inc=c.sv)
            p.wait("sync", c.sv, vdth)
            vs1 = p.dma("sync", scrf(c, SCR_SSM_DT + (tt * 16 + ct) * (128 * TT),
                                     128, TT), dts, c.dl)
            vs2 = p.dma("sync", scrf(c, SCR_SSM_DTH + (tt * 16 + ct) * (128 * TT),
                                     128, TT, b=True), dth, c.dl)
            p.wait("vector", c.dl, vs2)
    barrier(c)
    SELS = c.KB[0:16, 8192:10240]
    for half2 in range(2):
        vsl = p.dma("sync", c.WS[0:16, 0:1024],
                    dr["c_sels"][:, half2 * 1024:(half2 + 1) * 1024], c.dl)
        p.wait("scalar", c.dl, vsl)
        vsl2 = p.op("scalar", lambda nc, half2=half2: nc.scalar.copy(
            c.KB[0:16, 8192 + half2 * 1024:8192 + (half2 + 1) * 1024],
            c.WS[0:16, 0:1024]), inc=c.ss)
        p.wait("sync", c.ss, vsl2)
    p.wait("tensor", c.ss, c.ss.n)
    HT = 2048
    SCR_YG16 = 63_800_000
    scr16 = c.scr.bitcast(BF16)
    dtb = fa(c, 0, [128, HT])
    dthb = fa(c, 2048, [128, HT])
    dAb = fa(c, 4096, [128, HT])
    dBb = fa(c, 6144, [128, HT])
    Yb16 = kb(c, 4096, [128, HT])
    sgb = kb(c, 6144, [128, 4, TT])
    ygb = kb(c, 0, [128, HT])
    bt4 = c.SMB[0:16, 0:2048]
    for half in range(2):
        p.wait("scalar", c.st, c.st.n)
        for q2 in range(2):
            vb = None
            for tl in range(2):
                ttg = half * 4 + q2 * 2 + tl
                if hasattr(c, "_c_ws_use"):
                    p.wait("sync", c.ss, c._c_ws_use)
                vb = p.dma("sync", c.WS[0:16, tl * 512:(tl + 1) * 512],
                           scrf(c, SCR_SSM_B + ttg * (16 * TT), 16, TT), c.dl)
            p.wait("scalar", c.dl, vb)
            vbc = p.op("scalar", lambda nc, q2=q2: nc.scalar.copy(
                c.SMB[0:16, q2 * 1024:(q2 + 1) * 1024], c.WS[0:16, 0:1024]),
                inc=c.ss)
            c._c_ws_use = vbc
        p.wait("tensor", c.ss, c.ss.n)
        p.wait("vector", c.ss, c.ss.n)
        for ct in range(16):
            if hasattr(c, "_c_dt_use"):
                p.wait("sync", c.ss, c._c_dt_use[0])
                p.wait("sync", c.sv, c._c_dt_use[1])
            v1 = None
            for q in range(4):
                ttg = half * 4 + q
                v1 = p.dma("sync", fa(c, q * 512, [128, TT]),
                           scrf(c, SCR_SSM_DT + (ttg * 16 + ct) * (128 * TT),
                                128, TT), c.dl)
                v1 = p.dma("sync", fa(c, 2048 + q * 512, [128, TT]),
                           scrf(c, SCR_SSM_DTH + (ttg * 16 + ct) * (128 * TT),
                                128, TT, b=True), c.dl)
            p.wait("scalar", c.dl, v1)
            p.wait("vector", c.dl, v1)
            if hasattr(c, "_c_yg_out"):
                p.wait("gpsimd", c.sv, c._c_yg_out)
            vy0 = p.op("gpsimd", lambda nc: nc.gpsimd.memset(Yb16[:], 0.0),
                       inc=c.sg)
            vy = vy0
            vsc_prev = None
            for s in range(16):
                for q in range(4):
                    bank = 2 + (q % 2)
                    c.pe_guard(bank)
                    vbr = p.op("tensor", lambda nc, s=s, q=q, bank=bank:
                               nc.tensor.matmul(
                                   c.PSB[bank][:, :],
                                   SELS[:, s * 128:(s + 1) * 128],
                                   bt4[:, q * 512:(q + 1) * 512],
                                   start=True, stop=True), inc=c.st)
                    p.wait("vector", c.st, vbr)
                    vdb = p.op("vector", lambda nc, q=q, bank=bank:
                               nc.vector.tensor_tensor(
                                   dBb[:, q * 512:(q + 1) * 512],
                                   dthb[:, q * 512:(q + 1) * 512],
                                   c.PSB[bank][:, :], ALU.mult), inc=c.sv)
                    c.bank_consumed(bank, c.sv, vdb)
                p.wait("scalar", c.sg, vy)
                if vsc_prev is not None:
                    p.wait("scalar", c.sv, vsc_prev)
                vda = p.op("scalar", lambda nc, ct=ct, s=s:
                           nc.scalar.activation(dAb[:], dtb[:], AF.Exp,
                                                scale=acol(ct, s)), inc=c.ss)
                p.wait("vector", c.ss, vda)
                vsc = p.op("vector", lambda nc, ct=ct, s=s:
                           nc.vector.tensor_tensor_scan(
                               dAb[:], dAb[:], dBb[:], stcol(ct, s),
                               ALU.mult, ALU.add), inc=c.sv)
                p.wait("scalar", c.sv, vsc)
                p.op("scalar", lambda nc, ct=ct, s=s: nc.scalar.copy(
                    stcol(ct, s), dAb[:, HT - 1:HT]), inc=c.ss)
                p.wait("gpsimd", c.sv, vsc)
                vy = p.op("gpsimd", lambda nc: nc.gpsimd.tensor_tensor(
                    Yb16[:], Yb16[:], dAb[:], ALU.add), inc=c.sg)
            p.wait("sync", c.ss, c.ss.n)
            vh = None
            for q in range(4):
                ttg = half * 4 + q
                vh = p.dma("sync", fa(c, q * 512, [128, TT]),
                           scrf(c, SCR_SSM_H + (ttg * 16 + ct) * (128 * TT),
                                128, TT), c.dl)
                vh = p.dma("sync", sgb[:, q, :],
                           bass.AP(scr16,
                                   SCR_SSM_G + ttg * (128 * 8192) + ct * 512,
                                   [[8192, 128], [1, 512]]), c.dl)
            p.wait("vector", c.dl, vh)
            p.wait("vector", c.sg, vy)
            skc = c.SMF[:, 1248 + ct:1249 + ct]
            p.op("vector", lambda nc, skc=skc: nc.vector.scalar_tensor_tensor(
                dthb[:], dtb[:], skc, Yb16[:], ALU.mult, ALU.add))
            vyg = p.op("vector", lambda nc: nc.vector.tensor_tensor(
                ygb[:], dthb[:], sgb[:].rearrange("p a b -> p (a b)"),
                ALU.mult), inc=c.sv)
            c._c_dt_use = (c.ss.n, vyg)
            p.wait("sync", c.sv, vyg)
            vout = p.dma("sync",
                         bass.AP(scr16,
                                 SCR_YG16 + ct * (128 * 4096) + half * HT,
                                 [[4096, 128], [1, HT]]), ygb, c.dl)
            c._c_yg_out = c.sv.n
            p.wait("gpsimd", c.dl, vout)
            p.wait("vector", c.dl, vout)
    barrier(c)
    ygall = kb(c, 0, [128, 16, TT])
    for tt in range(NTT):
        vld = None
        for kc in range(16):
            vld = p.dma("sync", ygall[:, kc, :],
                        bass.AP(scr16, SCR_YG16 + kc * (128 * 4096) + tt * TT,
                                [[4096, 128], [1, TT]]), c.dl)
        p.wait("tensor", c.dl, vld)
        for oc in range(DCH):
            woff = (oc % 2) * 2048
            wb_guard(c, woff)
            wap, wv = stream_w(c, "l1b_Wout", woff, 0, 16, oc * 128, 128)
            p.wait("tensor", c.dl, wv)
            bank = oc % 2
            c.pe_guard(bank)
            vm = None
            for kc in range(16):
                vm = p.op("tensor", lambda nc, kc=kc, bank=bank, wap=wap:
                          nc.tensor.matmul(
                              c.PSB[bank][:, :], wap[:, kc, :],
                              ygall[:, kc, :],
                              start=(kc == 0), stop=(kc == 15)), inc=c.st)
            wb_used(c, woff, vm)
            p.wait("vector", c.st, vm)
            rxa = c.RX[:, oc, tt * TT:(tt + 1) * TT]
            vr = p.op("vector", lambda nc, bank=bank, rxa=rxa:
                      nc.vector.tensor_tensor(rxa, rxa,
                                              c.PSB[bank][:, :],
                                              ALU.add), inc=c.sv)
            c.bank_consumed(bank, c.sv, vr)
        p.wait("sync", c.st, c.st.n)


SECTION_FUNCS["l1b"] = lambda c, first: emit_l1b(c)



TTF = 256
DT_FF = 0.1


def _ff_j_body(c, p, it, j, integ, vbuf, h_sh, hm12, PL, SP,
               DTH, mu_c, sw_c, osw_c, CM):
    wb_guard(c, 0)
    wb_guard(c, 1024)
    wb_guard(c, 2048)
    wsw = []
    for pi in range(3):
        w_, v_ = stream_w(c, "ff_sw2", pi * 128, 0, 1,
                          pi * 1024 + j * 128, 128, rows=64)
        wsw.append(w_)
    wea = []
    for pi in range(3):
        w_, v_ = stream_w(c, "ff_ew2", 512 + pi * 512, 0, 4,
                          pi * 1024 + j * 128, 128)
        wea.append(w_)
    web = []
    for pi in range(3):
        w_, v_ = stream_w(c, "ff_eb2", 2048 + pi * 128, 0, 1,
                          pi * 1024 + j * 128, 128, rows=8)
        web.append(w_)
    p.wait("tensor", c.dl, v_)
    integ_j = integ[:, j, :]
    v_j = vbuf[:, j, :]
    tneg = SP(2)
    msrc12 = c.SMB[0:8, 0:512]
    P512 = lambda k: fa(c, 6144 + k * TTF, [128, 2 * TTF])
    WSL, WSR = c.WS[:, 0:256], c.WS[:, 256:512]
    WS512 = c.WS[:, 0:512]
    vm_last = None
    for pi in range(3):
        if it == 0 and pi == 0:
            continue
        c.pe_guard(pi)
        vm_last = p.op("tensor", lambda nc, pi=pi: nc.tensor.matmul(
            c.PSB[pi][:, :TTF], wsw[pi][:, 0, :], h_sh[:, :],
            start=True, stop=True), inc=c.st)
    for pi in range(3):
        if it == 0 and pi == 0:
            continue
        bank = 3 + pi
        c.pe_guard(bank)
        for kc in range(4):
            p.op("tensor", lambda nc, pi=pi, kc=kc, bank=bank:
                 nc.tensor.matmul(
                     c.PSB[bank][:, 0:512], wea[pi][:, kc, :], hm12[:, kc, :],
                     start=(kc == 0), stop=False))
        vm_last = p.op("tensor", lambda nc, pi=pi, bank=bank:
                       nc.tensor.matmul(
                           c.PSB[bank][:, 0:512], web[pi][:, 0, :], msrc12,
                           start=False, stop=True), inc=c.st)
    wb_used(c, 0, vm_last)
    wb_used(c, 1024, vm_last)
    wb_used(c, 2048, vm_last)
    if c._ff_body_done is not None:
        p.wait("scalar", c.sv, c._ff_body_done)
    p.wait("scalar", c.st, vm_last)

    def sb2b(pi, j=j):
        return c.SMF[:, CM + 64 + 8 * pi + j:CM + 65 + 8 * pi + j]
    if it != 0:
        va = p.op("scalar", lambda nc: nc.scalar.activation(
            PL(0), c.PSB[0][:, :TTF], AF.Sigmoid, bias=sb2b(0)), inc=c.ss)
        c.bank_consumed(0, c.ss, va)
        va12 = p.op("scalar", lambda nc: nc.scalar.activation(
            P512(3), c.PSB[3][:, 0:512], AF.Sigmoid), inc=c.ss)
        c.bank_consumed(3, c.ss, va12)
    vg = p.op("scalar", lambda nc: nc.scalar.activation(
        PL(2), c.PSB[2][:, :TTF], AF.Sigmoid, bias=sb2b(2)), inc=c.ss)
    c.bank_consumed(2, c.ss, vg)
    vg12 = p.op("scalar", lambda nc: nc.scalar.activation(
        P512(5), c.PSB[5][:, 0:512], AF.Sigmoid), inc=c.ss)
    c.bank_consumed(5, c.ss, vg12)
    p.op("scalar", lambda nc: nc.scalar.activation(
        PL(1), c.PSB[1][:, :TTF], AF.Exp, bias=sb2b(1)))
    vbsh = p.op("scalar", lambda nc: nc.scalar.activation(
        PL(1), PL(1), AF.Ln, bias=1.0), inc=c.ss)
    c.bank_consumed(1, c.ss, vbsh)
    p.op("scalar", lambda nc: nc.scalar.activation(
        WS512, c.PSB[4][:, 0:512], AF.Exp))
    vb12 = p.op("scalar", lambda nc: nc.scalar.activation(
        WS512, WS512, AF.Ln, bias=1.0), inc=c.ss)
    c.bank_consumed(4, c.ss, vb12)
    V = lambda fn, inc=None: p.op("vector", fn, inc=inc or c.sv)
    V(lambda nc: nc.vector.tensor_scalar(
        tneg, integ_j, -1.0, mu_c(j), ALU.mult, ALU.add))
    p.wait("vector", c.ss, vb12)
    if it == 0:
        V(lambda nc: nc.vector.tensor_tensor(PL(7), PL(1), tneg, ALU.mult))
        V(lambda nc: nc.vector.tensor_tensor(PL(3), WSL, tneg, ALU.mult))
        V(lambda nc: nc.vector.tensor_tensor(PL(4), WSR, tneg, ALU.mult))
    else:
        V(lambda nc: nc.vector.tensor_tensor(PL(7), PL(0), v_j, ALU.mult))
        V(lambda nc: nc.vector.tensor_tensor(PL(1), PL(1), tneg, ALU.mult))
        V(lambda nc: nc.vector.tensor_tensor(PL(7), PL(7), PL(1), ALU.add))
        V(lambda nc: nc.vector.tensor_tensor(PL(3), PL(3), v_j, ALU.mult))
        V(lambda nc: nc.vector.tensor_tensor(PL(4), PL(4), v_j, ALU.mult))
        V(lambda nc: nc.vector.tensor_tensor(PL(0), WSL, tneg, ALU.mult))
        V(lambda nc: nc.vector.tensor_tensor(PL(1), WSR, tneg, ALU.mult))
        V(lambda nc: nc.vector.tensor_tensor(PL(3), PL(3), PL(0), ALU.add))
        V(lambda nc: nc.vector.tensor_tensor(PL(4), PL(4), PL(1), ALU.add))
    w12 = c.SMF[:, 0:512]
    V(lambda nc: nc.vector.tensor_tensor(PL(2), PL(7), PL(2), ALU.mult))
    V(lambda nc: nc.vector.tensor_tensor(P512(3), P512(3), w12, ALU.mult))
    V(lambda nc: nc.vector.tensor_tensor(P512(5), P512(5), P512(3), ALU.mult))
    V(lambda nc: nc.vector.tensor_tensor(PL(3), PL(3), PL(4), ALU.add))
    V(lambda nc: nc.vector.tensor_scalar(PL(7), PL(7), sw_c, None, ALU.mult))
    V(lambda nc: nc.vector.scalar_tensor_tensor(
        PL(7), PL(3), osw_c, PL(7), ALU.mult, ALU.add))
    V(lambda nc: nc.vector.tensor_tensor(PL(7), PL(7), v_j, ALU.subtract))
    V(lambda nc: nc.vector.tensor_tensor(PL(7), PL(7), SP(3), ALU.mult))
    V(lambda nc: nc.vector.tensor_tensor(v_j, v_j, PL(7), ALU.add))
    V(lambda nc: nc.vector.tensor_tensor(PL(5), PL(5), PL(6), ALU.add))
    V(lambda nc: nc.vector.tensor_scalar(PL(5), PL(5), osw_c, None, ALU.mult))
    V(lambda nc: nc.vector.scalar_tensor_tensor(
        PL(5), PL(2), sw_c, PL(5), ALU.mult, ALU.add))
    V(lambda nc: nc.vector.tensor_tensor(PL(5), PL(5), DTH, ALU.mult))
    vdone = V(lambda nc: nc.vector.tensor_tensor(
        integ_j, integ_j, PL(5), ALU.add))
    c._ff_body_done = vdone


def emit_ff(c):
    p, dr = c.p, c.dr
    barrier(c)
    c._ff_body_done = None
    CM = 1024
    for name, off, n in (("ff_mu", CM, 8), ("ff_iw", CM + 8, 8),
                         ("ff_rb1", CM + 24, 2), ("ff_hb1", CM + 26, 2),
                         ("ff_eb1", CM + 28, 4), ("ff_qb1", CM + 32, 16),
                         ("ff_qb2", CM + 48, 8), ("ff_rb2", CM + 56, 8),
                         ("ff_sb2", CM + 64, 24)):
        vl = p.dma("sync", c.SMF[:, off:off + n], dr[name][:, 0:n], c.dl)
    vl = p.dma("sync", c.SMF[0:64, CM + 88:CM + 89], dr["ff_sb1"][:, 0:1], c.dl)
    vl = p.dma("sync", c.SMF[0:1, CM + 89:CM + 90], dr["ff_swt"][:, 0:1], c.dl)
    vl = p.dma("sync", c.SMF[0:1, CM + 90:CM + 91], dr["ff_hb2"][:, 0:1], c.dl)
    p.wait("scalar", c.dl, vl)
    p.wait("vector", c.dl, vl)
    mu_c = lambda j: c.SMF[:, CM + j:CM + 1 + j]
    iw_c = lambda j: c.SMF[:, CM + 8 + j:CM + 9 + j]
    vsmall(c, lambda nc: nc.vector.tensor_tensor(
        c.SMF[:, CM + 16:CM + 24], c.SMF[:, CM + 8:CM + 16],
        c.SMF[:, CM + 48:CM + 56], ALU.mult))
    vsw = p.op("scalar", lambda nc: nc.scalar.activation(
        c.SMF[0:1, CM + 89:CM + 90], c.SMF[0:1, CM + 89:CM + 90], AF.Sigmoid),
        inc=c.ss)
    p.wait("vector", c.ss, vsw)
    vsmall(c, lambda nc: nc.vector.tensor_copy(
        c.SMB[0:1, 0:1], c.SMF[0:1, CM + 89:CM + 90]))
    p.wait("tensor", c.sv, c.sv.n)
    c.pe_guard(5)
    vbb = p.op("tensor", lambda nc: nc.tensor.matmul(
        c.PSB[5][:, 0:1], c.ONEROW16[:], c.SMB[0:1, 0:1], start=True, stop=True),
        inc=c.st)
    p.wait("vector", c.st, vbb)
    vsmall(c, lambda nc: nc.vector.tensor_copy(
        c.SMF[:, CM + 91:CM + 92], c.PSB[5][:, 0:1]))
    vsmall(c, lambda nc: nc.vector.tensor_scalar(
        c.SMF[:, CM + 92:CM + 93], c.SMF[:, CM + 91:CM + 92], -1.0, 1.0,
        ALU.mult, ALU.add))
    c.bank_consumed(5, c.sv, c.sv.n)
    sw_c = c.SMF[:, CM + 91:CM + 92]
    osw_c = c.SMF[:, CM + 92:CM + 93]
    DTH = c.SMF[:, 1216:1472]
    integ = fa(c, 0, [128, DCH, TTF])
    vbuf = fa(c, 2048, [128, DCH, TTF])
    xf = fa(c, 4096, [128, DCH, TTF])
    PL = lambda k: fa(c, 6144 + k * TTF, [128, TTF])
    SP = lambda k: c.SMF[:, k * TTF:(k + 1) * TTF]
    xf16 = kb(c, 0, [128, DCH, TTF])
    ctx16 = kb(c, 2048, [128, 16, TTF])
    h_all = kb(c, 6144, [128, 4, TTF])
    hm12 = kb(c, 7168, [128, 4, 2 * TTF])
    h_sh = kb(c, 9216, [64, TTF])
    h_ha = kb(c, 9472, [128, 2, TTF])
    m1b = c.KB[:, 9984:9992]
    m2b = c.KB[:, 9992:10000]
    g16 = kb(c, 4096, [128, 16, TTF])
    pt3 = c.PT[:].rearrange("p (a b) -> p a b", b=128)
    NT_F = S // TTF
    for tt in range(NT_F):
        p.wait("vector", c.sg, c.sg.n)
        ln_done = emit_ln_ff(c, tt, xf16, tw=TTF, vout=xf)
        p.wait("vector", c.ss, ln_done)
        p.wait("tensor", c.ss, ln_done)
        p.wait("gpsimd", c.ss, ln_done)
        for j in range(DCH):
            p.op("vector", lambda nc, j=j: nc.vector.tensor_copy(
                integ[:, j, :], xf[:, j, :]))
        p.op("vector", lambda nc: nc.vector.memset(c.FA[:, 2048:4096], 0.0),
             inc=c.sv)
        init_done = c.sv.n
        p.wait("tensor", c.sv, init_done)
        for oc in range(2):
            wb_guard(c, 0)
            wap, wv = stream_w(c, "ff_rw1", 0, 0, DCH, oc * 128, 128)
            p.wait("tensor", c.dl, wv)
            c.pe_guard(oc)
            vm = None
            for kc in range(DCH):
                vm = p.op("tensor", lambda nc, kc=kc, oc=oc, wap=wap:
                          nc.tensor.matmul(
                              c.PSB[oc][:, :TTF], wap[:, kc, :], xf16[:, kc, :],
                              start=(kc == 0), stop=(kc == DCH - 1)), inc=c.st)
            wb_used(c, 0, vm)
            p.wait("scalar", c.st, vm)
            vh = p.op("scalar", lambda nc, oc=oc: nc.scalar.activation(
                h_all[:, oc, :], c.PSB[oc][:, :TTF], AF.Gelu,
                bias=c.SMF[:, CM + 24 + oc:CM + 25 + oc]), inc=c.ss)
            c.bank_consumed(oc, c.ss, vh)
        wb_guard(c, 1024)
        wap2, wv2 = stream_w(c, "ff_rw2", 1024, 0, 2, 0, 8)
        p.wait("tensor", c.dl, wv2)
        p.wait("tensor", c.ss, c.ss.n)
        for sub in range(2):
            c.pe_guard(2)
            vm = None
            for rc in range(2):
                vm = p.op("tensor", lambda nc, rc=rc, sub=sub, wap2=wap2:
                          nc.tensor.matmul(
                              c.PSB[2][:, 0:8], h_all[:, rc, sub * 128:(sub + 1) * 128],
                              wap2[:, rc, :], start=(rc == 0), stop=(rc == 1)),
                          inc=c.st)
            wb_used(c, 1024, vm)
            p.wait("vector", c.st, vm)
            lg = PL(0)[:, sub * 8:sub * 8 + 8]
            vlg = p.op("vector", lambda nc, lg=lg: nc.vector.tensor_tensor(
                lg, c.PSB[2][:, 0:8], c.SMF[:, CM + 56:CM + 64], ALU.add),
                inc=c.sv)
            c.bank_consumed(2, c.sv, vlg)
            p.wait("vector", c.sv, vlg)
            L8 = PL(1)
            for e in range(8):
                vsmall(c, lambda nc, e=e, lg=lg, L8=L8: nc.vector.tensor_copy(
                    L8[:, e * 8:(e + 1) * 8], lg))
            l3a = fa(c, 6144 + TTF, [128, 8, 8])
            l3b = bass.AP(c.FA, 6144 + TTF, [[8192, 128], [1, 8], [8, 8]])
            gt = fa(c, 6144 + 2 * TTF, [128, 8, 8])
            eq = fa(c, 6144 + 3 * TTF, [128, 8, 8])
            vsmall(c, lambda nc, l3a=l3a, l3b=l3b, gt=gt: nc.vector.tensor_tensor(
                gt, l3a, l3b, ALU.is_gt))
            vsmall(c, lambda nc, l3a=l3a, l3b=l3b, eq=eq: nc.vector.tensor_tensor(
                eq, l3a, l3b, ALU.is_equal))
            vsmall(c, lambda nc, eq=eq: nc.vector.tensor_tensor(
                eq.rearrange("p a b -> p (a b)") if False else
                fa(c, 6144 + 3 * TTF, [128, 64]),
                fa(c, 6144 + 3 * TTF, [128, 64]), c.TRI8[:, :], ALU.mult))
            vsmall(c, lambda nc, gt=gt, eq=eq: nc.vector.tensor_tensor(
                fa(c, 6144 + 2 * TTF, [128, 64]),
                fa(c, 6144 + 2 * TTF, [128, 64]),
                fa(c, 6144 + 3 * TTF, [128, 64]), ALU.add))
            rank = PL(4)
            vsmall(c, lambda nc, gt=gt, rank=rank: nc.vector.tensor_reduce(
                rank[:, 0:8], gt, axis=AX.X, op=ALU.add))
            vsmall(c, lambda nc, rank=rank: nc.vector.tensor_scalar(
                m1b[:, :], rank[:, 0:8], 0.0, None, ALU.is_equal))
            vsmall(c, lambda nc, rank=rank: nc.vector.tensor_scalar(
                m2b[:, :], rank[:, 0:8], 1.0, None, ALU.is_equal))
            m1f = PL(5)
            vsmall(c, lambda nc, m1f=m1f, rank=rank: nc.vector.tensor_scalar(
                m1f[:, 0:8], rank[:, 0:8], 0.0, None, ALU.is_equal))
            vsmall(c, lambda nc, m1f=m1f, rank=rank: nc.vector.tensor_scalar(
                m1f[:, 8:16], rank[:, 0:8], 1.0, None, ALU.is_equal))
            vsmall(c, lambda nc, m1f=m1f, lg=lg: nc.vector.tensor_tensor(
                m1f[:, 0:8], m1f[:, 0:8], lg, ALU.mult))
            vsmall(c, lambda nc, m1f=m1f, lg=lg: nc.vector.tensor_tensor(
                m1f[:, 8:16], m1f[:, 8:16], lg, ALU.mult))
            vsmall(c, lambda nc, m1f=m1f: nc.vector.tensor_reduce(
                m1f[:, 16:17], fa(c, 6144 + 5 * TTF, [128, 1, 8]),
                axis=AX.X, op=ALU.add))
            vsmall(c, lambda nc, m1f=m1f: nc.vector.tensor_reduce(
                m1f[:, 17:18], bass.AP(c.FA, 6144 + 5 * TTF + 8,
                                       [[8192, 128], [1, 1], [1, 8]]),
                axis=AX.X, op=ALU.add))
            vsmall(c, lambda nc, m1f=m1f: nc.vector.tensor_tensor(
                m1f[:, 18:19], m1f[:, 16:17], m1f[:, 17:18], ALU.subtract))
            vw1 = c.sv.n
            p.wait("scalar", c.sv, vw1)
            vsg = p.op("scalar", lambda nc, m1f=m1f: nc.scalar.activation(
                m1f[:, 19:20], m1f[:, 18:19], AF.Sigmoid), inc=c.ss)
            p.wait("vector", c.ss, vsg)
            vsmall(c, lambda nc, m1f=m1f, sub=sub: nc.vector.tensor_copy(
                c.SMB[:, 1024 + sub:1025 + sub], m1f[:, 19:20]))
            vsmall(c, lambda nc, m1f=m1f, sub=sub: nc.vector.tensor_scalar(
                c.SMB[:, 1026 + sub:1027 + sub], m1f[:, 19:20], -1.0, 1.0,
                ALU.mult, ALU.add))
            vsmall(c, lambda nc, sub=sub: nc.vector.tensor_copy(
                c.SMB[:, 1028 + sub * 16:1036 + sub * 16], m1b[:, :]))
            vsmall(c, lambda nc, sub=sub: nc.vector.tensor_copy(
                c.SMB[:, 1036 + sub * 16:1044 + sub * 16], m2b[:, :]))
        p.wait("tensor", c.sv, c.sv.n)
        c.pe_guard(6)
        vt = None
        for sub in range(2):
            vt = p.op("tensor", lambda nc, sub=sub: nc.tensor.transpose(
                pt3[0:8, sub, :], c.SMB[:, 1028 + sub * 16:1036 + sub * 16],
                c.IDT[:]))
            vt = p.op("tensor", lambda nc, sub=sub: nc.tensor.transpose(
                pt3[0:8, 2 + sub, :], c.SMB[:, 1036 + sub * 16:1044 + sub * 16],
                c.IDT[:]))
            vt = p.op("tensor", lambda nc, sub=sub: nc.tensor.transpose(
                pt3[0:1, 4 + sub, :], c.SMB[:, 1024 + sub:1025 + sub],
                c.IDT[:]))
            vt = p.op("tensor", lambda nc, sub=sub: nc.tensor.transpose(
                pt3[0:1, 6 + sub, :], c.SMB[:, 1026 + sub:1027 + sub],
                c.IDT[:]), inc=c.st if sub == 1 else None)
        p.wait("vector", c.st, vt)
        p.op("vector", lambda nc: nc.vector.tensor_copy(
            c.SMB[0:8, 0:256], pt3[0:8, 0:2, :]))
        p.op("vector", lambda nc: nc.vector.tensor_copy(
            c.SMB[0:8, 256:512], pt3[0:8, 2:4, :]))
        p.op("vector", lambda nc: nc.vector.tensor_copy(
            c.SMB[0:1, 512:768], pt3[0:1, 4:6, :]))
        vev = p.op("vector", lambda nc: nc.vector.tensor_copy(
            c.SMB[0:1, 768:1024], pt3[0:1, 6:8, :]), inc=c.sv)
        c.bank_consumed(6, c.sv, vev)
        p.wait("tensor", c.sv, vev)
        c.pe_guard(3)
        c.pe_guard(4)
        p.op("tensor", lambda nc: nc.tensor.matmul(
            c.PSB[3][:, 0:TTF], c.ONEROW16[:], c.SMB[0:1, 512:768],
            start=True, stop=True))
        vwr = p.op("tensor", lambda nc: nc.tensor.matmul(
            c.PSB[4][:, 0:TTF], c.ONEROW16[:], c.SMB[0:1, 768:1024],
            start=True, stop=True), inc=c.st)
        p.wait("scalar", c.st, vwr)
        p.op("scalar", lambda nc: nc.scalar.copy(SP(0), c.PSB[3][:, 0:TTF]))
        vwr2 = p.op("scalar", lambda nc: nc.scalar.copy(SP(1), c.PSB[4][:, 0:TTF]),
                    inc=c.ss)
        c.bank_consumed(3, c.ss, vwr2)
        c.bank_consumed(4, c.ss, vwr2)
        w1rep, w2rep = SP(0), SP(1)
        for it in range(2):
            nkc = DCH if it == 0 else 16
            if it == 1:
                p.wait("scalar", c.sv, c.sv.n)
                p.wait("scalar", c.sg, c.sg.n)
                for j in range(DCH):
                    p.op("scalar", lambda nc, j=j: nc.scalar.copy(
                        ctx16[:, j, :], integ[:, j, :]))
                for j in range(DCH):
                    p.op("scalar", lambda nc, j=j: nc.scalar.copy(
                        ctx16[:, 8 + j, :], vbuf[:, j, :]), inc=c.ss)
                p.wait("tensor", c.ss, c.ss.n)
            rhs_t = (lambda kc: xf16[:, kc, :]) if it == 0 else \
                (lambda kc: ctx16[:, kc, :])
            for oc in range(4):
                wb_guard(c, 0)
                wap, wv = stream_w(c, "ff_ew1", 0, 0, nkc, oc * 128, 128)
                p.wait("tensor", c.dl, wv)
                bank = oc % 2
                c.pe_guard(bank)
                vm = None
                for kc in range(nkc):
                    vm = p.op("tensor", lambda nc, kc=kc, bank=bank, wap=wap,
                              rhs_t=rhs_t, nkc=nkc: nc.tensor.matmul(
                                  c.PSB[bank][:, :TTF], wap[:, kc, :], rhs_t(kc),
                                  start=(kc == 0), stop=(kc == nkc - 1)), inc=c.st)
                wb_used(c, 0, vm)
                p.wait("scalar", c.st, vm)
                vh = p.op("scalar", lambda nc, oc=oc, bank=bank: nc.scalar.activation(
                    h_all[:, oc, :], c.PSB[bank][:, :TTF], AF.Gelu,
                    bias=c.SMF[:, CM + 28 + oc:CM + 29 + oc]), inc=c.ss)
                c.bank_consumed(bank, c.ss, vh)
            wb_guard(c, 1024)
            wap, wv = stream_w(c, "ff_sw1", 1024, 0, nkc, 0, 64)
            p.wait("tensor", c.dl, wv)
            c.pe_guard(2)
            vm = None
            for kc in range(nkc):
                vm = p.op("tensor", lambda nc, kc=kc, wap=wap, rhs_t=rhs_t,
                          nkc=nkc: nc.tensor.matmul(
                    c.PSB[2][0:64, :TTF], wap[:, kc, :], rhs_t(kc),
                    start=(kc == 0), stop=(kc == nkc - 1)), inc=c.st)
            wb_used(c, 1024, vm)
            p.wait("scalar", c.st, vm)
            vh = p.op("scalar", lambda nc: nc.scalar.activation(
                h_sh[:, :], c.PSB[2][0:64, :TTF], AF.Gelu,
                bias=c.SMF[0:64, CM + 88:CM + 89]), inc=c.ss)
            c.bank_consumed(2, c.ss, vh)
            hrhs = (lambda kc: xf16[:, kc, :]) if it == 0 else \
                (lambda kc: ctx16[:, kc, :])
            for oc in range(2):
                wb_guard(c, 2048)
                wap, wv = stream_w(c, "ff_hw1", 2048, 0, DCH, oc * 128, 128)
                p.wait("tensor", c.dl, wv)
                c.pe_guard(3)
                vm = None
                for kc in range(DCH):
                    vm = p.op("tensor", lambda nc, kc=kc, wap=wap, hrhs=hrhs:
                              nc.tensor.matmul(
                        c.PSB[3][:, :TTF], wap[:, kc, :], hrhs(kc),
                        start=(kc == 0), stop=(kc == DCH - 1)), inc=c.st)
                wb_used(c, 2048, vm)
                p.wait("scalar", c.st, vm)
                vh = p.op("scalar", lambda nc, oc=oc: nc.scalar.activation(
                    h_ha[:, oc, :], c.PSB[3][:, :TTF], AF.Gelu,
                    bias=c.SMF[:, CM + 26 + oc:CM + 27 + oc]), inc=c.ss)
                c.bank_consumed(3, c.ss, vh)
            wb_guard(c, 2048)
            wap, wv = stream_w(c, "ff_hw2", 2048, 0, 2, 0, 1)
            p.wait("tensor", c.dl, wv)
            p.wait("tensor", c.ss, c.ss.n)
            c.pe_guard(3)
            vm = None
            for rc in range(2):
                vm = p.op("tensor", lambda nc, rc=rc, wap=wap: nc.tensor.matmul(
                    c.PR[0:1, 0:TTF], wap[:, rc, :], h_ha[:, rc, :],
                    start=(rc == 0), stop=(rc == 1)), inc=c.st)
            wb_used(c, 2048, vm)
            p.wait("scalar", c.st, vm)
            vhr = p.op("scalar", lambda nc: nc.scalar.activation(
                c.SMB[0:1, 1044:1044 + TTF], c.PR[0:1, 0:TTF], AF.Sigmoid,
                bias=c.SMF[0:1, CM + 90:CM + 91]), inc=c.ss)
            c.bank_consumed("PR", c.ss, vhr)
            p.wait("tensor", c.ss, vhr)
            c.pe_guard(3)
            vbr = p.op("tensor", lambda nc: nc.tensor.matmul(
                c.PSB[3][:, 0:TTF], c.ONEROW16[:], c.SMB[0:1, 1044:1044 + TTF],
                start=True, stop=True), inc=c.st)
            p.wait("scalar", c.st, vbr)
            p.op("scalar", lambda nc: nc.scalar.copy(
                SP(3), c.PSB[3][:, 0:TTF]))
            vhrep = p.op("scalar", lambda nc: nc.scalar.activation(
                DTH, c.PSB[3][:, 0:TTF], AF.Copy, scale=DT_FF), inc=c.ss)
            c.bank_consumed(3, c.ss, vhrep)
            if not hasattr(c, "_ff_exp8"):
                vl8 = p.dma("sync", c.WS[0:8, 0:512], dr["c_exp8"][:], c.dl)
                p.wait("scalar", c.dl, vl8)
                vl8b = p.op("scalar", lambda nc: nc.scalar.copy(
                    c.SMB[0:8, 1300:1812], c.WS[0:8, 0:512]), inc=c.ss)
                p.wait("sync", c.ss, vl8b)
                p.wait("tensor", c.ss, vl8b)
                c._ff_exp8 = True
            p.wait("tensor", c.ss, c.ss.n)
            msk12 = c.SMB[0:8, 0:512]
            for hc in range(4):
                bank = 4 + (hc % 2)
                c.pe_guard(bank)
                vmk = p.op("tensor", lambda nc, hc=hc, bank=bank:
                           nc.tensor.matmul(
                               c.PSB[bank][:, 0:512],
                               c.SMB[0:8, 1300 + hc * 128:1300 + (hc + 1) * 128],
                               msk12, start=True, stop=True), inc=c.st)
                p.wait("vector", c.st, vmk)
                p.op("vector", lambda nc, hc=hc, bank=bank:
                     nc.vector.tensor_tensor(
                         hm12[:, hc, 0:TTF], h_all[:, hc, :],
                         c.PSB[bank][:, 0:TTF], ALU.mult))
                vhm = p.op("vector", lambda nc, hc=hc, bank=bank:
                           nc.vector.tensor_tensor(
                               hm12[:, hc, TTF:2 * TTF], h_all[:, hc, :],
                               c.PSB[bank][:, TTF:2 * TTF], ALU.mult), inc=c.sv)
                c.bank_consumed(bank, c.sv, vhm)
            p.wait("tensor", c.sv, c.sv.n)
            for j in range(DCH):
                _ff_j_body(c, p, it, j, integ, vbuf, h_sh, hm12, PL, SP,
                           DTH, mu_c, sw_c, osw_c, CM)
            p.wait("scalar", c.sv, c.sv.n)
            p.wait("tensor", c.sv, c.sv.n)
        p.wait("scalar", c.sv, c.sv.n)
        for j in range(DCH):
            p.op("scalar", lambda nc, j=j: nc.scalar.copy(
                ctx16[:, j, :], integ[:, j, :]), inc=c.ss if j == DCH - 1 else None)
        p.wait("tensor", c.ss, c.ss.n)
        for oc in range(16):
            woff = (oc % 2) * 2048
            wb_guard(c, woff)
            wap, wv = stream_w(c, "ff_qw1", woff, 0, DCH, oc * 128, 128)
            p.wait("tensor", c.dl, wv)
            bank = oc % 2
            c.pe_guard(bank)
            vm = None
            for kc in range(DCH):
                vm = p.op("tensor", lambda nc, kc=kc, bank=bank, wap=wap:
                          nc.tensor.matmul(
                              c.PSB[bank][:, :TTF], wap[:, kc, :], ctx16[:, kc, :],
                              start=(kc == 0), stop=(kc == DCH - 1)), inc=c.st)
            wb_used(c, woff, vm)
            p.wait("scalar", c.st, vm)
            vh = p.op("scalar", lambda nc, oc=oc, bank=bank: nc.scalar.activation(
                g16[:, oc, :], c.PSB[bank][:, :TTF], AF.Gelu,
                bias=c.SMF[:, CM + 32 + oc:CM + 33 + oc]), inc=c.ss)
            c.bank_consumed(bank, c.ss, vh)
        p.wait("tensor", c.ss, c.ss.n)
        for oc in range(DCH):
            woff = (oc % 2) * 2048
            wb_guard(c, woff)
            wap, wv = stream_w(c, "ff_qw2", woff, 0, 16, oc * 128, 128)
            p.wait("tensor", c.dl, wv)
            bank = oc % 2
            c.pe_guard(bank)
            vm = None
            for kc in range(16):
                vm = p.op("tensor", lambda nc, kc=kc, bank=bank, wap=wap:
                          nc.tensor.matmul(
                              c.PSB[bank][:, :TTF], wap[:, kc, :], g16[:, kc, :],
                              start=(kc == 0), stop=(kc == 15)), inc=c.st)
            wb_used(c, woff, vm)
            p.wait("vector", c.st, vm)
            rxa = c.RX[:, oc, tt * TTF:(tt + 1) * TTF]
            p.op("vector", lambda nc, oc=oc, bank=bank: nc.vector.scalar_tensor_tensor(
                PL(0), c.PSB[bank][:, :TTF], iw_c(oc), xf[:, oc, :],
                ALU.mult, ALU.add), inc=c.sv)
            c.bank_consumed(bank, c.sv, c.sv.n)
            vr = p.op("vector", lambda nc, oc=oc, rxa=rxa:
                      nc.vector.scalar_tensor_tensor(
                          rxa, PL(0), c.SMF[:, CM + 16 + oc:CM + 17 + oc],
                          rxa, ALU.bypass, ALU.add) if False else
                      nc.vector.scalar_tensor_tensor(
                          rxa, PL(0),
                          c.SMF[:, CM + 16 + oc:CM + 17 + oc],
                          rxa, ALU.add, ALU.add), inc=c.sv)


SECTION_FUNCS["ff"] = lambda c, first: emit_ff(c)
SECTION_FUNCS["ff2"] = lambda c, first: emit_ff(c)


def kernel(**inputs):
    consts = host_constants()
    in_maps = [prep_core_inputs(cc, inputs, consts) for cc in range(NCORES)]
    nc = build_full()
    res = run_bass_kernel_spmd(nc, in_maps, core_ids=list(range(NCORES)))
    outs = [np.ascontiguousarray(np.asarray(r["out_T"]).T) for r in res.results]
    return np.stack(outs, axis=0).astype(np.float32)
